# version 17
# speedup vs baseline: 1.0447x; 1.0354x over previous
import base64
import contextlib
import zlib

import numpy as np
import ml_dtypes

import concourse.bass as bass
import concourse.mybir as mybir
from concourse.bass_utils import run_bass_kernel_spmd
from concourse.alu_op_type import AluOpType

F32 = mybir.dt.float32
BF16 = mybir.dt.bfloat16
AF = mybir.ActivationFunctionType
AX = mybir.AxisListType.X
BF = ml_dtypes.bfloat16

D = 2048
DC = 16
NH = 16
HD = 128
NT = 1024
TB = 2048
NE = 16
IC = 11
SCALE = float(1.0 / np.sqrt(HD))

C_PARTS = 3
_BLOB = "eNo0mwW0VeXW91d3INjdig12zWmLrdiiYmNdu7tbsTuugVe9ducVuwsVURRBkY7THd9vnnd84x3j9bLP3muvNeMf85m7yMu0dsM89xI3DQs38vMqK/M6isM8iv3IC+so8cIy5s+B61RBUjpRmmZZVEZRngVZ6aVFHFaun3tB5tWpE3tFHXCRKgh936niNI6juMj90Clrt8jyIsqd1Od9acU1AjeosqKOE16LyzLyo9J3Y9/1Ui/zwyjwCjdxy8j+kLhJVju+lyRuHUZ8PCuSOisr/hn7TuJ6mVs6Re5lke+lQZ0nSV4XeVKGYe1w764fxbFTuG5ROm7oh7XvZGlVx3mdFHUZeUUUlR5fFAa5Hzhh5JVcKChD1wmrMvbjzCvTygmKuirqrPK9KEi9NK9Kxy8ylxh5VRm4URkEcRBmee7WQRnnTuVUXlQXXsUDOkWaxX4S+GGeEIakjpwyz9LM90Le5ydhEuRhmXhZ4gZBFCR+GAZB7rlZXDtB6iSZk0dOXdR+VvqEzEvrOKzj2uWpwtCpXT8p4opkJmldEv3MryI3TQonC3PHiQMS7cQ1YYsSgpzEVVBmqeNnsVuWQei5tZcHHoGMM1KexWFQEHwvDsh+GvplGOcJ+XP9OiCFXshHg8glwpkXVZ5fREXgpX5V+5VFPqjLzClLP44KL8yqzE8pijitoyCKnSpPCWKcp4HjZETec9wsqPySnAVhXRBAMl55VFTBleuavDgkJSHCZUIAwjrL8irw8tivk4xscUky7yaeU4V+nuVB7PlpFFMSfu2XZVXwBTmXDgtyRpC9qi74wipLc4d8RX7mFUQ6jSiFqg5LLwzJqZfEXuYkhUeK6jyNUi/0iizJvYq6rAvfD906CbOCeiBvaVAkkUtNFoQxyNKYF4s0z8og9d3aKiUsHIKXuhX9FvhJFUTEIqYSg8LhOlWQkTzXpSUSJ0yLNHWt1qOsrkixx/8LSi/OsyR2KYowzeqA/svr0EJKidd+HiVpYhXtV3Hg5EmRhlWSOxld6OelU/o5WSh4YMd3uELtUeVFxuPkReASfv6VBPR4VCbUK9nznDhLi8T6MArCKiwq3h1TMEHglnRfEjluXOahnwVOXXlZGEWUE+1V0u5+kfpJRJllXkDXem5MXsoidIM8oUbiMigyIMan4GvKJchrvsWumUVhTJcmTlanaZFHVUQWC58yoO2zNMzD2E9jP0hLq9Yq8OvM8fl2kk9nUFw+cSpINRnMgwpYSxOnyECUinKMg4zqrt3cp76pTercjUlLCvLVfk08otyNUwcsCP04SIg39eN4BCQr7eZL+qsKHS9PywIgyYOE8ICKMV0SlESdjKZh4gdxEsRZAfiVDnCW1LlbeSWNHaa+kwJcRVl6VRG7mZNGvht6nhOWhlRVHrk50S0ccLjKSbyfchdBChrmDi85BX0GuPlZURGdIM+BuIJGIO1uXWU51+bZ+M4qdK1JKnCSiuQ2XWAsrOKKNBV55SR1SK3HZcaVUvLugnGAaZZ4ZAPoJodVHDoUhp+kFcgY5h6ZqTz6NimpcZ93p3HOfSU0XMjjFDlFDewlfukWXlQ4sbUXGO3WfHNWOZ5HctMcnAqJVpwmqdVFVHJJ7tb1XGslABYcqvMsdAur5Yjo0SRJSKGAj0UYpXwF5RB5tR/HFchSpWQsLnyaNoW17DGIPk1HUeWRn1Z+AnbTYVXMQ4ReRitT5/zF7t+4jj/4HpjBgyXgB/mgghNaD0yMvdgDngBe6MOnPoguIB+CeVQfCUgCGIBy4sFqJ6IIM8oyjst04CoZXU+vECb6payywE9hIriLwg0NukB4N0myoAj5bwWSu3kObEHFlRF06tHThesAVAkZSbh8kYKXdE4O/sQlHEFHAVKEqSwo0qrIC3g5SSLaAmaP/cz1fOO61ONb3dQD00KglpCXWZhUjuPyoBBIFVBaZQl+wT0lCAvC1dQdzxIEJbUNi5cJ4Sm5ZOCmRZgDCWECKljrpqGbhGkJOqWgXuEQuNCPfLiHDDvQppOXdR3CehVAnpDZ0uUea3guqqsyCRKgLaF+AQVQJuKZasiloFPzBELJ4qxKUR9hWHiGdyktY2LAQ30UQJxhfOj6boVQiTJkS+7VGX0RwJMBtJagSyDstE6jyPHLgtbJ48gDIamIwPLkubwOw9dkpUwoQY+EJ2VGUmL4FILmj+QONPDTJEfQlCn9S2EEVGZC3XO9GtkC47ugl5tRo6CJl1I5VigEFTonKEXJk7uojDLnL/APZEGyIhAWzEIugJ5wYsCj0n0VKAuIwa+O4WoYcO2c6iVTbhjUGVFKSC/QCGc4Me1SeZQs1R9UtUO3eA7dEiKT0F2hoXCBeIGWE8rfg4m8DI73anSMC0fwnaEBILeObqG9uTeEBYrRJ89wTxzULqoOpgemHEgrTMMUTVL6KaUT0JLWPCH5r0oygTSEJBLHcKiED62vnYB2s2KgCslrCab5CVH2aU4Kmh6jF/lPlvoe1VbAdWXlgy8x30gAk5CsZxnZjirfzTODTBqKJydOcD/4BQ6A2lZcYGpNcYD6aNcCdZfHYZaaQnCDDJxApdgtF9bpXCtAYIGzEAilm+SAnlNZcxIHL/GJQF3H5Df3Im4cFvasM2rwkZL2kGpBDm5VMQzu0+mOC55DKtbVFYIT7q1MGPN+boIncYHoiNuu4Z8AbPH8yKWCKMc0yCoKy4PnTWGHDpfIaeqKGFC+FdCVJnBmFGQZyaYMoUYPuqSMktyHWpHa0Cx6i8asAnSwG4YVhBKaBMiIfG5ZSBBccekjX9BF9K3pSIAt5qoIZMQdCp4nh/kIfQQJUbK0NlqMloBLSFtcEQiH4g5o2Lqu4GbkD1yWQL2G3wCpm+ZlwaNQXz5B8tD/hVOjTNAvHu9JHJrYc8gPH8F4INO5Wd5Y8WeXduW+qObUiq1KeK4ydHiQwJiBR6rQMDSfS7Wg+SJzJQH+hFh6TurSz0TGJ02EM4ihPeQuKMK1IoMoF5BPChiytk4ErdHoDhAHnQExyH3wGT2NswkM0H1UXwKege90RMqXmr0J05oXYdSY2BFY10990A8Y4oH4pOfzOXipgDvRLwUFy31XXm1R5u9+AjLEKU+MSi0A8aCCdIIS6I7IFmqYygXMQCCyBsw7eVYjNito0cU3pQjGNADN0W+QDsXFRWE5Srey6kszbj2AkuEf0B2Nax+sYVOndJ06x7rwMBWWC+mNpi1y6iNE/7huCiMhFsIsS+3D/MNJfGqazqjJakFVUw3UulMU1CwZ8eH2ApGIPMsrn8qiN/gHFjGEyJAtNDGwj+qxHACEcAsUAjbxMYygyz9RSK5hRZ1iWngunhrCRpA6qGTHHGeNLHQLPFLiQj+wpPUBX4ZhMVLN3Azc9ww7Db48DC2iD1UKErg59ZQXKGCfSJaIFVxZjt5BJhX2L7o39gYgrkB6lygauA9DBiPXIFRseiyiV3AOCX4UwYSaBCMhNsg4G6Af0KO2dvE8fAzKEGeG4oEQyLuPMHUiDJKFnS6psCeYnbREvKF6sYxgWY77pfiTEjEcBXGdx1YhpWtMblq58jE4ae6bH0UumMaCE3IkM5xAp3kYBfQDfEm+3cRHnDkUmeEi31ib8qZJSmoTh1mXMUWEIM49agPFlhsEwi1uam3s5XQwmEJDWYcjvWnIsnKMp+s4QBnjnR3aDmdJvq2H8eFlEVBBUFUCIbgu8ERfoRlTbnVAUsSACXhGgB1sVB35Zl+gQ2iJCg9DJDtkWqXggGeG3+cpXcQACapR1l6VlXyQYNFJKfxSQD8oCsAdRk/t6UBJH3+UR8BgmvDN0EuFM/cK6/cUOc71eSQyX4YRbhfbhL3kpkyfw5/ob2jKrHBpEqKM4TPyRnThz9xPqGNkSopaR3Ciq6EyE3B8U5XCr661d458CyPimZKD0AxcVtpNVYYZJVIXpQrUeEgjtA2Vie6JzCfSvy7mkubOqG7PeBl2xINY3RDwPDaYSLG5Bh95XEDMuDj0C5UJTBuROIZHDsEi1a6TG/36OczGtxeOkT1ci8d0uH0gBW8RcVdeadabPiERBIeOr2wcwAcsuGaXanKHHYojVAWM4QAL2QCQ+PCeXwARboH1A+CBH4gSA+LAPgh9zCIwZ0bUXEJU4054aNiVR3DQLTUS3jMKLqh8woqiNC8E+0aEEw0EA4fmSaB8MJGiDOvY7o9L0fM8LCRfRCApJhKTgm/iEVEZmaFammGqkbY5ehf9jPmkOWEb8AW8AfxhL7fGx6ICEt4aGBSFGc6KzxIHug0AoJ/gUYDEeNRAK8SCJ5AO9Jx6sHcN/Dq0Yg4UppRRbBBJHKIB/DKv7kIQaN06QfrQMCndbYrGo0EikCXhNboehkEVoCDxwEhic+cuPtPILhz4GBVZE07PqhF5V3Obtd0NhWLeOTCjCZ/4NYyHwhqoKD4M2lJdBayJwMXmc78OUQvdLPBi8kQq/Zqbo9p9rwA5jTRDqjK1MQY6HVk74Ecj2IQ28AxY3MJGfF5eQvFe4LgJlg076P9fvwJYeOQInoGRDdZDz3xNkaOFSTzFRafypIFLmdKfHhRSuSWoD2OEIBTmkKTg02sbIJC4lDCVhvVgdYknC2ya6ZgegEOLLIA/IlMhaYHlQMcWIHRkbhSvG9LNntUOl6bXUvxLYSOAeEAp4hodM3xQHjwWoyWgMsDEDUw9Ra7xaoosrH0w2z7GLZM8k9+GqNQH0fTcsOA9ZDKHwrPChcIcxI0TIkMAZoC+QLVhkjBaVUp7+yHAABzjylDeqEra3It88ki5w+mBzbuIHjyN4CfOEc4EIUvy6DMArMSSJ4jSOKAzQ1gnDyjM2kDLw+JgIiIoAsmCWgfb8H8+/jw0WonxXphbM482AcAqkpo4iAIKnOYwssgQCIAhooFUeMQLL5sihO1/xYC1lQcaHDOOhq49itbF+CYhWAR5FLQlXgB1atGGwWKcBC6BSq0AAiSuV0Y26qR/nAAkzHCkiIEcAwXaOOB6xZ+MnmKaB+1Nq0M6ifUUHIUWIEyuB8TZPJkeIVe8gufkskVW0E+Z6/qmTWkFVBMKoypKrxiYWePFzT3yVhQ3vgzjGIKT5nK5AWgGWqYygyjEg1NZoHvsAQtlhCXCwvLFaG6kKlo69E2FG4AFpDL0gxQAQXGGyAZkJNcjmzRBZgoINvFs4EXADWtsssQ3xxA5VJyiVKM0QPIin2y8nJSkmd6uCA0v+EWeRTZTILkUJB1A9NEfNRIcWR1Zv4eYjSyi7HCPMDHPht2GdDPfPhqh4UGWwL6GPGDz8VKmo2Owvk6IEu2CsgNIUVMBUsS1rvKDKgwj17QqBpCmtfoxE42/pqUDtDQcgQ4iLZAEBIaWyvg3AODQVrAuPtatkUo4BSoaRUyF1rXv11gQQAyORuHHeBqwoEbMoul4GlKKWCuxiBQzkgf8QgeDyzYc4gasF3iuwhqbJ0lxd0lG4SBrEY2YB6rS1Dw0RBk7vuvgyhLKjpsM/SLOcUmVVQwBR4FVUFmUcivmY8AjKDF17E4TusaBWsBt6HXgyf0cZsTcxxWdTCc55s6SErVAa4OFFaRIO5LOkmjZVM/sEF4aVoRZ4CGvdE0NU1cDFVnbmJqM4zX4cvN9IIlXGnz7GEQauQpKhO/AwBwUTwqoIwR/YlLswC+lKRPaKqGz6FYfE+ZU3BUtTyPEbmkKJrOZcR5UEVLPjhPwoEVdhgO0Esa4TmQNEQY4EKCAZUJ1xJheEgOHVhG1GSPZ4Po8pF4pF/6bUapUnEMh0OU+pemlgArVXBVRTmEizkAN7rMmUYgxZ2D4gtJBBtJWtCoeCoMGlg4wq5dF0LLNsRIb+ic2wYYgMKEucpTcI58wM8jYxGwCkI0mRiQ6qETkHVlFsiY2r+SxQCEogLfQzoCNEUM9MMiszTAiD2NEJa4WT4RNpbmhD6jeZAiAjNusaWQg3udPQAysAOKjfgvkp2+jA6QR6QU2aFOwyrVZOeKngL69mj4YwC2bdkRm3RCfOCC43g5G8FVYBQ+hV1uBIeNim9gn+CwKJDJiz2nuwoDUtZFZ5AKfwA/Ey6fBGfQQnI7nD0yTewY+9H+E8AYn6zI3lPMQu1FA+VaEj57MgtxOUuD12AXTClMKptfRp7wCChY23KUmQlCcboDbIpAKbrABRI76A8Vxx/Clh9vNXI94FsQI4ZKZ50KTgMIJUc/pScQHqgl5ZBqr8hAmvpfbYI5KIGUudRzAVrVPuGubSfrISzgLfgii1O4t87FELgIupEVpsoo2DjEJESUyMEwvB8jIM7mEQsM4kiEfH+WVVI/Np5DBVYhHNVOWmbgLiUhlcszH30QOJEpvoyMKj9IofAcIqSAJdJEN8XOPdiZHOEzq1PdsYAw2mhdJkjDnOnB+jJSD5k24gEx8HVVk0wAo38uRtblhZWUHG4FpV1o1JNAIjMBsBIxcIPcjGxWS0YGhKcBFixAg8gacVUFg+rowVUPMyFEIjPn0Fr4LKUkv4WfQJXmFiYX50wEDRgEntAJcBXKHCU7Ixx0ixwK6ozBzXZkKogxcO/1I0ZcJmOWgne2sAbPmpnEOyWf4qiIGG4LayWxsiMrOENEDUwfMPh8o8sQZ4As70a1Ct4YBfRs0GwrSLChBQmKTLc9ENLxJYYOkmKB84JbRN3Xghw5VCXrYpBQbiK22ER4aNA9yD2FX1Pggk9zodhw0dwE0Eco6xb9ULuYAyZW4NjWi55EbKb6zdqELbsxxrSqigIB6jh2l8vVwBgodfoKr0wx0tqOVzO7a3gWURk5kSoEvs3PpAFTgLkgjwQ49QHVgGhFWnpfHNrTAm1IucD5RREc7kc3Pipw75i6ioCSNVBT1gzoh3RAN3gmBh34FM3yMamgMSXtYLOhzPDl5IhxFWgYVEg1GLfE2cGtkY9nMztBDOytxKoRFxjMABlUB/xjZR8AWKiOz0EN4kakzLCIyGKEZUqgp0j4IMRBYvKq2209JC32MuCnt3LDKHFQejcdtA+g8bjCgL7ioa9MBz/w2KS5KLARaHfQ3hWQmDlbFvQIxdL0lBS6h+rwwMFZCbfhBAPL4ILpHsXJvGayPIbETdbThwDyOBBV+BEyYlxrIMGCDZUOQeqgjPLflDNBKaVc/M86iCYJiYMqWFja7SQGRAlle1bwZ0qZpgFwsOIHDAlN1VY1PhVPMXlPcRLWCiKiewreGQ/cGgR09xjAQ4J/iPnCTselYbmxgesebYietSFVe22CU5i0wBDQSmIpXSUrElY0ZY7OhPAkoRwUWdpLjkFvykGGyKitKxIcdvmNcABfwF/ADrzEXlV8UyA0Mcxi4KXBPvHlI30YZMamlP73Qs+lKiAqhFgz/fMQ0LwycxWapnb2DrTZNzM0E0S2pbXmAC8TIiijD86UevGgHq05ElG1g5NqQwDH7HdCeCGcfRQpUwYQwD5ml2hzX9A1VHntWaJC/ZxNN9DiAXvA9PhYOvI9TPAiQEACcRRbWKaHyASG6PE3AW+4BAoIU7dglLO10kFrH3uIyeCePStI9mikEqJywruxYt7ATiAJ3ZkNOY8WwQuLZ0JzbNiPJhbO4tEM8AJkGwUvTatRSbUc1QQ6p+CZSS9fOlci2k2Cu6oEDVGvuyFYVnIHdAMydG9gsFePoA6JIlsCok+bC5cYOzWaJT0LHTvJCwoXOiayWTTSSdS4IA1CXyBc7F0Zk+GBVhsIKiQolbxrJIQYITrqacgSU0HUIHqvx2o5PrBiRliXPb6KLcqxhurT2Izt9iFB7uIDADizNUduhdIqzs6r0kKxOacwQ2RaCWWEsT0CLwwHYKMDPS01kAYYECR9V2PIF6TczBmgAZQ61jU6ndfkiHmRgzEmhxW5uQ7rURAyBCxHbtvNi022Cl8P7lE6CSrJJBQINRU5vhv+nhfGrcQw3ITvdiCqCPi2jGBkqzwErcIo2miN7ARDPd9qBKH9OjZQrOzL07eyrCGxeR63l2ConSGwHhfKKyUtt6xlkNTRx4kfI8doWFzybmeZ2NJEBIK4Z3Ai/ATNUoS01JHGBrKdPYLIyNLaz0RJKnEp2eXNiBtqOVm3xwMPbUXxeVsBLFAtu0eYGNUycmK5DKDp2CgU0c/uUJKWW43V5Ar7ZJkB29gcaUXF2thlbCbgeyOMHMAo2pSpsZwqSxu/nFALeBDtY2mC8xHJCpYV5Kc8ADauILyfhJvJQYBUxjXlwQApctzELLA5mQCWR65a2KRKEduhMcIEgQAS+AWD4P7KFb6gy2CPPYjvC8dLSxkCQQpFZzRpkmZTNK3g9xfmmdmwDw5a2DFbbloUVZZlYNsEjpBvQaksHnuPYCABV7KQDBxV2Job3cRCaEJydwUKpeQq4Q2ARir6wswsAwVY2gLHYzjxNaqSxbW3Zmhi6x1xGRK5xD+QRB+XaZCbMfRtNFe7AoTeU43FzcQBSoJjgaM+onn4giaEd9/u2I8SFgDYbpWS26wErQUIBvrDIfTrHtAffhpFyrSR8L6CGPFuYcKCgyHZLQjudQRiHtl0UBfZPDzkfmXzFwdoRrS3VBQOnYVi10MPlpCAKHgT3nUJfNjUMDPJsQpuB/a61kp/Ufokfw74EtlNHyaBJAN6BPS4+ACiTfPQjrW8njonZdhPLMC9FbzIfwgx8yNC0HbRi+y60sx/a/lqd0GBQTunauBhNAlaj0e0yqQ+go9WIK73JXVdmwWkg147PcODcQcZfkaxxXEEevJ8GKWglKgPjYntL1jp1ZetlVGqc2sYBPJKBt3B6hexGbkAvKZyMjrRmD+00D3OI0QChaHRbMUEB28IErgzYsUkY7RfaYB1EApN4KEQk3AUd2ezC5ekcBxSHwTI7MbUVNiQyGrEaWCMIbMvEVkvsJA3FlSUm432eIrTjKp9CSiAKdBgYhFk13s7ocFrRuMMzpWyn/7GdrFTchslfzAj9Cemljg3AccvUpmFGbtMeIAMnZAfMtu+DQ3CTgZrIEW++nXrG1qugQuVWsT8AlgAPcoiyLm0xxUY39IIX0DOEwUZACLecDBkJoWSQLMAC6S5h2oqIV/XAwQUYhVODtkt6CFsc4+5s4EB4QEaPUgJ4bbsms3KseT0IDLgQn3jvih4I6bfC2rTIbYnOsS0trlfFhLookswwEK+fO7YnQ5/aKXZtkpC0OT6fquyIJbdZNtIgsr0nMMU0SVXkNsq0tQakHQReUq+IY8SKDTkLW56whdPSNyDh37ZgkVD/WHVbUCDQMBliBjjjxmrqy3Y/+XKbv4R2GpbZc1svUkwuktyGLegDTBaAmtgKDUqMjito28S8pMkLGi9GlJBCFDX9GlDR9LzNX3KbilEani19lnka2ojXzpqh0rI26sR72+zX1oooVFsvtBM3vIzJnQDPa8jKw5TwpocjJwiUlgnKCj4FwAvbfeWu6FCuRQZBZ2QRhqK2OYU3QNGAh3ng2NwYNtTOLhPbN8F6OOgT7CoVlsFIrgsYBNAecGgPmNvajG/nvR6OE+yHQHjbgEywkXtpt+JXma1JkkO+MQX4EHDOwM5tBFMYVMP2uCMPN1GUtsWHOqdOwG8gjrqk35PKsck2cXWqIg3pUJuFOoEdiiQVWEy2M+iywClwPeSLjf7svCLNMhvT01QpZg18RU9VA9NiVA98TyfXtomV0uXxwKanoSmZN41kh0WBU9dWob7xCg1tZjNFL9d1bKhmi5+AQpbYuCAzxwWRAAkYprqMbIHRy1F/jptEqa3+AIdgEeKoSm0NM7eUQ3d2DA3gm0SLuDKdGtqwzKkzQMpmznlhK9HEOKKcDFIHJnPEJIfl8JNowNAhrzZjoQkc4gX1Q4y0T45FtkF8YttCaQoI0Zr0FxiP1QW9HM92abgCrQT2mdLmnbGNEn3uNLRtPcgAfq9t2uWZs69rO0TEuHJDtihVobRxlpgo7JsfV4amlW1u1bY5YINy1x6AkNgQHHmIACfaxr5k0MSgTVsANFvBoxZqO0PCdwzsFJmjpTPBjRjLi83mRigzXkoBhNynsWuyj92oXWAoxoAVNlcuApsx1DHIY8Mux/e4vh0Hm191UOhEAdzgC4BZYoz6RdMQfDtW9dMk8RFbqDrjIajZMWVZIfTL1E4EwrJElEEpJnuhFR/nzQNQ3L6N4bgokEQ88oywuSQfrQxZQBCuDRRhyDT+PzGS2vlIYXNhO+2A6gJ7AOSXrS3Uvq24l7YhTZXXdA7dX7qwUjZABFEBndWVTRV5U0VKQ6eqTEnEBvk2OiU1dsyI4gEfrUAD2xdHTdR26oG7RBzYmq1ZOj+zQaW5UYonsw3cqLIDdwNQvEMReSZ0cyjJBtyejRXIlclHntNIwTMbbrTu07YRAgHR6dnpGpWB1nZK6gLOt4XDyJiCZ3coMxOPyDkuYsesUA4waPsHVDZ6EfuJ6qK+EVT4pchg1QuhU7/2UMJ2pIrngTbt6eEsoAknYjvnSDaYvITw48orUpt2luYDoVEsPfIhoR3zLLKTIsrW3JcNhio8eWLntMQBt2M2nxuq7fzUSIiadqm2jOej7MvMzklSvpUo0w0wQZbZNrydGbpAQOLXNndAiwBTvnGquU3bsOUOSyvHOq5CzyQxWfHhSOQNNZThOPF9xhK26Id+BCKoSwilTu19sDuXhgbrlHuiR/E3QII1qdkKMwJOHlBgnjfQmZ6typa4OlQjNpT4gqV0XkUFZNZGRDW2DXKsC5QS204QfBAFZhkj271A1WcRvGRn6yV+hATXhQ0HIZuB/cQstGmLC96A6Ui7HGqrDEMzPzamDJw4soO3GIKGUQBi6gmrZfbeL8imh+YfWClxkX6e7XYiA3DWWZUAEgPqlPeawIhsC9dWGkhrYNvSBYBlS7sOfQ2VgG4GfgkeD0lB+slh7do2teUPxgckKHJbfqLGTfjj73PPxv9Vzj3bZh8cDWXBivRS4A2s6AdlEpdRCMvZWNUPbIkP6kJFYHwyILo0sQabEWmb/hC9yvaBK0rTHBlm3fYCYBXeheBCcAbG9V7oFna0a6d1RelkiS1agd+eHdoZHFR2lgHoUbVp6kWoBMRHjOlNHRsv5LjI3JSqrbjEvmlyKqSGiBLwJo3i/9vIsqM3bpjP1HYaHVtYCttpseUeBKjthCIMKJDMTfPCOAE8wY45A7+6IBsDv/XwB8bwuC7+J06cgswq23OKkBqJLSKCT45tTPjolNy1C8D4SVQRU/rVyNlDP9iNk1YELv1qGsJUdDCw8wrehub7BjYUkE58AH2U2Ilz7FPCFcEKbcCa2louDiqxTR3UhW3QGzz58cBkBcuFeEwAFkI8sHyTIjnyLPTMq3g5qGsxBZSKzBYcc54oNRXFK75r2GSlDRoQycB+2eDSsiA5xYmd5iFjyM8dsJOpbSLZ7KrgugX+F5z3zRuHMJotYCHtbBuPOBSgI7iNVcNV28q+Q2x4N8RX0pI8Fxxloxzu1x04IbJ7K+x0x6GrXV60fseTolLLwgHAAcg4qiM76yH7MIqd2iI7PdvNxBvgg0uiBuiicRGvNk+rSZQdlsCSSTlgvTGyeI/IfuljGttW8lL7iYKtz+eIh6q2ExDPiqg0XW87/7ZRwtUIIpqotMkGvFjbfluZ2K9ccJJAuunU3HbEfVvbR6DWEBhfgJu1TTHbcERZUL5k3348VaGMYFOsqv32gRcgJzszhIpN+SSVoUdVOWluKGhnZRnAVRlj2G41lG4Hx3CN0SXiIohsC9QcZQTkpoYDxpyZ1VZgezN+bQvIYIhvK80wGj1GKYYm6TLkKPSH/AYn7JwKruMTgCVgHkJXJCLMbKPOp5yBIeDSmMzFeNgwwPS2TyOHPlUWxbbvndnk2Ows1eDbHmBiP/qI0ddUNuVXwzFJnnDfkf3CDL0U4fDtuBimACFNpyP7+RbLjJ/YSjrFyV3bL4NCJJw7cHqSJhYGKqAE510DBRgN2ggzetbhoYHKCuluy76u7fGFlZvagbP9oK4uHCfDpcDytrdAHRMEH6XrxeZgQL+Ad0K99ks1Wxmz3/vQoUUG79k+IU3t+5FxvUsN1LZb6gY8XkgBov1ok5KbyWuzXxU4VdqxpPmQ/zubRVVlJWLMrEgQEUHoClvh2FkEF/SDsLDdCjradkLNP0Y4BurL9szthmA/WgP+t7EN5tO2hCNSiilL+JwHKJrMCM1D+QEogz1AvAR4azvgDPH1eW4TdzNftD0YjBO3RcsKnApqawdb7qR7isy2tmLbuIsrOzqLAj9CRSf2Gwhe950M5U0poTEqm+qmMCVOzeaqPoxC89l2bZG4tRlRWw2ktWwgmCCpbIyYeubEqNooJzDmqakb61QsienU2GRx6SJe0JcRJGaLb6GL9DITQUDwei7UyRNjUSrPVs4ciA5hURPc0hb5bMPao4PcHHaxH7nFtXWKVVNop9RFbnGkC0rbQQU9gJuYvGMoXFutAIrTHBSy6gOn7ZwYtZoMbPd6SRy4HtkY2F2wY10DI0A2dgYWuxAEeFRgm5soaCjIvrR9LZtAmIhPbIE+B9rsIMh+LAJm0G7YYfQxeQjhUIQFd020XWusyJb2yVhkZ+LWXVleejWqxxASiZnYZJb+s6hiYwv7YVJmv5ox0VtCdgjDCPqpEOsYGPojKeqUYEV2UDegcqqYPjZ+c4AH4mY/ZMq8uLZWTm1wBRq4xjkIpzg00W+HPDYTTYi+rfjUtjUAtaaUC57FfvMSlANjKdfkAJYWOgD8YqIJJQZ4KTtzzqsUtEs8G8JEDh4U52G/YLK5WWS/t4ns92WAKNyC7EJ8Yq1Bbv//fuQAlIS20IyyxjVB68SSdPt2imLbotbmWGZMWoGbyvA29osSCgR/gtC0Di/Iio1SMIORU9uucQF0GD0HdsxnQg8UoLl922uwnxNCpAmKiH5AKA1sUdS2UWa/YrCNa0yonRvifD2D9MzOEWxeCMC5tk/tJ6argAAPAAaTkaQxNZXYl2YhCj6w3eYk9+l36A1xaHttSRXZ9ozJKcjGsd03QkS+K9vDqp3STs49XqGOKO3EjXhUCtXOkyMbuHj0vwEPYsvG+LYmQDKcgWG8zbjBgxRpaSVgUJbZ708G3u7nCAobBZSgYjpA+8C8nU5RzaEpHVuHoiG577QITCYF9gs4ouMhMuzXVU5m5EOtAn84iRzxEQPx6LwwxZXZUZ/9ZA6NYWPuAknpZwj3OAeGaGijPy+NbBOOCHIHON4sDCFjAN9+4+3baVtNvTl2G+Bq6du5KXIVZohst4iCLwOs6cCiqe1UFj6Rs5McjBYlQY/Wgf3SzjUqje3HkahzwmpzTJs0+DjQyn6AbNvqAC5twtNmCU47HDBXpQFRwUPa7M0lcUhlzFxe2ZALHDJ/VcV2BGKGjHDiHUxK+y5cMeC1PAf0LezXgxgfHpIQGhXb9nxt416/tCN8O3TDN6FO6K3Ulm8wSGVVoJoKWwvErGYBubKzB1sesV/qUQdeOmDaAiJPtya4XaJmh2jUAI8UQ861HZXbjnrqeRQOEG0/IaGKcrg2t2VSN0E6pPgz+9EY31hTxxheDKrt0tjRE8hRpgMb17i1ujQxb9uyZHtgX7327MfklAdWM7efOtsZnOko136IQzh931a3KtsYBs1s7deB/+gr+rUs7TTDlm39NTcP9MWb22XT4YHe9kC7HHhXqiN/nie9x7kaD++VNy919VanV245M9W/l5wvY9Z3dfijvfL7PoHuemi7PPZ+ou9etkCeOyLV/8h8eWWoo/c/2y8jL/L0sLhb7u5y9MesT2a9X+iwy2bIuD0dfeOofunMQ61ntcrChY6uvGKfnPGJp8uc3SUfnePp+YO75ZKtQw2ubpX/jk/1qUvmSdd6mRaPzZUxS3q60XfdMnSYry881CmTv49018ObZM4Fma6XzZWeMNSopVUel1rfvmKaXHlhoR3JTBm0WabTxs6Vuye6OmFkjzjTB6mz2RR5aEygn23QLs5lgTpOu/y9Y6hbnNUqzdv7utR5nXLWQY4uuVe/vHxJoQ97M+WGVlfnLtojb6zr6B1P9MvHJ7p6/lrE6X7i9GWPXHSbo/EffTJnZ1/XOa1TfmnIdOTyc6TlcE+X2K5bJvD8o3j+rS8L9dr+FpEfXb3q4B5ZY2GoL67UIvPudHX1iT3yY6erhxQ9cuK2rq5zca/MvDvS4ROaZPxXnl56fJdcsa6vvU90ymdDfD1tUqe835XrFdksOXFRR9f+pV+GnxTog0PbJdkz0YajF8qaa8f6/NONcsBRke63ZbOMfTTTqe/OkU3+8vWuLTvk0kVKDf74R17eONeH7pwtaxzt6Uubd8v17w7ShddMkbN6uf+wSxb9w9dfduyQoifTWfEc2YJ43Xx2qzyweKBfTGiXazZ3tfWmXokfz7X5rVnyak+sd0cN8tpqjt75ar9c94OjCw/tk/XHF/ropTNk6taebnNVt+hrvl5xV4cMf9TTh97rkiffdfWVa3rkqbmRvrxmk/y+gqs7j++Vc26KdbGZDfLpxETPGrlAOpeKtP66WRY87OjqH/bJI1c7+mFLn5y7qq9D3uiUcYe4+vquvbLfY4Ue/M4MuX6Up/NHdMuQKx2d3NknR13n6JYL++T4H0MddnCLPLmRp6/c3S2zdsh1/XNmy5wTiP86vTL9vVQ3u2qe/DHU1RHP9kr7/Z4u8mWXNOwwRFc851eZ48a6dk+jnHjLcrr23x/LLZ/4OuOsDvng6kwva5kj0y52VIN+8b5ytPeEPjn/Qk/rlHp4P9TDLmuRGUcFuuEW7TKn9HWt6Z2y/RRHL9ie+xoU6QZ/Nss5S6Y65Lv5EowMtWNUq9xwZabzOuaIy3X7w375aa1cD3pmtvxxf6Q7ftkk3V6oeVervPRYpg+/PUdeXtbVhz7tlaZfPV1+9y454j+ebvtSl+zxS6DH7dMmM1b0dJP/dcsnO6V61hnzZdjpgT68HP09zNV3HuyV1ZYN9I3P2uXJHldfi3ok9Adre+dkGfWGq7vd1iMoLJ3f2Ssf/5rpBbvPkS3P9PT6pbrl1KNCXXXLVnn5plAfnNEin37q6zlndMiz9O9TD3bKZpt7OvbmbhlFvHY9u0OeafH02cW75ILDXa2275UD30513xvnyameqyt198rP8x3df9U++bUj0D2rNlniu1K/P3K6TN491N2Oa5Xtzor1wiUa5Uqep4vn2W/LwXrg9ZPl0l8d9ffok1svDvTPoF0OP97THdfvlv4TU/XWmi97f+zq6HN75OLPHQ1P7ZPjnot0k3FNstyxgX6ycbuc/U2gix/bJv29sXpBg1zY7Gm6VJfs6/l6SHenfHKsq+dsRJ/e6uiGU/rktamx3iMN8tAynn72ebecdaSni2/dLetOcfWp7Xuk6zae7/ce+fixVM9/e55ccMhgLXebLLfUgU6f2i7bdVd6QfqXdKwb6OAn2uWLM3w9eelO6XrP1/yqDvnP8/TFY01y2cup9t8/Ty7fpdbek6fJhFGuHjaiVxrHV7rsJX9JOC/RjjUWyNBnXX32Pz1yxm+OLr1bn7QQ9yVv6pZrNnW07dZ+WeohX7/7pEN6zo40XrRZtl3P04sf75bNhzh6y6/9Mus4R9cb3i/xup42j+uW03dwdNlz++UD4njxet0y7+5YV5/QIJ+t4eiZL/ZLy3WpLrFgnvQXjnoz+mXDNwO9f2ybnES/rUW//bBCpYeP/1u2AF9vBF+Hr0+9Pdou1y/j6ILP+2W5DwP99MI2eWHZWB/7tFFeB7fvArcXDHd1tft75audHT3ptH4Z9oqjD97XJ3/cnulOv82R5k19XfLWThlyraO/NsI7Vzk6rK1PPs4cPX9OvxxeBbrj3+1y81Kezvq6W179K9b7tmiQa07LtHX5uZLPc3X2Gj3y5zRXt9uqR/bcI9Fjj10op8BLqw3hOS8Kda2oVb7by9MjjuiW8zZYXAc98qOs3DlE3yknyYOrZfr5q3Plh3diPfy6BnlhdqpPrDNPFuv39Se3Q/YfG+h+09rk5pGBzjikXcb+EOlfhzbJnlWux/09W5a41NEJbr/8mdGfc7rliD093e7obrl1bKhTp7bIv8G/D2Y0yNChgT77XLsM+tvXPzfvkLcfcPXGz3tkky/hpZP65HPf0dM7+mWT8bHeeWmD7L9drPtf2CjjD430sp2a5dBOT3cuu2Tv5R0d/VG/bBPAK+3dsuSBi+l3+/wsk+4MdO+JbbLDSqGe816rDLrT02m/dMnJu5O/Mf1y6haBrnhjuxy9TKxbfNEox72V64Y3z5L7z3T06yX75eaWUP9ZvEVOuSDQVfJ2ab0p0CVmtMndE9ADh/SJ/95i2nn1T3LXN57+fEyXnHK1ryu3dsioAzzdZd9umTok1G0mtUrbg4ku/ukCOekI4q6t4m6QaN8jC2XDJ0J94I0WufonX9sP6JAPuyO9MGmSQ24IdI+5bbLXM4ke+d8FctA3ru59bI/89p2jux/RJ0/fFekLPzXJUpNc/XavHhn3lK+vPt8h938e6zenNMiwHk8fibvkwIcD3fcjnv9n9Mj+XbLO+ESfunSBLLwo0pWiZrn+OlcXLOiRN3cp9baT/5FLTwzUXbtdViK+7+7cLLtPKXTM9jPkqXtLfenb6XLh9EizTZtkMny+60W9ctBFru4V98puS+V6wjezZU/yfix5n3W3p8MmdMkTz3r6xpNd8t/9HP3vAf1y7eBYm39rlEvBXQ/cvfNpRyc92ye9ba6Gg3vk073B7cPnyxU7uNpzTq+cvOwgXfOzP2XP/Ss9Zv+/ZfvFqP+f++XM2+nbyX2y9GZL69djv5Kxf5PXzbrk6409PeHObllrVU+fe6NbHv2Xo/9btV82Gx3pbds2y1GT0UW7tMhe70d61OVNMvuKQNfvapN7h3g6YVK3fHxFrOd3NcjwieiOfdvkmKtc3bK9R8asnOjwdxbKyr8W+t4eM+TaZ31terJD9m1y9OBl+uR/9zl61dd9MveoWNfaolFWuNzXD3o7ZGJLpPsuDo+9EGj67zY5tIx0xD/NMuEW+PKvFjloLV/3+m+nLLlUrN9+0yiHzvZ1xLodMuvBTNf/dI4MPcjX5/fqlKPedHXrsT2y9eWhXt3bIgvuS3S1rxdI6z3g+fdt8vQeqT5/zHz541BHd96pX4oRjs44tV+O+T7VzUejO7fwNLuhW8493tdF1++UKw93tGu7fhnyZKi/vdIiZ6eBLjq/Xd69ONdrg9ly6RqBei+1y4rol/HX9kh8Mnlbo1G2kFBvubJVsk89nXtml+x5mK/H7Ngpq5/n62t1p5x8sKtr7dEr30/K9Yg9Z8kl+4bqHdwq763p6TXPd8vWxyZ6zcYLZfOFkd6yUpOMeSnWDR5qkKWWK/SbT2bK1qvEes1bjVLdBn/80Sa/t4W68+AWWfeJQTrujSmyyXW53rlwloxaGZ59p1N6yWu4TbPcdEChM0fOlBN2C3TdE9rlgGdc3f+/PfJ87urjs3rlwpNDLdZADz8T6lLPtMiyy9b62WfT5IrtXO2+oFcO/t8g3fvKKRK5vjb3dsoRn6a6zZnz5Br4ogW+OAjc3Wtuv3z/ZaBHnNgmJy6S6dp/zJXpV+W6edssOWvxRJf6caFMujbRvRoWyCFfO7rbmD55c69Ebz1ioSyHbv0M3TroS1+nnNQha81HH6zWJWNDT/9q6ZYbwKe577bKiGdzPf0/s+Trdem7J2bKWatw3bcWyp3vpjrxmnngWQSeNckBy4S6/xetctEu8Pe/0IGlp7/90y1nrJTqsu/Nl969E40OXygnbhjquve1ygW/DNJynyny+A+JvnPoAskPrHXuPtNk2OPoqbd65Mw2X5cZ0iHBgZ527NMt3++U6OgzFspXm/t60k2dct02gTZe0i5f7evpiQd1y4sPh/rohy3iTfG1b/sOueH3Sufv9Jdcio5z0XGT/uvo3k/1ybOHohN26pUz78EX/dAlPfB1Al9/9WGtx18wVd76KNax5zVQpx512iW3rYeueaxdln4j1K9va5GbrvZ0ZkuXLP5BrBMvbpDoEk+b/G7Z4Qlfz3ujQ+rNYp06tlEeRA98uWufTAWnNWuXESslevp7C2X4d6k+cOQ8mTOr1LXXmy5Nuzm6zPH9cnybp+sP6ZKNhhZ693Mz5VP48cxjFsrynbl+XM6SseOpv0va5INrC72oYYZ8NCfEH7XIK8/A/8+0ydq7x/rMmEbZfflQj/+oVRasFenqzzTL7ydmuvNac+WnRT09eGK3HHtAopvuu1CuQQc3b9cnm6NHbkWPPHdBqE9mrfLzY64e9HaPrLttrE9d3CgnPA8ePd4m34urR1zRK9Omlbr1VtPl5NMcXWOFfnGXdLTnu35pQO+shN755LJYz+lrkF+e93Xfxzpkdox/bJgrs3b3dYPjOuVl8OOBPTvla+ryROqy5XxHlyj6Zc3zXX2h7JXruzNdkMyRmRs4usHD/TLxZ3TpftznuqnePG6+PL5TrG+f0Sh3/pTqrwfMk/3+l+gBVy4QvzvQrrRN9l8s0gN/bpYl7nb1xx97ZP+r6bvWHvnjKXj5+S6ZcU6mGw2eK6tcGOg7abu8+kiid4+Hf7rxy0mffHO6p2OW7ZaDTqC/1uW+vsY/jmmRQdcM0SnNk+TOXyP9dfcm+Qf9vfEWrTL9jEI3W3qmpOBMAzjz+uWp3tkzT9y38A8398kxK4S6+QetMuy3RfWR3SbKV4eEetKurfLx/EF6zqpT5Mx3fF36+g75+k++d5suGXG9q6fN75F/CnzDjG55HVy8HVw85Ab6d26fbB1Fel1zs8yOfF23qVPevTfR675dIKMPS3T7HamXH1P9+OB5sv/yPPdHvbLuj57+5+Auafuj1MV3mC6H3ARfz+yT1Xdy9dUze+XldUK9/8lW8e+JteuHBrn3Fk+/+6tLfvhPqoe+OE92vy7W4xc2yDL3uPrVDz0yGNz6Hdyqm12dvlSPnNvo6CLL9clN+PAZ+PDP4dlT4dn+KlX/r/myJ/x31F3dci083OL0S8cg/N2UbnlwF0c/O7lfniXfT5HvydTnrtTn4ZNr3WGXqfJvfNT4F7vkrrNynbj4bHxZhC9rkhd/CvSxA9qkdW9HFz+8X9IbXF04t0eOWynQjd5rl78vcXULr1fWcsGz3m4ZFkf6UGOzhF6mbd1z5fRxka7wWpMkrzvaeAc+9b5c5ZtZclHhajqzV5L1fG18vFOebltUXxg8Ud79k3hv0yTB6Y62L9cvc1dfXIe+/KMU+MHZXzXJSi2Z/m+JObLVJ6Ved/Z0uR49MP+nflniwEIn7DNT/siX0J1n/SAjhhd6+gMzpeP6QBeZ1yazbyN/v3fIJr2h3hG2yKTuUPdKWuSqswJtW7xdHu0r9F1vhlx+YKC9e7fLiee4ut6QXrnkxFz9tWfLvmmiB89fKFc8VGj3xzNkzyPwmwo/npHqB8vMl1t+SnT6AQvk8L983XHLDjnneHh5/XaZ8UCom3zeIqd/XuoKp06XqejbbdG3wybgE0a1yQ3oiPnoiCPHR6qXNslhZ3u602Ldcgf//W3RbjnhYO57j075mPxfsGSPNA2JdJlJzbJDp6/nlx1y/WaRzr+lWd58N9Kx1zTJriMdPXlUv6x9mqtPr9ArFx1caLLHTGlcy9Xln+mV06j3lan3jRsTvWe5BbL8ibF+unajXPSMo+l/+2RX/PGJ+OP3tw/18vNaZaPGlPfNk0PbHB0xpE/WiDx9qblbXoN374Z3/z9Oj90p07/PmCtPPezoCx/2yW4LfT1xpQ7ZD948CN4ctDb6/ulOuQ8++vbeVvnrhli3mtMgd47M9NdD5srxox1df1t048XoxqBbxkzCf+7VJ/u9F+ghV7fJ4v0hvqNFem/1NJjSJfvAL4c/3ScHnDJY911lsjyJ333llD757Uf880EtMnoH+Pvc+TJinVhP+0+jfHSnrxdM7JCj/3B0ix37pLrL1X9+7pGZT2Q6/PU58vuUUEds1yInfOPousf2yehFQt3uj1Z5ZJCrH03plf2WcvTAb/rlVvhvKvw3CbzfG7z/ajT8uE0ndVjpgp/+lvU10scub5bHPwn0rbPQZfuDs/v1yrLo9S9+bgLHQ11zhVY552T01hrtcutW6NZruuWPy2vduWeq3POtrz8e1SFL4Je+xy+FJ4XaNrRV7tsLX3tEpyxcw9dVX+yU+aukuuab82XdHQbrk+dOlh3uyvXcn2fJCOr0DOr06e8dff5w9PERjn6l/fLyI4E+NL5Nrr4m0tamJtl3+0RHnb9QHlwh0c/HL5R7dgj0p3PaZbVvUn3z2Hnyx0+x7nxAg+zKc/+L5574va8HHN5B3aPvuZ6zZqb9L8yVVyalet+e86TtSEeHbEUfPpbqD+/Mk582cPXgh+Gvt1zVm3pk/IWRXpE0y8jraj104VS54gL0fdYrOxxQ6nn7/iMXvp9odvkCmX50qpttNl+mn+vo5oP6JRyca/vk2XL1irG2/a9RjlyW633WK8MvdPWBtFeWJp9fk89TXltEV7nrd/nl+lJHzpsugyb4OnVUhxxIvEa+1Cn3vFjrhIenylaTAr1mrzbph2/cdfpl8GuRTr6rST55GT1xf4ec1xrrIos1yNdnxTpm8UbZ6BRP712lW6LdPG09oVse63f0XadPFoOff7myVyb6ue7XMVtWQFePR1e/gO9/HN//3sqRXv12s6xwf60ffjVVLjox1WTt+fLNrZmOmTJHtugP9Ba3TV7ZIdQHz2mVvuUDDT9sl3lXUB9d6PsLQ10taZV/Xejo6mm/vLltrrdfNFvGfRHoGye3yT741sPwra+Av/eDv1ft52vXAZ3y4ehYL9ymUZZe3dWvX+mVMWNd3WhajzTPQt+t1yEXTXY02oW++ifVgzaeJ8fsmehmRy+UnW7y9KyZXXID3zcv6ZfRR4a67dat8i4+8vqoS95939HrLuuTec2BDl2yTZbE7044tkvu+9cg/W7VP2Wb2tHLptEfH4T650Utshz5+3gR6gae2Q+eGfyop7+/1yU3/p7q3J3mSb5morNfWIh/WlQPWWaiHPyjq3se3CODHnH1zw965OsbEz1h1gJZ7wNHx13cJ6ugL95BX5wxDv5+rUM2Oz/T28u58u3QSI95tlnOmJrrcjpLdgJfzkLvvn+Qo5fv2S8jH3L00E/6ZFwa66vzGqXnp1yTA9B3p/o6bUX6dvVEF768UFb9xdM39+mSNaYG+oq2ycmbBrrmbe0yHD/3wJaNsgx5+aK7RTbdN9DbD26XjUa6es+oXvkDnbwjOvnyHRLtPWehdL5can3/dLkG3duE7t1ztVyPfm223HIF/N3VI//m9fG8Pg/9MRT9kXwW6YLTwf0q0J3/bpeTx7g6dFiv/GuIr6v/2im7fhfpyUc2yXVr4NNf7JU30eNj0eNv/FbqHbtOlxXxvR9s3iVvTQ/0pk3apD4JPzG0WzbBZ98xuVGe3CjWl+9ulCVudHXC7B45Gj+z+Yb9svRtg/TrP6bIQ0sG+um37XIc+nSj7fvkZ+J10Md98gQ+6O2eTjnsPEd3qvtlShXp9n81S3vs6eCGbjkUn7UzPuvv0a5utU2vRNc52rqwT67Gd7TjO+4DF78HFz/a3dOLjoMvyOfd5HPkubGOWqRR1nvCxc/1yAOJo18s6JdRDwS66xfU1cJUv19pnsxHn65eEKctM13rurmy7uuxjrujQb5bwtOjv++WYu5gnTn0N3l4vKOfXNIn7u2R9k1ukhWec/SjcX2yYI6nq67VJe7y+JOPOmXGkFA3mtQq3deFmi5okYT+aaJ/bswCnTenXXbYPNNzb54rk0+AP9ZplR1fcPTsR/uk+zBX8x165YkDan1z5DQ5afdQ1x7TKjscC25t/I+0HeXpYlt0y7Y/uHrxYT1y8kX0R9wu+w1z9cCHemXT/kzvdOfI2hNLfWrf6bI08fyaeN5xhaO/dfXJP9uhDy5olSuSQHsWtEsHeqZGz4xAR5yOjlhl/0Df3r9dfrk00JFOu8zCvw7Dv551SaBL++0Sz6TOhvWIbu3qVVf3ygX7ZVocOFcSdMMCdMPfq4e65cut4qH7u8t+OX9xrv8jOjsM9IVWdM5gV9eZ3CtLjnD1+1PB1S1SffDG+fL+q65efneP7EVfHkVfLj0y0i9HNcuGxP+BH7rlcvxi/+AO+c8qgb70Zrt8OJ08b9Ilcx+LdM23m6Rng1qzh6fJNs2hXrpUiyyyKrz5Rrcs81aiX9+8QLrRO9klvM9xNe7rlUcuc/TDvj555W5f75vQIf+mfj/4rVFG7xTodme2y+ytY13nqkaRW129+s8e2WkLR8+8sV+e3tfXFw7qlMuKSJ2ZzXIZ/d1Hf990uq8zl+2U6WN93Wxqh6x8UaTvxs1y69KDdeqXk+WNrdCB17TKyp9Q52d3SDQCnj21VdY+Ed20dq/s/HmgZ57aJlPwazs81i0/Xl/rqHlTZWSDo4et0CeHv0o/3NMkE47wdZR0yoaX4jPd+dIzLtP41Tmy246BnnB2uxyA/9pvr05xzBdH/bLx6Fzv3na2TDu/0K2LmTKzcHT4jH6ZeXOkG0xvkoalQ13xy1b5ZlSoY0a0yoUb+Zrf0ymfWT1s0SFjtot12IWN0rFJoINub5fm1SNd+uVm/F+C/1sgx2zl6KbX9st5/0L3r9YtL22S6YO3z5UCnThrLHrl1FBPWLFVPlnF0fPe7JeVZkX6/vpNct/WuX571Wz57D1HT7+qT064agXdoH28rDvJ1Sf26pGn0c3PBy3y0URXLxzZIwdzH3tyH4du5esu13bKEptSP7f1yosvevrIw10yaGaqfw2bJ/OucnRoW58cdpKvOw3tlL2nRnqENskhH7i6x8U9ssjLiU55YIHcu2Gh3987U9aEP1/6ol/W7cp1XD5Liq8dnTOmT7b4IdCbD2uTcfip12b0yan4hxXxD/u8PkQPv2OS3Ac/f3d8tzyKP33/igXyOLrjbXTHufDh4G0b5eU9cn3gmNly7E2+bjKzQ36JHR3Z0C+n/ljrSgdPleXw85/h59ff3tEnzu+XVy8v9J6eGTLyEE9H7dYtx2mswy9vlPPRnTW6c7UvQ339pBZZbcVIX3u/WTZ/sdBbH5khN/yQ6bxD58jUQzLddte58kNY62Et0+THDzMddcEcOeU8eL1ulTWei/TlcU3yv4civfJj6ujdVEddM09OH57ocg8slAl7B3ro4e3y70ZfP1y2Qx7fMtR3rqOP2xfRnkV+l5O7Eh2aL5DJQ/Gpz/bLK+j9B5bvlWOKQDeb2S7jpqX6xlbz5JK/XQ0275H56Oc1nBa57/JIv+1pkpznmsNz/bi5p4fe1C3+Z4O0+/Qp0oVuzdGtvxwX6L7D28Unzt3E+W3696bXu6WpxdNlF++Sj16mDu7vkYvGLabJaz/JiBZHz1iiT1YUR8df2S8n3+npGr90Setxri42vFd2afL0lGW6ZMkr4YXODlkOPP7011Z5ag9XXz6mV26aXOmsEX/J7FmDdN31pkiMzm46uFveXq/Wmx+bJm9Qt7dTt1fG6JyGTrl31UAnvN4uOx6Z6jlbzxdv40i772qWNx6A9z5vkONejXXDexrk7scc/entvgE/9F/80DK/Bfrlbm1y4gqxrvVBoxzzsaebntslyy2S6Gd/LJTRJ3i63brd8uGdsV40sUH+2t7Xrc9DJ1wV6/y2BhlWe/rQtG45bINId36kWdIv0Usn9ckRe7u6zWh4e0yoqw9rlfM6Aq3rNvnrQ0e3vqBPPuY653Kdo6YEutX2bbLpx6nefu48uXGEo3NP6ZdtHx6il3w0SZ4C318C3xc5P9SpZav8ZzP+fUurnA1/LjZ7gZzzY6RDDm6SDfBB//6TPvvG17+OBRd2jHT42c2yipvoO70L5ejLPd2it0t+vTbUvRta5NPuWM9OGmQv+O7oue1yx3Kx/vox/bufq7sc2CsPnObrl8t3Su9PnkYHdIn3pKN9r6ATWwM9eLE2ORxc2QFcWZ26fY26bZxV6rLrTZdDRsW6+y6NMmj7SP88v1k2fs7Xu8d1iP99oF2j22S3V/GB93TIufu4ushhvXLooq6O+AVddG+ox37XIs+ek+nTg+fKrzzHHsd0yNPw9nPw9qfovjMv6pMxF+e6YThb5jzi6Drj+2SvCwfrkelkOfSYQEds2i4botceQK89B+8+GbXLckfm+tnWs+WqJQLt/L4d/xvhf5vl4VGefjSiW74nH0deYP3r078dsuNKvp71Xqd0LepoObFf/qeD9MrL/pSFi3u60oRuGfGeq6dc3SOjn8QPvtIiz3dk+kQ1R7a9KdJLZjbJkO1jnXR+o7xza6LXT1kgp5zq6Gor9cuMyzzdqK9Lwh1DbT27Vc66JtClmtE3E2P9Yd8G+X05R3f5uF/evyvQK35qk7//k+rmL86T5s1dXeamXlmYlrryvH/k0asTHd+yQDbDL92a98roIY5u/yv6bfUUfT9fzrgg0mVz8OfuUG/5sUXGNAY6fLk2mdvo6drLdslfCxPdcsUFcvkX6NOTF8iviwS6x+/tctp8T1dcrUsOmwW/rd8l523q6+DbOmW7d0u95NrpMvcdX4de1yHb8/qFvB54sXZ1N8qMR2Pd6N0G2eF9V8+9vEe2chy9tr9fTh/q6rLP9cpeYzM9atocWZ5+/4h+3+A1Vx+9q0d2ucnVk2f2yF9P5rrVy7NkJDh7CDi778W+Hhx2Som/nDG6QwafhB8b2izjPnP11dN65LV9Ar3n0HZZflKun+w1S1aeGei7w9okecPTxtu6ZMiEIfrrqEkS/ONp58Zd8vrmjt55Ezy7lKv3ftMrn6E3zxjUKM3bJLrUJQvl9Tnw/lotcswfuW624yzZ+4dIRx/WJC+84+hj1/XJ+vun+uj+82X1U119ZaVeuSsudFLDTJnG9269QZvcuKKrc//XK8Eirrb/0StHdqa6dTlPbrg81fk98yT5qtAFJ8yQ2+9N9I9vF8hXNwZ64qw22f/KUPfvbJFV1sJn/LdbLvZCDbpb5bReR1cK++ROSfWXK+bLIS+GuvsjLfLvdRL94MmFA/OExy9vlrf2znTs4XOlXUodfMU/8tW1gZ7cwHWXDfWAz9Crqyym5771s6xH3J8g7mePDnXRbVtl1AeR7npxk4z8Ap3yrw4ZfRY4tUSvnL6uo8uP65fFHvD15y865Oa1fZ3+VKecunOmK51OP84JdO+12uSuj0KdeF6LXL28q60f9so/C2LddOUG2XksfnFal2w+Itexp86WRxZz9YOfeuX6iwOdH7TLfWeH+s2irfJjleihfy2U/17r6NMNffK6G+kdPc1y9hBPF/21W65vAleXbpC+jwL1z2uTT+f6evaaHfLYyq6+9zZ9MMLXVU7plFNuRM/P7pK/vw91q8NbpBfciUY0yjB8y8P4lrv34D6PIZ58boV34JOk1NkL/pG3D0I37NkuezwW6nHvtMgbZ8d6+6KN3J+v3y7aKTf/FOisA/BNuxf615iZ8r+L4eOgWR7ZNNWPbp0v/4Z33od3plHv21zWIw+2RPr/iHrLKKnOrov21CmvFhrX4G7BnbVwd3dP0OASXIO7BHd3Cx6Cu7tb0+7V5X73+4077v3Vo0nJU+fsvdacPaDzIJfMy1IddyU5xbPMXP8pFXvFu66U9mGq+HWk+HXOVhq+GRmAMzIns0e/weKmYUyfmAjtH0b6S8i55XrtfBOA6e+szHj1GeUztDxUyI2dlcTndvjwbyeFS3qFkEf457nwz4OjERxzKAY9rpnYbm4ahrfSseJI6b+Fek622dH7k8pWLb3Y929WXlj4GYUvG3hjWSY2FlT54qYPY7fKdXngxwrJofhM5//9nPpATy+GdTDw1/42OM7pmXO9HTklH99MD+KAeMo/kSHcL6LhuH8DqDdcw8UVAqhdXeGKDSFcTDRxbfk0vKwq3LIpFcXraXhpUQCL5P2s8n7ef4Qj/naiQGoWPizxHakDdCwp05t2MJzFT8XBtEUe98iLp/siOeR8NDrGajmguhtO8aGc4kNVfpNcr+4C5xs4zyPzW17Lvgc94iNh4iMJOJIvkkce/kTl3gp3tQwhtqCO1W+6kK2Ujp9Ou3C9QyRn9/uJYtFGXqqbgXLzTDzqS0OxIUZermnF483hHPY4Dt+FoykcbT+vZ+61drQroOOgey5sSNTwbXnxnksyJyvcSJPPXWK95PoADasggDX9ZL8bpcMs3JMm3HN+mZbrEtzY8lHlo1ZeRGbXMfqDCxMWRTCfNQaZJ7XMv8uNjgNV9q/vw6VfFa7aFUKU8OzX+CA2BhW+UIP4R3p800QbqhVSuOmGeEGa+EHRoJxb4ZWawuvTc/Ka+Q0iD5r483QabFkV5v0cwju9yg52H/z5pe/u2xA5QmVcOR8KOfW8ld2OwvcjeH1MDG7Ny87p3vdYujgrU9M+o/89CxuNS8KsqTrqIlzYc0feZ2IQSckqy5f2YqTkSRnJk4sdIriyXyx+i1RY42cIrfJpOPJRAENlXqvs9OFohXAe3hePTrKfvfvbcVy4a79w1/a7kbw3PhqLoxVm1AniSi0Nl6wMIN9CDR/b/Mj+KIwfhyegwA8dH9R3YmR3A8t0tKFzD3md9g7MVFTqQz4UkM/18IENUVV0/LbVhaKP9fx3qB2DDptZ93gKKiYaua9CBornMPLCOysqiG8d0IeQuUZlvs9eOMVzc30Uj59h4mVDOsqvDeeRT3FYIDngkhx4/FTDoYPERzpE8XP/b6hVVMu1VzzY1dvC6y2SsemRQfwiE3Ebtaz23I0cNVS+W+eDclrH4HYn5u8w0XMzDb0zhUvyO7FtuPBteen9/jrmaSic0tzAreNsWKXV84fbgTPlZL4PuTDqmp5l5toRfU36ZY5Trr946sRMpFcR/tjiw8HjOp7Z58TuXjpeb+XCb1Yza/6SgtOSL9tHBFDsvomXxqRhrXzeL7oQnrxTOLRDEJ/HRrJ5wZ+4sknDZU/9uKRquMobwA6dlrecHpxfZeaG7ynodUz65qAdDXZZOOdaEhJOKKy8J4hx03QsFCa52TiCv0+Nxb0k4fuyQQxuIn4xxYW/lpnpSkjBgJMaNtztR9hFLZNWuVF1i8Itj4L4pbTKuyel564ZaZqbAXdzPbOOdeDULD13qtJXwv+Thf8PVrXw1KZk7Cuu8Pz5EHaJ59wQz9mfW8MLzwOoXkLPjecc+A3ZWXP+B/wZrmdUvAMJs0yspKbjzO9abqviwavjCnvuk/dtqfL+Hz5cL2vknCNW3BefH102gN/aalnzNw8WDTTSVs+KyNtRjJnyFeUOqjx82otyGyJ5+E00KsRoeKCmH4kPTaw4Ig1/rbPQ/T4JPcUP24ofPnilcHSPIA6GwnhWSUDcVy2rNHJjslX69hc75oyMoFImFh+Ga9m+vHj7Jh0nPHXi7Cvxsx5+bElW+LR0EK1vyv5M92NzBZmrfdLvOY1c8dqK1B4almwnnCe80UQ4Z+NeDV9cFG5KMrJe2Qzk8Ct8bwxCmZeVGv9njE/TskAxNwYelBw77cTUhRZG2JJwd4KGk/IEsGyjnknP7QjmM1H3MB1zzTKvKS6UPyf7sSGIoyd0PLTHiYgaGsauC2CT5P9z4eOJBg1z2wL4JL3UsqDk1iHJrZNeHG2h4bHRAeyfp+M/XidanBOv2+DF2f4y3w1smJVHpfaZD9PP6mjZ7ESMzG1tmdsuPcUj2rqw+qXCH72CqNdJxyW9XRiTYGLxX9NQaaWWu366UX9iJBfl/glvQYWWm8Lj4icLugWw50E4r4yKg0OrY263cH2YjoZEF9qnGDm4VAYyDutZ8JgdY+ZrWMTjR3ykhlWjA3hcX8NhCwI419/CDQ2SsV7y90M9H651UTm/m/RiWx2L/ubCpGZhzDMhETPXKdR/CKK8+PBh8eHsbRR+HBrCuosqP670oo7LwJVZM7GspImJZ9Jx8aWRq3tl4GUeDXs/DSC8WxjjOicio67sz2IXrKeMLLQjA/kDJj7Up2H1Ky2/9XDjX4vKJUk+aFrL/RjuQrt1Ov7+Qe73xTC+XJmA1LsyB+P96LtZw6ZP/PCGGxgRZ8MC6RmX9Ex4V+GWri5E/RrOr7vjMeebiUrDNDzYpnDs3SAWlNTQcyYApbaWykoPes3WspVGPGeymZZsqShwUOH900FcfaNyURcvVorfxJyxY93tLPww+Tu2yOd9Ip83yxfJq6Z21DoRxjV7EzCvpoaB1QEcX6LhvhQ/XFO0zBrlwecFOjZ3OpG3uYZPxwn3vDIzZw/Jh8p6NtzuQNvvWg6lG9eyyl5+tuKviUa6c1lxtqWOm0e5YFqrMuOTF5vuK3w2Oog48fPq161YJO9nk/cL9DTS2MaKTVd1fL7AiaUy30ky341qGzl9hTzuoJ6Zp+xY0V48YoAHD+/oOHqiE7mEG14LN5z067jP4MTpDIU7CwYx/KqWFf9ywyg8ZRWeallBy9H7PSj/1cADjTOxaYmJz1LS0OeKyhZLvNhdSs/rpyS3N6nc9tSLOIeZVXOk4PogI+fVsaKlnGOMnOOl9Etv6ZfqX1Sub+r9v7/nalngxt55Ci97g/B10zG8kwvLFsheOuW6NhNvm+BA5C6FMdeCaDhUx9mVhT92CT/+50dnme/eMt8Vr4Vx39wElF2n5YkPbrS4pMr7eaFrrdI93Icq+8O57VwcGlzWcvYyN/6skIVZ9//AlMERzFE7FsPHyef9xYMmgzWcXDsAW1PxzIkBBPVa6u0e1Joh/WAMwJGhMldBLypcl+swOxNF/tLxqsOJ/AOMfEwrXgiX9avmwsHhOp4q78I48diCd0L4lE9hi4fipeIPj8Qf3nyxsFuTJNRZquHyZPG4c1r+vd6NGWMVmgqJnwW13Kh145f1kbzzLlr+XEdzIRfKhlt4ND4Zs2ebqNWk43t9MxsuSEXb1Tr+9s2J5TaV8Xm8aJKkcmo5Lzbk1vPdc+Hz6QrrmUJYkGGgq2AmDM1VOsf5kPd3OXdVKyY9l/7vL/vUVGGzSSFUcFl4MGsSgj2FL9r40CrWwD+qZ6JtVeGAzQFMFK7MJVxZfYCBG2nD+5omdlidjpGTolg25zfMbyQ+PM2K43KfDzURDy0k+XDDh5bCi6OFF5/NFY70O1HkspnXlqXgjnj5pAJO1O5g5Kr+VqQ117LkWOmjo3paD9kx9YjKLEe96Cpc1U24qmpWLbd+8WB/b/HgFgEc7KbybCcfVrzXMK6dH5eHabm8ogf1XkRySd9ovNuvY6d/nJgovZJHesXxi8rct314blTYPyOEx31VDm/iwzrJ8U8nvNjiFz8z2BEar1LN54NJ+MM614lqXfTc1F28zWlh72xJOLtM4eYE2cf1Eaz2NgZ1j2u5dL8buWYY+cpoxfW2wg1DkrHGqfBbtiDO3peeG+0HlyqcmxzEGOG3osJvRR4p/Hd4EJveR/FZu6/Y86f4XWQA43VmFnClIiBzZZK5yvtGz6dd7TCtlPP8dOJ6f5VzGvjQizq2nutChQg9D8Y6sPSp8MYgt/St5GdpF1JckpdRfkzvHc6wlvHAdyP/YgacPxXmrBUEFkuPpAsHDdNxZUUXdmwQLn5tx8JsOto/ujB2v4WFzyXhrfBZN+Gz+qfNXLw9BQduGnluWgY+SE63G+ZCqYp6ntnrwJe7OjYbL3k3WssnxTyYI7wfeu/CAtkH5/9+rpNL+uWlD9VkLreYQ7gh/j5b/L27+GFH8cOn8RoOqeRHj91hbH81AUvkuqd2c+CF9EV/6Yt1KzT8GOOHkkdL9bkHxwrqeeSmA3tr6vjvahem/7AwrH4Sbsw0cbYuHSu/6RjbwIl5ZXT0HXOhzQQ9R+R14EN+4ZD7HjQ/ZeaEnSloQ8nhuR4MPWRm5ZMpKN9RxyN9XTiTw8ytb1Nx9aKGi1b64e+jp7GZA8+/aDioiR/7JF8vSL62GS5zVMGHkvUjePavWFTor/BQwxBu+zWcavCj/QoDB8VmItuPKH6q/xUzu2mo7xxAp+Mm9tmfhqhtBkbfy0TTKAsnf0vG+LFG5itkRbbcCj+/EL/pZuB/nW24Jfd3mtzffrEaNq7uR+koLU9/8+B6ag7OLv4Wp/9UuTPSh+YVxU/3Sr9PVPl3Lh8eVzNz+N+p6CXPbzXYjU4d9Ozb34FVYVr+SPDg0nsTV7ZLQ+XxCnfkD2FRl/y0dnuAznk17PMkgG2pBt4rngmEmTg/MR0V9ksOnsvE9YIGzr5pw0LZJ7vs0zyZ96DM+19XIulaHI3urQ3sNNyG0sL9Zyo40OSUhtN2+rFWev3TeyfmCscEhGMKS29dyxtA85N6jt1tx1fhlMbCKcfDFB5MCGFcwMyC+hRECLfGlAtg0MUsrLfqO4re1PPydDvKz1V4JBDE8gNaJpxxY3ML8dPRHkRIr8R3duGT5H6rfT9gqqEwfV0IumcWegcmoX4DAxfPtiFOfKCa+MBJ8cVd4ovvi8h8/Cv+4zdwuEF6zyEemiOID+Iz7cRnrrm0nBflxq2+Bk5rYsPb9Vp2e+vGoG4K63YOYcYsDS1a8ZDyKi8f9OHI4DAerZWIz2vD2OJjAkovMPKUKwON5Dwz5Tw9cmjZ8Z0HaZ30LN7LgQt3TFw3MQ0j6yss91cIs98aqessHif5clHyZcZiAy3pmThfSPrj+v/+fZjCeRbh0yUKc6cG8e5NGDt3SUCVgQZuq2/DP2s13PzRj6XTjEyxCC/c1wpPuNG/kZaNpntgFz/J8zCIfxOEeyp6MUUxMlvIig+S9+1ahPB2ip5doxx40EG8oV8AWxZo+MTpx4uNOvZ97oRH5jmyUwAvJD/7SH6eMP+PgwJosFx6N96N5fL68fL62S6o/LLGi49bzWxzPwUlfgnnxTvxqDNa5YpiwscjhG/LhWD+qGVqazfeDDGySw0r/i2lcNGpEM6E67g9zoVbU3ScFuXCxMdhzDU0AVfrKVy4MIQXBSQn7lqxcJuZmXdToD0cRv/xBFi+mJjWNA2FhO9uCN91y2Zg1082JBZVWOFyCLl/1fHlbhfSCwknXw/gyiyFS1Xp7+J6tj7vgLu2mVHLU9F/bBgbFEqE+szIwMAMvFursNPHIBbl0tL20gPNDy2D9d14n9vMDs9TcU+4a7xwV97zOj5bJz2yXs8t7+x4PknP/jkcWF5W5vWIB6uHKfxeMYSaswxcq7XhcBUNT24RXmwo92GGcOE8DV1eP1YlCHdWtKPAPQ3vjfOjSEkd/zvrwumOCrf3CWHV83BG94tDuTAzDyemIvyFgfF9M3Gwn/hyIxeK5JA9exdAT7eBbbNkYsNU4alw6fXaeuZd4cCKqpKXm1zITNOzQFE7Sp8TntngxIOfZo6tlYKkairL/y19WdvAD8ttaPJU5ZTBXoyMV1musvjiIpVHrF4MahLJelN+4tQM4UpDCMmSO2Ukt0JV9VQ3OTAvu4He9zY08mg4PcKPfm0VNvkthM0W8cCkEC6kK1xbOAj/fyaa5qUhUvolRvqlXEc9D/d1iK9qmFEvgOQ3Gpbv4od63ED//kxUl/nd+KsX16K1nF9HOPbvSNpfRcMqe1xQ9nh9d/GcDj60ET4cKnx4b5OWY5/K3j40sO7ITFwJ13JJnPSK5ObBXP8/t8yvrKVvmweO8uJZB1wwVDXSvtmKV/1N7NkgHXt7a3m5hQf8W/bwlfjjmkge+RwNpY183mEOLB6sZUYtD4rnFN59E8LvVVXW2OzDZuHfJ8K/A/qobNDch0UbLLS9TkLuG1o+nynzP9vM7koqXjbXs/dYB9b/beDbl5noZbKwbXoyXm5S2OdpEGFyjpSG6Zgnfe+Xvv9ZTmWtQz5sjRSfiQ4Jj1l4QLjTu0f263IQbY0GDrXacOCMjue3OLF/hJHny1qRpJN9cIYw9piZhQ/K/jTV0TNJOOK8cMRaP/4qo9J1zIe2FRX+tjeE/ckqz5f2Yua7QtR3vI454nFB8bgpK3TMGutEpYk67srtgquKntm2OND3lZ7Netpxdovk0UPhzsIRbHotFr0P69jmuBOL6oXTvjAeY+4bWXRMBqJu6/htihNZhhr4o7IN/3WT3O0kHvPMzD8GpOCSX8sVBuHl33W8W9WF32Ufqsk+PHHL+SKD6PpWyy6d3Xj6h3Bycfnz6Wb+bkrF56waNvscQMkJkTyT9ycSD4Szwpk43CqpcvoZH+5GKZz0NYTvrRRyRAiuPyOZPfIn3tqM7JQnA2tkbr7K3KypnI9ftj1Ct5Madt3tx4fMMHbIl4DzR7Jz/ZH3yJsjgo/fxaKzxsheASuGl1FY8XgIRxXh+WAyFuTV0vPYg67igV3FA/M1iOKT2d/wx2QNS2eX/dcozOEPIaapgbUm2lBmh47HbzlRuLqRNzZYseCkQs+uIErLPp3+YIOyXaFGPHdii6zMNeYLAv01NDQIoMZOLdffcOPYCB0PlXVhhniJ4VEIX1WVTb0+5Iwz8H3VTFTqbuGOjskoO1b2uJAPNJo5z5qKW6cjOG1bDBbK69nl9SaV0DL3OQ/eGeX6ZniQp7rK5xt86H5Wxw6bnViXNzc/PX4BZzs5/yCZv5pG7l9txeXSei4/4UBP8dy2ua34UTOC9VfHYl1Ble9v+pCwRsdfPzvRoZiRAy5ZkdFAy8KzPMiaQ+HXdyE0EN+ce9qBxHIaVjwUwMv5widuP94nqmxf3ou6vY1c2tKKPcLjV/e5YRggnswAThYzcc/FdPQWjmlZIhMDShjIczZMq6ll2BoP6phULkv34ckR2YsjmdgifflI+tIvvWvulIFXtw3sPTkTtfPpufqRA7ruZro6pqKQRXg5Wfa7h+x3Ow/qizcudHigPtLSP9yN2qWEq0+74HmuMrKfFwvb6Wkb5MDOWB1vVXPirzQjnUUzkH++jg89TjTMq3L2Ex/an1E4aKvwRj4d2z90odkohRNKhbBbzn3tHxuKDRGuqOlDT9XADj4b8ilRfBr6hgqStweNkqPyvKfyPFMvycnWwplttOw11INHjQ0c+acNWYSro8VnRiRoWO5XP/JFafj4m1ynuXru89tx7ZGGc4f5cWK4ifvKS66Id/vFu+3CwXmFg/eX0fDiMeGbv/S85LCjpczPaJmfcavDWPhbAqwbNSz43I85O4TTb7lxyqHnrhx2RNWS/lrlwNtssi8fA3gVJ+eq4obtD+Hd4lb0SDCy/a8Z6PxCZe++XuSopvD9xhD2SO9ckd4Zmm5k5SIZQBuVC4b68LCLylHdfGgqezIxILn1TsciHcWH5ytMdAexP2DkBV0GYuT7WvK9tbbKwst92NRDFd/y4dRSA3clZaL9HjMHXUkRj8jNK6tfYFAR4birIWztouWjbh70r25iww3pOH1bw+2T/ci2xMTPqWlw/e/vxebIwGDhl7rCL1naa/hzYACWbxomN/Qj+bqF5Wcl4UJ0BNfWiUEeyZVn30JQdyr03wjiZ0cz6/RJxd22Wk4e4sGyF3LuPkFULSt+dDSE7OJfn8S/dlxQeHt1EDdHazizaAD7r0huL5b5kflynXKhazYju32y4kyeKG55+g1Pm4qfTAxAu1Clz+b9/7j+1hotZ3x2o5hWyyseD6oc13P7fjuWyuumyete6qRyZS8fZiYq1FUIIkl4vcIOP7b0COfjdvEouqAAr7juYshODWvd8KPmr2FctzsRpfcZeOpCJkblNrH0i3Rcj4zg7OhYxIiv1RRfCz4wUDsqE08LmjjoZjosWcxM/pEqPpWVpmZfUEv2Y2V/L1K26lj6vhNjK+lZZKcD2ydoeSePB2U/R/BY8xhk9tIxXysXTh3Oyt3HPmPRaj0zvwoPLdXwfrIfnctZ2PNwMg4sU/lPghe/Hgnn3qNxuCY+N6+PC933qux0yYumBy2ceDoJGVclXxZIX4n/ZGufhrS/DCxmz8SyWXomqQ5Mn6KlJasHDW/oOWemHZYwlamJPmT1G/nFmIEvnbKxaa9POHU+jLvXJuCI8PER4ePjTw3cPygTUQ/C+P2PBFy8rnDVrCCuxZq4oFoaTsn87vzdhwXrhOve+5GaGMmS5aOxb3sYL9xOwC92M+/kTkGWfpH80fgnBiyXHooPok2UjsO/ufD0p55Datmx5E8N0yPlPufW0PMigO7ChZ2ECxsF5D7r3RhoUYlk4ZxPwqEt3DhxIUJyOAYOyY/cNaTn5PmT5PkbxX9fzRF+/UdhaGMQcQY9q2QK954wM8ueFHw+p2XL9W70tVnYPG8SOtw2ceCUNPTN1LJZfjcqpxi5o1QGauw3ct25DGwfLHxaK4DXYw3sXtCGwDmFpvVBvK6nY8+FLnhvS+5OdqPrZZVdlnkx/Xxuhq97jq6rNez2zY+/NRF864/FyjM5GbflNbY21/HRWBeOfpDr20ZyKUn8s2wmJt2yMMfUJJxNMXFbyTREf9azbjM7yhSM5MlbP3FxvY6r3zrRXTGzcygVy2rKfq0Owb5XOO6iE21Oajl8txsrfXrGmey4eU/HGWOduBAUDlfFb7NK33x2YMIALfPSgz1hel5JcGCM8EWRLEGcS1C4oWIQFaTPD4aS0WS6kVPMVvTfZ2TjCxkopup4yefCm+Ymdhubjr9rGfh6pQ0phVSWuu5DfYOFC2zJmLRKYc4fQSwdozC1cAiTZyrMpQ+hlXD5H8LlVfuHc2vDeOxvmoUXJv5AY8n56ZLz4w8Zmf9kBtJGKixROoS/Kuvp3OZAoWYG3ppgQ986GjZfFsB6o4YfMwJoLxw7cLsHc+U6BuU6buph5It2VtTOCOfyQnFICgrHq15ka6zy81QfludVmPQ4hLj3Kqu28+JCTwPXtRHuRySnzv+JP4Ubo8pZMfuimbpVKfjjpJkld6dg81HJ10Ne/C69UkN6ZURlA8tvt2FRVyNtXayoIXu+/oETjSVnp0rOvpQ57CNz6CwkvHA9hCUO6a0cfmzP1PFOPieGrjSw6s9MnJJc2L3DAeMahZlfghh3S7h+aibm5pHP8zSAiFc6xvV0YmZFA/V7bXBK/mSX/Flb28ivy63wXbDQsjoJszNVqvm9SKwn51pow6arBj5dkIm4gmZWvZkK32GVlmNexJWxsNqxZIRmaKk1eLDtkJb3Trgx8rGOZYc6kao1spTbipnD9TRUcKCh8PNMVwjaaTr6wlxoKHw7U/h2brjsVZzwS5yGs6v6cbSLjoe7uRCf08gqr61oU0rm8LQH9a6rXDzbi5ybI/juSQyyyH2KaevHmSgDt3+1wR5hYJ4YG+5GmDgpJh1p+XUsdt+FfjsNbHwjE3sXK7yUFsSX81FsuvYrHoy1cEzBZFz5U8tlkR5MbB/OXAPjMXOZlvpEN/JM0fB51gCeyNdhUQEMO2VmpZ0paLZb+uiq7PtELZvm9qBHdeGQDS64ZY6yyBzleqzy1VAvWvsUjjQHMb+dib5B6Zg4Uc/cuR3IJ/fzUZYA9vQTTmnkw6ds4vsfU/FUHjdYHre+Zxa+bfMDChWq80Ko9Fw8sX8QWc8Z+HVDJoo7jbyYPQN9xXuaj3Ng5lwN9QE/Jn5Wmau5F9OEc8PvBDHJomPOZDnXDA2zGgLQ7JBevRXEW/na5WYQd3sLN7UIoUWSynHlvHA3URg1WeZ5iYZVU/z4fl3PhrOEX+obOOYvG9oLTwzo7sNj8ZCh4iFrhHO+CueUP6fnkQ12tPvTzCFZUjGAUeTcbzD/kN6sL3N6Scd7y53YXlMnPe3CgKVmNkhOQaV7CneNCyK8QDjj78Wjfl2Fi5dI/1czMbAxHQtserrz2NFT9qSd7EmvEirbnvOh5H2FZ8YEccQkvJ/mwpErBh5dnImhsueVy4g/fNHwcBM/xlSJYuGt33Azv+T9fQ8a9DBwTnsbvv+lY0O7ExWEq/fd8mG25IEqeWCebmCaWfbwp4aZtfyYfVRL3WE3eguXtzzpQIrkaBnJ0WFXFVb6S/jpq8LzjYRD7yrsMD6I5e1zM2nACySuCWPFzwnYKDn1UhdCubeSy539uHlb4YzJ4iMxFp6qmYQyVRWe3BzCi1E6Dijpwj67kZdzZWDEboXlrwZxfKXKA9FejGmmZ7EJDuTqr/BVwxC6V5Qe2xvAkClm1s6aijvZLZzyPhn3yysceyCEYq/D+G/3BOi0Cr2eELZVMvHejnSsEF6JE17p3SCMLWcnonZ2LVd+8KDuMR2XHXTiqXj6b+Lp76hjpzkuFBOOurLEiW33jLw7NgP5Vlr4+GcS1lZU+WWPD8ev6HlwsR1/yHUpXSoNO5spvDU+hNWbDfz+OBP+SDNN0al4Jl4w5KEDX/IobPY0hDuvNJzcww/9AJVO+uB5GMbIEQlYXUPLH2s9WCP59e1zENOkN8KkN6alGxheJBMPVygcERPE14EaNq0XwObHWj753Y16BoWLbCH8XUHl230+tFyVhWN/fMe7oPimmoB8wxU+qiDvG6vhlGp+TJLH55bHW8SzksWzPj/UsvkIN6IULb+FPBi3TMeCiU5MPyJ5eDQJi8cbmZ7Pil4TFLbJG8KnVyrb9PCilHj8GfH490aVHTJ8OFfVzA2bUmFrp2G+QQE8WKjh6Ew/bPHCZZWcKJhL8vqVC5tzin+8lq+HIvn4RDT+7axySQ8fxv/UMl9tN85u03HrXbnu6/R88N6OR0EdR6lOjKymYZmNAfz2THpjoB9vCoSx291EdJTrMGCoGxPmmZnHl4LJ4vnZxPNL3dXyzAQ3Cq/V8tonNw5Ijp2VHJsxXXjDHMJot4ElsmTi7jQTJ1rSkUNy+GN8CGXmqTzu86JvKwObjrThRzEdedGFfBu0fPzGjb31JLcW+jBqmYElEzNxo0oYZ29JxPZXMmc90rC+ko4fdrgw/0gYPUcS8MdMlaX1Psz/VUPfrgCe+iM4xBCDpZ00TOkVwL4mEbwwORYbD0Tw9ZkYdE40sUeFNGwUn3shPrd6mJbfK3qQ47iBH/ZLjv9i4JXbNrw2WtgrIxmjNutY6okTW+uKjyz24KVwRl+TFWMdESycMwaXX+fhym5P8et3hXsYhLognEFXHHoJv7YWfs1/SPzypBNDXqus2V16rYFc31kBfGumZZPxHvTuF842jePxvY+ZbJaKjjN07Gd0oVoBhRvvyX4NiOB9xKJHGSM7HLdiXpqB3qKZWPNW9qCTnDfaxCt10tBNPK9LTjuentdy8Fo3EmXuK8jcp2RXWfa9D+PEewqJ9xQXbrrwaxDfSmrZ+IwHx76beRgp6JWqZZsSbqS1MrHYiHRkeWnh995JeNFfw74NAvhnnoYbvX7U8Ri4IiITf7RTWXKwD49GaTiyZABRQZlvrRulv4bxTOMEnPw9nHuqxGOO+Kbmkzx/jsJNwSCaxaicUNOLon4TrxrTYK8TyTxLf2JpLS2TV3rQuq6Jw5ekY8sFIx+uzsCZ/9f7HmlVjnD7kFf2/+mTTJy9KnO8wPl/v1ckYUwIb7ziYZYMxHYOZ40e8Rjc0MA6M21wDDQwdz0bps01MjyQgedXjBy0OAPm+yrTxngxp46O6jIXsjdX+GlcCM+kJ4YsDuFHZy3Zw4OwhiqTZ/pQapbCU9oQTK10TBvpwrACKive8+HE7DDuVRIxT7jE97sTrlFZGVXyC/JuFd98EESs7GdN2c8x4gfFxQ9S3+pZupMdq1J1/FncibAI4bzYEHb+EsZbtxPxopDc/+suXPlPz6Xz7NhgN/B9rkyM76TnL70dqLZH5uJKEAffW3i6XRK6ts/ObgM/oMNTDQcO9kMrHuAVD5gWFcbwb4nId0/2apwbRYRvrgvfzBGeVNp7kXtFVr6M/Yzb4VpOi/Pg1WwNeyoBFNPL/DuEX4XnmgrPud/rmbWd9PQ6lWM+eDFLeEsvvNV4j55Tr9jRYLSF84olY2MvDV+2CqDhM4WzBwZR8GYu3p7+EnM/RjLQKholdul5/podM66pNM/1gpNVzs3uw+sIlT1jfPDHqzRV8uLa0EjOrfQTV0NG/qVkoM0sHYdqXVhfVM/3lx04c1d8Y3waDlcJ58kt8f/387hrG6wo9VDycqQT5tXhTP8WhzLJep4oY8eToXoOreRA8s4wlruegETJnwpeLzaPk5ws4EKjHzrOrO/EvZUaToj2I39xDR9eCOBhwMxRuhQceGnkmV4ZONhdy7MdPDg7X8+tbjvCnhqZMDgDBSYY+CCvDenx2Vi00kdkPZONX7Z+xPNxBg4sIPl2WMf6x5zo+y2KzRp+xfMMhf0LBjGinZnlB6di53YDr9/OxMepKluH+/Aj2cgGpTOw/KWG8b38iE3WsUZpJ7QPpW9HBtH9mp6d5toxuYXKHGN8OCmcuVs4s/JUHbdHuFC8pJHnz1rRel4ER/pikN0huZsziAlP9Mz3mx2hZOHp0m5sFN94Kb5xY5ORM59m4MIYHdcVduFJvezCXx9wQDXxH286VuU18Mdj2aNteua4a4fhvJm2dSn40jKczUfFI32qhkXCAxibL4yFHyViYRsTnb+nI1hAT91dB7o0tLDnzGRkbWLml//9+/cVJo6PTcPTCSb+nkc4+qCWJU65sXmXwsf/BdF5q4a9HvjRq4CJbe6l49wIDdeXDSDhmMJfDwRxa7yWU/N5oNsi/PbIjtb+LBxh/I5+H1U2au1FpUIa7r4hvmXWc1CqA/XP6Llwqx2HY3Q8UcOJscKLRVc4cfmmjsunOTE7n4XKo2QUEF998M6J1+InPfN5UUZ6+/hINzq1MrPvyFTs76DyfD8fLIvCmWyNQ+GAjtf1Tpz7XeWGKj5Mm2JgWFYbCpbNxTtHX2HeeD29+RyYlC48UCSIQ2Fankrw4BeTmffSU3F+jobrguLTC03cl5mGE92zcH+HH/Cd1NOyy47za8VvPwax/rqF72YlYZl8nyjfx7pV1oj04mNNE1utTkeZiQYey21D/w7ivf2t+Do8O5uU/4B95Q28dMCGOo3DuWJqPJbXVJmw2odoi4Z1kgLoJBzcRzi4eJSRF75ZMWKgjhXru9Aii4ZjfgTQv6eRDdta0XdIGJvVTMTFWVquUj2Izq1j3ecu6EZr6SnmQXyiwqrlgzAv0jLd6sYM8XvjJSd6CGd2EM6sONnAvdlt6DRNPDMsgBj5WtsSQMF/Fd5eFERLvbyfI4Bq2TX8+0MA6zap/PjUiz92iqff8KKN3OcRcp91wp/ugyFcSjVxRfE0FC2fjVcPfkK1cmHcdDgRsR+MrNkmA392Ec/oHsKCO3K+icIl93U8N9qJVmu0HPHFjRoP9dww0o74geLV9XyYeEnPXCvsWFPFwC9bbFh2MJJJp6JxJ13LSYXdeFvbxG7L05H5QPhplBufW4ezxbB4REpP/ZSeql7JyL93WnHzHwNn/Z2JZ30MHNTMhrHiR0XEjx7tMHHkzTQMv6BjhTVOdDBrZS496F8+gg0OxqK2XcMVuf3oEadl+6oyb0PD2bdyPE5kGLivYCaOlzXy4BErNu4VPr8YRKbwav7CmTiTT+H2hyGkpetZtLAdl+touHJpAIeyhvPE53hcXqlweXQQN2tqOXO1B7NmGanXWmHVKPzFH0IF4aiDO8WTu4s3dwwhyygto0t5YGqtY8ZwF7K80PBnXz/ynNXwxWY/Zmw00fIiDVfFixaKF2V9pOHn4X6ED9Iysa4HB7aIrzwMYou8/kN5/X8/6Li4jRPX/eLdBj9cqsqsXh+qmlRuSfeh7BctTzR14+XCCPbLjMHejpG80ucnJkiu5ZFcq/lC5dq+XnRfamSn5Azs/KDyRhvZ71ta7prqxqZbCl/8GcTPieIpuVKxTpOFH/0/0LOcke0PWxGfZGaVsinQrQijKzYB8W/DWLVTAj7NFN/SOTAsj46Vn7mQKSacXzy4UaNIzpj+EwelF05LL8RK3teQvP8xMIKsF4uP141sPSsDk99pmL2jH3d+KpxUK4hSNjNP501BWmENi/4XgPkvDVMdfhzJY+TRp1acaaXl9hEe9LlsZItlGRhVVjzvaDpyz5fr6vFjayEDH123obfLzJZZU1BRo+fegANbr6p8sMALX1bpy88+rNKaGe1OxdRYEyOrp+HNsCzsVvEHHoq3jWwQgruRjpHTXLj1Uc9preyY0lTOOSkAw5ostH35jv7C342Ev2vLvq2UfZtWXWHEhhACJUzU/ZOOwXs1rHvJjxbPxSv6+3H+RzjX1ZMeHRfO47/Eo/gSLS+murFzs4U3Hyehs8xzH5nn9H8sLPJ3Egrf0fP6JDs6Sn/2k/5800d4rJkVazdK7j9PQdFuWv7bWXI7i+zlDw/2fBPOaJAJR0mFOc+EsFRyN01ytz1VDpzrg+mcltYNbqyUx8d89yDffS0fjnFjUtUw5t6ciJ+7tKz9nxufhLNa3g/C29XM8C6pSF2pY4loJwo9VnlzqBcVt6rc/8CLfVk1vPg5AH8FmeN96ejzh4ktS6QjYaaJVXTpKPS7hjeqBvBDeLx+o3gcqqHy1FofIn0axpj9aP1Yw+FD/ZjeUEPTzACOmi08mpIMj8x5pMz5Qb/MjUF8rbSZHU6kou6TCC7/LQYrxVtixFta6/Qc7nLAUd7MXAdSMSlCz9yxDvRqaWabP1IR90lytEUQlXLJfL/y4I9tRpa4l4FSdnnd3E50mpiNvXN/Qt1cBi59ZcPPFpKjowMYUVu4ZkUi5v6hY7C4C5EDFcbWDyHshvDJTPHqTeItz9wo6dHzXIQd9Og4P8KJLzV0bLLWhTcbFHZ/HcSOSypvL/di7SYNvzz1IyWrlmU+e7BNzn9fzn9b8mxKkguv+unYu5ELOd9o+b6rG90lXzuNcSKQ30DjfRuWn9QxcZcTK+aojAt68bW/nk0bOPCuoHDLTQdG95D7096FqzYNF+bxYwRUlpvvw+hftCx+x4PB4q/13rjQOK/M8RMHfmtvYq2B6Sj4Tc+bDe3YUULhnX9CKCbceDnWhwmVVebf7sNdq46TCzhRKZuWuz958GchLaNueBDxMRvjW39E5hAt89fwYPzmCBZ4EoMH0hd/SF/0En5pGxFAHdmzZV8kl47+7/c5ZwDCY/MifDDs1TLzkhvV3kdwc/sYzPzPRMP8NPQup2HLwwEsL60y8YQPg9NNrF0kDbGDFFavE8Li4VpmlPfg9K8W7tyVjC3C7Y8vB7FT/OKW+MVBg55nMx24J3444YQTuil6OrM6EN1V5q9LPO4J70wQ3pndS0tNaw+GO8Q3cnox+4RCzd4g9EPNdFZOFf4Qr6oSQrfxOnbL70Ltqmau2JwKNzXMOieAMn8YebyEFefT9Py7qB3Wzip/6SHc2MzM3eNT0V56d6D07s0DOs4440SLa1qOmevGsHVaVv7gxsfzGrZZ68clyZdVki/lO2h4uL/wwX4tG55zo31lEwdvT8dNk4Yz0gL4ulllk8de/JhuYD2TDUmddSzXw4Vyk3U8lN2F6rPEI7UB5G6t8sVwHyr00vNgaweCKQq1JYNQeoQz1D4eocVmKmkpyN5QuH+mzI9Gww7+AFbd1zN6tB1N7lk4ZVwSpjw0MPvITBweo/JkYR+edQzj4D6J6JekY+NyTtytZ+HEhclY3iEL4/v9wDvhuI7CcUN2iR9ey8Rg6a860l/9V+jYMNaJZR/FQ1u5kXOwyje1fUgua2DpIzZMPSG5uzcNvdQwtvMl4r18jg6qcG9XI4t0teJVnJ49q9hxfZyGcwsEUGCBwvuuIDhMyzm/evBotZnDv6YgZzWFbzaG8FD2faTsu82nMK8piNGVNCyxM4CewontEj2oIp93WyCAPtJ7zdt6cTs9jFMKJyC1t8oSLXzYNNvI54oVBr2edocDGeWFww8EMGlWOPNo4/HnXIVZAkEknjGywpYMVIk2c0fdFBRaEMnbrmhEHzKy/okMfHugsvEoL7q20bLLUA/myt4FHjsweb6ZOTwpiPlLz1p2O07Ifd0v9/VdDw07tQtg7XeVn+HF+PFGFshvhfN/PjLah52/hvPmrng0k/s/wWyDbbaJ+ZV0zO8qftzFhVahKI7WfEXgkOzRiTQcF27YL9yQt3AYn19LRNHNGv77RLynQwRr9YtFyecKz/UPousx6ZODGRim1bCiJ4DM5Rbmj0vCaOHA4sKBCZcMrLQ8E1M/G5ileSYuSQ+ukh48m6Dn5op2FBY+uBH4gc8DNGyOAFb+v78f2LPKxCzf01DurcLDnYPCrTIfV71In6dlMa8bq66K/yzIxAjx4PLt7TjX1cC/u9iwtoWWn0Z7sP+bnK9BEPNnmehT0/HZY2Lz8DTkPGvm680pyCt88Wx+CFWzGrjpi/ig7EH9bC4EIzXURQdQfYCWG+iBVh7nkce5zijMuiWIFJ+OJU1O/FHHwBLLbNjwSsM3Pfx43VBhzxkhfL2jsPHEIF6fVdhjUxA59xj49kom9CtU4R0vXsxS2FcN4egLLY/0caNwGR1vHHdhttlAbaoNke2MjBksnFRNz6p/O2Dxa5lkdOPRWJUjC/pQsa3CA7+F8OWLhc2aJOH8Ah3XO52YV8rAwCkbFiw10JWUiXZXFf72VxCL5JwZS20o+Y+G/2z0Y3AOC2u9S8aT8Rr+lk9yqJGGJ6YH8N/XcC5oFAd1mYG+xEwUdht5PUsGwt/oGNfVCeVaGENzE5B5R3JqoheHekfwdItYFMmdjddefELnn+HsXTsO+z6pvNDCC101Hd0bXehT3cQWG9JRZ7vCFXeC+H42nA02xaHcRT2PrrIjwW5ghVwyF/OEf7x+jGpuZOlxVjw/pXDgjiAeWlWOKiB7MF6RHA3hwDTxfosVbcprOfygBw+rqPxjiw/vzpjYaUsaOgtH9Nqfjt67dWx11Ym8v/zv93x7UNJo5DmrFb9vFd58kIAjrYw8MsKKoX8qrJwlhCmHhINPepH60MxSI1JgVoxMD1kR10DLqrM8WDBAPg9caFBCxznnXGC5CM49HIuHspejZC9nnzJSuzMDQfFc7fUAdlQ08c6edPRdpLCJNYil73RM6eBExkEdC59yomxaFh4r9h33ZN4nyLznKm7gqws2+P5WaH4ZxKX8sif3HaiuGrnBZ8W9veEcfzEOr9uEsdfviajzI4wr6ieg6yczu7dMQU2Ece38RLTeE86RV+JQfar4VIQD5fKrPPbAh4LrdLz9wQmz9HOq9HNUXpU/nviweGgWWiv9gLeCjhH7XMhTSPj9Rgido8VD66Zha1sDHw6xoYR46Xnx0jxPw/hscAJi/lZZ66UXrmlmZrOkQn/LRMfUNGQaTfwlIx17hFOuCKdUXKrhvmQ/NsWH80WlOHiX5GBEylucbJyF+/78gfq3xFumSr5LniaLR4wQDykvHkLhgnk1hQs+6FmgrR1jCyosdCuE3FNUvszqQ/kqcv+2WlFcfPvSmnSsD1f5Ps6H09X13L7egZWqjtFeF+JamVl9RCqGNZD7PDuE6tKnG846MbVHFCPaf8OaHcJVN/049K+Wpxa6sXe7mVdup+BvczjfpMSjzFwNjwX8+C+rhQs+J6PiwjDusyWg1ji5P784sOKdmfEdUlBpkpZ7cnpglzzLJ75TfLBwbe0APqoatvIGcLu7hlM7BFBK5uPsjQCWHIxixqmvMK5XmPEuiGBd2f/FNjTIrnL2Bx/GZVdY4EMIPaeb2c6cijviw1OW21HepfJAVi/i+8jnaRZCsHs4tR3i8c8XlX838aL6VTM3/ZWCGwXEe+/aUMAWwbt5Y+DvnpOGDm8QNUpys5QN7XNpOPhVANsuSR8uD2JQJZV1d/rw1B/J3wzReJbTxEGv5T5OjOLlXN9QKcbAHTUzEb1SZf1oLx6FFA5XglgiHJ5mEh89ruGz/X7cl7wY28WJ/DmEb9958LWayqZ/+9C6iYWjpiTjwetwjukWhyFvLKzRNQm+aWG0WBJhkHl0yDyeltffIa8/NVrmsa54+X4dB5xz4twNmecZdpwRftom/JRNPPJLuRT8d03lX3O8qJVN4ZpPIQzfqOGvL/xYUlhh6n8hlEzT8mwx6Yu84guPA5h+xUzLkhSUzq3y9AsfCoc0vK6Rc9cycMxKG14EtBygcyP/Z5WPm3sxp7WRwWFWdJlmYK8wG16vtLB7dBJONDJy/zQrrsxQuMwQwsIoOf9X6VnJh1v/SO41lj6bGsDoxQpLpAdRTvLjCF14ulvhoH+DSHsovDDCi0IODW/m9KOS+PmuQAgd3kZwQOcYUNFzfsiB3u8i2bJjNPT7FDovBDGwuPTUBRf+ya3j5ucumMRnMsRnCovHXReP+3lZy7pL3Xhx2My+x1IwJ7uRmg9WHPimkT4U3zfpGZUmuTJGww1FApi8Uc8cL+z49b3weXs33j3Rs/MQO3wzdDQZXOh61sSum9MwTfrPslU8qLGcc6rMpVzP9k8CeHZf5aDRwpV9FbJJCH/0Fj9qKfmdT2G/hyGUHa7yaAUfUufrWMrtRHxzHauNdeFFTx37t3Fhfi8Nva0C2HRE4dMjQUz8mpN5Gr/GmmiV3+p40SNWy47V3ZgsPJdDeM5zXc+IWXbUuKvhugni87/reaSKA8V2mnj5RhraSe//3jAIT8EcjLz5DpebGrh8og3rxafei0+duaXh1j/9sD9VmGdQEAu+6ehp4ES7urn425JXeCb8O2iY+KV4RLR4RKV4hTsrB3HwpuTDNDdqz5Qe04fwaZ2eLd/bkbxIw/IZfpwtZOS261awjIHzj9tw5VA4l52Iw4WWcs5RAdyVPpkQtCKsro7JS1w4UFnHs9tceP+vwvYLg+i/ycCGzzKRI6+F758kI0dOhR/fhHDzlMrpO7yodErPXTvtWNZbz+QWDmytruHj9QHoy6l0Hvbh0Vk9R2yyY45wWahfEF0MGvawBZAvp4FP39igCDeHhJs5SzxT68K8Ezp69zjxQfa8rex5sLeB+hZynZ5r+K6fH8MMWVjJ9gNfftGz+W0H6nbRcGn3AOIqa1llmwfzH0TQNyoG64epfFfRh0F+C2GUvVig5V6nG5XNct1SQzipMXL3//4/MR0jWL5vLJJrm1h2eTqenTdz0NoUjB6mYfFfA7AkaphSwY/u47XslN+Doj1V/tvWh5iQytqKFx3Oajhgsx/aOTr6Q05MemZiroFp6JlNw3afAtjaQ8uH7cTH7ghXT8pAp/l69vHYMW+tQv/HIN5V+R8fpGPh4Fy01XqFvI1VPp/qQ0KyyiqlvdjxJow3uyTAMUTDHDUCaCYcOOFnAL949bwTZkfBqfLfIxLR6p6Ro8ZlwHwijGl7E1Awwcybv6ZgWT4tEx96UM9t4qIsaZh/U0/3NMnrDxoeauvHlLNGZt+cgdtndJy6xQlbhoZ5C/pRtngkj174iVYXDBy5JhOvxfu6v//f78cRf/uRhnxtFD4aGkLrI3qOPGrHA8n/0VFeWCVXC0uu6stnoePgD0wJNzBrvA0rRkUyvuRPeAZpmKVOALFTxHujQijnlz00evHvUYWLDgUxvZqG5o0BfNqussVtLwrUUfhwWQhFRuTgf+XeYckSlRkpXgSe66nvZ8fFYXqurujADOkdc/MQ3k0Q38gTwD9bdNz4UPanrZFDfrNC2aZl8J4bCZNUVsrhQ5W1ktefZJ5sOlbKK5za3cxAB+nj6HBWqhsHZ2cTc/ZIR+YX8f8mMUATI+dPsaLsOB1P/OLCDUsYZyclYpR4YWnxwkrTLdxtTkbj6yZOm52GYzrxV6dLPE1h8Z3yOX8383DVVHT+VcPeuwM4m0Xllu8+9FfCJJ4SMUR4oPbIVBzP1HB/Pj+WCnenpNgwWLinrnDPgM9asrkbb7dY2OlhEu5siuKkp1/RNRDFrvqvmPHdQBMzUX6DiYfepGFe4Sz0//cDt85YOH1LEkyRJqb/TMdy8cF48cES/1p4flESvkq+NZZ8m3VCpX6vF7VvqVw51Yt8pXV8clLyQDj83LQA/GYdDSku7BYPvCYe2FH8p6/4z5POKof28OH7yzA26pWAUkY9T1kdeDDTxLG6dAzsIr3c3Yd2t1UOmeLF0kgzk6PlOgQV9tKKj2xV2eO+Fw3SzZxXJAWHB2p5rJ4HpaS/ThtDePtTPLOWeMYm8YynXlStLddthQ8v9whPX07D9Y9Gzm2VgUeyF6PEj/qLrzWh9PBdHT9OcOLLfYXNR0uvyTmLyTnbLM7BYelvsXau+Kjfi2l9jAxvbsWNqTrOCnch5Tc9y1Zz4I8+Kks19+HwTgNPXM/ExRkyZwYHxkquFrH68dGrsp3Fi307jbx4PQNdThrZfXcGFp2T+VzvRd9i0rOXAtgZG8bb1RIwN5eBwZc2vJW97ip7ne2Lws9Ng7j4VsPVnfw4sVjlnjQvYiUHa6gu1G6gcsVsHy4N1nFlLRcSWghXjQ6hifjzFPHnLyYNW6QFsGy45HZ5F3J31PJVXw9yTdTwde4A6lsMXJxsQzmjeKo1hGoOhZtyBpH8wsyyfVKQWELPX/9x4Lb0+Z9BB+aIP6pmJ/YK515K8+GQ5PgpyfEGE7Scndfz/xD1llFS3F/XdlW1jwGDawju7rJ3cNcAwd0laHB39+DuBHd3d4fgPq7t3u/Jf93reT+xwhp6qn51zt7XRYZu3BskPlHKgRvi55PEz6dcUWicEcRvP4QLq3qRv594WeV05C+r8Or2EDKLt7wTb1m7Rs+Xz50YIfz0y3cvwjZoTHjgxd0MJo76kg6X9HNm6ecNrc180D0V445GMnrrDzxrZWT3njbsrhjOk+vikf9X8dFLLiySOU3Y4UVCCRNL/ZOOZi0lf/o4QPHa6euDsA81MFsRB24LN48Wbs7U1sjPXWwI5tdRd9mDzH6F701BRBwNY9zWBCx6YGLCwDQsfa/nj8ZOnC5r5N/bbIhtKuc9OIQdufS8dN+FkZ1MzNciHUlWjSVyeXHmgsaV88WbK2Zmj3Vv8XeC9E0JL3pk0LHOVw8qVMnILas+wyR9bH1rx58zdSzsc6Oc9P22VzL3wtV/zP/v83+MbBopnnlN5bLJfvQ4oLLufj9W9tLzYy0XVobLr3Eu9L6egbUnfcHoSzrmmuPG7jp6np7sQulqeu5d5kLWnzn5qvJDlCpv4P7NDmT+KfNd2QvdQwO9A+y4UcbISTts0A5o9O/3YoTM668yr4nz9CyW5sS5l+K5HfzY/0Lj4Y5eqK0UKr0ld8roiZ0u1PBZuMiShCVjczEm+gHypCq882sQf90NZ7bhMt+lVPr3BLD9sZGXe1txopKOG9d4kPJJz8J1ZQ/Pamy01Iv+z8Vbu7gx/YOBSiPxp3SFv+YNQr87nN5TcWh9z8gew6wYI7mU/UAQzXtrHFjTh1x/a3z0yovIVSrj3vrRCjr2nuWBZbOBabfseLtGZYvnftxXLPwzmIzMkmPvJcfedTaweVPxrt8V/ugYwrd5ZtZOS0F7v8oOJj8MHhNdkWl42U5hpw4h/PvSwDYd7Oi7XGPVz16s35iFT+6+xv3ueg6t68L39dL7D704e1vhslFB9C8o93POg/dfjWxcy4oHz8Q/ugaxqLbChNny/f5VWKtlEAlJCksWCeKGzsSJ7nS0KqGw54EQxvY2MHNNB6zX9cw70Ym7WTSOeu2Dd5KOkWZ5XcmbZsJx7k8aM9X1YkikypI/A8isCYf5QrjdVuOYzj5YXhmZ1t6KfcKHR/YFcPlYGGduSsCRzkbuaGrD0wkW9opIRgebxt9zehGU69XJ9Vbaq3LtMT+8SxRG/ggiv9zntXourO+h8OFvIVSX57xUnvMt2Ydxsg9d4sxsVjYFFx9Hcm7vH3DvFM8+m4TiG008dC8Ntb5qXFBbcjqjwo2fQ4gSX/8pvr75eAbe3PAF+CK+ygTY5hqZ22pF3yNyntu9eNzDxD6/pWP3YSNP7rCi0Zj/3o8zhE2bTLxzOw39VupY9b0bJ6pGc9Py96heTcdlyzzYsUHlxft+FLml8dRfXlQ9buGKjUmIkj762SUOi9rLvLQLwClekEU8K36LkaWuW6Fe1hic5cWjLCr7vw4gTjWyvP+/91nU8UZjD0oXV7j7UAiNC2XkiDOf8bqFhe36JmNoF4VFm4Rw6YHk60ArHv4VzgFZ4lGvSUZOGCZft8jEtnFpcKzVMcd/77O0xsh3L6z4/bGOnfu4kbFCBD9tjMUvFgOvJztwvLGeG/90YXoLM9V+qXg6SmPPXNJvr1R2bufHr0t0vPzTjT9TzCxYMAWOo2Zm2ZKCvQYjj9mFn0eYqeRPxcxFRgbirJgifGYSPtuwQuPjD15sXCP8+dyNa2P1nBLtwtelUeLJ3/DgQhgHz09AkRJhPHUgEZ9qqWwwV3hV8vxeFS8QYeKc2HQUypWB5x58xYnxZm6MSsWg5yaW6ZKGHXYTL2ZLw7AFOhZNdqPQdIVnQ0EcnW7i1mAaCpc288zuVJyK0rj2mw8baoXxwdxEPC0pfrXPhYEx4SxXMQ51V0ouvbfi1XLxgk9+/F7Mwi5HklG+TTi3dotHmyImdjuRjn/lnFrKObVJCWPPgsIzsn9po4MIrdZTfe3EsW0ZuPXyF5zJbuKKp9KP5SVvNvlwO17jX6W8iLmqsfJU4Wnhp5vCT09TjeyZ3wpLpImpP9MxfLKOBUwevOulsXktH+r3NXJCVRvO/ypedVG8spqFrZYlwzNMz6hCLiSfV1l4oTyftwqvtAiiYrUIblwWiwvlwrhwSyKeJ8nzK+LH0eFGbs9vwyiPwjyRQZhv6Jk+wYmDlTNx7+qPcJp0zJLmwdsHklMD7ZgnvGEV3mjXVmHHLiE8kH0eJPtsS9YzZyEnOu7WsdUpN47OE15I82LrFRNvTBfu0BuZ7rTh9jeVY2r48VTyvee2NETKnMXInM1rZKF1RDKOH9Bz0z4nZh0007NH5miozFERG5IMCovbQ5gzQqE9Xwi19RrnunziXRrfdhdeWSh5mugVjpZ5u5uC7n11rF/Vg8ZXdRw5zY3mUzX21/kA6dW50qsxhY2sdNKGEvtVHjjkx9pRWfg81xsEc0fSePcn2m/VscM1N2a/MtDTTjymucYbA3wwd9MxvYEHb15Y+HvHJGTrbOGrZsmIHSecntGFBgcNHLvXjmpvFC5vE0TDD5J/jbyI/S0TK0z9iNPfNf5dzYui68N47FECWiUa2Ku4HXG1DCw/14FmpcI4dG8iSiyJ4MGfMZjwh8qINgE0klwf+ciLxAEqS5QL4N42PYdfdiLDTYXfxgVRNr/GXVd8+LpEI354cVL6dIP0acuaOvZd4EElnY5rPR6EKhuornagZXaVfZ8FkCp++Os6P8ae1JhtrReuaxqzTPbCPl1l9qAfUaOi+DP3d/RqYmHtYcnIJT74cGMq9rVUeax3AE8qm9h7dTrWSc48l5y5KP01X/pL6yc9XtmHmJlmVvKmoO46Ayc/saN2YwvnDU9GxW+SvzWDePCHkYNa25AXBt6Z5cCHCOHSmABqdJJcbxHCkFt6lvzLiXIzdNwacONdySg23/cdPfOoxJ0AJiyVef/uRM1lkutfvSgt97f7bweC/RTqKoWgG67R+6sP7x0am2Xxgk6FMzMH8cmoZwOrC9sahvHaqMT//dyA6WQqjlYQ/94QQlJrydHuIdw6a+C4pXakZBLf/hCAxaMyJdKPT+Jx9cXj1hW28OnJZKQM0bFQMQ+qfjZwdR07/MtUGr/4kXLQyMJ7rKhyzsjVS6yYPCQjzcU/44v0aB3p0bezhAvc8nphZtZPSMWEMB0jEj1wTNEzm96FPC/FL/8Iok95E2tsTsczuY4ech0zxgr/ZA4hz3M973dxYsgM2Y9AEMpu4ZbTdjSR+xthc+Ef6YVD0gu6CI2eWB+KVlR4fH0IKxYo/JIUxKgtGvPc8GKP5NsJyTf/XAPNVjuWuVV+i/JjaZyB38vYkVv89+7+IK6OtHBqnmQYhQ/swgefcpjZ4HEqlsnrfLvuxejzRuZcZMXuh+Jl/f0YLhxUSDioqMzL8T4BjBsUxejS39GtjuT5lAA+6WVPnR7k62vgjaoOhJfSMXGvBx+35WCDy48x+bHKsD5+rFkRwRcfYlCwoZHnR9uwQbjioXCFts9I3xErqnZRuaJJAPl26HjrghvNhkh/FE+EtZ3K3B0CqLU/gvMPxWDHc4WXOgdxqqrCNcuFf2qZWWtuKn7LK758Kx6OmXpm9zoxZaCRprI2POiq45CGHtSto3HSFB92pKu8kMePPk/CWLNXAm4Kf07Y4UJ9XTQnet5jYmY9w9+6cPBNOPe2jsN88ZEU8ZH85XS8stWDdsLPHYWf1wq/PusZwkHxsL1nA6iTYuD0gnbMPmSge5cdA0pqrLDfh29vdKzV2o369XScOMGDOOnbsj/ciL9uYKmJdhTLqZMc8cCYYKGtZBLyxKm8U9YPzyWNkbO9cKw2MutrKyp00ri5hQ+3Nqscd9Mv+29iv2fp2HoqjFdXJ+BzSRPr7UuH8kH6Sjyj+5GMrL/9E2ZONDAQ5sCpq0aunWpF48Eqh5cMwFssnFGH4zHnikbndC98PgPDzXaMv6Yw05Qg3v0iPXXVh9ITddwb7oHPojAiKYQLZRQu3BHCu9dhbNY2Ae/Fx5psC2Gc8FEm4aPhdoWFsgdRtJSZx/amwh5jZs4KKbgzT+HotCCWXtHx53Q3hjwIZ8lBcVgtz/ONPM9K64UTH/mx8I6JKSPS8FR8pZf4ypE5GrfbvYj7K4zlsiRi11KF574FkTda5c33AeQ8ZuTjzVb8Kn1yRfok9ryO5Re6sV68/LF4ebNdGged8cIWozJXBT8Wu1TGZvTj/XMzm3ZOwb3sOg5/6vkfr7cXXu9+SWHDOUHEnlNZYbEfEf2E7yrLvp4PZ3BhHCq0DOOmPomo8YvKJdcCeFnHwI6THeiiCJ+HbHicprB3viBGRxiZK9aGHl2iWb/Je/x72cDWM+0ITzMy4RcrhmwLZ4krceI9ei4RL25jMbF7cjruNZPrGehB8YEGHirrwLMvOnaHG12e6ti0hxtJH/UsXt+JKzBy+kwb1g7IyOflPsO6VPbkm1zvLZUxf/kxeEM4yzyIQxnh9t3C7U/ehrNP8ziExkZTH/0eC/eZmXT4P26OEG6OQbYZBr4O2JFri5mPb6Tg01gj60XbkL+EyisHAuhvs7BSziRcm6Jwij6EuvczceqQD/i3ucrW/QNQPuqoNXTjSF+NO6v4cGpcGP/OmIh7J00csSYNv+7UePmcF58zRrL+p5//ez+hzo9kDv8wMbJ1Oh5mVtj/3xDWVdXz2XIXGm5XOOZSEEGZD73MR6uj4hdbk/DolHzd6iAihU9i0r3oJFzRwhHCq0sG/jFbenujjivuufFkq8I+V+X51zNw0AQHXtXSsd1cDzJK/34uH0D73xT+MS0E9yU9o2c7kdBRx5ItPTi2xMgtP6z4MiwzWegtajuFUzJ78VHOtdEYP6z/fR7yWOHzFgbO7etAUl0LS0xMRk7pk8c5A5ifS7jxfgj1mxs4foADeWVvb8veZu+l8EXtEMo9Cef2XnFY1zWMzxsmov5N+bpxduyZE8YT9gRk/MvMr1lTcem1vH5bOzqvMbDFCzt2/m7mhY6peFs9nC0XxyN9lJ75c7lQ6KTKs2v9OJBf4f7LIQwXfikg/PK8aSZ2HvwRbyWXWhWwI6/JzNvpqVifU7j9oQ+P5+jYz+7GMYuRm5NseLdEeuyH7IWmsYnXh/jBGkuX8EF/UKV7rx+32wjndvWhX4rGKgW9OPiHjntae3AzUc9xxZx4XtTI7kdt6DlXT9qcKDrLyOMeK7IPUfmieABFzCpPpwag62Ghv04yFq2X+XuYguFvVRZo4cf4bHpmeiH+kM9EXE/HuN4qo2sG0FG4v61wf3SUyg/fA2h+IoyD1ifg5wS5jggfqvuEMyxB9B5iZO3iNvzlVZkt3I9s+VW+uiL5MFDut6wPzZdHcODnGOxpnoUn+r+BOZOBqR8dWBMXxVdlvsGX38Kwy8k46TdwndGOLTGyFxViUFz2+fAvQdxfqnHINy+6ltKz8V4X2kbnZNf3j1D7tcL5vwfRo7zKupsD6C3cA+GeYSGVRVU/SozX8WAGD25fV/nXRD++L87AmjFfUHe3mVNOpaBnEZV1TgTwV8DI7AYrlB2RDF34gelpeobyOfG+pMLG+0JoJL06Wnp1o0Xj/SQfDrbRc19XF5pK/gyb4sD8kZFMyfMT4ecVxi8Sn5b9HiP7fVX2cnp9N3pM17FuyI08XUy82yQdNWrruGiOBwU9Bl6ItONQQHJe78aheyp3DfXjygSVsyMCcAtPZhKePKdK3vkDOFpUPOjoVyT/3+eA7/6k56m6TsxuKr032IHa+0yceyQNDeuaOHpSOoa8M7JkMyvOVTBwyQYHdjZWef7PAOYIz7mF56o8VPj3AMnbruGs2DAedalx8gwfIoSL4oWLLsucz5E5H3ougsWWxGC9PIdH8hwyxxr5trwVA/80slxBG8wdVaa3CuCfHEYeeGzDyN9U5p0WwJpiBr48LFz+r5FDWllRabme6z47UT5vNLfdklxuKL4zyoN9GxUeuRtEvV90nHDNg859FTavGkKWTmF81yIR94M6DtXc+JlPYZXrIbTtamaXRqk4O9/IZSlW5OssntQsHUtn6fnD7UQWl/y5TAm4ukbj9OdeNBH+Gyr85++vp7mC3E8NM4stSsU/V1UemurHigYqP40J4M1lydOZfpSWvtg9y45V8vrv5fUHLJTzSYzDIqeZ8dGyN//384lbsyu8/jSEQUUVljoWwlyfiVaz+DPF66d78PgfPfv848SMMwpDy4MYU1zPXIdceLYykt3f/UDcND3LKi5sH23mpRypqJrHxFV30vEgKpPM+0csFd74Kbzx3+dJNKnqQruWGjv28aH3e5U1m/ixbkVGPvvwCfV3SQ6ccUJfxEjPCRsK/9DzdFUndmXV88xLF4YNiWCh4rGwLNAzLdmJ3+R6pq0I4sYelZOP+zF6ioE5DQ5cbGbg/IEOpP33ueMLXcLpOjbI4cGF6nouXOxC1n0qXx7xI49w/H3h+J/VxLuX+lCvu8qJ9QIYK3ObNTKA+u/COK5ZAqLHK/yQIYRMXhM/h6ehXyP5+pE+DHgvfNbEi/1jongo23fsEM67JJwXJbnzXXLntfB2+xgfXB0tzNIyGY8vSl7OdaNRWdnD7R6U6KLxcBMfXsn+djgeQNqHjMzf8BNuPNJzUl8nRl8PY+5JCejURceWTTxY3ELPuL4utC4Qzp4X4rEwt3DyXQ92ZzHy9GsbTHMy0Ob4AnuC8HcJKzZG6Xj/mwcVshq4+ZUDvucGhnW2Y1OkiXd+pKPM7HDudMXhck4LZz1Mxm/ljJyx1YYJwi0Zl/hx6YWOczu6USijxnOffbi83sA5D+243Cmcs5vH48xiPVfEOOGJVRlZzo+y0QbufO9AsSrhPLIqHiV+6vlPZScqbTBy3QMrNtQRj5kcwunkrFxd6CVCF/XU5jpRX3hhgvCCt4OO4W09eHDDwsHjk3DvkcI/+wZhLRvBPNti0eBuVv41/CVKbYni3hvfkLOI8NmJEFplUNnnawCRH4yMa2TFmHxmZr+RiuHFVBY8EsCY7BmZ89lnlGqlcX8vH1r+rbLfKz+eiO/3Et+/NkzmqFAIj4wK+1nFZxZmYMOEL4idqbCCN4grwv2zJvvQRHhthPDan2t0LPjCjZdTdeykeZBzjsKHjiDsZ/TMuVx6a5zKo5kCmLzCQtPHJHSoqbDdArnObTr2vOLGusPC/TuC+KzqWc/vQvmkLNxa9DV+rNWz+lOZ/8c6uvu4oWk6+nweNK2k55C1LjSJFt9/78JAp3BlZj/e7RefPejFxekmzg+mYd5WjbarXmRZrPFdrBfzSxuYtsuB0Ebx/bt2uBdozJjkRWF5bmdineh4T2HrYUHELFdY8VMQ94rLeR8Mwf5TY/ZKXsRuMrP87RQc/WTh9rpJ/+/zRs5uM3LFZSu8g4TnS4UQNkITHvXhbl2FwydKzxywcOT+JJSR3tolvTWzjZnBrqkomt3A488cwivyvMelYGJJI6P222QvI7hgcSwcsu9ZqzixVPLz+1UP5p020LZSPDRMz28JLswdrtKWP4APG/57H9wgjoqvbhVftdY1MM9EB7r1Cmej2vG4uU946LAdEYfMjNudgpXioe/EQ7vGKWxUNojM4SrfxgfwTbi2djsr/hQOLCQceEXOe4bXA8+MzIz0/4sBc/Ss4HAiNkr2/JsPDUMWjlaTsCZDRr768hk3kzWOKyTn9FRlpe5+TO4l3lzbgWw24bScQfR3GFgxqx138kiP3k7FvGEWWgsl40WYiV0S0hGxSseEt25Y3mhMa+PFmSMmrtyWhk9VVNZbGUCr4Wb2/jUV4VNkzw0ehBaqVBP8KDLVyJM6GxqUEx7YGkCiLool3N/x4GcYh1ZKwF9ZzMz2JhXb44y8WsaK+aVMTN2Tjms5hLsfhxCSvlfEN8YNU5m5cACPpynsq4Sw16DxuN2H+vsVTjwUxMZLJt6fnYaFrcOY0j0Rivh2qIsP+96oPNLaj22GTLxm/4i8Vw28Pc2O6dKT6hEHnL2MzFLLBkXOU/V7EC08/mmFC2fixc9LyfWLv52e4UaFaB23vPfgnOT4MumVJdnFk54GsGF6Tj4MPsToAZmYu/xHnM9g5OIvNmxsouODoZJnR8K5cXsc0sZq/CXah09FI1jvaCwalTBz1AHpw+9GHqpmReiMXM9yOyYL95iEex5KTg2UnFq7QuPzD15M/GlhROUkrBRP/SCeuvlPjbcK+DC2ppzLggD6vFVYo0UQ1fKbuOxKOnL8rfDpqyCuW/SckuRCyGOhEpGEg5PE380B/BDuqDrdh6w3w/lqXBxO1NBx/UIPXjcwsv0YG6J3y16e8kITHvHndsG518SsR9OwWjjmrXBMrgJ6PrjgQoXKGjf97UPtv6I4O+t35Bd/u0o3Fo4zMTFjOvoK/1YX/g3tzkz9yX9RbJKJhy3psLQ0MKWPA88XqewW58fZ7nLdddPxbrOOzW660SerjtVfeXBtsYnTYtLQQ/ahnuzDb8GMnKaT3gnpOUFxoskhE4ftTsOPWwZWH2PHAOHBCsKDX1+pRDs/Zv5mom9qOrpLnzeQPj94RuM/y734R/LvH8k/u1nPHCkunBwoPl4mgN3/mHnmnxTsEg87Jx42u6qJruXpGHrHwuIjkzBlYgaawr8ivqeO5eCBu5fkVC0f9v5h4YnWyZgtPOxpHsKTkVHsk+c75icrTC0UhEM4KLtwkGdGODP44+AcaGbmMqkotdfAvcfsyLMnnPdOxCGLzM3rYR5sFp+/KT7/MFbj4HJezBbO94T5UU56eJv0cJ4NwgMP/IgXXym9xw/LeSNTFllRZbKOq00erBCv+yRet7R/DsZUeIIl/5oZ2zIFiYP1LFnChfi5/31+uRu2GSpz+f1odDBSePsHeg41k0VTockc+mUOswuvPhde3btP4YnDQZx9pOPSvm6k9FdYsEIIlWtr/HuOD1k+qnzTwA8uDuPM2ATsEQ441diD9kuN7PjdCp30kn+KD/Of6pna3YnckkcPCnuRJrz2i/DaaZdZ/DkF/ZcbWemzFXle6Xinvfj1MAO3F3JgWF7JvVs+5D2g8Pb+IF5Sz47TXQhrojFpmA+1mqmcN0i8Xa6nslzPlaF6zijiwuSRknN5HRiUR2XZOwGU7afnzsry+1Uy0LzqKxpslFy858Vvd8Ubhruxb5Rwca4QIo7rGL/RjfZJGv8o6sXSX/X8ftGF5rsjOejUDxS/p+PRYW7E3YxihbHf8NdJA7OttSN8muSCGsDTf1X2bOmHI032Jl8aKolPrA8EME14ThOeezJEY69iPiQO0LFEOQ+KXpXznSb8QCPzTbfB90mjpa4Xy2sZ+GWuAyvn6fg+zY2G8eIZpdNwqopwykobbr7Rc2JrJ/a2MPN431Q4/lSZvUAAqzqH833TeBxaIlz9w47DsWHcXS4BdZpqnDZEfDjMzL4JqVhG4b/pAdS6qHDhvCAmLTEz4mcKTvWVfagSwK2yCsdtC2G93P9juf8PbcPYqHMiHsn37SvfN0t/jW8q+tA+Qcc/SrpRao2R+15Y0VO4jiWtaHpEfG27HbTrOSu7E3sGyn2WCaH1OzmfZn4Mc+hZNKsT2ZoY+GqYA327Si41Eh9aFMbY+ARkWWjhm8QkXJsg3BMRwqE74dwzIg6/HNPxxmY3JubRMfKOcP9NhX3GijdnFG/+HEDyOoWFHweRT/LiluRFQgWFpTeE8HSLhb2u//+fU3KgYjgProvHHLlfh9zve7eBjaPsGB1jYc6KSYiQvY2r6MKCx7Jfva3YbNN4J4cX013SS5nsyJRs5MfCVpxIMXNjgRRUeaJwda8gLoUbOSfOhrcPNbbs78UK4YfP+2yYMUZlMFsAywN6ftU78biIjn2Oe1C4ucazA3yo+jqMq35PgKeRwsgRIdzvIj3ZOBE1Oxq4sJUDKZYwFk5KxItjGrtu8qKS8Np64bVkyZ0ikjvthCM7Ckcena3nNqcTTRWNQ0M+tLqmY68pbhjlvp1y37t2qTx92o89eTSevP3f+5sYuG6d+KxwxFzhiJIfVO5r5EfUS5n7P5yYv0zP9C9OHKgkXLrGh7CJUUwM/45KOeTPPXGg3DQDt6kOHJ6ocFdYCH9ECkf+DKHmuAguzBQr52aRc0vCkq0KY64Gob+l0fOXF9vLW3hxUzKydlX5qlEAJ6TPNsyXeaypce4CH1a+kTluHQezWbggNR3je1kYXTsZwQ0a9fe92FI4ijdOfoe3v4WRFZLRU3KKR4Jo1lvh0JohpL4NY4HmCbjwQOGCgUHU+ZyD0+o8xtwaJjoWpqPdNBM7qunY3dfAM1UcmF5PenJCMtooKruHAmj6MBP/HPAB906bOHxlGgo+M/JsNytWVg/jx8WJODxZz93G/z6P18C2igMl1is89CiIxnrhX5cL+eaH82ZqHGZPMtBtdiB6h8JPF4LIe1b6e5kXdbKZOfVFKk6YzNyQlor7n+W6f5P56aRybnPhpTYKi3cN4WgFM7dsSEWGmWZ+96VgSe8wxtQQT/8ZLRz1Dp/TZJ7z+TFrhImefOk4cdDADXvsuH1e418Lvbh7ychRs624cEvhojHiCbMVznIGUWWIwjXFQ8i9zsz7T+R1xSvixCumrdaovvHixGk91690Ins1HZ8uE96sFcaouYloVsrIwXttmGg2MjLVhrFrhaOeWdE1t3DtvXjMyqWn574LtX7PzvmdnmGdeOjTET58z6Sw1ocQaqTIdRQMIsKiMj45gAZPxF97JWCb7MUV2YunMt89Zb7HHjUy61arcLmBvq4OxED6YaYNR3+o3F7Fj2plVS7bLvs0ReVnfQBtEo3sVtyKnI9UPu3nx4ckIxsWsWJUmUjm3vkTozqYmPf3dHhnGBjut2NJBpnHLyFsGKTyYakA5m7X03HRCZd4RrR4xk7JuQuSc/2EJ6q196PgPwrPHgjiw2vh+rbiUcKj2YVHe5YykHuFpyk5NsOBeRbxmyQf9hRSeOp0CEX6qTxZOYCMl1V+meVHchkTi+xIx6l80p/XffBcU5hxsrxeR5XZWgZQvbOZi5ulwnjWTPuyFCy6L94y2IuXVzR2mu7F4HlGlk63onAePU/fceGa9OaUT1Z8G6WxVi7xxisW9pmRhJLii/8UDuHNVI1tNfGCVRrHvvXiy3YjcdGKY4l6bi3mRHe3mfUzpODRbAP7O+0Y+NTAcj3s+EX28rrspbJMofItiOZuhYMzBFHqsJn7d6ZgaR+FP6uFUE9yZOId4VGTyqzpAfjG6WnJ6ELDXOEc8yAeK8dk4qdsH7H6iY7/9nTjSSmFvfaEMOCDnuUbOTH1lUZDey8az4vg8PQYTLsXTnVYHH6uUVjleRCfBupYr4wHrnp6Ro93wajoaQ+58N1nYA2zHcknFBZdF4SvRHaG/fMMHerr2X6cC1fEP2dE+zF8iYUFfyah4LgMPJfpK553jmC3prHimybxzTTU6KhwcasQpo8Xb8kQQLPCGgee8uHHxwhWrR+D3pqcq8+HGYNMVEqn43qbTJzSVe5no8p/7/qhuOT3o9OQsC+apQ6/w7a9Kq8dFa6dIb7gDyIqp4U/HiVjUCsdy/SSvs+o4/HPHjwoL/u+KYTBpc0suTsVZsmJFIsDDTrLc2omPThaz4ScLpS0WHggORljNocx560EFFyk8GJ8EEdiLdxWLglZfSpfW/y4qlc41Sl5EWvmjnIpuJyocnYxPy5L786S3r2SI4KzH8fi9EqVq975sWOLjhevu3G+icpFQwPoV9DCqueS0Uc8qZp40uMUA/sVsKPeqHCOzx2PbsIJDYUT0s/pmHexG7fKmTl2Syqs4QrzxIXQ6ZGerfo5McCYmeVsb9FgeQTHfo5Bz4NG1t1rRctzRvZZYkXRaRpPqD4UO2zg0Z12tDab2TM1FZuXqrz7zQ9V9jQke1pgjcrzL/y4o4VztDceAw4K3+8NwnBboWN0EJ6MJmb4lI7shQ18dsqBSxM0zonwYZb0qld6deEAlUnlAqjfw8SxddJx4ZqFCyYnYYz0SHbpkaEZTCz6NR2eBgozjAnhTlHxsqOp2DvOyOMZbbjYUM8Fo1woHTJzj5qCXIWMfHjGBu+fJkYUSEdJ8cCDz2SvW8pe9wkgu3DviyYeYJGcc3wMIsXTYgr68CJvGDveTBQfVLlZ+vfwVjN3XU1BCUXHAyEPnolHdVvtQ+CuRtOfXtwQPhwvfHhirnCjVeaok47Lm4sXrsrEQf9+QKrdxILZ0jDrseR97zSkiXfmF++MX2RkmTgrdlYL54Wl8ShdS+PeeT4Mbh7F0gO+4+hMHbd63XjnUtkiox/NukVxUIPvOCF7vFH2OH8DjVf/8uHhQMn3MjaUkx7fLj2eXEdlockBQPZwVlkPmocZOChReGSsjoWjPci3RsdbL9z4JpxVSzirXm49J99zoV5iBCcWj4Fvh/DseS8CcRaayiRB+dfIYCsrWs0zsE+6HbGdhNub+xAOhYmzQpjZRaGvcQj6EXq6f3GhfBcTtzRJR5ujOnbf6sb97+J91dLQTrz2j8gkDMkoPfk5hHdFTGx2PB37AhqP6r0YP8zEqMLp2KRTeM8dwvhKUcy49jvSq4mPL/WhZ5xKlvWj6u/CXZ0Ssa2agVeXOnB9nuRZmvjJiDD+mi8RgbphNE1MxJ5tCk9elpx+F8EcTWPQ5rCOPXe60WSykcNMNjy5p7LPUD8yXoji5wXf8Fp49PdPASTs11j6oBfNs1s46Fky2q40s/P7FPRzm1g1QxpKytf9I9zqLSh9etaGlcf0fL/JCe9/P0/dMB0rNiv8fDOIBW8imdz6B/qWk1zcKpz1U2Xryn6sl/57KP33uoLk/AYfxhcRLz0heZ9H9vNOKl7bItkuxw/sbGfghQ4O/LigY9X5bhRdpvHYVy9a5Q9jryuJOJaicUsBL3JMl/4M+VHys1xXHT/eLVbYLCaIjtUUtloWQkp38b+6IXwfJr1YKABHiuR+AT8Kl1J5am8A164J10xOgcUezuTscVhSMRNj1n1EdfHXxeKv1g0Kc98PIizWxITyacj9VuH9FkFkeWXmm/YpuNRZ5ZymAVweqePsPB5891tY05iEB+ERHBQXC7/ct7mSH3rxOY/43P4NFh66n4Rz7TQu6+DD60g92/1w4WhJhVv2Sf96hXPDvegifNBU+CBfHTNvTknFWuHVl8Kr/baEs/KNOCRJ7xXrbsfwbhoLNPAhw4Mwfh+UgI3VLby/OBkb9hj48LgdNUtrXLjbh/jhGsvll197iIf/JvwxyMjupSUfviiMQBC2oxbm3pKExsJbI1xBxI03smyUDRf6aVxQyYdvp42svdKKmn8ZOD+rAwllpNd32HC+i5FLGttQt5jCKUdCaNxIxxEjPWjUR+OI6j4UyavnyVsubF1n4LXHdmQebeG7nMlID5iZV5+CuWcMtC23o3cZ6a2dPsTpzCzrFn5sqfFbHx/ePjKyZV8rZhUw0nPehlUvFL7tGETbgIWdDUmo7BUuDA/iXZyBzcoIBxg0XnP48Fz6q2tPDyqe0HH9ejdMGY1M+2zDzqxmnnuZig1ynw+FK6Nlfz82CaHrOeGkJUF8q21kzdk2dJS5bStzO360yow5A+ghnlynlR87F+l4Ls6NwktMPPMzDUntdCzWwYMNz/S839WJ9Z/EP+s6camqwjnLQ5iUWc+wty6kvzQwfwc7Go/WcWROD9r1MbBjdcf/e5/UiplN3PBW+H5ThHhODGrIPi1slI4zrS1c0T0ZizYJV932ouFuHceecsPwUKVrgB8D7xtZfogVi/ZbmHQwCYsaWJgwJhmWehFMnRCLm9IrE6RXRo/RM092F/qL71YW39X3M9BV2YHmrQ0c2MOBxg3leY32wS+cbS4WwsYmmXh/6Ed8Xi99+NCKAycVHlgTxK4khWeKBHGoicI9Q0NIL6Qx32kfotLC+fOXOLzOZuDvzx3oKbwJNYRh//07O48Tg9YJTzxxY6r0gS57GmraZS6ze5F4VfJwahDDR4Sz4C/xqJfFyIlvbFiQ28yku6kIhiKoV2LwxaNnnQgn9Dn09D5xodoaE5e+SMPEh+KfA4LIXFTh+2Mh5Ghp4pM+6ag/VuG4zCHcvqNx1AgvOna3sE29ZHCs+H1mF8rIPu6Wffw+MoLV88RifiUjU9fYsEv6/XRKKsqn67gtrxtls4ufPnNh4yKND+O8WFnexA+b0jF1sYH6WDvWRKp8+UP8a4+JQ06kwfjaSOvvVpzbIdd3Pg3txdf/qBrAmn81vmjpxZv74puDg3guz6fb+FjUPKnjwrX/vb+Rwi7WIGpFa5z/3ofMh418t9OKU+v1/PuhEyeyiie9dKBAfSMvjrOhy7woNk3/htZynz3kPmsX1HHeOQ/e9NaxbQ0PLFfCmDwjAXXl+0+S7z/PG0FrWAyS++pYpIoHN8TDJ4qHr1wt8/86iONdwrmpcTyWnI9gzMIYjJbryiPX9amNxvpdfbCLF+SoFMD3jhprtPTBKV6UuUAQ1/6W/nzpRBfhryYF7Uj8TWGJqSF8Fg+p286PctkVOT/Zo56yRwxgjj8jncZP+OdnGA9XSsDab3o+q+FE/dLCbbvjsV449LFwaBuvgT3D7fgoz7vBYxfardazwxvx++wRTHsaiwidylhPAMefK9zcOQjdTiPd5ySfhMcWCo9Vcxi4PKsdq8cb+DbKgQjZt1jZt2nCDZpww71OYfyzeSJmL9PR+cWNuV6V9jA/mmdWOOhtCBdiVS4u58dJyYW1i4XrT8k5rPYiTPwi+WsQFT6o3NTIj83Zzbz9VLiyqJG5j9pQL4fs25MAju1SufW0H+d/17i0ow93CmgceV7y3iC5Yg+hYYLMZUkvjlhN3JYrDWvkvF7Iec1LDWd6/jhk35WBz898wZ2mJo4cnI4u4SqbxgfQ8ar06LT/3jdKz3/kXD7K9TeU648WLn4/LIB1HzU+q+9FVaeBKzPbsXirypirfmQqp+OHrR6UMOp4yCa5eFbPDcucKFTEyHMnbOgrvl9FfH/ifT3Dhzjx6ZOOdeu6sS2Pjldve7DnpYmnOqRhhiGMIXsibOniqXl+oO0u4ZwzbrRNVtitcBAH6uu4b6wH9fLJOdwIYHZHHV0tPag+UeOScOn7DdL3D6S386t8diUA2+sw5m6bgK/bJS8uBtEgYOA4gx26Aha6LyTjpOTumpZ+jNkXzpxH4rBiqpGfNRuG5Y9gkSux8GaxMPJ1svSFwh99QvgkeVa/uwM3U0wcXyANSwYpjCsVgpJmpJLfitnCma65PrTtp7Br5RAqvzJwbXs71obr+TzOhSlujfoMXjxbome3H060Xq+y5yM/LlwX75joR3xdM8tMTMW1X1ROuxpAnjl63nE4se+0ymMr/Yg9oLDiviBG1leZZ1wAYTlNTHqUjmaTVA62BLCipMZP+3zYu0v47HQQEwYaGFXWgX9mG3nAacWdwWaOLJGKUSkRzFMwBtWbR3LpgJ+YLT3nDgviUS6V/e4HkDZJx1/NHsx4GslA9x/YUVbHC9s8iJ+sspQx8D9e7SW8+uOmcM9YP0490XNtTydGSK/nz2DDUjmX73IuHVpFsV2v7+ghnlJPPGWDfJ+H8n2Gboxg8XsxWB1U+FoL4vwm8c7bQVQWnlzdLAY55XUeyet4D1oYsScJpfIq3HsrhFC4hbq4ZES9U/mzmR+56xp5d5INEVXCGb8qHtdT/vt8DD++7ZCePW/F+28qm9T479+h6Bkh593VZWHjTEmYLJxsFk7+9Xc9L3dyoetHlY0byByLp3wWT7l3T+OwoV70/B7FOtW/IVa+rmJHFzpQfHqGC40LKxx5KoSTMq/rZV5LSa7ulVztID3xu/TEib//y1M72twUnxnnxdlYHZeWc+NtQY0tzvrw3mNmk4gU/K4zsJPHgSrLNa787MWpc1Fcs/gbMlTKzO9r36LDQgvbJyahTS6ZpwchVJLeWn8vFd1OmNlwfQrmX9Qxba4b71+rbNZWPLa1wkvdQyicNYwnXyXi9gKNo5PEo6YoHK8PYcPybHz86Tk8YRojE3zosVvPuqeciM6fgR+vfIWvpnD2fPHASsINa0J4Nkz2r5AHdyuZOXxNKnxVLbQsT0awdhbqZ7/BZ+GfOuvc6PG3yt9e+fGhmMYmh30Y8UTHfL3c+LnTxMpn0/AhS0Y2fv0ZV8QfZyopeJXFzHavU3Eqh5lrH6eiczvh6j9CSFD1LOF34fkqHbv960af3zTxbx+UapILyxLxOC2KffN9w3nx+KXi8cEOKvVtA7D0VplcM4BueTQ2uiPzny+cx6/HwyccFiYc9u7XcDa9GI/BA8Q/yktOLDcy02crIudLbqfaUThe4dnSwh8LVZ5O8ONYLTO3zk1F2ADZr/LCTZsz8uTNTwj7oWdSVSeuFYzk5LM/US1CfDTW87/3tzosftX0dAYOXfUFESsVxr4PorHTyBGZraglOT2/k3it7LdJ9nvhTunPswmIln39uN+HrTEGXq9gh+Owxiw7vJgRpmcgwYWsQ/V8U9SF9/UVNh0bwg/J4WpDA+gqftVomP9/70d+c1Tofz9HEdvIikqSi2slF48bLNxsT0bG7ZJnl6zIa9Pxdk43Hs1UOMAbRG/h6drC06fEt9aKlxVKUnm2qB+dRkayZd6fePxMfLSrHTuqiCetdGBUV4V5GoXwsqqBfyx3YNYu8e3TaYgyRPCHIxaZ6lj4cUoy6uwzcOoRO87sV7j6oOx5Vx2LNvKg6UGFf+4Nol6HcE76PR6q5G9Q8jdPX9nfqjZUl/leLPN96YLCufOFX7ar7HbRj/lGA9OtDhwQHv9Hn4T5HeS/2zpQ510YpzVLQGQdA2OmOHC4tY67unvQ1mNkt0grHvWzsF+lZAw4H8byixKwfUAYr5RLxMpfdPx41YOYDxGs2DAGs2urdM8O4NeiJl45lo5HozT2y+XDkHNmllySAsrezizvRoeNKtvf8+N9ETObHk+Ffpr4gmrD+7aSJ52TUWKqxkM6H8pWNnLH3zYs2axn3E0nJn8Sz6+XhAvuMC6ISoBhRjgdgTh8OGpg0y2SD8KvPYRfP/7fzztNzBvFyFvfMXu9id6Haci+ROXzn35sWCrc8i0IfW6N7ns+XG+q45TBHlxso+f8ri78GKuxerQPOzbqeOmuGye+qlxfS86vrcK0ziH8cjGC1+bFwB1pZIYfNtz9KPlVP4hi1nAeyR2H6L+kN7L60LKnnn3oQt9MKqt+DKD2BwvnNpJz36Jn2nUntpbW89ouF34OM7ByIQcWZ5I5/xDCO+mvFtJf//3/3KxlUhGvGVjW64Azj8rMtwMIFdRTO+vC9Pb5qba/gkYxeo6s6MTVBB1nlHBj90iFp/OEUKe8jlM3e/73PtI9+viRdEfyYIQTn18LL7R1Y3k9hV/Gh9AkVs/h5Z040y+aqyq9R7EZeh4JOLHkVFb+XP0SLSoa2X+98PY3C8/USIIpScf0om70f6ayQjc/SjcwcM9fDjQaLz6aIYRDTvGN6CCu2PSckcOJ3Bsz8sG9T7hRzcSJS9Mxup7wwYR4GI6L12wU7knOwNhCXxB5P4KxQ2Kwzm/kc6MVl8NUzkoI4NTgaK4t8R4tZG/7jpO9EX/vLP6uP6Kna7vw6N/ZOPHVc9Qal5NzMz3C2Jl6ZvM5oRrMDDlScTFO5dwyfjyReeqz2oaULzoWhBuzZY/dOdxwLZWc+ObF4DlGlnBYseWKkTenW6EzyNw40vFxp4X1z8pzk/tKk/taslFhzF3x2l4Gdq7twB+PheP7uDGwYhTLr/+OZo/NHNonBVU+mri6QRrKVtNz5zIXUtMV5s8TRKVBUVxf+jsOT5U90zy4kkOe22MPiq428uQbK/4Wr30lXtv3dBSrrvqG6UM0qsV9+LWziRebpWP6LIWKJ4jFw02My5+O+00jOHRwLPRd9XQ3cuGGeP4E8fxVOcTfHyfjchYdZ70W3/g/nvWKV0aWsCL+h45lq7hxKlnj2kJe3Gqi57ihLpRM0fGfgm50fyYc3c2LpErCF2tisaKR5NKIZNy/buHgiUkYKM+hbF4PVvYw8ONvDjwcrOOAEh7c26bj8Mtu7Hhg5OWBVpz8YuB62PFIemaA9My6ZhqfDvTBW0nHyDUe5BQ+fiSeaG+pY7beHnTcpGerO05Urid5OiGAe4X1HHbShVp5DJx/x4Ex8yzMlZ6Es6qOy/weJA7SWLKUDy3kPgYU9sKxTmGOx0F8GBLGpsUS8US4pY9wS3Q2Iz++sKGhPpJjXD9xb4mFI34kYZpmoepLxjvh/mbC/Wubm/m0fyrKtdW4pYsPf/ZVWKCq+KrMwzOZh34vjaz2hxVZ3xv5pokV46UXM7gkn8eHsXiGRBS2ZeLpnB9wwyLPIcmGG+0ycVKHj9imRfKK9yeqdLYIlyUjJL2hHQ6hyk8TV1VOQ+R16c9JdniOhDNqWxxqpUdxYd5vGPC3jhVfufHyosKOc4PYONjC+yWSke1XHV9d8uD5DpWdzvtR66XGBX94MXab5MQVP/pID1WXHspyXs93i6Rncxk59b4NB35o3F/Fi4+ynw1lP8Myq0x5G0CZtwbuaGHHmxpm/r4wFTeklyZJL72vo2ezyS50Oh3OFqvikG+6mbdCKVgVb+C7Unbc8Bk52WzFl4U6/pbgxtYfCm9UCWJplMof3wJYfTUD304VP5pn5qi0FMStNrPs6xSUF87YKpxxX3xgWG/JqwkRnBoZC884jVEZfbj03cB51ex469exldGNQ51M3NM8Hd7Z0cIh77Avu4lHn6ZjbbqFL/Ik4aNPZQOzH2PbRzFz++94k8PIto+lP2qIVy0MYOlX8Y5aQdgkx3NKjmc7Jx63xI+vdfXERBd6vTOTzVJQtJzGY1uFk0qrVPYEULSwjsdPedCtlMYGe31IXaOxwHMvjjaN5tbB72GX55Jdnkv41AjG62IxdLfK4qf8yPpK45v2XrzOq/H3mz4cqahyx7oAWj9T2LNbECuOGflpkxVlWui4s58Hh8sq3L0thKJ/GnmioA0lxVf2i68UbyB5/JcLn7tr4nk+/PZHOGe0iccl2efZss/VElWuKO6Hll9h8EoIH6S/Gkl/BaYYadLbsEVe/3ZfD04Mlx7LHwDEy2c/i8XeMQqPZQthQAWVFTcGUMUv3Gzy4ktWI397acP1RXpOjnMioprGhGU+1EvVceKvkpulxRN3eVBX9n1SSQ/GjpSeypsqexvNNnfeoUSKgYcL2hG4KHw2NwWjZc5y5XRi00gD7+ZxYMFwydH8IcxvamDKYAfGz4xktO8H3Gtk7p/H4fx3CxdXS8IMqgxMD+DyeoUzHwaB+9IHQ/yotFjjmlgvPNnDmPFpIoqtVXnkmR8FPii80CiInuszko8+4RfxnOviOSfDpT/i3mNlnPBhGS86Z9Ox+QsPPktf/RYVgr+/cEWFZBT9ovDEf5/3kWZg8Bc7Kgt/rzniw9CMRhb5LFwl/bNU+ueQ3sTdznTsu23k0VFWPN9kYLfbdnwT3qohvFWorY5nu3iwvZzKS1sCMCyV8/ruxpQFFhqSk/DtVgbWHPMFOVXxt0AIDdcZOOaJHeZhci6FQ5jaz0BDZQeGSc8XO+dCRAbhgq+SP2cVPlkWxKW90mdH/fgzm8bCL3yYMEFhVGQIPT7pWL+eG6sK6fjhtMxTKSN377HhWYdc7NH2AbqNE87NFMCZhiauGJWO3HsNfHDMjsQXBpboaMexFdm4+cNzlBeO2zInAHtPM3MiFYslR+KK+BGe2czEt6mo+0jh1H5BvOptZocaqSieTeWhFwFsraLj1ZUebIhV+bCcHz9mS/44/Vg1Ws/3OVyo2yADJ//1FfUWhXFifAKKh+TPqX58vqyw/swgft2q4+VrbtzsoHF8W+G/j5Hc2eAHqv/39+9/+BCfZGCZInbMqKMwNDmEcpqJW33pqN1dxzn1PCgtuba7qhfbNhp57a4VDceZOCZTOvbI3p2QvTu5Q+OG814czmfkzus2jK5lYp556TjXwMSlY9LBjnrOauXCtComKqvS0UU8ppl4TPfvcn3Vg3i2w8Du5+242UrHCT098F3SM2y2E1nN0XyV+h7XT+TgxHWPYRLvtz4IoL744vhkLzoJt7cUbv94wsJG65JwcYvChdeDeHMmjG2XJ6CMzOWesl6USZLrKmrFdPEKzRdEzTkaFzu82L5B5ui+HxPFZyJ3etHyho59JrjxaxsDL3ZzIPsNA19MkOcYbmaJuFTY65qYc2I6ssnrvJTXOTM0nCuKxGPhChNTPqTh4DSV+5QAXF4TM4el4brBwCl2ByIvS07O8uO3/BZOv5IM7Z6JvmFpuDUxmmPD3iNc9iZO9ube9CgOC37DySYWrh2aDFTWOOtvH8Kzqkx6FcAyVeV3fwCK9IWyOg55zGbeT02F+76eGQY70byLgYOaODC6ovT0epnr4eF8+ms8GspzHC3PccGk/94XMICP7Q1s1M6BWhv0XPDAic4NdWw22oNlRzV+2+LFntcGnmprRw6Pgc8j7Tg8TzgrzQlzzXCmL4hH8awaj77yodhUA4/pHMgo3PeZbrjvKsz4ZxA9xoWxTqZEXPtp5LRKVvFrnfi1GzXKyLnv9KHuZD2nmlxwiAdljZCcFo/Y+SgNO4bIuReT3HoXzozN4pA5QuG/sSEMOWFgyfV2vOqlsEOtEEouNXLfdyvedFf4e90QssTI11UMoo7k0lTJpX9raGy70If6wpXjMwfxsKSRg/bZ4P+gp6WhE4tOaoxf48W9NOGpfG58vCp9PdWJkSX1zLPfhfiyZpbelorYczpWWOzGkqEaY4vI3tTVccckD1ImWlgwLBlHHoZxV/8E/BmmsWCiDwVbGHm+nw1/5xZ+uOtB4YE6nirrgSwjZ+32Y0yayhy/+LGtRiZeW/gRFX7Rc8s1F9ZtNvHpzTTUhsp5swL4/3g6yzApzi7aVlX7GAwa3IM7BN8bd3d3d3dJcJfgwd0tOAGCuwd3H2Ok3bvv4bv3ub94YJjurrfO2XstmOnpMdfAmnbx/f7i++XDmCS8m054t6yc55aKNvyZlI5vi3zEhZ/vU+Dy402f9GxU+QN8SQoji4TwcqFcf6IPG05aeHfVD6y8YOHb2ZKPqyys/vwHsj4380mbFNg6yJ62tOKF5GWrQBgHhY93Cx9XfG7i6jZpqHZZ5mOaCxMvpWf6Ge8R10lhuaZhfIwys9b3VBQWPjr2TwCnDDqudHjR9arKhpMCWOBQmJIlhOEZNRZ85YdHPCZ2rBNTC6o0nRZeCJv5RUmBy2dmpgh5HZdV5poawMIbBiaPln67qKdxpgsvi0Wy5f4EvJaeaiw91ba+hR3HJOPxOh17PvBgf1cDD9Z3otliPfvHudCwispRS4IoKZ64RzxRO6wytCOAwufkdS4Sz/z5dWkpVozWaczp9WO1loEvfG8Qu8PIj2eF27cqLHUxBJeqMXPAj3R/G/l5gw2VpX9WLAjhSlYdpz7yokdbhbU7hLFBeu0OQkhKMbNI/hToRhjoy+fE/CV6Wr+68PcOEzeeSUOFkgrX7Qrj+c/30cvuRcUh6bi26CfUiNRxXoIXN76pHFchgPZ3YthmyGcUd+p5ILNw+kYd1930YFBuPUtedeO+nE9fOZ+KjRSuHRbGsLlmFranoPNujU2P+/BMfLaj+Gybmnp2nOGGt43wc8cwvHdNTD8wDaF/dNTP96DcWo0bHwunxllYr8wPtNmjY5djHtSZH8kpqfHQn1DoXhtC1n4an1T0Y98/0ifzA6giXLM8axhTSms0b/Pj3FozFz4S3ow0cnOCHfNvmZk2PAW7bpp5emQKDOWj6FkfhxHDdcxXwIv1UxU+MIbR8aLM5ywf3q/Ts84DF0p8V7mvfACbugj/NkzFgr0aUw/78P1NJCs0jEflKAOXxznRK2AiTGlY9EHysqYHca3Ek7sGkW+XgRdPOjAzr47Bi17Uk3MdL+fae4aO1VQvqjcxcv4gO+b9lY62Ox/xjzz+Ynn8gnc0nh7iw7Xl4hPvbWg82cihkXbkqKfy9rggqk3VcZHJi1N7DVxz2IEMn4RzavhQYKbGc34frPlV5j0XhJbLzMD1VFR9r3FhHR+qj5Pny2wHxDdmb4tHnkwqr74IYorwT6Twz/mbKuePlB6/pLHydB+Oq3puCLjFf7KxQod7eKpp7OTzY+9DA4/2dMAo82qXeU1roTFPTz9GCG8WyCf8v0rly+cB5C5j4ZUtySgwS+M/Xh+y6cX73G74xkQx+pc4ZB5g4POyTqwxRvK5LQHDlygs+C2EiROMjI21Y3dVhScWhtE8rLGP6sOIGXrmV8VLtits+U8IhTvJ/jQL4m19PRuOFg8+oePENdKTHaQ/W8m+iefsFs+ZV0tH61ThI5mvnY/kdaQKh+ULYXPhKF47GoeFr3RMaurB/A4GprV04tYfGkcGfBhbKYa/rPiC9INVfigexE1HOo7K8hEPHuvZr6sL6+4Z+Li/A8vEK74sDqDDx3RsxY9YVNjIpKN2vOmislHDIMqKT24Wn+zxXzTrdPqKxoqZw8Kp+CdLFBc+iUP9nbEcf/ot/jUZODPNib/PG7hprgOttxrZ9V8bzBsU2m6H8HGjkbVuiIfKuT+Xc9+/XccD5yRvKkfywNIEjFyuZ54PLixIkTnN70O17ioX1AmiomrimqAVW8ZF8VKmONzMb+Doc06c0+u4xOXFvhLiG7vdaC++3FZ8ub+cY4W1HmRZGcH/XiWiojze2gLCIZeN7DvNhiri80sH+TH4so4lpnlgkr1Lk73r9lRhgw4hlMlj5LbLdqxw6fg+gwelBkVwV8kkdC9gZN2zdqxfKVz7Uu5zvJmxpVPQ/r74Uj+5v79ZmH9lMrILZ92f40L7genYrvQnHM0tfHoliBo7JZdOezCvTSytHd9h4MJIlkmKx3nhgblRDrjF3zN29mOdcPhj4fC9pcUTtgZRQrxi7x9BDBBuKifctOSB7GlvD2KEH75ZXSjeND0PDvggv+q4f4AXbx6obNI7gOeKyrahII6N1XNLFjeswuO5l4Ywb7KB9ggnio7Q81A+Nw7N0nG3x4Nc5RRe2yC9L372UvzMdF6jbZ4Pe+VcjrYPoXikvI6EIDJd1PHlLA9Y2cRZy6xo3zwd2/T5hLOpKpfklb4R3vsx0QO1WgTDC5Iw/5DC1J0hFJobyRP2eHQvprHOAT/MY3VMzerFQPG5MuJzq5dqfPnZh/+KK+y8N4w5o2LozPkFN8ZoHPeLHyWEK/bk8SDhS3qWqvweZ68qXDIxhAL99DxX0Y3YzGa+fZaKhhPMHBGbilKvddzVxIPsfslHSwj/Sd91lb67WUPhqNlhxIV0LKt5sKaUjv9t98KdSc8Mz93Y8MLIOy1tiBWf+ig+5R2kY7oSXuw5oOPRvR4knpDeWONCrmUab3zyYXwO8cnbqfhc1MwaB1PROSqSzeISUCy9wkMfwli/z8gHB204nxTB+UUScbKPwtWVw/gxRWMRsx+Hb5u5c6h4fFY558dhZKgZwbczkjBtnHhVZidqTlI5IzqIszKXyy7ZcesXjSMe+JFjoMbbP7/PvZLGgSv82HxX8nBgAAWaW3i+TzKqHBFP2JaGngNU1iobRBWnyqWZA4g9r/LjvADiz6gsvTSAToUzsuWxVxh1UM/ce1z4/EBhjd4hbJG5vSZze3uWymGeABo9ll+7BXDxZgRnj0zEto16XrjhAuvqOXuCG13aRrJRhwToBxroLu2EJUljSlEfOv6issXDIG590nNEdRcMR4x0bRM+lR7a0Uh49G8d32zwYIHse2rtIPrvNLD8aQfmDzPRWtCKBsMVjikQRmfhj2bCH8UWqTyYEMD929HsP/Qrpm/UaLjpQ8uOCnu2COPm8FiOyf8OdSIVTkkIw2pSmDctjKjfxBdWyh5Oj+I8XRy8N4XvR7pwWOZ2l8xt+/IWtlufjJm+n++f4sBo4ffswu+5sut4464X5cSDNlwNYfef4t2vxXd267jguAc79ArPusJY0UF6rKUfhe9Z+PeAH8i5Vcfb/0oOPVb4rFsIvYQbanQKo/p8PRemujB8v5GFDgh3tjPyXVs7dmw18/zFFJyXvzdP+GKDXPed/GFU7KJxdSM/el5QiDkhvOmhsVFNP9Y2lLkd7sXeEjKXu73Yflg4f7vwbedMPNnkBXIPjeKNwnFotzOG7U9/Rvs/Ytku+BaTWliYvlcyMk428VWkFS/PSR8tDAEyJ7NkTh4Ij/ed7MV51cgFATv+ipH5/WzHf1V17LzQi7nXdbSP9eDhNY29xsu5R6vs/i2IwH7hl/0+lL6XmTsGPMWvwrHHhWOLWYw8lGzH5U6RnN40Ac4mKrMMDOJVvDxvqRA2lYngzc1J6L/VxHL/pmH3QxNP9ExDy20qe10IoLbwzOSHLlgKyHycDcJ7XWH0WPHvDNKTb4JIVgwsGHJiy650vHLiI0p/jeD2Sono+0PPKoVdaKIzcYjXihVv5Zzr23AiReX6/AG8rxXNulO/YcsGjVdv+fBndYXv5oYxvbmB+j5OzIuW3voaFr/Xc1JrN66cjOQ08cMtwr9XtgeEb3Sc8N6DoFyPXq7nYQ4T+96yYvYTPd2dXdh+J5IXBsejGRQOnBVGQObRFB9GadmrHVeD2JI+ipffx6GY5PhhyfG9tQ08MtmJSsK3f55Ng1Zex8B6mbu5Udxs/46FzwxMaetA270qOxwJ4FGxdOy9/xO0NI3BPD68oImtf7di81QTrxmteHdSY72fX1ciHJxcMYCqTyxc0uUHSuSM4t6bwo/St/mkb4sZFB52htGinsae4/zY8queV/52Y3QuE3Nct+LvPum4sfInlHuhcUMrHyxnJc+XhDDumspMEwJoNScTuzmfY+sGEy/fSsN+yfODfwXxKebn++f7kVO884545xHh1B3CqXmzxfDf+1/QeKaBw/wOvKWZjX5Pxeh04g+f/Cguvb7X7UWyMYaFbF9QTvJho+SDeZfkyUkf7v0hPh5w4OtTIyu1tyFVuLRgvBelfjNw10onQp0Vao3DSC0h/rjbj0sj0nF63k/I81yuq40LJ+9GcvXAeAT/U2nuGMA26dN/pU/vf4pkv+rxODZFxy1mL3IO/Pm+TEEszC08dMWOH1UiWGxxEl721rNlVff/3he6zS9B/D1dz780N0pJPu2UfOogudBacuH6/giO3Z+IX8oa+GiTE+fvWzi/7w/kHi65WiCIYGw0jW+/4d1WjXUv+rD4k4Hx1R3Svxn4uuhrVFmmcPGnEJ6c0dh1qQ/z5L7Y5b6sbazyv8FB+MeZac6UiolxesaUFc/upmPOel70XGEk3tlg/MtM+90U1L6scOq0EB7J9XaX663UTearnhXrB8pel7Jj31tFfC+EKWsjGfE4HlpTPX0D3BiXV7z63yDGTjEwu8WJKjWlb2fIHO+QczsTwHDZywKyl3cGKxxWLIxlWwz8eMmB5jcs7DfmB9QRJir5reg7QmHVfGH4TmuMXC6cP1I4P48fHe+obDUkgD2jJU+zh3G/knze8mS88wqHRYXwr/jLrHIBHJ5o4a6YZAy4pbDsiBAaPUjP4X3eo2ARHc8c8SJmgcJvySFszaTw3+dhnBOeW7TVj+iPUfzG7zCv1jP1mQuBXw00/e1E4JiRlo02pBee+Sw8M2SansUMbkS3NDG+hxX3vlvYv9wPfBSPqCke8dcoE+/ntOJoduG4O0F8WaSxarwPdSVHJhQIYG9GC4+8TEbD+2aO7peCc8KDi/r6EPVJx7gaMldJCgcUCeHSfYW/9w1hUl89oyu58dGvY02zB6uE814L57WUve8le790cyy/XXmLLSVMvLTbCus+lXkOBlCyhcJdvcJo3dnEbk2syGjR83WyG3MqiVcuD2PwNgOLX3AgbxE9Lx9xI35kFEvljsO7j3o2gAv6lgZ6ezhRpL3Kv9sEUcyq8nAu6fUfBg4v7MCvx+Q+bAohJFyuFy7v3EA4ZlQCUt8amL++A6NsKnPnCKB8Vo3rHwu/vRTvbyF5LVx1SrhqYWkLE7cmo8wbHbc28mCczEtWmZcBuTOwwtU3uKQz8nePHfVl/sbK/FWV3lm8KITW+xR2OxRCp8waWzzzY7XBxGcOK5Jbmvlr91T0l/6sKP05+V/J/T9C2DNOx+OZvLhnjOIgWxyGn9BYaK0Pd9eLZ96zYZXw8wvh5wJmPc+munF9qp4TjG40n2dmP2sKMi+y8FnCDzScJx5hdWD9RjMf3EhB10VRbJTwHdeFpyYu90vfmdhpoRV3vykcXCGEjRn0vP3ajVHbNeb6x4fJ5/SMWORC7CWF72eEEPlaY2ITH3r/GsXqx+Mw7UwETcsSYfhLT+ddl8ytTubWg5vSI6OlR+5nl764Y8Xu83oen+vCn2XlcTaG0dUpeZg5BQWLmXnmQCr2xkk+lwmhZCmFu3cIT0408EGMUzglgv6IRBzLIXN5K4jjkkcbJY+0mwYGRjmQfkUGfnz3Gl+rq6w6V/78aCQDW+KRtZjCJweEKzaauPpmGqY/i6CuXSKO15TPny6cVjCSo04lIC7KyPLf7Uh16pg/kwfrcip8dEM4SnI+/nYa5jqMdGax4fN5I2vMtaFkPo17L/hR+YDCFftCWCQckygcYxfvyC7eceGRxrndfTh8ysDtfzpw8pDKNTsDSPgQwVI1E1F4v8zfgRB0XUz0NbJizA3JnTEOuA6IF+31wTxAfKGsF7YPcv41fej/VmP5Bj54mxqZrr8dL9ro2KqjF61n6tjF70HjsJ5DVRcyNTTx5QgrkqTni0vPd4/XsU5pD4ruUXn0WADXZZ8nyD7vSNTxXDEPvuZSWflaEDWHa/y9gB8DUyNYJl8inpRPx27rPmFlFQNfLXaiUDM9T/dzY+/vwl8hyfsveo6r7EKZGTpuV724/lnyoWoAbWsZ2H6aE8U+GHiolgMXF+o4M9GD56cUtv0zhFZTItjDkoR5FpWOH0G83qiy2Y0ASvcxcFsVJ67nVDn+RhBVf4/k4nA8br7Qc2xLFy5KzsyUnLEuEm+Od6Feijx/AReqyX4szOzFtArCMWudyL5U5d0vAczdbZT+sOHLZCOrRtiRPpuBH+470b3mL6wz4yGm9bfQUD5ZzlXPCg1cyNhH49sqfgwQDir7LYxvRVRWOhxEizkG9nE68OWjgdXgwLj9JmY+kIac7TXeaeNHq+kmdtdZUeVxNJd1+4rBMrelZG63QvJspuy15E/h1kF0iTewUWnhVbeOp2I9yOlWeCM2hDU9DfwPTmSWvXtWMYSGOo2jvH5caKfn3LZu9F+pY8VXHmz363ne7EKWg8LBe+T8c1s49koylpY38Ns6J+I0I8v57KiaQ+OS237siTLwxHcnzF1UpjWS67gh/DrGhy/yvFXTh1Dls8Jl1YSvNxrY5IYD6X6L4KeVSaj8VPqyQxo+CUfVqBCQ/FD4NCEkfG7m3fypGNwniiWrxCGn+NTNZQEsPazj1+0eLCoQwcQzSfh4zcya41NQ7onKDV0CuPOHjkMCHiyR6/squfKklo7dpnpxa4rKUeYgypQ2cOs2J46ulF9fOtB7rY547MGDNxr7NPQhn0fhxXTiH5s0zr/uwxXxmml/So9BxyKzvPD1NzO6XCqux0Rwwuck9FttYMVnDrzzGFg/xgHjNo2OCz6ck/1bLPtXdah4e+EgxotfZBa/yCYc9kA47HBl6ZulYYwtF8GsG5KwvanK8/2DGJqssmihACCeP3uSBwcO6Hlwrws5Z8j9V/3IvMTIF99sYH+FM8uHUbaRjpuHiTd8NtNZNQVNa6gcOCeIioOjuKp4HOpMVjkxUubhL4VN74YQJT0bdzeIq5Ljkzb/5CTp7bdODIjMzrIJd/E4PoZdS33GyOWRzPshHj7xv6gbPmTuYOSzVnbk2G3ineNpKNNY9meIE3X+ysBJd19jZj0LA2OT0dJuYI9sDpjGi19mtOKRcG6va1Zc+sfC6fN/YOsPI//91YbbsUYOf2uHtZieOfe7sVJy883+MDq5dGyW0YMqwv/LP3iQJuedV877yggDp+V1Qucy0JfRIfOll/lyQbfcQu+HHwj10FNf043KMSr//BJE3iERvFg0CXrxZK94cqnJct6RYfSNN7Ny6RSk/lCZ/9cAvv5iYOUHTvSLlVx6F8SoCypzzQngXqpwRd4Qdgp/nRH+Sv5/34cW8ziCX7slot8dhZWGhPClXTpWbSv8u17Pq/ddmCJeFSle9WRTNLtd+wpr8Sjm3huH6fUU6seFcfKkmWtWpWBREx0TB3qRW/z4er4A1CFGKsXsmLpOpelhAM0m6Tkg2o0OOxS2ORvC7FFGunPaMV16QCc9sOqKwueTQ0hYrbLE0wBOHBdOXic+tlnjoKviW03Elwf68b2l5E13Nxo213NEHzfOzFe4NCWELrkUNr4exqsGejYf6cbZggYuP+XEQ7nPPX++//Bvet5cKZ83ROGIonJfthvY/B8HEt7GsnT9t3BE6Jk10Q1TLwNtNZy4FZeZI8s8xY2bCseMDOGbeMpv4imVpXdXbApjtfTwC+nhesctHL/+B2bKfQkWkf3aGsHJFxPx/JaR7YbbcP9tevar/x4nq1i4enEy1uXT8fF5L36tJd46zYoJYwzMkM2Jig3Scc2oT2jUUuOwHn5sm27gRc2Jtll17PDYi1YyB91kDjYM1PFuKdn/h8IjPR2YEhK+1gWwxiOcHxNAvfQKx38Io4nM2yCZN+sZHfMu9cAeNDK73oboQAS/mRIxRfwg4rgbOY0Kb9nDeFPUyMYH7Zhj0NPlcOP0Kgv/fC5cO9lARjhxoark6UI3Fuw3MnW/DTcWZ+CY769RrZ6eC8a50TsxmjWKf8Xm0hG8tjUJS8IGximS93kUZrkchs2lMUcGH15KL7SUXhhzWmW2FQGMlRzIKjlQeZiZfxZKRdsZBnZUnWi0U+OI0zLnFzXumOXDhn6SFxX88K9XGXEvgMU1dYyf7oX7ezQzlPuKB+KB/V1ezNtipvVSCra9U3ihbggtdsew7/HPqCm8PkN4fUBBheVOh/FOeqO+9EY++fOL8ufXM6gc9zqIBmYLR6cmY0OnCN5umoTRSiRzhRNwrq3KRe2DeOsxsVFMGnL/HcmrG+LRuJ5w2jgvBsk5lpBz3DlP9isthOGrxCdf+DCxrMbYTX78XsjA0AkntEEqAyWD6NBb+KOaF73EF3HVjlYlZb92fcO9NxYObvgDKbvEU0948KKUnq23u/GsmXh83yCUCiaq66zYul/lpf0BpE/8+e/1AQzurrJknSCe7Dew634HIuT6kulEk4kah6TzY+u2CF46nwin7GfWvAHo52r02X3Cz+n40vEJwaYajf1/vj+UgRMOOdB6qpFdTXY0bqjn8BFuKI11DA3x4oXMX2uZv7O1xO+mhjFlop6R6dzYfFjjle0+FP3NwqMrk3ExSuOs735MSKdnhk9udOtkYb1myRi8wsCS7xywrlOY50EIsxar9H8PYFJrlTFdgrg90sARuZ34I8XAYH4Hrs/XcXyKBzku6XhvhgcDHAaWzerAzeI6jt4re/G3xm4bfOj18/sCV9qgDDIyVNKOKTFGRn6xo+RTlbs6BNBDerR20IP366JY98F3rL4Xw2f9PyOv7PWlDUE8+yuGbe98RrtUjR3y+VBYnue4PM/bBCMblbAh03uNL+v4kNsruRjtwv7ZMTzk+owLZRTO2xxGZW8El0cnYr9w4qGjARzda+TWwzZY/tSY+saHZMn5XzM4sGmdkTce2DBccq7QswBMVfS0LXHjYpVIzlqcgGkFzTSeTkXVZ+I77ULIkz49L3/4gI2R0bwT/w1jn0Twly6JyNpT42P60VC4bYRwW9fzOjac58G2q+K1E0P/+3711a8D//t5Sksm+dB5rMLmWcP4T/y6U94wEvrpWaqCG+2Fl9vt82FqcxPNfazYXll6e2kQA4foWbaoG7lHpOO1fJ+QL9LECwlWNPykcFSNENwLjcyYaEOM8PT3AXaMbaVntm7yeL5oto8UjnshnN4qESczmbjmuRUVSqpctysIxys9f2nqQkou6Z9rMv9LFOJbCFN6a4yo5sftkgqH7xQ//UdjyQU+fBWPqiwe1fRDeg6q9R6nk3VcUdADl7y+jEXcaCu521H68HOSympFAugqvN1wiRMjb2vMN8wHCj/+If768/8/vlpdqJ7dwPl3nSiYXeHZu2HcqKjnmFVutBBv6ZPZgxE9FeZnGMslpz9ITl/MqHLOyyBsmxTmvBZCcp10LDzxE25WSs8xyz/g7etI1m8cj67/KGy0IIQDyyJ48GMiipY188imVGxfouc/X13If1vhP8NCKBOlcFuc5MQXA09XdmB7Hcn5iU78O03hH/ownr7Xs0NtF1Zs1vH9FQ/mVdRoW+XHvo06HrzhQQXZv7WlvchXTOO/B/xYf8fAe4Md2HFL5dnhAYx/qDJTrwBuvIniuIbfMX6mkZn9Nkx8GsPYDp9RZJrGowY/Ppo11krxY1db4en2XlQuoXL5HumXNRrvPhFfktwpv8ON47n03HjNjXs7VA46E4BlocqUpAC6SR/XG+SHtk9l8FAAb+9Hs3HfrxhjMjObNRVNXykc1CyE4BYjjZdseL9Bz/q3XGhw3MQx69Nwcq3GdY98WH5Rz48zXZgqPWiUHpwbqdIWH0S5oiZuPmSFZ4qZseZU1JG5nLQsiNNy35fLfbfaJD+yh5DrocZbvXzQCRf4erjR0aiylT2IwTXF32b40fSh9GAvD2r0lv6q5kavqVGkKQ4d65jYapIVMwMKQ8YQ7hzTcfBGD36pp/HhOD+ytzfzXptU+JcrNL8PYWCtGJad9gVxh+VctrvwrpHK+kODePXOwuZ1f6D8OYXrFoVw9FfZ+7/t+ArhqZnCZ80VVukTxssHClv2Fr6Svakqe1P7tcIpTUJYvUy85WMIO8wRPJci/iD9sFz6IWloJIv/moCRMtf5iwZQRonh9vAXnJLeWym99+8c8TmHDxsXRvN24ldkkhx7KTn2Z7ye70u5kDQglsXLvEObtUZ2fix581H6hB68aRXFJl3jkKGjga9bOJG+k46fmokHfDeIBzhwqbHC3weHEd9Vx3L1vViSU3r8hhMxmVV+exZExzJyblusOJzfyJ3n7Dg9QHioTBivC+nZ7IQbWSPMfJKUip6TolgzOg65QsIXuhDQVvi+Qxje7TGMOfcZFUerXJVD+uy+jmp/j+R3JKuvjMfhjhHc3TwJFf8zcm0nGy50VTi/fhiHlmbg7s+vcbqZhSv6JiNSuDRRuPTKMiOnfrQhR7Mo3u4Xh0wFNL46K/e9k3hDsyA0p3BZZh/+ilJ5/3sQ9wpEs/+Zb2gq3tb/puS7XmFDl1xvdfHWufI8Jy3cs+oH3tkiWC97InbJPp3pGUDBbDqeve9FV7uejbLJfoqXvxUv31xTPHR6GJuKWHjtcDJcfXTMWNmLkC4TDZ4XaDBV5WhTEMWLG3hwnxOjjinMtSmE/oM1Vizuh+uWmRmHp6BrfZUNxwSRoZPGN838GLxLYfGTIbTfLN5/NYTBefQsftmNNv2j2aX8N3Tppmfjem5sFD+6PTuIu98tHFLuB/6epXKjJ4DtHyV34ELNGypnjAnInkvetHfi4l3Js4EBrB6t47PsXgxsJn3az4lI4ax44SyNwid/BDFinHBjZi++X1dZfmwAaUET8+nTcHZ8FJdkiMOXpzpWa+9BW9mfzibx2xdyn1tKX8WYefVLKgadMbDUMgeWyuN/6etEhftGrutnw4alZt7+nALLFj1TLsvrm6FyphrEfwYdOzskb4WfZgs/5U0Unijuw8faJtacbIXvooFRMx041TGKK5vH4aPwUE3hoUunxcOW/8DdTkYObmqH67aBmYc6YB+lY46cXqRO0lgwyo/9k4w8HGVHnaYKJw8IY2lJjV93+nFK9nKV7OX4CuKza8OY1l96sLwVA6N0LBPnRdV2P3s4jB/Sm4WlN0cXMDHXWStW5zHx+SUrHI9U/tI9gMiv4j+VPPjWJJIVBibgvnB5fy2AIbdiWHzEZzyuZGD35U7kLKHxxh7xmGNGVtxow8ucsk83rAhM0tESJTx708T0I9MwXTxALx4wbq2BWR870GaBxs7JPjguq8w+NYAZ0sOa9PDbPTo2PurBauGtF8Jbe09r/Hu5DwMuKywzLYTfp+ipWtyoZYng1OQk9JmpZ1W/C7dXGjj8pQOF8+l5/IIbPaJV1v4WRJZXKp82C6BEGR33bfFidx35+ETxLLnfS+V+3/iqcOxvIQxNUFm4ZADl5oinOV2IfarwfYcQvr+KYvmm39HooplDZ6XgrOTFks1WjP5FYe6HYUydFU2L9ytMF41Mm2XDvzKPf8g8Lj6hMn5NAKkpFubN/wO2plHM2T8OdY9qnLDFhzaXDew8zYHGMveDZe4d1xVmGxvCV30kK7kScKCPwgOVw9DJfnplP09WtnDV0mTcP2xgv+0O/JrVzBOPU5EjNop338Xhlj6KI1xx2Byv8lqpADpf1LH5LA+8/VSmqxBEAWtmnsv1FL2KqsShIPpKTlf5FMKxNRZufPID97wKB0aF8GmvfPxwAD9W6Fj4rQcxJ4z8utaGw5903FndgwX7NCYf9OFOcyOH9Lajm+RBPcmDZzmN7HjDDrvXwJxRDlwUH5v5txuPhb+7HXRgVVh4U/FBldcTqhhEuKhs5kE/Tvv0XBUh/fOXiXXvpOGx8H434f2cxYy8fcCOMnct3DroB9q7JTdiQ6gpOTdDcm5zFROvL7Yi7yYzL11PgdbCxFAvK8r5Ldxg+YH9hTQeOuHHL8UVPtwXhrmHwrRa4m1bVea4GEBF8ce1BhtSiiosdDCMBQcMTNnrQIHsGXnu7it0rSX8Oc2Lt5VUNloexGpHBJ9nScTKLpLHDWXe4w0sX8qBP+7J9fQPoOEclaOcASySPU8o5kPAHElLSgLWvTPyUV0bmh4WftgRQtXvei4q78L9lzr2be7BtQmRHJ8+AXn9Gi9ZfLh5KD3H7nyP0rJH229aMSIofWkIoPg8E/dZ0/Amr7yui16sUFW+CwSx8JDGxJ0+7Myv4+lzXmQSvn9R0oaLbw2cU9+BWd2M9Ne1o/QEjdti/VhxVjhwsQe7V0bwxMtE1BujcGK28P+8+ab4XqLwZzHx/gPjYrg/0xdMGGlkpjx2xNaI4cc5X9B5k/TidRdmxWr0v/Xj4mUd50z14KjM9bbNPryT+1CvpxUv/NnYynwPp3MI397yYlM4mjeUr9AP09FTyItF31UmlAvgwycza1VPQZ4cko+3peekVwdIr75Pr7HOez9myzl75JzT91T5mUEUHGDimbJWtJ6osEu6MApc0PjPHN//f3/tuB0Ky58J4U+5b2/kvmklzQztSsW+MToe/UV8b0QkW+dNQKufPzdtnxtqfz1D5d2oEhfDZWU/41w18cD5YZQfpvGvQn5EdNUzsYEbMxoaqI1w4nVA+Mj4A8GaWWme/hjWt+LT9ePxQjy+lXh8s41GDrxpw7UUjZPy+/BlvpFVUmzYL/m+X/I9c1YdXzz2wtPAyJiRdtzMqHHUSz+iahsYP8WJYZLXhSWvu5xIx6ZrP6L5BSP7zLFBf8VA9xThmScaZ3T24VkmA9s9d2Lqz5yN9uJpB9m/lna0+Evm6q4H/QoqrHg6jPyZFF58IV7QQs/xPcXLGkdx9uA4tFFUdg4HUWawnluLu3E5TeXU3AEsa6PwU8cwPo0ysEZOJ4Y3/4WF+jzEb9Kfq6U/Cw+P5PECCRi2W2Oh4z5Ml700yl7e6azn0MZujKovuTgmjMbZVA65Lz2ulx5z2dF2lcZOL3yIkn5NkH49kqTjtiIeBOT6Imbb8Ntnjauq+XClfQyntf6Cbwsj+FtiItzi1xnfOjBksoXFIpNxt4OJQ1paYXqp0d7Ch5V5FL69FEbB/sIz5WX+xVdfHgvg9/omhkdbUVF8ds2WMLqLF9VZ7UYO8ZTbVRz4u4rGTYv9qK7pOd/vRoU9eq4/5oJFODwlRwhaj2gGan3DX5KLdyUX54rv2sV3Zz5WGOwagk/yJLpmGGcjzVwcn4qCXYw828iOzyOEI/J6MWuIjv4iXuQ+ouP1bdKjG6Qfb9nwSHqyp/TkRMnn9JLPh+eLz6cEsKO6xnNz/XhzR7hpcAAHC0Vw34kkzEgzUZ8nDau/qHxWOYD1R3V8sNmD5cJDH4SHSjRRuXeQ5FMm+fjzILb9/HmxW8Noc0FllzkBGDuZaW8m/tJB46BWftRYbeGcZ8L3LRX+1SOM3xYrXBkXwr62Fh5pn4wBhzVW2CF5InNz5nkY647o+WCrCx/bG1m7tR1/RBgZTLSjywHhtX0u/FnbyPeT7XidKr/P68Jy6dXP0qtZ5xr4yO5Abcn9KaoPl+X1T5fXn/G4kW/W21D5t3RcsfITzAfFj/YE8GZWBJt4EvElLYKVcyeiwFAz/ymcCjWDgaE3Tvx+TqW6KICso/R8nEvmrI/K3FWCmLE1gtq/iehaMh0b7fqELdsVXj0XQva80bz77zdMLBLFDEfisE289Lx4aYccJra+bcX7A0bW3WvD1BOyP2s96Fw6mk22fcOvQzWeKCweeUXhyckhlD1m4eZNP3DwmMp9GwPYXMPEq7OtgDcDZ0W/RtI54diFLixRNH4P+bGjqsrzC4M4VzWGixZ+QXW5n/OHSM6VUVlncxDLhAs/Cxd6dypMd0r8upKJ+5dbEfVex/g6HgyVffxV9jEo12nI6Yb2VHi/gw+leincVUO85IiRTbfaUK2WcNy0MCqpkq9BL0ppMkf+IKa+0hjRzIcG0REc/S0J13U6jvd48WdlHd8t9SKXV3gt2odz28xcdD4FOi2SAb/sczMdf+3nRRfx8Ibi4f3n6VnO6kKlOeJDTvEd6bXZwq9ZmojfDQojwwAdX5eV3CmssOPRMNa2N/BxayecH3TMUtMDdJW/38CPMsIpW4RTjndWuL5xGJs763i1sRc5asbw3owveH3HxMaD01DynY6763lQLL2Bhz44MUTyrojkXVvJ805nQ+gofNhC+DDn13S8U+kjckVLn3z7hpqFTJxx0ooP4lW1xKsaVtNz1AI3fs+vMfTz+0OTZV4KulAmt8otV4NY3ygzHwx9hgeldMJRXpS9rufmcS48e5SRHbq/RHCIkYYidhT+bOTRajakL6Hy454g/k5VuTlvAEeocfvvflzLrePkK17sbW7gkd5OvBMvbtA7jJnbdAye9yCyq/hcgyDajpLXnyuM9pHp2TbhA6Z2j6SpTgI6JESxdcnv2PlB+qxmGi4eN3POuhQce6Tjpu4eOBbH8Jfvn3HgP5UHOgZwqm4kV4+XzxPOb7XLj8fPI9mjdTxyNtF4c5DkWAWN89b6kZZDYb5bYezoJ3lSQfq+ioHvFzsx9oaO2cZ40OCknqNXu9BjZjRr+b+iv3h7RfH2Us/13NnGhXvjzOyfKRXl/o7ghg2JGLDcwgoffmD0GSNzLLPhw9EI1tqciPZzdGzn9KDNUJWdCgeh32ui+0gaEjJoLPXaj8z9DHxW0fn/v06pSp2fPwfdj2K7dTx43IOxq0zM+iIN/4mPdGobxqRteqa74MK6RZIb8Q4Mjo9ksdLx6LZJzvV6CE8fG9mhqw2RqxX+eBbC0cqx3Lz0HXKVsvDmjmSs2Kbw3fkQihY18+ChVNS8aubMSSlwC1dnEq4eNVtlLrd4XwYd8732Yodw6jnh1KcfDOxQ04F/z8q8LvbhxgyFY5Qw8gyQHCkbRq/yOmK9F9OzG6m/a0f6Lzp+quLB/Kpmpi5MxaSodEwf9wm3+8RyeOV3+Fe4ceZUL+5vVDngRgCfZB/Z1AebzGmOE1ZUUkxcFbbiv0HCLyXCyPfz63aknz8lC3cXDODMTNk3XwhNzqoctCSAPxqoDI8MIka87GsPebwcRuKWHV1fq2zcRDx4neTVAx/SvYrgp2aJyB1r5LV3drRfpGe7BBd+/0OjEvQhu3jrffHidCtVfnkVwIp+woEVvNCmqQwagij/q4HrjzsR3zmapRt/w0OHnn2yuJBT8v5Wkl3ue4zc98+Yv0ll2rUADPki6LyQhKcNVXYcLp51SnJhZQiFZhl4yutA8dZ67u/ihjuf+N95K5K+mljstzSs3K/j2/0ebCyt8dZWP6os1bjsiw+hQSbqSlhR+0Q0p679igKSl2e2BNFrZAYyzxsE5foMTeX6ouS+xIVxvYCB4884EV1S4fddYUyEjulmebH8gVxXbw/OCg8sKefFlm8Kr1UQr8museYdP662tHBK92RMKhjFqNNxyHzAxJf70lBYfPjvfGEM/Plz3E86UVDy49Q/fqyeofGZ4odtenrm1j5gVDiWedS3aOfQ2C6rD+3/iGCbYCLKJxu5sZANr8ca2CSLE0fLKty2MYyhaQqL5Amhn3Dbb118WLHSwvcvf6A0jdz+hx1jNmjMftuHjh2j2aLFN8z6JLxe3YYFPcSzavqhNZc86eNFfMEIljmVhBEy//lk/meXV+hZF0Yh4fpTwvV/pRl4N7cDlbrGcmWDd+ghXFxTuPiheF7vjyE0/KbniIouFJHfHxPvayIcMWSLeHZYeEkJIC4xiuWKfUdu2evrste1WkRxWq84+GQfY2Qf82Qx8cp/VlwQDp433InKm0xcdj0NfdaprPwwgHrPpGfaeVCnvcJJbcIokabnvjwu5NKLV7glj9pH8XzrOCwSz0oQz2qfS2Gb62EUO6rw0JYQDHsUOo+F8CZdDJt8/IJJXzVGV/Kh8c/3x90UxgCqLP9HEE+aKuzaP4xtxTVe2OvHkrtZGDfwCVZ0MvFdUysiHBb+yPoDmyuKB68K4pRL9i1DGjZulzk+Z0Pz8cLTGe3Y8FTOq70DB/IYufeSHeM2q8x0NYBb0jMjq4eQt0IEL61NQq2gxukGH4w9LHTUSsa7OkbWm2hH6lk9Cyx24XZljSOX+vHuNyPr/2lHr0gTkWBF/gqRvLBW8rpwDFsd+4Lp0rt66d3Bl6JYYsZ37IDKszODqHXDyOljbGi9QWXX2wE8+kXHXg+8aDPHyE5OGzZ80PN2TRc2yPXelustNMzAk4WcuBpl4uTvVhxvoHHDSPEkvUaPy49pDyNo7JWIl/F6tijlQp5SCq/skPMaEcsLed/hqjz/xOnfEY5QqEsM4+kyAzt+dOCWz8CREQ7sHpmFf+f+D4NOmVhqZRqyfFb5X7UA2uw2sMtxB2rJ/k2T/duaS7z7WiqGLtex8AcP4iRHy0qOVr4Ryz/HvMWeAQYeK+OEwxPBrDGJODjDwH2KExOfCfe0C6GIeNGxVBu6f49ivfLfkV728ZPs4/xkE1MLpmHufI2OFB/2iA8dL2dF915m1qmRiqv9zJxYIRUf8oofXvTDuFpH5zMPfhsoHFs6jMEXTSw5Kw3375rYf2Aa8gqvXBr08+e6x3BM5y94s1LPRi9dyJ/Nwov3k5FisbDgj2S8TTKzUZEUPDTJ+ad5UbuD+FCrII7lFT646EZI/FBf5jOyPIzm015fsduh8kSWADo8U9m2XQCLnXrGZXJh00cjb8KGZ8Jv7YTfhlQxs9iSVJgva0yb5oNzqo5ZjV4se6Xjl6ZybgaV5RxB3P8Ww74VPuOQ5NNuyafW4uXdxMtPJum5qogLjk8G/lLdgZMbdFx7y4N81WT+F/jxKlJl83jJ71km/uVNw7zDOlq3e1A/wsDRSU7cMBo5zmZHhUQj1xW34XpzIyf2tqNZR439W/gxVrg266YA4jeaWfpGCiZuUZnucgB1VAMnBZ04Jfu7yiU+3E7lX+2C2D/EwANFnPBXVxkxN4gGMQpHfQmj+F7h4yMB5Git8E6XMHJVlRxZ5MWV0+k5ffl7tC1gYsezVjBXJGdeT0D72wpbDxPumBPNHU7J+7xmnv03FTeFb8bs9ONjooUs9gOJ8zSWSPMhXjywrHjg4eIW7tibjIZWC0fl+oG6DyRv+oTw0Kiwpy2MctJXm6SvCg5TeLaQeGG6aJ759A1P7hjZdbANzp0as5zyoa14XYfoEK7ONnOyS+67cHsv4fYafQ2cW8mJM7VjuXzyO1xvZuK4vlZYjspcbklDs+vCjeM8eF4jHVvP/oSB4vulxfd/9el4PNKDMfXNzDEmFUlTDCxqduJNWx0bt/fiQE6NB2740WCS8EW05KFB4TlHGEeKmbl9fyreNc7C+oP/Q6YNCl/fDmGpeNqXpz9wfb3K8fcC2Jle4en3YSh/GqjKObBOBGdNSsK3NjpW7OhFN3c61o/9CE9VI2MW2tFA5mJMQhAFm4oXD/DixTMzW7VNQTc5lwZyLpff6zm9tgs5whG8oyZicjmVkRvEp9Is1PL8wICf7/vWyY11si8PZV8eXtGz92QX9jWN5OH+CZg0LIrRheJQU/xihteL3t0VVq8TRrCARsMZP176zWxpTsHGIxrvbPWh9hA9pxZ14+z2SC49F49AC5WmnkF8Ff6rKvxX6LzKk/MCCK6Mounld3jE72IrB9Fd/Lau+O2cRD3dxVzoLb5eQ3z95/vq1RoTwuIcKuNuBZGhrvDfhDByyl7clr0Y+0hhth4h9L5uYPVxDkwcF8P0mb8grpHGckP9KC09u7OGDXfEx4aJj6VEGlgw3okq7WO5rM07TI1UaEkIY+kUI7+a7VhYzMCU/U7xO5mrhWHMO6jQvjuEcZkt/OVZMvr7VFaIDOCK+OdU8c9Wl0zsNiMNs9ea6XmUghr7dJxzyIP8yyN48UMi7qkq+weCWBmn8k2ZAG4PMnJYCTvqH5W93eLA5HzSfxf8MLc2MaWL9O4aHWc/8aD0WYU7loTwo4WORXp6MVP4zhfwYcHICKblTkLHQRnZsuQrVGpk5IphdhT6+f+yVeTx2quMbhNEpKIwKRzGx3/0xHwXhgovFhFePKGZud4n3N/UxE0DrP97H8TcZi8az1Y4zB3CvN802v/049E+C7sf/IF1J/Lw8ZpLuJBs4JyCDtjW6ZjrgQcZl+v56oMLBvFsl3h20GymPiUVTpnfLDK/B7ur3Fs7iKq3NC4c4cPKBgrfjAxjzhgdXb940TYyEzslvEDxDxHcXysRTYXzBp11oqs5hg1TvyB4W6VhqJyz+PY08e2Si8WL40KoP9DAcaWdWBRnYkKZNMx6a6a/fgreH9Sz7m4XNr8XDqwdQpalUXz+5TsGjtWzTFY36il6jg+7sbSjwq/Nw9j0Q/z4V/HtCjruWuvFphcKr7cMYWFW8Y5HYfy7QOMfP3w4lNPMXTdSsaW7nldru4U3NOENP7KdNPLBahtiVyn8+CKEaXkyUX/5BYasFv59lob2khPtJCf+6qvy3m8yt3Vi+XbSO0z7bqC+vAPVQmYu0KXgbS+Njar7wdwGzrrqRJ4uEbzSKAm9JTeqOsPIVDYdX2z6hHprJM//s6FyVj2XPXYjRnLhm+RCtdoKF00Jo+QEPXfHurGqoIUvTiUjfpiJZQtaUXOVkb+/sCELIvl8VgLe/L/3UzpTxcJli5PhLSE+ttuN+Z0U2pqGcXeGiUMVK6yy/7ll/8eGDMymc6CsKYKbrUnIOlnlk8ggVr8RXm7ow4TnJsa2SUPSdwuLlpOczhjJCa8SMFt8yCM+1OwPI/sHbTDKHDhkDvrK41dSrVBOGxhe4UDvN5InjUJYtySCT74mosKfCte+CeFMOiOXf7Sjdzkdq27wwlwsmqkHvqHRnmgOO/YVXyUvKg0MYugHhb/WCiFkk3nJHsDkbDpG3/di1y6Fp06EsEH6/W7LIEKLVRq/B1BUPOKIeMTlM5KLS124JucwQc7hSQ0Du852op3sW8deXnSZYGGT2GR8S6/jb++9KFZQ5cHTQZQ4G8E9SxIxrLOJBZtYccujckRMAMZ2OtrbeVHzkMKZu0LYIrl6paMbHYTbWo0SbpM59sgcJ0kOFJMcuPwyltOav8X3p5In7QN46tazU3oXKu1RufxYAH37RLFqlTi8SFXZOm8AFQsqXHM6jGzCx/eEj4d/VFmQAewZqeOx3F48+Udj1/k+hOvqqIz3IjHCxOKJVkyso2fsJLdwmvy9Ml7M7Wims3kqdsvcn1zjxbXhshf53ch6wsDHax0YLZyY/ZMD7Q6a2WFPCv4QHgi3DeJ7b+GXqkEsFB5NEh5VLFEMJ8fhoiGKcxxxuL9GPPhJAFcrRXHS8jiYxVtS1otntVRYukcYq+oa+GK8E5PPRTJ6UTyy79HxwTEPHs5Oz76u99gjn3dMPm/QAI2ly/qxYGIEU2KSsKubjqfrerFxnok309IQsyKCX94lovA7hcfrhZA+v3j6P17M36in9YYLTqvKLDkDqCz8tcLkxdRhRpoL2aGT++1b5sIK4ZP3Z6ywrxYfeyq+HqExZ6IfKWONLJjFjpniU77ZPjQpaODQ007E1bGw7MRkjO9jZMYqdqg1TAzMseKFzFtrmbevTzVWae/DjSIxHH/4Cxb/qTHutQ/v0ltY730yHqwQjngr/FfSwvI7k+G+rGPsVA+C5w00znVAn8lA7wsnrKX0zL3djYnCQemsPnyS5+cpJ2aPNNOdOxVKCRPVvbKXv5mZ60+5fzX0PD5b8ihTLK8+f4euGxU2uhlCjkka70X7scyTkZ9jXiKb0cJH9mTYq6ZnjoUf0MQk12V1wvhdpbN8AMPOmVhoURrqJRk4oagDrbNGs8vjbyhaRcejS7yo+lzHxW08WPbIxM/d09BmjXjZfyFUMpq4wm7FgirCS4utUDsZGWxmR/PnGvu08UH9rGeomguj6xqZe4IdRTObefhZKv6S131fXvfg//Qs3smF7SVV/rMziEvRRs78asfxrOLXj+yoVUNyf44fuTPrePWZF0WMGo/Y/dgxMIL/lErCpqMqb24O4IPwa62AEw2EP0Zv8yHHCxPvtErDggvS47MdeOdSWS9DAPPSWWj9mIwKlfVcu8yNytOzcIXuP9QbFM3xJb9hRkc9tRZubPrbyBt/2ZCpsYmvhlhh+6AyZ80AZixXqf8QQDeDmQ2cqWBrlTO7BNHuagTbT0rE+5XRrPPyK5o119ivjx/HhkdzY/5vaG7UsY/dix6zI1jLnYiB53Qss8iDLXERvFomEZZtRiZfsOHpXeHUgSEkTTaxaIQVuZLEY4va8Gp8BFtkSMLIl0bma2HD4AcRLNZHnu+0jnWWezBiqIV5CyejkvTHn9IfleS6V86RXH1rZLUGNpiuSJ9P8cHeQ8fsNb0Y31tjxp9fr3bayKHLbaggebdG8i6HcNBd4aD72/Tsd96F8DSNqt6P6q11XNDFi787WLixZTI62Y1sns2G55cUtpseQvNRBvbL5cT37iaWry3zOUfPvA4XzI0imTYsAcfyp+eWcx+wKyGSJ0vEI3ZXBN+fTMSaMtIjm4P4NjM9K/neo1cXTXrAjyeXdOw23YP7nTUOaOxHzxl61lTdKNZAxwOjvCj0zMJT7X4gU04jX9y0I8s+lU8PBTBnpY6Ol+JDDYQPRjkxf19Gph18iefvdGxd14NzHVUuah4EWyic1SuMtz2i2LhmHD5c0lh7ug+jxM9yO4NYdkHh59khFIsVjn/nRx2bjhNzeDBPesj+lxcJwqWlfW58kHypLX63oaPMW3MrCufW8++rbhx5qef25i7Mskje//CjnjzueHnc85WjOW/pN2i9LQxWS0bP3UbyuA2dXdLHGYXrPmtMrepDPcnJcROcaH82mu2WfEWWsQqfZhXPqiB8sjaMZUeM/LzVhlEjFebOE8Ya4YSn4hfL/zLw0x0H6t4wcPIYB6o80ri4hw/DZb8K7grCusnA3Nck11eobP8uIFxvYEwhJxavF5+950LW3nrx///77+nKP37sqqfy9Nig+LOO56VPPx3SWGOnD69eG9misQ0nBkRwXZkkeNuZGdM2FZX6/Xy/RCteDZb5KGZHi5169jktnizne13Ot3JNI5fNsCP8QqXWUvrslPjtSg+q9FO4tGL4//D0FmBelVv/9+4OQMXC7uTYetS1bEXF4hy7u7tb9NjdouKxFePYLXaAYmGjotIxMEz3vJ81z/96r+dIzPxm732v+Ma67/0ghy4e6XY/NctTs0J9/h8t8ubbvt58bYecsm2sq1+ySEbanPUz+O4yT190umX1abG+skWjlH7qR3UVOkXiuVmdFHnk5FURu1kSBWEa1U4cRUFRpnkcpq4XlYUbl17mhU5c5UEcRkVc+VXg+Vlc8YGgSOIqy7PQi6qoCOMkq/IoTh3XS4uaLzi1H/K1NCvSzI+qOkuD0g3TIq+SPI4cP6kCv/ADvw68oPIqNy7KJK/qKgi8vIyKrHBrruCXWVbFWeLnQViGpVN7uRc5tZuHmeeEeVb7cV6xooDnKeM0daPMKwPHC4M8LlI39KM09wK7XupmXpWEeVymDt+p3TQpKyf3yiL04twhCp5bRElaxpFbxq6bx05S1HGUVkEW+EFcBpGXOHFSZWEVRGkdsuw84f5hndfc14+92oniGLqPnbpwSy+NkoyHqOIkTYI64NcqcfI09sOIC1RuHSVxwY3zPKnSqvSStCo8pw79mFAGrICnC+M8iXJu7HCjMqsrni9hoUUYlIXn51mRZXVWEJIkLvm6V7h+Gma5m/G4UVDWeZz7BQktqiSInJQFhG4Z+lWZe44XFC634WHDgjxmbuyQ26x087qs0sjxqtDNEz/K3KTiMROSnLp5EThZWQQ8tJsEflkRxDLLvbj2vMBNLOtkqvK9MnNrIl/WpZ/VYV6lLDoqophcuIlbx9yg8MM45T/WH8aRXxPSqMqcNC/dtE7jqHTKKvfqPIhKP8nrKKu4UOlTa05Ysirfd+uw9ivPc5K0rnyHcstZSEL0/LgmBDkVECdFFdUhN819yiypq4FYWaYi4lu7nkOVeb5XxVGV1HGYeKy78mJKkGw7kc/3CrdiyZbaMotch7h6rkv5pqWTO5UTF5kTZwExpEAzKsVLvbpyg6TOuFOY+lWauHGdlGkSu4XrVY5Ho3l55pe0oV94QZLkRep4RRUGRZV6fho7YUF9xAXJLFia49Y0apCmMU0YerkT+ySSENZ0oE+tUkRpQmF4RR4UDhXrJYHH/SKvJhBBnSR+EMWh53MVVsgq+am4poJor9qLU4JWJIVfEue0ymjJinBQXnVceVmYxWmZllEe8gtJyohHUFPTNKZLGUdxQVly8bhyfIcaKwhiFoAfAT1NUWVpRdPFuTVDGoVZzboiwCEn66lXUSOFV+ZhHAdpRM/VZcHvLnXqcs/Y4zPUSOCkZe7TRVnpUS+ABoutWU1Y0r5OZBfw0twN8qKgmyjbOCiyIAU/4iRygjzLHS8vHNqnKlJ+TxPaPY1cj8r1Uuq+dPmi71DYae4HGWktgpT0BJGbZL6Vt+uWAf+5RUIHeTldkzqh79SRE7txlNP/SeaGUV7GteOmZUIe3KD2/dSNrfnDKCF31LafOkEZeEBw6VdcuYoSJ3KzcAB5uUJZ0UFhFOdFkhAt/u7VND/ZcEsnqPhpNypBrMJ+IMvjqg7qLA/SICi9ys88Kr3w3CpPDK7s+0GZuomTuE5pEaIGa9+gJPYybpplJbCdF5Gfhb4bejxeVHphFDqJE1ZllpJ4fsYnhn6WhhFd6RZ1mfLgvltUZKT2+YwHnidRlecs0PUscoFPV/BkteEb6OXm1HoaZ1nqJVzXy/hcSavEgIkPSHhpALzVCUUbhHRqXYR8mQIAaHzHy+oiCYGqgP6vkgoMKFlYQM/VdWjXBS/9kAe1cuTDAEeZeI4b+AS35rEyD3xPo9jKqsgsaQHI5hC3hDS4MEQEnhCiyoEs6ZDEDR1aO4I+S54eFvOM7qzdiT4lxgUpxYTQlIQvpzASJwjJalBAljw61yf0fh4GAAwPC7TVblRVbkTfBG5JwEJYwYc+jZ08B8So7PYBQBMSKO7ogSRJ5RBmh0B54GBaOQkAR1ektVPm5Ji2LGrgKScKKQCVwOZFkhU8JWUCj/pRWFdUFK3pB56XAZoFn/bCKqzyGsh1+J9PbdOZxMj3C2Agp1kpkjLzXSej353I88l9XkL+dFDlREXNOr0UrEy8ILeSIRcgLbFI4xCABC6In6GryzPWVF7sV3FNidRuGdWJn7gFrcGKfWoRcITygEOSR3s6QC945iaEJkyziKqpLbOgre8UDs9WUsJx7RaUGoReGVigZgbuR1pTD1WSFFGasTRyW3l0opdTrVWVV/BEHBYBi6EnAAIQpOSeIWLA4yEp/SoOghhmYvURVVwFYFZFBABymiyDTQK4AgB1YYncT0BogDDIIF2H7k8rz6lAX1LkxmSNzqbLSpIBdGWGNlUEOFFTHsAag9e0Kg2Zk50wr3mAOvfh5ILI51kCvoMUMXEuweAsdKuSWvACv3IcnhkOIdU1oGfcUvnIG4RCGYUpfECVU2AB0J0YIxIbIMaLDWziEN4CvxK/TnJui4SiCdzACDqrLYD0Rmw/m6GyEE9B6NgzJkAMBEkpApE15YRKiTJg3itoVXCsgp1rW3Th+FUO9oCQFEzt+jwYys+NCGOQZJ5XJpYRhB8JTayKXcdJaboigEn9ugx46pAWLsuwBu6rwuerrNOo1fcTaj4mfIQc6KAYuCzCgUyTWL9KkEWG20UQ1QXFWZemhKCf2K1oFZ4e0eL5ZWmJovaBcXi0LByPFfp0B9I38RBaqI+kpLHqFEVGv6CL6wgccANKFFmSBHyWwvd81wtrp8jKKuS5HUPOOAMe7G+s1TFaroGjvGKtKF1qDGwtcqtBBxCCQlBWRMGEWE5pQ+bIN2AyYh0hBUtHJNA4RJZSgFlQoHYyWBlYJjnEI4NbwEw6KyiB+jzykygLA5eoUTzALvcE1GvfDarIL+AE0IO1kC+fxnIT5DcYib6PAt9EC3TmAMSEh5AkBvdEkDUXIZomKxB08DY0VMdeAYlB8VUBuBVGFEgJ2pjmStCEYQ6rRD4NF/pggo9kNf2P1UgSK5o4RJD79iM8Mt8swTAfJg1SdCCPiZ6BzklKmDgDcSiq0veDxFRlFsOwIXAEzwOCiCPw3skpKXotgA6wG2UIzwCQ6GYqyw8Bqpq/Qv9p4PvIJQ+DkqG7ES2BH6KeSyDVz2lnFEpR4JxIBjRKORp55CjDwsVs5I4TDvgSlJdpeyxX4UeRT6HhP3jwKnBNowC3blibrq1MbQBbtTEAliEFkNwcUCCSEA6/0njUDlgYIrhZM9oRRUoF0QIpK8hYE7nIUfJ4B3wHlJKnHjiMlo5QR5FFnO5DpqNczHlRRFFK+wcuPJyWCD/sGRhNX2QxSTDpBLNCGgTRo9S9kuTmwUBlgoKFqX0/A49rbCHBBXYQvHy4piLKGtBCqxUp2Q2c0CJcJXQCWONSK0h9N8EN5RSgX6eUrAGUpRuhC0wDZiyvZIE51JpVThmEqHmuE6MdaGa8ZwAP8aBRiqaAQGFhnKb5MOiz4CHpz9R1PKIDp7ksvKas0JmgEevwQSGWggemRDEC+KqoQLvAU6iZhOrMUKlgaO2kntmOELWdUnrkrIhwSSh/KJC2xQmBkahYKzacKABWou0dGpfmAkoyxHVmCfcHHC7hgYl8B5WVlSVdg84pSFeY+FllHhfZimBBQZi/464QaGTmLEO5AbgO/J9lKBz0ZYKaJ99xYByOXE5MVpReAUoTKZMvyDaIw8OOux7+IcwtipAXGo3KR2ohtLB+4AoipUIZJ0UQR9wKE4DLTrlA4WI7cLgY/5K0oaudCoDJfAivRMDgXhAxOSiIB/cqXHuZmdqhkTx+yEPWINrp9ZzyQBGD+aVDmafW4AhcVloZ33ppmOY0ERohggfIQU5AcDVYMiqGTsA3oxDpyIKiw8ygdUlugIzASgPWVh5uiPYPgQNMJbjs1hj8HCcFcgNbmLKCwIDgqfUWxQK61H5tV4yw8aGDZsPL0mxYsBxerkz8OVVuPYRkpxNDA0GIPkFzxsA5mt8JaqcG7KjWskq8mDYKYdYo4LMUG5hMUnDysVMZUlLjeB70sGcu3gk9BAjYQ8Hzfa7KXyi/mAqpEnMBPqYJpWMiHGUUZggJ2pRaRrgVAX6M+MDSNDxkh/0GdChEdERAt8WEHaNIvQD8UEkUxTAfpUldEDGeGHlVETw0JGonNevtF1GEWjbHTl0DWagy85CpDSkK0lDUaWWYRz59EuFESAmouQozEwhBWOF0Qz837QkohkblBcI6AH0qgJo0hyARfRCi/nMPwKK+jbVqs9kEyUf9ROaTCxrZPkrfeSHJNrZMbBRTmR5CAhJwyhXuzwDw3MOAgolUm4+Zp0NAB6ANYk8S2gUDBnbC5/Qc4hI8rgkP0jmlm7iQT1WzptjDzlGtHhUAFeHIcdlcy/EcM5cogwjXDouHRYKI5o8ZTh2QpIzSsqanbeaEHiwzEkaRADCwC0BJAxN9q7AkKujlEnFKD9P9OPACiQXcRB4VZgowSUsQH4THcdG+VWUEhB10Q1Q5mhjvBVamVYkURzPAvbh/kIHLmoxAs/OYRCimHwkses+3qyF1PEwMwcQumNZAYBrrgNqxqTUYzk8hcXyx9bnrmGbgD0AwPFgAY5Arxs+La7IGogNn4CqfQ9Z7xDx3TSqDj/QVK+J/psazmI5NAe4IAPLq3FCWNSJ+0QCoWMQYKaaowGSzcTXsxK/4UHowM7vuVy7ODKFNDnlUAltnhhnuAFxHNdEkLfCRZ1My3FUc2jCDzgWGQAnC57pRBs8hZDK4PAiNRCzhkYcMReJkFD2YVIKl8DH5IJNhVBmtBDbqc+s4B2ZCm9C4oEnGNeFDsCOgUcAOBDOrctCxAXKJRdRlTgDp2jpLCHZlLVzhwCovgti8hEY3hY+HTyvcdgm9g6s1HWRDVqQ5DhAkhG+iGP7mGRGqjv0fHUXWEGkgaoRMB/QQKzBPRs0RW7DR5ps4hzKuKUDH7k8r06kJog5BQ5PTG+gBRBAeEMRGWmNv/QFIoNR51rKiLEzY4kWRVQVCAx4yRR1ShK6LhUHC5tjEDAMQwdwmfCMy5ePiMT8AmqldFHoyUK4gXh64GEa6At9BLcSAGe0d5dAJKi2me/E0UWGzacQDMcVhAIYuSGizHiIKXLvmWvlRtCBoQzMhRlyMM/RXpIm1BoLAt4lknpjujalpGjoxNYC8zzJWCKrCUVWEAqKMYUfP7gel0mFIRc+mHDitsKZkUOEBXY4vp2AiMDXHiRTmmOEPIGKgsCJkZoxwgbIGgInet9FdAr7FNrHD+mZAZ+2YhQv5HeSyIYURHcqMUgMgPOMUvFFVAlwIUS+w0Wnuu3kekwN0IxhW4reIDrWQYsnApjit/NrarM7RDnlW5DxEGBBM7oXcwNciFtAEflWWyPvEN5DALqGU8v+bhmEBqWpwEsmJFvbBdQ+Fh9AmzRFOA5fKLQgKn0uiKKygPjR3RUEP1FVgWh38DW0YmtcsC0/qAD/IrDJA/9F1SeCiygo4EjKNQtZO2uPYMsOfUFYYbFQG4FeZISHqCWzrmbAFXIo4Nhiy7YAaZ1PHpbVvDk7UwAYJwmPZnJzyBg0BajSyTbjymsUmKaoRsEGH4DWCqCIFRLeCYOsAkIh5EsimNoDFXhF81x2QWVRvBF/EpuMDN7V5J2XmIoeRvtCEzciBzdQsEdnnj6FVmXmyZACoIi5dOmGCQqE4EbsRXQmwxeQ3xDKb3QLH4SDXZigBoa4QihW3jc1Uo4FMznJdpC69YwxTUaU1EJfTfKFhTw3LZgH+qyLSQY1ddcGaKuXhwE80nefGMZ669L0yglVRx8hF2j020gSmuDdqnIIDvbBvpvNtZOzREQ7olVkxOSYQbSIeR1jfEnODrnLRmkBkbPdEx5Jv3EtFv6CZkITkJCts28iNkUIpuUYRmySidTOgARI0iWBz+GjA5RCemmW4vouUy9Gb3CK2gSLs6hq/8zUcEg3voFZCU68V1EGf1ATd9C1a1gECuQiKD5ucFRCl+U1+1k3ywuYkISFJLZpl5vq2g2QqE/nomNGvkAix42FXXLq5NFFb4irJMBLZtjWM9tyiIN0A2MBwKgfN6ziIuYaZIuIdl4gItBFrgz1DW5pro21KnAai98whhDaL8wMa149xYdwkAkoBE1NNQY6owPRiIgqaOgDNIIoqc0zWU8iOiTtDsMSQA6cPT7lkP7Ur41DAcdv7GJiSByYMQCiUgm1kxLangtQLbSCOTnddXAHg5JhMscE1NQzuUA84Dp++BNV5LDQaBEVplXhhRFha2WyQHFFoplcqpGIKa9mWX2ajgdhHCSd1jiBHACc2BkVDgaLoipJwoO58ZC7OKTb7FxvngxnI85gl5YUPFQ7MAT3fGBr/RsnWfmATIRiBWsTe09kpz4xcpEVs2oCCCxEbBbfGgCCqHGSXG4apDUxQbPS+b6WcgPyR9Q7cGAMoZYVhyQrPJnJBic7hgjaJS2gxtCnPG6L5qH3SasaEJAROXiEQ0f428YKlTayB3RH+rajhOmwLi4hsBJnbJgvMU+Y8N5DKMircCQafiw/MYWn1iqoqkPN4c/AyNINNtdV0CWaFe6TARAAhByYZqNkwRG7FA7SKtbIN3sD8CHWaWUQ85C0Pi3PCyZU212KF5i8SGAlZGpmLBoqDhGaqEOfGbDZhtoTEGA+3KtyAVioyxGdmTg6XB9rAQPhxD4okCthgJ6QnMowDtU5IwB5EU2CmjLYowe+QfkBhw+tYamQr8JTFblghXnG8SGPfdl6sNQa2qMvYZKltkdnM3uZ3UDKPQlSICLYZUUc7oG0LguuR5CREGDg+Ysk2ejPKPwOxcJ+o0AEkz0k1yh5NmNjIM8AeRfwJLRwHXDQKyF7qR/AqYgi0gposEi4XBGLhFiyRz50IMwEA+SswqKiQHcB8FdqmL/lKHD7l8mUvgLfNNxhZ14BzipYAJ9D9oe2G+OYkBjSQg0wJHQSoWacQb+yi02PsdebThOb3+GlSZ7sVyMXQJxZEwg3ME7tFmZRUne35mme0nLBILL6htJPntNrAFq6NQMkTAbeNe4oQvUd0kePWZFwGQYyBNb+F0cM7FxAsVOYERg0VT1nbgLzisxR8VMYRhqImKU6eGlSWVjMVBejwrCX4YhtpNj3EW9lxgnSghPB3xpjYQ8wEgXFNvqAMIxPutr2EQ3dtU8yUas6CwFTfQgdJsUzfzgrYLn7FIyXIdH6YrAdWxQgxbCpkQzwLzBR2AV/NHckw4cZ3YRcqdElMtgbkLAwANuJwkwSctR0k2g0Egi1QCXFJWyY+wahQNTZGtLMMblrQ0eijjMoksFhP/J3tAlAfaOwC3ZA5ZQoMmeDFb/lJybVrBwRCe2KbHBwrVRNiOOBBmqAoM1ZKQfg20awtfhWgWKOmYnCOINg2BwVd1LZdRwQThHFKeuswM16yLQkKMaWMCttPo7IMEOhaHCmF6DhwBJq8sl3xzLQ/jsh2XjNzGMAONYLIZv2JaWrPdgho5iQwyIG+KnOiJj8S+LrMUWvkJUAn1QMbY7bPD1IgUNDEtnOK8YZ1eRTT1bb9D/PbjAM1g7i1rXX0L/4UtezZWJVAOHAa/INJA8UwixF0B+EX9D6K0IYhlY3WYXH4M7J0RTaOqQuvQr4B1UVhsI9vBT0jtERl0ys7e0AMzWuETmYWEY9iIi8ZEC5co4J0fEjBqfAyEXyQlmAKcinPbDvLpGQa+rZ3VvKxDJrjrzkdSC3AL6Gdpqho98q2STOrA6fMK2sAqN+vTD6ksRVCFNlUzius5EAm2wImFT4+0MVo2cmbgblygJclTvznuAPbK0gDZHBC0Eg+ucIvOHZfHHYAy9gubG5nAmA3g087eACclbbvjk+pbVcf7LP9JR/x56AcbVICNCa2Seg4lc0u6P2ywttWmY2+AUwyxl2NlQKgAZld82uF6rOtBMwb6hw6dGhs+MQEKCYKpZJVdoKBZ8jwkV7hOHZQgYqlHDHdyBoPOA9rY9TIRd5GgHPABzxCRQLIXWQO2CoiLe2gj90HdscdUasEj5ozY4VeB07sdIXn2ug4BdRixK9NxPidPEe4p9xxK6wxFgDXjWwAsxCdOZogIxGkBtWGwAVCbKCCTsFCI7wr0lIWSZ5SOzBEWphoTf0wxy7xwQLfTGLzyK1yO9iBP7fdvMpFS+LDwYIKcPMy2t5F/bv0U22smAxsftalDWoCmtFwyBxrZSMA21gKYqtiO/wR2djWhrVhBmTFA0qmoEyzGDNe5YBJhM/JaT0UO661MEkDkdgeJYXHkgynTPi4A3NMEs13B0imxhmRfKoBOvVxozFoDvBiAuE2sA6jjYnIwU8gIHPtyECEIkIuE1zzZnWB+ohtVmwqpXQzU1Zk2ug8g0WjsIyywHeQGzbwdal/kMxopByAEm8AW6CnxEnygTk5ksjm9iUiF6kPCmS1HTdJYlwgVEGJUN8QiB0osjMXoZ3fcRDnhKeg2KnmAFjnd6uWGkqqwQkfz2XzbdtFoRipLJRJZPMD37ySjVaBEOReUdgOHgzmprC7uRM0E6oH+nFtfoqjQRgApJGdDbKdoCyiDuENbl6ZWaL4EQRQb4jbwEZjL+2Mj63NtfmtA5hUFdXI48PwABk+g+ZBiNt5JNiqiu1IEQUc45I9E2a2CwcrICfssIA/sD1IxN3KpDO1Qs1TZzldF8eFqSkXxooqvlgVMQSa2mEdazGcbGqTD3rXjpHgnuIoAFMzm5SAqynVidLGWAL4aZqDorQAJYRXGZhjoYiqELWAm7LOSmzfBTtih6xwNGCtjTyrMHMcg7qa8NmWE7Lax/HZfD/FZ3tRnGWowtqOrYBd8BeiL7Z5p+2TWf1At+hpJI7ZttrHC5v79AocBvgLWOdujSWtPFZjJ8iK1C3RjiGFmmcD58vy2MX0gf+wqWsH6wxdYjudgdNObHAJ78dWKoEN0MDRlDJIsJPwFBIy8LEYXLO0hDimfUH/irjBBmmAxrVCqgNjD1O4lZ3FiSqcTA6PJ0i6CnXgJLA7C6ECSig6sg0m0mkqNDKr6IR2AgCdBJvGZqqhMicBfzGeyIWAosT2ebntm3gACVToxaYJazw7yq4ObB4XV9ZXmZ1CommxWYhZbEcCl4QOoYgTjDfsCIGkPJSdfrGdMqRTltB5SEH0FT3Oc5puh80g1dDO8hG5ukaqWHOSjcRmTHAHaGcmzfYG+LkQpEQDBbYvAvnDRjHdiCVEVZvXizKiTQ8PHOOqSRraoi5QL3ZoAIVUAX3pwP5aRVtRcDbcRWPHSZ5j3QcOaLEKU7U2bs5Q265tVdlEGZRFifEtO91mmjqyLaus5g/gIHKxBPIK2tfOpyHAgSXHDqtFvku80O6hnS6ilikjO8yKAk3ycuBwzcCEH/VYFtRsCa8kHplwuZ0T8cHQtr8QqTgCDEMZhJAvQhRVR+jsXAos4Zj9QBPZh52Elrdtc9utRAWgeYArOpMi9/DaPJud7jJUswEm+p5rgeghftxOczjYRYid5GZwcIB2sj11GMm3LRtsk5OYH6H1MwgABMUo4mBxjtwxZnFQDtYWFwVh17bFUud0HxoSI5+41FQU5LAf/sA8d8UyDDUqi3rCM1VW33bShWICDBKXpQPFRA5Xl1FLhCD308qIPccj0bgsy87S1LbzE9igjjSkxoY1kBvyHbgut91LnCZqG8azzSZola5HhcZwsM1XsZNYD1QCgjnKiyKr7FQkvBRmQQGGALdu6P7fSRnbmwIWk5wv48hZtSFOAA7a1qBv51VtXp7lzgCZI1pZTxqAxFXE9XI7RITyNByOeEzKMc/RNKQhyZFqtotiPiux7RnKik8TLCxY5fh2QBDdELJM9EWKIGFFdvAgBk5LbB9cScbjAf9E8kuMGP2V2HlVerpGBseu7VtYCcHIGPo08+oSMVJGpq3oaNfORUIiODqzp0avhBP4NndQ+wbr2Kai9J04MrHnoNHsoimCBf6GazIbCVpXUaPo9nKgEBxLaWqzOde3rQzPjrD66Ac8g00gWS/2N0G3Ib9QEyEsBovYZj7CHvNbeQnEGQOHGU1DmuLKsTMauWfncgz67dCG7TdlgS3Pg8kCF23huiYqB7ZwkTmpna9GJ8G/5lbwO97AIe/AcNZlnaUFzMyCHRwLzBb5WC6HTndccgOnmzFMY9u7BSB9jFWG3LSjoqYkkOF2wAQKoBjs6DjrpIIxLq5ZONqXZih8sohCtx0Wm53ZDIqAoDM9Owcd29F2GDeES3GpuWPTKljaRc3YIV3DVh9RjnHz7WSgZ8VoE1IHPVWUdgrOcW1XIeDhbVuSGuS5Qye0csBsYZyogShN3cyhqXLbCMqRp7ltE/gl7W5jNuKCLOYnTbiHVYKihnCooTyM7aQ4zwHfoQxjSKO0ozRoXXQZFImo4FulSSZ+mIJI7Pi7zdDyIKc5Ijv3HFZ2LLwu7RxPYjMWG4eij3zPTEpNY4XmWjOUCi6I5FclshvPmSNcShtEI1jx4nQCEqAurTwIeGLzmdpeLbBjHCQciOKR+JEEzYbyc2x8akcyUTMZmaU3UtYWJbAG6tZGkpQcMSwKs8toODsRmtkg346HARH0t82BcZ6I5YTOYAGuTTXMGxdkGiEFbUFI+DE6MjLSjHyeyfxM5tnxZQjDQWfT/azLw6OANOhL2/1G+tJ3NosCt1LumdhOqE9BoMZxJuDswPZ5xf3MpdPBmQ3i0AR2cNN+MfHv2Z4YQrIyIkwH9F6WuRGcjdWtgXhqsEB8oPJwCYWLPASCkGt29Abl79c0iG3s1DHSP7adEaS0MVxmxUIL0cLELQhtX4mqd+xEBCCRs/ows+PtSYgk5GP44YDqKuyocmJnnW0Hx05JIojgxdAMXD4gUohYba8QEFt7AyLhd2Q8wYIMShYNEmKjadrUB07tXFOdYpNdO1aOIQkcOwYKmIc2mbXhM99PBgQ9/rzCkAMY6HGHjxeYTnMuPmsE69wYCIbhSVJdDahiOBEti4sxC4FNwbDwAHnCwlDjWIK8qAmTrRx56ZpnSm07MXU9Whe/mtvpbTDTIx1ORo6wpr6JX9e2tG3Ex7IHJJ2LywTIqYvSZd3WYagpMAcHi59ybJ5fBjU6wbHjfbia1MvtWBid4VaWDOQe3t9O+oSIJuxxlaNykYA8gb2dwpWxSaFrZzkrAN+QB9lqbz8EA9tDkYmdzB2w/UYZvg1TnNQEBbVma4hs89iuAT8PHLWmxDI4O7OzPTQ6gUYLgqCpnZb0zdqWtn/mZD7l4POZqHSMIGCo0Ax0arbfTiUmOQWfcDEzSTa350MYSzO7NldK7Qhd6iHQKUtCQJuVliH0V0oAgA0gt8oCm2hDmxSFTcp8r0JhVb41LHbHyD+BQ+wkYmiGBjRA5psnsYM/np2frqvYRtrIDDu5ltnUj5STYdufy+xECh2Be81IodlBQA8jRnHzcSA0iz3b5sAVmaLBZNuLKDRFklZUTWl7G/y9tJEkLh7VaacKoboqSu2Vo8AcMt4WXLT9Fjufb+9x5IbWvh1sNhuPaA0DO49dDsh2KMeMD5/HcXrUIcULZnho4CALC3vXB81S2mHsMrH0W/KoT0DN6hV7CTLYzrTNNfCRmY08SvycvXkBKRTmszIwyI7SJHb2C7cSYcvtZZHA3hmxc5IRjoeUgLFUEl7AxYtTRKZ3oDOIGdtYAjGpveeR8eQk0bQyFWMTZ/jVdiNN0ySp7TWA5A5FZOMEmwejLjw7+20zWh/3h/9N7bwP0UV9lx62z7Y9sA9h7OP+cvw0zYNx9SNywcLpo9g2bUNDdTu34OZ2dpOUOLbDbdPoPLVtpiSwObRjr1Hh+GzKakdRPNeziS9Ql6OzPctzbecQ7OagtqE8RZaYwIvt0AeBA9IQjiXw4UDKHqshXEmdWJhKe2vH3q3iniW8YycokZ1wf8pKMeoAZkqEAW7fM99UFPZGim8HaexVhiKMQZjY/Dn4jqQaOP+CRMuTAFzNeWhDb4sQ2JYmme3S5nYezMMS+faqjA3ITWxYqJwUNZg52Hxat8JgJaivJLSpnc24IUAby7u2MQAz2ytESF94cqBo8XMpP0EUnBT3ahtgNhOp7NWa0uxvYaSdktwCOYZACOz4q22v8C2YvDbZh8tNogiNHNtMB17gPp4dXzSlZxYXzZb4AwSc2fwGf5Hge8yrRXFOwSaFraRIag99WsZwLbrCGtnmUEFpQ36TEg4uBhE14HHAT1xKNLD7k8O5tEsJMplDJJWgO4oS0rNEwBZOFRdVNuC54eQsZsUVbI05RNPWNu/OKeTcJk3UQkUoqYTYDrygzjL0FcxQ2QsWaOmBrfAcXLYdBuf/TkyiuqoS5UcB2wApqiPkEmyEN3SQJRZjFKtNtSOcSMXiCRLuPPIHnsVHNhmmlnYSxQRy5VXG5WBPndhkO0+DIrIzsHbI3Ca2JBP+MH+BIQ+NaROqPrdBAt40gqGd3DyEQ7eit2gl9D7BS8IQgCrD3A5SV1VEyVdmdcMkGTh3BUX5lhz4HOHiwvKB4YM1qOMRMC82P28v6tlHCguIHTrMre0H3l6rMDGgskufeoW9dOfYBBckQ8r75qzhUKokTnHA6LbSdgUplsxecqCfPXODoGENE9l7OH4QeraLkKQ2z6htP8lECCKZPAbo0BCijCFF6jgubBOEDrTdmsTx7TVKREgM1ibVwOYA4ru2MazDoxBqGzlCt2S4rJETkSEDVhjzhPeA/9OkgLTsuGSeZ/a+Ux1jncIoMY2Qok4TO4QE05c1MqayI3tkJ6Rnw9BexQroOHtjBPjHngEVNvUBIRH9hD+zA878fFZQKKWpYNvurOykgY3Na5skoTxD+DePsRZEwt4Dce1gPehAJYJU2FlEe1lR9BGmma8jXWxoAQEDxujyzM7qewYNuCl+pq4KpEtprwKWwJy9nAQjmVyLAeSqtrKliV0+kZd2rsZP4hiXYm+uJLgUtBNRsvMDNAWdTwNELBBechApjvm9PDRTW5rwJ1BmmdFv9naF7Wx4ju3ugwA51g3FOXCa2VZj+w+kDXUWpsQzKuw8LCRtr7T4kR3O5ZpgR5nHue39gL+ZE4X2dqcfYtdtMBkCOcglH9GboDLAXMAlLO3NwywBHCxPlqqaBwaeSHBsrw2HFjCTBOTJtR0vfsrjp+EbO2Ftp9fNbLogmCm7Khl4H9ROdNHluR3zhWMHXsmKfTtSmiNcPfCitreULMoVXiWxeQRlCXuV9tJXau+0xRlIYqe0E88OZRCBHILG6yS1a7vQSO/Sdrnr3E4igkaFpSmkI2p76wSUcpOBcT3eGTQtUnvB2t7KtjdPcViOnT3DMwaRSXuEZmavYbE+itKHHCPb1yixvqaBuS7OGN2Wxgjw0F42C2z+G+JYcDW1i5i39zVs1ltbuQU2vUpT24VFZWIOQbfA9qFMKhvN2rt3iDsHFRymEC4QZNuJIGkBH0Vma2vbfg4iVDN4Qin6A+97u4YMoePbAV87Im/+PcpzOx9q51xce8XYLex9HhPoSR25RpvId1DTpXbsBJ2hNo8ap4iLOCj/74g2/9nOkGvnBvnF5qd2vhgI8e3sWWDH/X3bobKDdElk6w3sxBI4YG8Lww+2bY7DqmquF1FRTjxw8i+uB0QDqSNtdJG9rQWWApQANfxmO+A2+ge6a3MdXgRO2/mfFImRlIFXF0CPvQGUx1waMcdXWDG1bBtzlLePU7dS53J1YOxRwORYJzcc2Ou0V6HdqsbhBLUdk4BniClK34GYbDAKv9jePDVi7x7aay72froDgngDL7Ih1BBeU8tIt53eLJtf6+ttCzvkjIdyXf6D2XKJX2ncOU3+e8MQHT/7V3ns1EhfW7lZHvp7kH7wzz/kkB5Pd4m7ZJcmT89YtktkhqdXb9IlyWO+Nr/eIf0/eert1SWbDPX0nsnd8vFmrl54a684raX2LzFdgmtD7VrYIp8+mOpZH8+X4Y2uPrxCj7S95eli13fJZTv7Gp7RKRd0BFrXbbL9KF8vPqBT3jja0ds27Zdod1dbjuuVB1aL9PNXmuXEXRJd57SFsuIpg/T91abK5GVzPfDzOXJp6ak/o1sW+zjQ385rk6M+83XLMzrk+vMcXTioX5y3He2/rk9GPOfrGY93yIPbuvrRxb3S+HOiK++xQL7fo9J9j5wmN1yU6NyE6785SN+56Q/5/LBAT9umXQb/Yv9uS5f8vIyn+0zsllMvSXSVcKHccb+nv3zeJaP29PXAIzplkIb65xWt8vKtjo75o092PcHRU9brlwUfLqarXPizfH2ep0cN6pa9/9+/N3r7zEj/2LBJVtw70w8OmSebVY7eOa1fdjzd03NX6JYxy7s68cNe+WJZR0/6vF9efS/Suy5vku7DY81lkdz9VaaTj5wrl19XqNswUw691tUdF/bIepcU+kQ4S257y9U/ru+RhiMjXf2fzXL79o7+fn6/bLZOqHc+3SpvjQ31xg9b5LyOShev/5aTL3J09bRfJh/v6AHr98vKb/n6zg0dcvc0R3/cvE+WOSjRiSMWyh6bRHrMXc0yeaSv+x/dKVPOdXW3Ib3yydmenj+0W06+ydO1ZnTJzRs7+vfd/bJev6OPun3y9qq+Xv9ap4yy/z/ZL3KfMxw9cFi/PLyBp+Mf6JZh77j6xTU90lVWWkyfJlvM8vSmDYj7x7nude5saSAuayzfLfd87ug3p/ZJ078cXXbffplwcKqn7NQgt++X6297zZExWzs66ap+2WAtRx94rl/O/bPQxXSmfNkQ6AmrtsngB139/ZMeGXSkp39u2S0vj3b0/q4+GT3X0b61+2SCkLfR/fLs8Zk+tf48ubBytZzWK+e/GOuQBxtlz9jRoxf1y5FfuLrVCT1y7SWONgT9st8xmY7aZJ5c2pFoUC+QG/cZojMOnCIrTHD0/VP65MbXQ515a4u8QXxvJr4PHpHoh1stlON3rnT4GdPk0WNjfWvDRdJyTabLNs6VD6+u9MKmv+Wr8bkeM3q2lC8kOvOhBfLad7nefeBs2WlSpuccM1eWv4B6Lnvl47m+nrd2h5z7oa+LX9QhDU84utpLfbLX2qke9kyD3NVV6Y/Z3zLhO1dPP7BHlrzf1+8mdMibUag3N7XKpZ/Sj2d1yHZOopf2L5SV2n19b3CHHLxrqCNOahV5NNTRb7aI7OTo6LP6xV871q5nFslN1M3MpF9+2NfVfffpla6fQy32aJGGb31d8+AOee4/mT7dMld2vs/TsyZ1yfG7BDr8tHbZ5BdH7xnZJ28O9fXm7zrlgQZXJ6zaIz/N9HWfDTuke71Is8eapdrR1Rnn9Mq325Z66MUzZLcdyON57bLUhEi/OqVJ3nrE1Rvf6pH7t411wsWLZMSCVE9fZb5Mv9HVLWby9cGOfvFbvyz+paNTjuuTW87wdNow+ni2r6sO75Afni103ydmyiZzBund6/0hQ7ZO9Pf/LJTLShdc6ZXzDgh18MhWuZ14/fp5h/yyfakjz58hy3OdT5frlgXdga6ctMk5rzu6xG198s/Dc71OqcNhgU76tF22eCPQW28B3+7L9YJJs+XZPQJ96sh2qa4PdPq8Nnlpd08fOLZbGr8NdYWDW2Sz4b7e8XCntF8Z62IdjVJ8G+isQ9rkS/r1OPr15N0dXeu4fnkcXH0NXN10z1jvPmKRnDW21KU/mi5/iqfbju6WW+mfv//RJV894enRL3XJCkMj/WBys2wzO9LLhjfJFh/keuMls+XRKZG+u0uTfPqJp+ec3SWDL3H1t7BXLv81Une3JnnijVBfvrlF/jHH0wfX65L2zzMdcupc+R0c3wEc/2tDV7cZ0ysrP+rqO2/2yFLk+RvyPGZ6oF9u2iZ7fezq4ef1yJb/8vXa/TrlKp+66Vgkny1y9cxhPbLydZG+3dAk+20Pvl7QLUefHulmKzTLre3056AuGXuwox/t1C+XjsrUP2CebEI+7/m9X64eDw+N7pBp8NCW8NAjo2N9u6tR5uzg6fBzu2W9/3j6dGuX7DUn0MPWa5On3UCf62mXZVJwtaFXBm0R6tSbWuWCealWa82XVSZ4+vYpXXLYCo5u90G/nHOdo0s19MnxbY4OX6xPDpzo6x4ndcile/jqH9Up8x8KdK33yetCXy9aqUMOmLqUjtz2K1ly3VC/fbJV3l8q0cu+WijXL+Nqw8Reuf2zRKecvkAOov92O6hH+lfM1B0/TxZ7PtLfHm2S3YYM0pOnTJWL9/E0Oqhb2jYsdfExM2SvMx09bNl+GRKD54t6ZdZlkW7gNMuWa8Z60/8WyZse9dXVLo+Lq2+M7pWrT1lcm1f9Sa5/ytX5z/fIZs8XeuujM+WHIx399z/75e8lfP3n951y4c2eFtO6pFw81pk/LZLwuEBbNmiXA/50dKT2ySsTXb33xB55uBUcWLxDLn7V0fSuPvlpsKv7/NYrI4cGevzkdjlqUKpbTG2Qdb5I9OkTFsgxR7i66da98lp7qHcMapGhn0b6/VlN8srpXG/5Xll9VqIvb7BAxp3l6XNLdctyoxydcEC/nBxkumb7PPmtwdFdVu2Tmw6IdObuzfLTU9TF813ScLarqw7tlXFZqOPmtsq3h/h62A6dEu3gaut5vbLs7r5OPK5T/os+ee+1DlkXfB4HPj/zg6fP/KtLDvza0d2P6JNVXUdf7+2Xs/ePdKk9m2XUSE8PPqZbtrsz1ot+apR3wb2rB3XIyJM8PXatbtlNAj3xynZZ+/hAnxneLvWz9u8g9Ige5urobXtl5vREN950gQwF/yaP6pXRa8Mj4/rlmvV8bXysUw6+odZd5/wpb4zw9daTOyX5na/v2CGjXwi0eyx9vVisJ/y8SL45JdRDV22VFSaF+uExLdJzp6vZjz0yySv1uK4ZMmW6ryM27ZCPM0cvmNsvR32R6j9PmC/XX0OeG3vk3jN8/XpYp1zPczes2S23b53pr1fNk98fcnTn9/vkfy9l+tiYuTKGfvmSfvlvkepHMxvk0ung3Gbw2MauLrq7V8ZMhj/3J/8dvt5bdciohZEetFKTNBw9RFffdIpsvlaitz+3UG6+OdO//54r3TdEms5uklf/4erdY3vlhyUS3ff7hfIu9XL1nX1SLkd9fbJI1lyHfD/dLSOez/WkR2fL3XdTF981yQrDXP34s16JN3C16UFw5HhX3x7eK8+f7uvjy3fK/js4utd5/fLHDonudO5CuX0d+uWpZlll91DfOq5V7lsp1q/fWSTZjEDnb9Im3cQhJQ5b7OLqLaf1ynP0+xPzwTOu63Ld3kccjd7qk8vPSbVv8QbZyw/0iM52mTjN05M375LHwaFXz++WpoM9XX6nbnky9/TF2d3y4ZjBeuHE32SLx+GrVztkB/ru/O52OeQz4ntGn+y8RKDn/NAuT8PD/4OHX1BHH768Xz5+is893ybP564+NrtXWh4odclPp0uxhKfTf+iW4R+E+sglLXKMk+mm/fPkSvRjJ/pxMb7+C1/f7ORUb1+jQVY4NtSPNmqV7p8cLfbqk7cmh3r9/i3y4WxPL16/S2bGmW7UOE9OPhlcWqNd7loT3fZ8v/w5x9ft1u2Q8avHeuVLi6TxwlRXyhrknzt6euM53TLltUBH3N4mLT+kuvS/5suINTM97X/zxN/L187DOkWuKPWq3uny/TWJ7t+4QCY8F+tpjzXKP7NMb5w3T5aanOs3B8yW5dBRn3X3yeKHlPrTjjPktNGZrtY9V0bs6OjJ56A710r1tuca5NlbHH3yzz65+qBcm3aZI7ujt469sl8Ouz7V7efNl6MuBs/jbhm1a6AHndQu2bWxNixslIuPizXdYJF8u3umhx47T/6YH+kOq6OfwYOP5rXKWT+im0bNldef9fT2J7pkg7tCffCHFqmvoF97e2SHVWM97/VF8toYdPbEPpm/YqJrjF8oP9/l6t7f98imO4NXZ7TKqU8soau//L28dUaqNwxrkMV5vl95vhOvd3TdeX3yw6+u/nvXHpnwaqCn3tkmw2ajW4b3yDm7ebrECd0ycqSrxxzTK9c9SV++0CN/l45uOb1fpt6b6rZfzZfVOn19teyQ/T4JdJ9z2uTfXqj7dbfKBs2hPrR0izzzQaRPXdwkaxyd6UubzZPTLwx1xbxVntg90leObZYN0Jdj0Zchz9PO84wd6egHR/fL6re4+upfPXLBJUtrGU4SFx3c959+WXENT99/sVu2Xos8PjdDTs7Q5/PwSyuhZ97tlHK1XGe+Oke+5PPHoZuPew3+vKNL0oZIF63WJHfcHukvvzbJ0/+I9MWxzfLy75net8Nc+WZyrIfs3yj7jfV1r486ZEP00Bj00P1Pezrh2S456qRAt16rXQ583dfdb+uQp15w9X9j7d8BSVV3ny+/+a6O6OiVLQ5Bv+0IH4NHU2c1yMrtib49eIFsupmnd97WLRf/O9D43/DMhpFudX+z/IJf3LNqk2OvS3TDhgUyfdQg3Xz/qTIR33HqdHwq+vnKRXNl1j6hbnxgqywFn38Dn/duXWp81QzZe2lfD5nUKf9G//0b/XfqEEdXmdIvO6zr6nlP9krzQa4uu0uv7DvM0VGf9cvpe0e64iHNchb4sSz4Mfe+QNf/sk02WhDp/as04c/AM3zosvDhBPjwmy8yPeL4ubLq0FjfmLxI5tFHax3aKS+eEOtD6y6S0as72v1Svwz6azGduvXPctaZgS69bLusuIKv4z/opL7RK8e2y7E8x8YjeuWclz1d8r4uOaK91K0HT5fnWP9TrP/J91196dIeyVdxdfabvbIGePcSeLf5q/TZXfPl1wWDdOTKf8hbJxDfdeHrPVI99qgG+Xr5TI/8cJ6MX+DqVSujj38JdeLuLfKfJ0NtfaFFnh2X6LhxC2Sj0wO9d4V2+frwRI+UhXLY6Ei37W6SK+NIuxub5aT9XV13z17ZZ1CiB09dKC8M8/S/n3ZLuFuoHSe0ypZ/eHr99uiyo+DjzTvlyoML7dpplixWZ/rzX/Pkr6szlaa58vebnm51Y5fc9q9Ip+7bLKe97ulKt3XJty8mevgDC6RzeK6D/zuHOgt074/a5PbvPP3twC5x/+2rs2+n3DwOfhnXKP3oDm+rXmnCD6yAHwjB/U503obb5HrfZXPkuINC3WBEq+x9iquHrYZf2s/V8/fqlYtXQ9e82i/llFJnjpgue84I9ahNWmQx6ufXg1pl5vK+bvhhpxz25RDd7rhf5aHmSD9cqknuXcfRb5/ql5WOG6TvbDBVru12tDHpkzlOoOv1tcvSz7j6zdM9sggdsCI6YOXlHX3no37J8M3zj+mUFV5O9KP7Fsj5T9Ravfyn9M2INdi4Ucbj76/C33/bEKMHGuXIIa7+c0qvPIM/Hnd1j1SL419+6pZ35md63epz5d835rrfrNlyyz6+/n0g/byWp7Of65Yt5np6yzpdshP4dTb4tdLcUN9bp0W22c3XK06gHndzdezxvXLJJ67G5/TI5Mm+Hrh/h3z6bqjn2v9f2Lmp7rL2fLnxgUDnfdomK9aFvvfXLDl7K0eXvKZfHrs10Lf/aJP/cJ1WrvPQxuiJuxtk2ysdvayzT1YZm+obH82XYZ2hflq2yF2X+fq90yl7X7OEHrboe3lki0zfunGejFkP/HysV5Z+zdGv7uiTly4epA9EU+X8VysdctffMnxlVx95u1ce3jjQd+9ulzfnxHrLuo1yXlrp4g345vPwzYMWibNNqe7lM+SgQeDN1E45c21Xhz2DHuuKtCdrkm3eTHT0TQvkwocdrd/tk7kTUl3n5PnSOaHU+uTpMgy/9PkGXfIxeHo+eKonh3rVGq3y9X98PbIF39cU6chlmuQ1cOHOT/tl6SUc/fqHflnuSPz3lujyj3zd6PwO2ekXfNTILrmfupt46RwZ0hHplLpJxt/o6+iZHXLoWYnuuPRC+fOGQLeZ3SZD2zP9YfBcueQt/PcNHbL2VaE+194ic/DP6+Kffzgs1v23WSTfr+Hqfi/0SpAF2jGvXZp3jnTJ05vljLN9XX7JTjnwEV93fbtDvuG5jvi+XxrQc6uh55a+J9Avvm2Tu26N9Yc/GqW+x9G/vu2TZ74L9NkD2yT5utKmI/4WXcrVy74G3x5ydb33e6QhTHXVlgY5etlIN5/QLO9/5erlR/aIx+f6+FzyhK+LXu6Q3Zb29MRJ3fIVPucofE4yGH37e6ecOCzUtT9rlcXBzZ/BzXWvxh80L5DWswNdami77Lv5UB11y2RZYl9ffxzVKd8dHurB0ior7RfoB3u34ydi/ESjLDrP1RUH9co9rzj63d194j+Xatfj8+W9twK9/Po2OeXcUNdYrFW8m9DTM1hno6c/Ld8lr17q6b1et9xz3yD9/ss/ZN1bQ31qaovseiQ8sWW3/DQ20n0+bJJ/nRXrgUsvkvcudvSKqF8e/CzTj0+fK689H+hdj7TJ1IPwg7v0yxerBnrSa+1y4+OOznmlT9ZBN49DN+++vqvHP9oruxxW6unbzpDf3nN158t7ZNc9XD35qF65+TR8/ortstuDpZ7yyXTZenF08U/oiZuJ87QeObANX7lYh7yEHn4APXzViui/8f0yfWKom57YIivg6z68sleO/MrXrY7qkHXAmafSPvG+sn8PqUOOeDRQebNNPh3r6Jkf9slKB9AnI9tlY/j9Pvj9JvBi5o/dMvfxUIe/0iIv/OXrQ1t1yA07wxun98pVwyPt/G+zHHrKEN1+tSny1XjyOLpHFn6a6KpnLpC7ksH644LfJYt8ndPcKVe86KrzYI+scqqrb67SK8OHl/rowzOka5yr5bgembkKz/1Gq3x+jqdnLN4tK++f6zt7zpFyXqDT1mqTdbYjbhf1Sr6fo/P2xtdHkYbNzTLkiVz/eHm2fHZ3rWd996f8kaIH5zdIw9eBrnZ4mxyxWqzbvLpI+tDvHvr9ncLR62f2y3rbD9InLpgqX6wR6okvtMr9+InPP8UPrpzqhLcb5KozIu0Y1iwH4Of3xM+vtR36/CL4aMNYO8cskrvOc/TnQf3SPTrRrGuBXJURj7nN8vXavh41rlO23Ageu69bfn/F1R3v7pFd4Kkz92uWdzfHJ97cKWvSl8+/2CuLpVynoV+ebHX05cX75Lk2dO6QPvllq1RHXt0gF6yR66AX58g+F6I/8k75gJ+7lH7e8cZIz5uF3vss1lfPaJQXJwb60IltMgNfsnlHu3TtWmp+4gxpWqfUZZ+aIQ+DE+Ov75Bff8511z1myzrozufQnRtt7uk9t4D7OwR683ntcvg3vm57WIcs7Yf6dWer3L6cq1M+7pVfFit0959nSffLteb3/ikroGc/2K1HHvnC0beP75Pltg3140taZZ3rYn2yAby4vdZpU/6UOeDIet+0ySYbZXrnffPkwa8c/fTIPulBT6Vfdsrpnzq60lno4ymu7rtLj6x5iaMvhv3yn1Hoiv1b5bTrY11hXqPct5GvX93bKbMnpTr86Ply/+BQJ/zWKgd3uLpL3SMT0bcnVgvk23GOHjKuTw4+ONEROy+UNRPytrBTPv6GeB3aJL27ORofD859EOvqFzfKVR962nVhl/y5I3Vy9iIZcbSvp23WKUfsnagcslAaf+D5/tUn08CTLcGTHL0wZ+teWSZ0dEJrv8gt+KO/GuWk7Rxd+6J+GW7/ju9D6Ax4rndqm7zUA19GPbLZhYXekc+ShfMqXXnNv2Wr06mPFbrloAscHVH2y8b/xTe/1yH+GZl2LTdPjjzGVdmkV8Ze4elHPV0yvgV+H9ool7UX6g2eKQu2o88uXCj9K3vqv9Utn10f6Tlzm+RM/OdyO3fLbTdkOnX2XDmoN9RdwxbZvz3UfQa3yOLHOfrLBv1y3b34jK965NiNQt3ovlbZBT106jPE44VYO8Y2yiG/1brzzn/KhaGnRWu3vFqmevf0BnlxgadjV+6S1qdDXerZFln1BlffmEMd7O7oW8f2ywrr+vrxk52yzN6hTjqkVfZPQt17ITgLv3TDe1dtic67tlW2nRLrxSMaZdypgT6/crtMud/TEZ93yY0LfZ27YodcsWaqvc83yB6b+XrMbZ0y/JRAH1mtXU4c5Or6U3tlDPr2q3u75OYnE53+wgJ5Yryjr4zuk68PTfXo7Rpk6mRXt9+/R85/Jdf6HnzmcjzXJ52SbxbqvNvwReN9Po8fv7PQC3+aKWtIqK9c2SqPXIM+aWyTF08C79bslG9WRPeN75Ut73H1um/xSR2ODq775D7P0S+7WM9hvrZt0ynnnOzqEmtQFzPhzY3I+4mFPrj2LLnmE19bzu6Qc/xMh3bOkyHfOfrbQX2iK4Y6+j10Arr7iDvny4KDQ111J3CI+p1A/b74//TMzHs93firLrlpzVBnP98qP24S67/uXCSLuO5yXHcZdN4kdN4fl3i6Q9Atuz4d6MnPtcne7y2uh17xo7Qdgd7Zapo892eoj0uL7PK8o6c/Cg+M8rXrgE45YP1c93p0jlyycqjh262yws6efngGPu++UqdMmi7HXu/rxvM65KL1Bmn++FT5bPlEz/pwoTzyK7p01z45YBVHR77ZLyOeD/W0R1tk3n9TXXv8fDl/j0gHH9Usa/wr1Vf2a5CVzon0nSWapTon0FlLtMtqdYDvRI/8F701Hj/R4uqXQ3tk7pmerrtMt1wbu7qwEV0+wdNlTu6SyXf6etCPHfL1IE+P+qNbdif/x+FvJtWhHv1nq8htrv7n9x75ZPdQzz62VZbAJ/7c2Svu/Fh718Bv/uLr3iM75MUHYx37caN0wpuD4M0xjYV+tfxMGXOPp1980yXfUc8Hze6Rp+C151edIh83lHreqtNF3wj1ilta5O3fIr1+pyZZFt76/JkeeXcvT688tFvO0VCHXoHuvznWjf9ulKbc02Gzu+VY9MlG2ip3/+joj/v0yTlPuLr4yz2y/EmxfrzWIgkn59pxwGx57kZPn5zZJdtumepl1zVI4/KurvRhrxy7rqMbP9kvv2/o6y5jOqX93kDrr9rkl0GBjvyjXTZ4P9YHLmuUYf/jeR7ukT/w4zvhx3+d5unum3fJZo+5esfrPfLv3zLdd+e58tZtnt74G/5qqVDHf9UqN+0Z6vTDW+WnrWsdddVfcs2MRFs3XiCXb+Nr/6Wd6KxYfxq1SEAmHTy6U/7XBL4tAx4NcvTKP/r/f58x6lhHD9yoXzbbJdfbTpsjR6LPtkSfbb6Xq7ce1it7OoP0qP6p0ntxqFEEbg8KdcTUVunt9DQquqR8JtbZT4M/V4A/PY0yaSdPjzmzW16c5epD/4AX7h2qL339nfz+Xag7Htgi57WkOmTJ+bJxc6L3LL1ARu4a67EnLZI9T6v06JWmyVLc91vue/8DkU76tEl2bqHul+yTQVGpU5tnSM8rnsZ3g9OvDdH+O36VL6Y7etKmffL+EoFe9n27nE2/LK0t8vBgV9/9rVcGbx7olFva5cNlAr1kYru8uKSvY7/plKPpyy3+B++tVurTr86QH65aQvdt+162ezbSi57kviMTPfOYhXLIQZHuMKJZrr7a1eamHrnu9lwbf50to3sS7YkWyAmXBDo8bJcx5HPCpY3yMPru3SEdcvi4oarPfCd7zyz10I2my+0bODrlAeKMb77js26598JAv8na5a8VHN3m/X75YHyil49eIPPXSnTNZxfK+kN9fXRyp6x9fqbP1vPk+Lcj/cd1TbJf+2D44DfpMF/zTp+ch39dAv96+iRXlz8GPp3v6qmr98j7+L7Lju+Uth1cHXpur5zrOLpYf79ccWqpziozZNB1kf7d0CQLFlKvK/bIc8c7+sT6/fI1uuKom7tljb9zfXnL2XITvmEWvmHM67V+eeufMnFMqqdMnC/lN67OPKxHHlgp1c/faZAXlPq6vFv2HJ3p0d1z5Y57Uv31m/kyHRz+Jzis6ODL0MGz3gx1wxtbxMevdN6DH78r0OyHNrkPHfwVOniv02I9ZKVFcig/t917vXLlrvDPif0yqqROZ/TLol7WGfRIPjbWho8a5fHXA3391jY54VP45Sz6ek9frzm8U77+MdQj9mmR46YG+o9t2+ToRb5utlyHHIKP2PmGNmk+lTpZuVXOP9HVQevA12+4+snNrOcLTyce3yVv7JnprYfPkx+vCPRfPW0yvCvQR/M2WeZ8RyfWxGlRokcOWyCfbZbrGbfOkWd2cPTZc/vlRifSmX3N8v2uke53YrM8koG3c/vlJvzDrE17ZN/bHN3n9z7p2Bwcu5nPtad64KD5clAneqjskVOXDnW1Sa3SfrKng1fvlh89+rvrLxn7v0A/+m+bPDAn1c/WnS8f/eTp+Xt1ybgefF3UJtdvGOrCMa1y6f2xxhMaZc6sQv/xj5lyyQH0y8humdNIfJbvkZs/cXTa2X0y791a17rqT/lZHd3jcvj/BHh2vU4ZP9PXqzbskDv39/SXPbplt3/ketJDc2TxzwL9+Yw2WW5dfPOT3bLwbkdX+Q7exjcdiW86kT5ca7M+OfbDSDe6qEm6lim1nDhD9kGnHXIRz/k+z3lpmzSNC3TYuDb5ZIav52/cIQe+H+oel7XIwvMCXXkQ/YsOX/qZTtmAOhpLHb20Jbh/7SJZ/59D9JEbpkj7vp4utk+3vLy5q2Nu7pUfG30dtXyHjIvRxY3dMpz8Pkx+p9wU667TG+XiXvRq2CGL94f6i9tCHTjUQZ9sHbl6bTP1N7TUePIMmb8V/vnqXmn7MNElLlwgq7/p6cs3dcmOh3p61vbdMnZ/9O8e/TLjSl836eiQJV9w9buHeuT7nwo9YK+Z8u5RgV69ebuc2o6+G7xAtpgW6C1b4Pc39vT7e7rlgo0KHXTfLDl3p0KHnDVL7jks0K+3wSdOSvTkYxbII0M9ffu7bnnn2lCvXdAiY571dcITHbL7FoEef1O7TFzd11Ne6pR1h0Y6bnKzDJ4T6O/rtclKDYm+t9oCefXNSO+5sUka2x38c5+MfDzVY1+dL3fO9vTX9bvk3jO53zLtUqKTZ6CTl8IHfTu4X4ac7envS3bLY5Miff3oJrlkWq7RFrNlBL7+FHx9y5aOLn1tv2zTGOjoFdrkjq19/fmqTnlmd0/HHdstyy/r6EcT0BET0d8ndcgJZzu6/pL98q90MT2g4RfZ7PNA7zitTXa+OtQzmlukIh/TyMed8NEv8NGfk9Hv+zfKGpGnLzV3y3treDr6hW6577VSJ90+XV7u8XRM1CXrk/fHR3XLB88M00ue+lSOmx/qJmu0SPctgSZ/tskSU0P9adsWWWu9UJ99vFX+hz9/ZJsZssoGqb71YIOcMCnXdY+ZLcO3y/Sxi+bJ4Tdlus2MubLPJ6kedM58GT3M175PO+WFR0N95I0Wmb5Dopufu1BepA8fog9/9RPdvWOhbD/G0wu+6JJfd/F0t1PR69d6OmVBlyy/M/1zRqe88QI8Ona2bDA80P8+3C5HvhOqXNMiHeiAGh0w/yX6ZEyfXHpCoOF67bLNRrFedt8iOesvV5fZukcq8jKdvFy8DX7nsn65BX301289st8evu5zVKccfk+o237bIjd5+POuThlWhPr5rFa55fVK/7r17/+PpbOOjvLsvvYj4xFci0PRFnfZGymFQnG34u5SKMW1uBd3d5cCxaW4u0MgrpNx/c77W99frCzIZJ77Pmfv64K0QX7pn/s5PHh7O5LNhsXin5DOZZoHjbZn45/n36Gl5EE/yYNt/XPyYtWX2NlAPHByGKs1M5/4MnDlT41TbX4MyWdg2ftuFNgrvXYkHZerRHDm2gS4KtmYc2MS1sl5PugXj0bi238WzERoXiSNqbGodU/h0oEh8Y5orvkag6JVs/DCms84JR62Wjys5e0I9hsej/ynTby30o6uvQ1sXs+NeRN0pmf14pnwerdHAdwup3PUXi+uxqicVjOA57cUth8RQsJUhRW0ME6UMXLDQScSaun8YZEX30sun5RcriPctUC4K5Sh0lgggNqjFS4qKP2RobFLQd///dzgdaP8SN9sEE5z4cFNOY8xPhw9aeSW1Q7xQ+FZZxB+4VmL8Gy+klY+PJGCPi+NrN3GgfrC29OFt8/lMHLxK+H6YRa2L5mGl28sbNssFUnFdZb8R+6ppplcmoG5klPpklMjnEaWzOVA3dUGznvqgj7WSl++FDTYKb14JgR9l0LfqRDCkjOa5MxB6e290ttlJkfykDkOfa6prDsxgB09NJ79yS/cqAg3hvDyVg62GfEK/f7RWHup8LZ4VrcPQXx8ZWDDli6UOmriiS3Cva3Fe3qG0dak8LfMMAa8MbByc8nhf3SOW+JBpWI6t5734o9zKrMtDOB8WM5NCWFWsoW+kqlYMtLK2CIpOLPVwBUXXYh7p7DyzyG8vi4eMT6E+weMHLDXgTZybj2+82NJOQu/7U3D0Ggjf4xxor2isUPYj+CLCBrbxqNYQ40XJ/lxdrjOxSW8uCl8MOZFGP8Ihy8TDq+VOweXPn2Nq50snNQ6DaPkdQvJ6376wcj6u5143kpj517CUYdNfLDdji+DbKxXPgkGr5WeqGRsmCI9bQijY4Gs7PDfR+wubeHpw2mobZH5SAsjPF6lmj2IlXt0vj3qwZH6wi9T/djjjOaJnOL5v1s5P1cKrkleTTJ4cOBnhXvHhHHbbuPw/Il4lGFl3wLJ6JscxbqlvuLaZMkFk+RWFXk/azNxV/h3iPDvd51lzltlYuwvGr8b7se9fDoH3PNittyjS+7xleRBG8mD0cJhhYsGsEbm48kTF3x7VUYeDiDX75F8nisORaRHLkuP3Ptk4AC40HOcTubxYtgzjSU7+TB1uomGsB2VJ+ncYPXiXT6Vze8FUXigyiuVgvj5psKxY0P4+63KF00DOJGgc90PHhw/o3LtsgAaJAsvlvLgQiML5/2RhtMVDZJjbqj7NIYO+bD0jcqYZgGEZug0BDz4JWjgSKMLZyW/Fq3NwIklKjd8CaDMbp1HT3jwSPaij+zF6ZEKVxUJ48gpOd9VPvSsprDB32FUzWbl+vcpSCyt8ofDwf/7+XU3Nsv7+F7y8FQG/lxvYORdF/ytTIzolYmRcj/FpvmRLP5Wen8AtZwql+YKoNFLhZPbhLCwiMbki370MUjeup1IfGxm2e7p6DdKZ63CXmSXHH9fyo+vnQ2s2dKNk+cUrlsQwqCtCitdCqG79OPP37kxO1ql+0sQ5a8p3DExhMJbLbx6KRVaxwj62iUg3q6ycv4A7v+usn/OIGI/mVgddqRc0/n9HzIncl6HDvqwo4qN59YmYbXk2hPJtTy08cGMJAxJkn0o40Aj8YDJ4gGzZuv0uzw46lG5NTqAKnXkXuZ7seKGznfjPLj7UeOQ+j6clx5btDEFc2cb6HS5ECEen9w8hFKa5J4/jLL7hD8PhRAULzSLF5bM8r9/H/7f9zFamG1TKq5uUjnpegDjEzXmLOdDyYnyeVFhvLwheznOh259VTarFcTfeYx8/siJSf00Wmv4UXFKNLcYY3BroOxnxTDWTzTyfqQTd06LB6xIxmuXia2y23HCbua6/OnI7zfznjUdKZYIlk5NwMopOl8avEg1Chc50lBfem2a9NqXiZJXkX4cFQ7cJhw4oJnk0VA38ufVeeehF4MOSM7vC2FwL5UVGYT1msa0iT5UFk7b9FcQNY/rXLrBg8wSRuY/7cSDumb2nZuBof3k3mtkYMHZKCYt+oqRBitLuFPgvGdgrgEu1Dtr5rzF6fjFonN0mheeuSZms9sRHhVJrVAcnsHALjPdKCw+fmWO5KTwXFXhOeUX8erhXixuL/3cMYw1hzU+3C570cvAj3Bj9R6FT4+G0KSNeGOPNJSIVHkmLgjf7xojcvqxaYF4fmIAXV5a2KpNKtp3t7BzkzT0KqiywY0gdr+y8GTLVAxdrbLs0wCmTzFRNWaik3zcXj7uXsHKJttSUGqDcOltH3oJpzeo4kHqdypL3ArC3E2n/RcvembXWf+tFzt+U3iuYRgf62qsP1dyaZPc23UHrhZTOeVcEM+36+x0Tva9ncJ5ncNoVEblhENB9JPnrCXPmUe8/Il4+ew5JnocdvzdN4ova35DyzJyzocy8PSLld1rJyNmrHh0Xicq9LFxV50kFPWrvGSVfDGaud6RgarFo7j27Df0H6myetEgXkkPtf8tjFVTVL41BPFSvLiteHHwhJGmtQ709Eo+RIXw0WRmQ3sGopeZ+PWjHcEyCi0Hwzi+WOHmryF0e6qxaRcfig6w8nyVFKwKanxl8GGDcOQdux/aUAuDZdLgmmJjbkMSun4RfquTilyHdT7f4cGx+0Zu7udA6wbiu1P8sMuvhSf/z3sVHqkTQp5UhU+Kh+DwWpg3MhXO72zMdSsJZcJWHlGThauycrn2AXUaG7jgdzeWftH5ubYHKz1Z+Dr6E4bcUVhuaAjlZggnB32I3Whm1ZvpWHPdyMfjHRiXy8Q8zzNxY6bK8b4Avpf9OyX7N3+nypTTASwTXvoovBS6EE199hcsWWvitwd2LDqkMW6nD5sGGnizohs+8b8o8b81B2x8vDcRBWwa7yb5UWOvcN8RD6p/NXF1DTvaLtHZI8aDfc1NPDxI+qikjd+dSMLb5Aj+WjIeL6Q/20l/br9u5dnxyXgRZ2aHiukoJ/59YGgY6m2FivBXkvB5WeHzarqNq7xJiD2ls+oqD9JPRbDIqnhk2W5lzPlkvJXP+1W8fdDN7Kw09g3sow0sWMCNnbK359bLOW3W+fqqB3Mr25i5PglX/zJwUroLlcYLN+dwY0WNaL5fFoOZlzQGpvsQc9DEmrvtOCXzs0rmZ3Z5ne7tXmzKqfPWCy+MVhO9KeLbbzVeaOpDi5aRHNA3DjUnqVxhDeKi5PJsyWVTIIJOczzWNDHw6Sg3OkkftZY+elkuku32xqFGN4UrfwmjgezH9PMe1J1i4XxjGlp5c7FX1DOsz2/i3TuZ2JhF/OyTF4/O6ey9wIMnDSLYfXICqkoP/h0TQNp8A0sku/7v59A2+y8ETfoqKH1VfLrKf8L/+3sT4bd/wzg5ysA1hdzYd9HAIzPFj8+YWHiZHUt/VPh5ZxhO4bQ8zVz4eYPMye0ArkmP/zEmhGkNDdQnufFDOSv37EvBIb+JOy12tNll5G+nHNj6m5GXGzrRWjitT5MQttU28sICJzTZo8ChMOzF5D7OuXGlmsrJq4J4vE5jz3s+tGkvv3byY749G9Pyv8OtchrH7PXDsldh2pEQvpPzvf+LD3e3WDnocjJmTJQ9i0zD87+j2PnZVxTupfEK/ai6OZKrr8XiVtmsHLX/Ix5ONbKn5sQsv4F+iwsb3pt556d03G2scfg4P4pERPNyQgxK5IvkmftxiHxiYmI3O9aVUvjgaBg50o18V9iBuTKnDpnTr0111h3hxfkInfPivRgFObdZmSi+PSvPnv+AjwslH+M9ePbZwK51JfcfGTi7pwv3sisc/CaMVlcU9poSQgfx3g797TAdyEbnvnfIHKDyu8pBdFFVtg4GsWiqeJbmRanRkTxZMA61zDoXZXhxaaLKmZHC9fK+Ssj7al7ZwCEb3Hg8zcJeShpe9bKxJZJgkT6xz3Jj7SGdj3d6cEN8+g/x6c7f62xzyotJzcVfB7tRNr/s2d0w/oqx0F4zFX0u6MQcD/rKHtaplI65HxTaG4YwVDyhnHjCJiq8MT2MFo8NHNjDhWalzBx2LAPbhUMuCocUHCr5UMaNRSuiGPvmK+bvNjHtuB3WCQamZnPjRR4z2z/KQNyX//0c6wBunVE4clkIvaWv6uXyY9zdCOYbHI9tG1VevBnAnDwmOh9l4lsxjbXO+eE6aGCO3S5sEk64Wd2P3jst5JlUFOis8b9WfmyWebgh81B5gML1VcI4vUXnyssejJX7ySf380E46yfhrNdLbGzxJRGbZM+vy553Pi/7OD+E2yMMHFHMjY+D5R5/8OLyeIUzsoex5z+VJ0YFUKyDwn87hLGuaiTvro6Dz2ditM2O8aNtzFEwCREtVSb2lTyVvUqRveoWkDwyh9BWU9ndH0TnozpbbfFgd7qBJwu5cO2SzsnTPchnsvB+ZhoeVNfZd4UXHYSHO05MQ64Ilc8Sghg0wcYK2ZKwab+RN/Y70Ec8prZ4zOBaJlZcnIl5dS1Mn5uGw5107mrtxe/Sw3kahVHkopmXZwkXrdBY6o3kUIpG5/fCieJPZ8WfNt/Lw5sDHkFNE84oKucke5tf9jYg+WWzBNHaq/G3KB/aVxTv2eJHeXmdHW8lF28Jz46Qe5D+uiH9dWypgZs+uXApZOAMzYXcOW18/jIJh50Kt+cMoVVXlX2aB/HuXBSbLPiKzYkW3iibipc/qmy7M4jmtTQOXuyH4YpK35QAZmyxMHw5Fd8/VHi6dwgTB5qZpVIGTonP/p1Tnl94o67wxtNmkew2JA6dPpjZrlE6rmUonFQgBGNLI919nViXy8D7z4Svl5ro/mQHuuqc2dyLzLPSQ4sSsU/u61CfIC4NNnDmD26c+9nAhWPcmLxAoS0phC/CXRTuCr1XafopgK2vhM9ahmDLmoWJHz9jgpx7tr/SMEL49nvh2+M3jdw4xgFVuFH504caf5i4PEsmWvylcFCGcLJwpGluENsmGXjRIr1kF6/O78Mp8dTVIT/ql43i1APf8NJtY9usiRgSZ2S5Sg60sFk5ICkF5tmyD247YrPnYJU3r3ErRuPImj7Y5HXSvvNhUqqR1uIOFKuk8t9NQbhPKMy+Vnxiu5lXz6fD7zPTZktHlvgIfqkQj/s0c8D0DPzT2MZl45KwNZ+Bl+658Syrxi4f/Jgm+6c/zsSC5QpT3oVQV/xt7o8e1L9s4sxpdjS5ZuDYiZIDFSM5dEsc+i6U3EhIR2/x5Hq9/fg3ReWcEgH0GiP+/J0TJ8fbuDZ7ElYKj7+dGkb8jxaW35mGhWaNSel+TF6ajZbP75Ag51FRzmP9A+H6PunoKb1X/7MX+WqofLQ8CPWgkYE9DjiLqMx1MYhjezVuOiz9u9bIQQ8dMKdpdBT14WJfI+fWdML8l432jESUymvkiYdOPFuqsuOnACY1NTJipBP6RwMDDVyYvsFC5XYqehTT2eS8F4eHKtxZOows4gNfxQcWCPcmCfc+k17qfFg4a4+B+4+5kHEogoV3xuNkvQiumZ2A3uKF9S/5sd9s5sH0DHR1GPlrHgdenJR5Xx1AD+GjxsJH13YJf5x0ITDEREtZ6RGrkf+mOFFefGdXlBN5hJOfdgljZScL37aWfK+vstXUIG7LXA6XuexUw8i2y51oMETl1HJBtPxPYb/RkmM7dY487UEX8eIW4sVxX8ysUjsdPTqb2aRVBqY1tVIZmYLvjxl5apMDaXIfxeQ+Pg2Qc67sRe9xCuvlCcM12cqcphTEi++UF985ucDEtYl23JZzHvHAgdkBlV5TAPU6Ss+1E54w6Dzi8qJaswiuHZqAFdc1vhsv97Jc5ux9Ogq3kznskgG7+Ht+8feZwgEB4YC+4gm1bjhwYo6R6xwOLH9j4vtmdmTdovDTlRAuZKqcmzeAVZJvryXf6myK4IIb//v7YhNHvM/EUenPLdKfNyVHfl/kR5JBYRlXGJZXJqa2suNbXzNr1szAgMpZWWXDR6ySuXkpczPlnJnmhekIeXVaIj34Jv5XbW8Ia9fpfHrPg0a5xV+fBpH7pJFP1zjQe7uFdc+nYqL0ULY9Lvj3S07uF7/aZWaLU5I/0DlpphdnCkVy5dU4DHqks0JPD75WVlhjfRi/i2/nzhLAuuEqH5YI4pPsI2Uf9e+j6D31DeVnKtzlD+HfaBvnfkmCuZLOzE1etK2psuvSILoPlLms5MWNIzp/3+qBInyn5A7AKT6dR3x6htfAcKQLzxcq7BAfQk7hm+fCNy2EY/vtseOPJjpzjPZinOxdgV1pSKmgsNTWMDoIX3UQvvpvp03mJhFFhC+uCl/UuGfiyoF25JwfzdcpX1C5goHrt7lRKEl6qIzs2YKszEz6gDPC+Su/eNBM7n+oW/y2u4FZf3Zj8UMzv/VKx9MXKru1DaDyGSs3L0/Gja1GTrzowKKDKuN3B7D9lc4LLT0Y+VlytJ4PXd7obNlcXm+FjcPeJkJ5Z6TW9H//vYpG2wvJ2UkWbrSkYan4WExnH65LfkyQ/CjtiuSJHLE418zKxUNSECMeVlM8LPaQmdV2pqNgaxv/65mEancj+ffgWPg6Gxnd0olFO42MPe3A/EUq02IDeH3LxNYj7JhZXjxgux9j0lUWKBzAglkakz0+3Kxs5tj1GSjYOZtwx3u8/Etlu/QA/n1o5Lxesk9/Kyz+LITG0rfjpW/DWwxUL7swaaqBEbobI8Rnih0TP7ZE835qDCoJx2+dkoA5HSLpbh+HLf2MvFZd9j+fiY/uZ+KES5fc9KCs5PXBGRkY0ySSBUbHYdSCSBZOikXGbI0FXD4szaXyy7MgYkqrrHc4iPvRCvt/kfya8r9//5A8q6hy5+YgYsvKeezPwFiozD8rCOWMkeoKB1bOMPFtwI6qZ3WuXize1sbAg93dmOux0hmdjPNxCudXDOHMOY2rFki+St/Pkb6fZbDR70rCie461/3sxWTJR7NwexmLeG1aGCVmZuc//je4mCj8WjaACeITWceGcOm+8EI/FxIGZGf5ym8xbL3G0nd9iIowMDbBjRp6FFd4v2HbIBMvls/E3Z5GDq7vxB7xkhO3Aig83sgrOZyodz0b50x4h3BpE9XDmRjWyczSbTKQf4J4QzY/Dokv7RZfcpU2MPdhN1DDwFnL3dgqr3NZXqdafiPX3HViV08rT9eXuflTY12bH/335Wa1Q0/giDMxd0U79DM6Pcs9iF6r8OvDEJYbdX5wePHTagsnPBWeqWpgq9XCD1FmWr5lYMIIjdmL+1Gsh4HnG7vR8CfJhwlenJScWSM5U9mkcX2mH5PeaLQ296HiV9mPGslIyiZ7+S6MC82MnD3EiRjp1Zrx6fj3Hwv/WpKK4UMk38qFkbWGmZ+XZyA2OopVv3zDGOHdQlc8iLyQhXFzPsEm3pco3ldrv4FLD7hgdkYwI1c8YqXHq/wgvD5X5a3MAPb4LTxhSUW7dQq73Q+hjXDUbwWFo8RzXAle5D6h8+U6DybJn4u0pmLRWY0JiyQvFDNHh+Trt4pglV4JaPbJyKF0oEcW6ZnPGTgw3sz92TNwp6zCofvDWCDnmiznWv2NwhXNQ5g6VKOhjB+1JV8XSb5WOWri+i12eA0as7j8uFJc57R/vCjbSOehiV68k9dvDAfe1pEcnOfFx/oGNprqRoXT4qEr7dj/QOPBPj6MFz7MJXxYyxTBJZkJqHXVzMWT0rFUMTAm5EaPMhobH/Kjb3ET65zNRLeaGpst9WP2pEi6LHFYsMnClOupuHxV5fQ/hfsPWnh+j9zvAgvbJqbixSET2++0o9MdhW2GhnAwSefe0h68+F1h25xhFJaPr5bxIH9+M+/flZxYY+atx/L16yr8NjeM4pkGns3nQmJRIyv+60SzgSYOq5SJzZITNyUnDjUzc9eQDHSaKl6ihzE5m4Xm92mS49Lve0IoUVK48YQfqz9pfAYf1AJWBv9LQYRwc+KVVAz+SWGFCWHMnCH+HwhgUbTOxC9eNLVaODwlDRMkh7JJDk0LCafqPuwvKL193YvlbSP4oWsCvrYysEYvN0rasvJE0kc00S0c401Dyx9U9t0TRIm3Rp7+xYELOxXOOR1C2T8kl7JIv6ZlY/Wi73BBcnq25PR1o4ETHG40KiT5fM2Pgs3k+Ya6YbliZtqUdFy6o3LWkAD6ZBOfee/HvldGHmrpwMstwkmXA7DtEd46FsCw8ypLzw8g7biRxdc7sCSHhTGv0jD4s8If6oWwr6mZh0dkYGjWrCzz8SP+bC0529ON8rV17ljoRYTwffyPboTuKTQMCCH8SvinZTLuCT8PFn6u+YfKZVmCuPajiX/uzMSnZRGs/yEeYyerzGcOIq20mUUPZ6DlOJ3983jxs/D1WOHrf2RPlohvNJ1r5cjMZOClwtltQij0MoLX2sSjjnDswrAfbeU8e/R2g/0VzqoWRiPhtj+F2xpPUPh7NtkL8eE08eFX4qctxE/jxVsqiLcE8im03pP7j5d8qODD0fIKN28Po1/raNbpGYM+kyWnzH5072RgkzZu7Nqk88x1D8qPMHNn8QxUF25f9T4E/3caI275Ab/CudYQKvSM4M4GCeB2hTPPh3But8YFx32IrS05suAbtMnC12Yn1mZa+ChvKrpLLjSRXDhWVOGmf8OodCSSm7fFoshRK69sScbJVjau6SV+Krz6X+cwWrtN7JPNjjPyOitNkl8xuvCXB/fFq/uLV1frqfPvBl68ymthqwdpKCLcd8npwNzeCp11w3hTXGPLf/xo8VNWDpjwEXfaWjm4awq2bYvgv//GY9lRhZ83h5BdvOX170mYWFRl9IUgalaN5pI1MRg7xMZ85ZIQ097M2h0zUFHyYKvkwbGxJm7Om4lbtXSOXuSFctVAZbILE36yMOeENISHyMdl3XCdVpljhdxHvHhYefHECxoPzfbh38UK530N4fh/Vm4aJRwjvTtBerd6cwNXDnajVC/pO4bRU3qqYQMnuubU2Pyl///+ve/VUQ9ydhR+bedF6/1m9j4gXDRVZRctiMtnNM5Y5sPXu2bWGJQO5a6RyhAHtjQ28Oo4N/rX0FhrufiEauHBQBp2X8rFU9OfYcAzG6t2SsTBfhburZ6GgyOjubdIDEosl3x478OhrQp3XwzhMVX+Nj2IBzLf/WS+u8ueN+mWgHv9NA6q7kfFZJXbSsmeTbGwrDENzj5GZq/txOroLHzx5TM84vHZxeNPJqlcXzqAhp+tnFovGdZqZqb/LTzyxshCzR0ovVTh0c8h+DwGRkW7sK+ezsOzvfh2XZ5rfDrmDLXQWToNM4brDJTwovN1A1tPcOHpPiM7H3Tgu3YG3uvixizpRa/dj4R/jKywxIGxsg/5n7vhTpX3USyEP5rrzD7Yi988On/O4sH1PEZOeOTE7+IBuasEoQ2R+S3nxCbJ2/8kb0dlN7LoWycGNxMeH+rFeZnnhfdCGC98kFP4oGQ9G0/PSUKLYsJL58WDfhJ/Gy/939HKmu1SMChOZ8VKHvzyl4EjMlx4uSiCbWPj0XJGVvYPfkC1+EiuqhCL+aMjmFIgAflmGXnf68BomaeC5dz4IyOC2QrGo/l+6ccDDrRqJvcwNBO1/1C4KEsYbUcr7FkwjMujDZxZwI1VtyP4alg8Nm5WeO9qCBPjjIyq5IBR1ekMejG0RSTL9o9Dg4ERnFYpAXv2RPP40S+IDloZa0xGiZ0qz54R716gs1GSB4vl49jTAUzpKPnXLoy4X8Svh/lR94PCvxoJx0wx8aMhE78dUthwl3xc1Mz3/2bgmXhTJ/GmeT8bmTHGiVNrrFzzKBmTZ0XT4v2CCdOszKqmwDtXY5RdemRNFh549An3pR8GCK//5bHQHp2KKdEGmmPcGPNIOLWnD6vET16Jn6QPN7NYiQysLCcf7w2jRWXJhw1e5I008lGcExtLW3jrcBquVdE4ca0fc8U/XFvdmHLHSuPQZOzqL3lX1QvD2qz0PvyAK+M0Ts3tR/5iKm+fD+LsB5WLGwYwCAorzpI5WKyzyDcPSh1WeXKHcH4nhS1ah1Fompk31AwsEx/6JD4UFr5SyoaxcLnOhHfCU0s1xn/2ock6yf37ATjuRjDvoHjkMel8nOnF/jSd+4t4cOKm7MeYAGDNwjkpn5G9SzTftYhBr4MaG+7xSQ6rksNBZMur8dNDP1q+NLJPGwcuCk/OfhDCkYvR3DHzCz5I3zbypKFBLo3TnvuRVXrhU0wQB1SVewNB+HSF0Z4w5l1RmDE5hDk5dDpeCXeXsNJ6JgXnrlm58I9kVBYP2HQkiJY2EwcmZeKGWbwx3YsRK438/rVwlMxpM48D6dlUFn0XRJZcCmOehzFsrcZyD32oNEpyrnAKurbS2Kq3H31qZ2HdhZ9x56LO4TNl78Tv/xC/byVfp7d8nUV1DIyb50avBI34UfJyponzfHY0PGDj1H2JWCN++NTlRctfo9lvYAx+7mjkmHZOpA1TWLRkGCPLR7DojgS0KaSz5zUvvCUMzHLajZ9HGTm+sBO32knPdk5B9jcq3zT/3/c5RPB2qwTcXSM9/siHQTK/VWR+byQpHF86hHY5srD768/IMis7v3jfoKBZ4fWMMB73NfG3mplY+NnAhLouWPOqTHkovXspihWnf8W8iRozIiUXJkXwlFV68nEEZ/WIh6mv9FOtMFqeMrD/3y58uWlh3TGpqLLOxE337Tg80sTdRTJx/UoWTpz8CeMm6swX5UXPmWbW96fjYk6dM194MfOQxtBOH7aWzMqLxz9i2jyVWloAXaub+evKDOT8JLxND4rLc56V5xy+Te7rggNddkey1YlYHJJ72yP39lj2+7dED/LVUflwvsz7HiOXHHXA/dXEbNXteNRUfn+EF0magT/43PD9ozFyiQ8/CP/v+dGDe38pHJgeQi6LzhdpXgSrWGlem4LYXhbWQBoais9MXuNGtikaPxr9eDXRxFaRmRjSUF5vkhsH9ig8cDSEwhlmXimYjuTxwmnZgxgnPfyd9PCtozaO3JwIlhLuORbGvV0yhyftGP5MY6lOPsT8YmSdYU4M225iufN2/CV+kCl+MG2Sjbo1CWovK8NMQSPpi0nfe3HALD2Xnobfd0Qx59mvsFfXWHCFH7UGqlxSKYgZx/IwuPERrtbSOGWRH7n3R/Hpga+Y+lWjXsOHPOJvz8TfLlzUOHemD9+Jz9/2+jAt1UC9uAtN5f2PyBHEpeEaZ5Xw47cCBjb4z41vAy2sVjENhTorvNEqDNOdrHQM/YAnkjO/nZP+HmvhrLxpWFLIyC9XnUjINPHHvHbUrqBz4TbhnC8K29UOof4TI2d1c2DXSp2nX3mwTub+rsz95R+NnLHTCS1gFu5Kx71nFg7smAqvfF5W+bxTl4SXpntQ/ZiNf29KxLqqGu+t9qN1ZemNDWGk2W0smj8RfTcorHU7hF7NVXJwEB13GdnulAPntpq4+KIdrYVP+gifdFuhs9lbDzp9kd+v40CR/wy8PNqFmeVN9G/PxLRbEdRGxuPmCo3j3vgwu6JC7+YwxjCSBWfEwXvKyiyrksELKmfMCeCx5HXPWCem54qi+vwbcvfQ+byxFw98RvazOcB8OqffF49rrfNATy+aNzVxyEiZo1w2/vA8CSHxXZP47tK/ZM7Sfeh9UyHGhlBUuPCicOFbo4XNHWkYWl9luWlBWIVDky+EMb+SkWkb5etOUhiyhHGwhJH7Tzuxp7HO4+O8mDPLTLcnHUWk/65IH2rTTQyH5V7Wm7jgrh1Tu1mo/5KGwu2ieKXLNwyvYuL36zKxLVnn5ZIeLPlT5zebF42fCKd3C6GJ1ciRKU68nquzjd2DYtuN/Pe8A9Z4A9MruLB8tc53TzxoGamxf5wfuzJtPJU3EXmeRPFZt68YsNrEyk/t2FMmC48e/IwnBgO7udwYEKWx6jc/9j4w8EQfFwoOsvJGhRRUq2PimvmZ6HBV7mmSA6cyjFxVwIHDp43ctcKBCdIHOYJBtK0vez7Ni2yNI/nu9zhUFf9bLf4XFD+1DBE/Gi3+XMCLwwfM3LM3Hfoxhb5NIcSID9SyhPD3FitfXk5Gwn6VP+4PoHWijX3KJaLBhSycMucTlrVR+bl7EFvkHK8JRzTbZODwGy5kO6Pw3fIQRrawskj/FHyUe20wIhOxm1RWvR6A/b2FhX5KRZ+mKuuODP7f/59myTAv/qorXDA3DW0OK+y2I4QDcVbuq5iM2+fMHLkgHX9OF94JO5Dx1cZC1RPRrZmVzYamILfHxhdZEhF1XuHX+SF0fiv+/UsIP8RLHlZwoFFnnX+28iLBIs+RGsQO4fXzwusNHDqn5PHgjuzv0JZhDI8wsVRCJkbEmVm8UjqGV5I82iS+8aPc2y4/Pp5V+dOiAPbJvR/pGsIomYfipTy4MtXKKVoK0rpEsuivcYj/LZqVG8ZgXEED891w44FLY7/sPjyaZ2LvVDsaR0sPxYSxaZ2FN++lYuQRG4ttS4S5t+RcPT/yeDQ+yeID22uc2cmPvnUNrP2XG01bmzmqZwaev1PY4ecQ5v4svTMmjBibzlqJXqzuLftV14tYWFllZgoCX7MwovonnKqrcfVcP7Z2UHmxfRBZs4vPv03AqecyD+1DqJfFyLmfnfhTE37wi2cdtLDl7lR8fZiFtXt9wpJyBn7d60YeWy4+S3qOP//TaR3tweDHws09PGgwQ+HUYAh7xGOPVg3jqZxb141+OC5GMPfMeByoYeD+ZW60mSvPn2nHbuHck0MycW+P7PtRO4LlDTRtdwNFbJxzKQnDuyos2TyMvn1MrFMnE+2MOrs5/+eFOls+8OLdzWg2HfMFwUoWmjem4Ujcd9xR8Rb8OY20vnCi+6/ivQPdUBaq1BMD6HRVZetJAbQ5YGCvfS50jTKw5Tc32htM7OrOxLGvVm6snoxbFgtHpaahyjEL129KRXyEeEa8E19zm1nzSQZK/i3v+7kdBulr90sv2jZR2W10EPUWGPhXkgstp+jsZ/RiWluNhm5+/NQnKyfW+Qj/wkhGxMfi7UkLf1mdimWxGr9U9uFMaTNXHc7AObONi9OTcFTuZ5PcT63sCpe8lRy7ojF1sg/NGhg4dIobo6paWGxNGs4t07nogwcDniqs1CWEP3ubGF0vE/kLZefta29RuKOZV9tloPhAE89VEr5bZeaEF+nolyhzUs6D+dskJ/8NodguhRdPhXD1jfR5MxcsP1uZNjYFUcKjsbeDmPfKSHtL4YuO0u/tMjFpokJbVBh5/1b56HkAe0YbeayAEyenq1wTCuDX78wcdDsD3drq/LWbFxPkfWb96IFLfD63+Pzoh9LLvX3oL/xSXfhlh/TM+doOjFNszB9OQt5FCp/EhVD+loF7Rrowz2xiSnomrLuyMv2U+NRrjWta+PAubxSbPviGxNcKy/wawomVCte9CmHSUZXWLdI7m+VergYQ9bfCuOchTK0WQf3vBFycqnOu5kW/xUbW+ubAtP5mqtUy4BUPif7ixvQuwvctwvBJXkVKXrWWc+p9IYR7kjP9h6Qgq2Lip3AmCj+Q3ugbwNCGOstM8mL2SzP9rdNR9VMWruMnpLTT+H1nPx4Irw2AB3sipHfivWgi8zpmUxqeD1fZqUQQ3ocqs/QKoFU5nX33edE1VmfzKh5UnKBwS7Yw9Pgoeip8RaHaGq8t9ON3r8o8UQEUz6Hy/OsgevysselYP1z/6swz1yM+r4jPh3FxpvSvL4A+7xXWaxzC6/dRbPHTVxxX5JxCYcycaqBXc6NmKyOX9hYukT4oL30Qk93Aum/cKDxZ4RVzGA7h0bzCo4flfHf873wLiKf8F8Qzyb1O0oOb71t5rV8yBk7Oy4rmhwhlVWj+EMarsQa2yetGss/M723pyP3OxCdN7NhbSPLhqvS5vM63m0F8fG5lg/bJyGFU+MYZRtQ+jV8P+TBB7jG73OPVmwb+OcaFC7sjOPd4PE4ExaMMAVwpo3H6QT+OZGjcWsCHr7uiWOvkV/QZFc3ahWMw9ZaR+kgHJj2x0NYtFUUjrbwYl4Jy0zXuD/vQXHJ2qOSso7fKfHWDmG5TqCaJZ0peD5S8rtbLxL+ZiYdvNfZs6kO1OVFc4/yKznEaW1fywbVH3t/RAHLDwMez3FjTWPJ6nB+dTQpbZ4axIIeBSa/cOLVR46qbPkyNtVCtkooij3PwUo9XyDZf5ceUAFaXN/LZdidG95A+a5yE3GuNfP7QgXljhF/yOzHokoWVZqSihOTpGekhk8tMR4507Fqk8p/YAIputfDipVQ0/l3nhFxefD/dwDNhF34Svp64JRElTht4bqUL4ZEq1SJBrJhu5ttQOv7N1Dk/r/DIXZUDBgeQkCsryz/7iEfZJB/fufFrNxv7/5KEEXOlBzPTUWG0xh0F/Sh6UbhrlgMT7wt/9g9haU+VMfWDKDlI4ckKYWQ+0/ldRw9KS34dk/w6sNHAfTddWF9b5b0FQQyQ+6tiDKD0IcmNXQ4sK6QKFwdRUzhguXDAk9Mau63w4YrkzdS2mTAv02j/6EPjEVZOLJ6Cpi908UIP1q/WeOeJDy2yGjj4oxsLAlmZYvqATIPCPK4wdtyR5x7iwhjhmwIFvajwSuXWVuL7V1SWnxLAuSMaF2/14b3k8M+Sw1uTTLxc2o45bY10dHWipXBXX+Gu1cKVT6YG8VT4u5spHRs/GHiroQtJUQp//Cq+uDAre8V/wA0aOH66GxNWGZjjpQsL8hqY+sCN6HHCiXm82LVL5cmTAQxZYeEPb1OxubzG69v9mD/OyNTcTvRNFd4qHkLScZ3l1nuQY5mJrz/a4e2nM7q6F9U2qFx1W/aesvczghhfWmGOI2EMkDytLnmKuVbOzUzGjRbi8f28qF8lgjPWJaDoBYWX54Sg3pacG5GOy+LPM6564T8q/Lc5hHIXhLvneJC8WDjnqw97l9p45FMiljQ3MHaQG93kdZvJ6y5tbODncW6IhHFGvkQUTBUOLZ6MzYt0Xo31YMtnjVfr+lCld06uq/cSSSM1livix4YsGu9+8sPSwcT0Dpn4+3oEn42Px/uz2dl40RtsaG3k3d+caNMqC7v1/oy243V2zyHndd/CM/1SUThRuLJcAA3/MnFKhh2VClq47UYaXkQb2f6LE5PdkvfZApgj82kv4Bf+1IU/PajUXefmJnLvZ1RuXx7AmI5R/K7dN0SVlD44EUbrrxb2rpGKLZERvBybgK31FV763/edbDKy/HUHuo5V2CxfGKPXGVjgvgsbzVl5J/0jjkmebvkphNjSuVjt8HPME5/LEJ/LP0OeM+hDwwoqJ28LYu8qlcdfBPBDFpn/z27saSb5P8SLlIIR/P56AiaMtjF7wSQ0STRydDkHDpY2cf/hTDR+ZuP4TomIEZ6qFQjhZRvhlO5u1IgxcXktOyad0Whd7kMu8bLn4mUVd0dz64kvWHla5+sVHtS5a+L8wXaUmWThEWsarkv//fkshJt/mTkuPR1X+tk4pXoS5vY30VE1E+cLq1xwOYjbJVSOPB1E554Wtm6Qht3jonkydwx+k71pJHuTZ5nk3EcX3hQ28tfLToxfrjHnex92hgw8q7mwon0033aMwYVyCmfvDaN7e4U/dwpjnnBQRlc/+h+MYI098egtewTx2AY5cnPa66c4tMXM3ZfTMX+acI7iR82uKpc1D8LQz0BPDfFr2ffqbT0YJRxYTDiw5ALJmaQQ/phoZPYoJy62Vzm7YxBXJmicnNWPSxsVTr8pfX/dSuOEZJyepnKlIr7b1sCO3YSfI00cGZeJ6h0UruwQxodEhY3LhtA2t8LuT8OoLt6wSrxhlJxrMTnXn4XXx6XZkbOpxpcj/Zjy1UhjDQduSN/+IX1bS/hncUEnplzOTdO0J7jS3cAZP7vxh3BgDuHA5ZUt/LQ+DYfb69zd0YsuJQ1sccKN2S4b3dkTYdlnYuohu9C6zrQZXnwn83ZH5m3BNF3Ow4t88w28n+LCoF9UVhoeBKOMnPXNiXXCQfd/9SGOEaw4PQEH86jc/yiIcQYL87rT8Hum9Fq+AILVjDSvcqLhIvG1uE/4QeZz9//m87bGI8N8qH7ZxNXT7EibrLKYKYgM8cVCU4Wzb2hsNc6H/EONvF/GiYLR4ikxafjst7KuJRkDfrCx6p4kFM2v8+Jd8Y3OGqu19OOfqSqXakG8Km1hq8NpSPgvD38c9QhXKhg5eatT/F/li8ZB3K+occBmP8r3VbmzVhB5Ouh83EH8v5CBQ6658VHmvrHM/dseGpv/5MeNswrHLQqhfulITj0Sh89bFda9GMK5kQoXFQnji/xaR37ds0DlUeH9bjKnza848UnOqYGcU9MhJtmzTNzNpXPIMy+GfVBZqpH0738KH40SjyqqM+NfL5Jmm1nWlY70SQqLWsI4vsTEdV/ssMxWmOEO4XYjA0f84cbhOiq3zQtiyJosLP/4E3L2s/BljTQkPVJY9jfZ2/Pi5fMc2BPQeMzkg3d4FkaX+IyFck9Jck8n9mpcf9iHp5LD3SSHGSW5+i0Jx2WvN45yobvkRlPJjaDwuGlMCtqm2ti9uHj3BxsnN0pEz0dmNuiZDsNZM92L02GraWLy0kzMHCVcV8iN9B8VFtkZxotfDGw3TPxTvLWfeOvz1hHs/FsCcptsfJKZBK9DZXTugNSHlY7kFIwsqLLojSCyDdb58UcvQudVGuYF0POb5EC1APJXM/Pe3xmIkZypNciObLU0vl3sR23x1SXiq5WX6tz4WfZibl4uznyAe3N1DrV7UDDZyFulHKj0l5GbMxxYeFn8Y6p4unhpm7/8yLFUXuezD/vGiVfnFj7MlYWfn31Gy05G9m/jxOMzOnsv86Cv9G+tRi64B8h+VfZiSslomk/EYM0J4a+1PkzfEE3l9hfskHM+K+fsG2ZhVMk0XIhROLdmCN2XKfzlYwjDAjpLmz0oKj1+8VsY+1saebCPE9vrZ+P5qe+Rf5vKuxcCeOo2sHNWFxbFqoyvHMAvsg/DZR+qWYXrUvyYsykrXdc/oNx+Cw8eSEWVgsLXN8LommJhs+9T0W+yxppmP17PUNkqIDkqHPa0aBCPYuR5anqwuJGJsX9k4kRSVq4t/QG95fkgz2ccbaKzYCZeTzGylcGJUZJXhSSvDLEKPVVCuJFN44R3fhzvoHFDez+06xp9E3xIeSpz3TmAh7IHfWQP9HY2+rokwSV7kFv24GlPmaP6SSg+TOe/pbzIK9z2WLit1Rwj+zodmAWF/plhbJe8OB/7CSeK5uLaf58jtUMuft/+OZZVsfLj2hTk/NPAVxFudL0kfTDDgdHtNRbu5Mfnaiq5KohOwqtthVePrpG+fBTC+dkq57sC2GqM5FVHHL7GWFm9ZjJ6j7OSeVJQLVrn2hgvrL8oTBku3r9VY51LPmwcL5yX3Y+ejRU2/D2MyjvEj88G8Eb4pJnwiSu/ytx3goiXfq0o/TrHotCZGkbMeoW174Tg/DGSeXbGYat4wtX/ecIJlTXWBpBdeu/tlSDObZK5vS59OkLyvbj4QDYj871zIvgliubaX2FrojF5tB83k7JxbOl3aLZV8umSC8Vvmnh+rB2HYiRPa8q9VpF7XRdEUpLC0qVDONgjgvt/SoBNfCBFfMC2XDz2vR211ilcdj+Est+s3F8tGfmHZeO9Uu+xuqnOpyO8KDhf580UDzrL+21R0IcjT63c3jkZG9MM/K+IC1fbmji5ayYuT5GeMbjxrl40m86OwfsEnT/94AE+q5xTL4CPE3Q2yupFudMWHliZCnsrhQV7hZEzZxRfvfwG82cjM+s5cG5QBBeUT4D1mPT0Jh++l/k+K/PdXJ5jaJkQPt2PJvt9wYxzZgYXpEM9nofhDY9wb7rO/iEPrm/XOemcB2MtKvOnBVHuaA4e3PIKF8SXZju+otkwE4eWykRQuNwoXP59dYVnV4aRp2w0Hx+QvZV8fyz5vr2lwgt9wlgWY+HHmqlQaupUl3lx7onke9cQir0w80K7dPgV4b1QEIuHKvxWOow+syJZ2xsLRc5J6ZqMhWka44v4cPSY5PPGALzdjIxuKn673cQfztlRuYiBmy+5sd5j4sNoO5rtNHH4GTvy/mXlw4xknHxj4OpmLpxoIHs12Y8SAyL4T5UEXJc5HC9zuPmV3E9LD/q804gmcl6VhZM2xGDiOzOjmqTLeWs8tNKHuiU1zj/hR+nKZh7ekIGmNpVjkoIoZrXxQkoSTr6I4Lq28bB+lrmoZ8fGhkb+96cTiYrwZ8iJOT8Y6NjtxspdBr496UKngUZ2qOTEy2FmtiqZga21hesXhLFrgYGnEl2o+oPCtXvCWPiXwoT0EGYVVui7HEZCjgiWf5WA7j10Nm3sxdYqGi+v9ePQcJV7SgSxTz7v8G7xmHwae9zz46GQSK+ZKagq5/23nHegp05bfS/KnFN4cGEIJYw2nnYmIfjAQmOfVLgiFGaPD2OEeH+Jj2Fc90suWXyILRLN6hdj0DFGOL2W8PT/z92Ti8xcG5uOX2S+R8p8nzmrc+UiDx6MsrJfoRR0+lNhu4gw1g0y8375DEy7I/0zNIDXqfLcxdJxQbh6jnB15+IqW58V3umoMH+7MLytJed/S0O4hXh8vzASW5lYrlcmWsr+9pb9/SO/hVnvpqFfXuGUh35Y96pMPRLAzQoGjtnqxqhC2Vn42ltcWyL8/iUAHjBw9j4XOoiHdREPK3PHyCNDHTC/1pneQvazn5VtaqSgXF+N+2r5MUx8unQ+D34Wfhwn/Hh0n8odBwM4OcPKdYFkDA2rLKMGsMyq8lNyEJeWmDnrSzqGZhUf/JiG2r9ZuKhRGgq+NvFWCzvydRTuaudEuuRLUckXx3kDc89zoVHJvPzzxEMkzTCyXMCB8SV05jzjRd2aChcsDeMn2a8Jsl8fz0SwwbJ4mGQOHFfCUBapVOIDmPZCo9rOB4P0obdWCN26Ss4096O3cHdd4e55KcLhJXyotlLnmtceGNfY6HyciM5+le2sARwtIP51Mw3Dxe+KF/VjXicb01snYZhPni8igG11LLw4Lw1fxX9ri/+m1zey6FQn/ssu3v7Gi5edTGzbOhNtm6vsMjgIXNA4e47wkZxPlBLAMenLrSY/TOVVunYIjy6V3PoUQqGTKm+sCeBTWeGe/RnIejsXPw9/hg/rNf50x4fixXX+c9aLbX00XqgtfbhQYSBBvOeTytEIYFNdnTfmelE03crzhZNRIlrhmZgw7EdUFtwq971S4azXIdwRXx0hvjpmoI15K8lzZTOxzPtMdO+isEmLMGp2N3NZkwzcLZ6Fg//5jFri5YvFy1tW0Nlnmxf29ibm75iJQr9Zea1RCh5J3vS86EbFZSq3fpQ+zqLS+zmIuVnNdHzIwAqHzne5PXhe0MyO1zPwda6Rte0O2C9pLDDdh3avDOzSyoWPkcJfsUGUEe89FONEFq+Vn6KS4fpqZs7q6cjSRuOXHn50E55vLjyfXe79vdz7fsmzQ5JnYzdozHPbB82ZhcFcn5D5QOd3fTz48s3EelXtwBYr51xJxo/COburBLCnhcbj/STvJI+uSR79F5OHI2s+wuUcFk57lYY20ne/Sd8d6atwR80wNswy8rbHgcLlbbyyQ84tn4Wl76chqpjKb+eDSJW+/V4887tSCu8ek9xKV3iqUAjfjhtYc70L3i5GRv3qxITHCnP0CKFLf42tq/mxNyKKx+O/ocPdCLYfHA81JHNsCCDqQiS/zYnFwsNmJm9Px4guOou3EL8W7i0l3PvLQTOH70lHnl3Chad8OCY8tbFAJiLkfpM++1HpR7mPXUGMF97OKR6t3c7G8PB3+GYysprdiXzCCQ/XBfDrRZ2DZ3mw/icL749Pw6n9Glfv9/3f92meyye5vM3K9heS0dxu5rDv0nGicRQ3jPuGhH4ay1f3I/UXC0sOS8PAmmZWWJqBdQs0Pkr04bsrcg5TQjj8XM67fQCLeklvwQ+X8HHOdAdqDzNyUSknln3V+Lm69Jmcy5buIYxoE8miPeLwXHqjQ1079N0avSd8KH3UyGNbHDhdVeffq70YmdfK4g9T4Ja9yy57t0F48I7wYPtyOjvt86Jpbp0jnnoRXmOi/siOROmbH6VvNjaTPzckiMHlxIP2ZaDr0Qi23BKPLWkqrxaRzx9jYdfv0uCcbGNOUxJ6XrSwwaxUvJ8le+jxYXljlR/GBVFztC5e7EWtI1Yu3paMOnVsXDg/Cb+eVzhwfgjP7Bo75/dh1hQzvYYMNCigcdJ/ftSLtXFOlUTs/y2ShxrGIepKNn6b8g5dG0SxxZRvqHpPOHGgB3GHFFbeKd4+OprdC8agwsis3Fb0I0ofMvDYLhcqiy9uXBxCgf4q/6sWxB+ldGY95kW1wsLtV6QP5L6yyX0V9Bt4w+pC+Q0yx7dDeHpcYdf1wmfrVR6+G0C7CVHsnu0ban2xcGmdVNRaLs/13oNVxRS+OBfG7ecGjmrvQrcXBjZv58JUmQeDzMP+4zoPrBcu6hvNArVi0POz8Gu9EHb/qfKELYiqlTWu3eDHOYuZ81MzUKlQJDddi0PKEwOLd3UBHSz8q0Pa/83JPZmTgMlAm92Njm1Utu8RRGivhfrhVIz6XubhVBwqVVS4aUsY7Qvr7HzFi+HSyyUTwjgm3r7hSxrsY3UWyOtF2ZkKD/hDuDJE57SyXlRoJXvQOwh9rXjjQxeelLWx637xBcmpXJ+C2CA8fUd4uplwwJAGXoxYpbHES5n/nWaeO5OORT8aGbfTiaqbTVx7zY7q0merpM9ixUeqjffhz7+NtD13YGBLnZX7evG9V+WpKOndCRrHZfUDK02c89qOM1s0LrvsQ25bNj5Neo+d1Hl6uhcrJmbnu8i3KHE4G0/veIcvB02ss9uOIpLrlxnA2Ldm5v8lHasSzHz9QzruSA4O6+7HtH1G6occaK1J//v9eGmWvk5PgTpFesEoHNxb4ct6YUTuMTH+mB2jP2gs0siHkcIdxStkIG2AxmKV/Rgp3l1EvPv4Y5Xruv8/os4ySoqzi9ZV1d5juITgQQMBElz2xoIGDe4e3N3dXQb34O7u7u4O4y7tek++dde9v2Zl0jNT9dY5ez9PMjQ+fLiqsdFUD8Y+NDPnPykYlmhl0VIJaHXcyC4bM9AmXWH7/AFUe6Bx2QAPNgj/PdiQjoSSRpY9nAlDjJ6OPxxoHZWVnap8RIk5Ko/bfVgHPZ//9z4P1zVem+RBE5+Zg00pGH1dYZ5JAeQupPLlFT+SNhtY8qYNDc4KZywLYP8/oTxcXubmbx2LdRB/GKTxTikv7iw2cESMDTWn6ThXdaOgqvCmP4h+cl0V5LqKlc7B0wfegJuNnH4rA21za2z/zIum4uUDpwVweaqJM7R05Fih57tPDsyWc3Y2Ev6eH0JrShx271R4+mwAM4TP/cLn68VznmUEsVY49rlwbF5N4WNvEDaLnrmTnKh6R+GyEbJXc4U3MyXfl5rZ/EcK1hnNfJyRii7Cb427p6JMksI9JQKIlFx8Kbk4VXJZL7k8eb9K8wEftgqHXxMOvyz9OEf6sfv4MNaLiMbUISYai6XjvMXCRUnJePxCYb8OAVSOMHDtNzu+DTaTxVOxaLme8Z8dqP5V4xJ6UGCU5FpeP85tUrj0dgDZWxr4obsdd0/qOSLSgdPCu5G2RNybKfnuSYN7a3aGXX2LEqdUHlvjwxS5H5PczxmfwtXGAKY90DPY34GWb43s3iIDUesUVn4cgJIkny/hgPmZgeldbcgzRrwnpxchG41MvCe8ly8LH937gvG5DYx4Zsfk3XK/J3xIeGti2eZpqD4tCxerX1Blvo7LU13YNVjjyeJeLDpuZeKGBBz267lb78AIVxgLRERhTx4rTzxORGOf+JcpAyXEW0+Kt7Zpr7FdKy+afbFyQB3J2alWHtQSUd2s55JU4ZREA7eUsuHcEguXRSXhqPTVNukryxzxYLsHy1bn44/Xd/E2M5Qt88Sguvj6QvH1HPktfHcrGVfu6ThrsAt7nhp4spsNi4X7Yz4HcGpTCCNvx6Go5OX5tg7U+axwSj2ZG3muT/b7YJ5kYIbFjpY1hSvm+bFAeDZlaRDLuin8XjcI00SNthAvqt0O4ZJRcTDJvKd3dCPtscoCvYQHepsZWy0Vb0qobH3Uj1sXdBw/34XM9uIHLcUHKP023Yn0Xnr+XN0JU6bCzLzCe7nCOOR5NOIaGFlhZCb+GKHnpvxONA2XvPvhhrmInukXnJjo1jMkzIFEp8LSWQLQ39DRNcGF7W7hzVAPkqV/i2+xYYvww23hhzviZ6PEzxIm6VnS7MS+Q0Ye+jcDTYOyX2oAYY1NjBmSjtp1LJw+JRmVZE4inwSQZ4jCR8WCuJ7VwEkf7Xg7T8+WaQ4sympm/MdUHGiu8mAvP+KWGfn71wxckVyd3cuNjm3ysHn7JxitCuf5xTv/NfDEGRs+/Pf+1Q1S8XCogf8UsWPsVD1z6pxoKf7UU/wpPL+e0becyC2590Jyb4TkboEZbnwRzqqr+dColZlDu6ZCy6mj95V4xjwT56anIW2v5ONhDzrKnDaXOU1rLPw12I3ZB6307E7AmSiVKyr7UHCU7GteL3TCrT7h1tMeM1daUzCwv4FlytlR7xcjp5zJRO2+es6s5MSuNhpPt/Mil+zHK9kPx3Yds190wS4cmfu8H+d7hXFx9Wjo9Bq9Ti9+qqrn42VO3BNuHirc/ETmqLfZjkjhozfCR+6tKrNc9WG4fF2BGtGoMsbC1TmTsfxXld/2+nGnhJ6jjjpxtI+e2yo7MaG3kWHVxE8cFnbKnoTVl1W+melDymSVvxj9SG2kY6Ghbhz/bOTmuhnoL/ddfogb1XuZuKRGOnatkut/68HGeybeH5yGU3nCuPZxNGqGC1//SIa7pp6hc52YNEJ6Mb8dZZcaufNHBk4/l+vqlIRmY1T2z+lHS/GJHuITl1MVziwUwK2fdBxz342HIRYOjEvG+Mp6Rqxyor30UCvpobkndMxc5/rf+9EaIz3oWdRAnLNjqd7IaEcmeiXrWaOYA8mzFJZ0BTBf9iZN9uaXMB3PRbuxw63jxVAXBj+Qfhlgw5B4jSXLenB8p8aNZz0oEhXOy1W+46MiHh8IIq27eGlt4WedwpzuIJ7PDGdXz3fcfiC91D+Asc3kefX144+x0pvZ/Sg0K4xX3FGo1lR6YEAQsUlG/lE8A93vhLPuiO9QQnXip25kfW3ml79TME/4MfNnNxbf0Bgz3gOTVfIh0Y7QdTkZ/+QlroiHThcPPVg0F/edfYGaObJywZtPqNs7jJOrRcPskvuL8CHGE8qK1hhc1Exc6EnHrZs6jh3rwsMLJv4zPw0584Tw1ZN4tMimsdcHLyrmMXH9k3REyn28kfv4VEXln8v96L5HY91jHnx5ppFdPPj3vZFnm2QgdJCBMaXsWJBPZfJdPyrfCeWqETEYNl04KejBzDNGBpZnoEMzha36BvFdeKJGkQSs+BTGT/XlPIorXHIiiHDJtx+Sb6d6qlxb048NA1Q+LONHkdVmXnyTgq4bw9nk3ndsnW7mtUAKjpU3c/OWVLycp7BDWgDTpc8U6bPzSVYuLZ6AS/X1nDtWcuyLnv3r/H+ui36osFK/AOru0nPKKQdCrumYOMWF6sI7i/7w4khthTumBlHBouPmZDcGztZYxunBziYWXhyYjHHLDMz+zYahWeX+PnlRKUllZAkfZrwSrmoj5/WXiQ37paP8boXbTgRQ/LiFJzcmYbTsWe5jTnTLsLB+viQs2xLCb9fj0H6JgS2jbRjzSGOuvh6M2apj7msuZGmp59fuTlikr1O/OBCcq1LL8MGxwsScH9NQbIv0yw0blgsvfCsewNwWWWjv8QWDYjWWruBBuMXM6ORU7Bto4NHSdpxI0bi+iAf9zyssuygAx1uNOZpLryEHU2a9QcwLlRU7+DB/ouS91Qn8yMK51T7jl+UWXviSBGg6zvG6MeeShY7ZSTijSL4F/PC0tDKkWyIu5wvlnLux8EoPhEgP1P3v94yXeHE9SnK8sg3bGlp4fXgy5jn0zMzmwLtQhS1iZG9WW9j1dRLcwoFZbT5sua/jzYEuND+ksfcuDy6Zs3B+yheMrpGNeRd8wKJ/NMaW96JwT+mdWl58lbymcPIuyf3TPuG33WYOOZGC2GE6VijkxrOWKnt08yOjhcZ8Pbz4JLn/51k/fm9k4OZhdmxkCO9Oj8cy4fuvtcUHZe5VmfsCc428mZmBNet1fPlQcmWDQuv9AKpWMHD5ZjsaHVQ5bI8P6zdofHLfA+fPocx2OxalfzXy8L5MtBOe7CA8WWGcnluyOtHgruTXMBd+K6Bxxw3xevHEw+KJa6J0fFPZhbTOKgs28kP/q0zxviB4UDhij3BOF/GFBn70EH+vkz8ThQMRvKT7iv2trDzcJRF79xt5dH8GfoTqWTPGiTp7dJx+zIXO4xQ2yxrEyvQIfvz5K8bNkPP1+zA9u4nau3S87KBn++ZOHBwZwj0/xYsX6hgp3rQjm5EX32eiqPTGuWs+vM+j8K/HQSS91bNEcwc+fdDYsJEHZYZr3FPQi/aPQ9iqdxxOucMYGRqFZ+Ws7L4tEV0my+tMXqQ2E5/r48Qg4Z1SwjuvGoax3fBolHAbeDxM5kL8/Lr4+XKZ92/dnLjWXc/ptZ342kZl7XZ+3Mxm4fj3yejfTGO5vl48+t3AARvs2NxQOH+4F49zCec+D+Kvy3r+M8sBxz7Zj4NpmCG5oKYHcOiGgXvG25ClgJFfbmRiQpLG8BLiJeLD9466cfGrjgvgwhjx4jzixafGmRmZJRWzyuro/teNRUV0TDovOTBCYZn8QdRfpXDcuwDaLFTZPtGHYQ0UFhkVxIrOBn5pZMekJuKpg4QvRinskTeIraHiaTF+bF2h8fpHD/qWNLDyETsuPgjhgv5x2D/MxIOF0lF8kviEJYhDA1XuKi3z6jKzQHgKDPYQunLG4a70yYg+fpSbZeAutw1bHap4gA/nc6hc8tqPuUlhtBWPQjnh0Z3Co61cGrtFeNCghp5jFjilD0xMj0hDz60m1r6WhlY2hV1yB9A5WWWTYj4UKS17fiAZAblP3c9BuHZrzHLcA/tYPfNmc6JTRBgbf4vGzJtGesdm4HxR4d6zwqWldUw7IHOzQuGljwHYxRNziycWm6rxrM6LFdJrn2LcyF5K5btDflybonK63o8XY3Rsn8ON1INyfrsD8L830NDEhrbiH+3EP8q+03FvMxfO7tBz2XkHnq5W2eu1+Ib0vHG9C0k39Sw11oGkVzqWbONCFZnv1TLfF+9m54KhbzFd9k2RfcsqfP1RvKnqd5XLa/hwZrnGVZ89aPRM/KarA+0qGtl2bSasOgsT3MkoMkXjBYMXSTNlXj0BqKcMDK6xodx0lf8GfTjokLnP5sGVJRGcFfUVaaqRhXyZ+C75WMPjhq58OF1bf8CfTU/DeycG9zOyxB+ZaDxC5dD8fsTL8/5NnveAfSorHPLhuJzPejmfS5EK5z4PoPpi6aFYD+KFY8oKx1x5ZuSsLhnCxUa+KJaJ3s9DWaVzDCyLI5gS+xWNt4rvX/Pgt9ZyXZ29eD0hnG1Cf+A38YB/j/nxU1sdH7V1ox+EA2bFYvmJ//7/ZRIOXNa4f6YHYdOFj4IZeDBe5eBwP1yS/1kk//d20vNIEyeuDzZwcnE7cginvhNO/drCylo9EvHisY5de7lQr7/KieX8KDw6hJdzxyPLIeGXXSkoui+EFw7FYXNbjXfbeBE5VuHzbMJN44Qfs7ixYrOen2468FI4pWO4D6nZQ1jkbTxONlK5ZqgfJV8rPPK3ePBfer7oJ/nRTmXpv/1oIn47JDkZ+okmukLSMb+dkamtMzFf5idF5mfZ5Bz8YXyD3aMieCrPN0xMMzGkYBrCJRd/SC523yBc/8CHnz4Z+PBPG9b/rOeT2068aGBml5GpuFDIzAWXU9FL8rnGGD+y3df4cZAH7ztr/KuRFxN7KAxlENVqKlw+L4g7vnCONn5H57ziGY9SUTM2nPMrfMegZkaW6puJTX8aeXt0Js5f1HPxXAfGjbcwR0Qy9mcN4YGP8TBKj9ilR66Ib03/7sbH6iY2WpiODZ3kuTVxo1g+A8/es+PhejP7PUxB6e4GHqxjR+0cVk5/k4jBsoelzwXRe6mO1X+4UOiVxittpdelZ5KkZw7l0HHXazfKHQzj9j1RaNHfyl7lErFEzjEqKRn5hMseCJeVjzRw2wsbRtRSWWiOHysL6PjluszlYxMX9ErDhrU63n/qwmPhoV4OD36ReTgn83B/soGDjHaMzC/edMsN/WwDnU4b0oZpLFTIixz//f1O0kPjYo3MXiEDX4rrWf+4E3/v1rPjCfHxLyo71fEh73iFDyOCOPmT8Of9dLR4o7B7K+H0K5Jb09PwVXyvnvje8uHiwwWCKLTTyCvnMjCguoW/LkrGkOLC6ye82DZAx2tl3HghfN5N+NwmuZlHctN2JhdzLX+Oh1/EA2vb4K1voGWsHROE+8KF++YOiKCtzDfMnGSm15yKIkMVXiwqeV9Y4dhLksutQrmrSyzqJxs4rpgNFXfqGXlO5tetsGNoAAgonKMLoJbwwOwHHlyMs3Lubwlopars4vdjlBrGfP5oVOutcUk1L/6U5zhWnuOJ27L/I13YIB8fyMcX5a3ssCURzh0mZj+fhnVdDHzawI5//9J4pp8XH6/o2XC6AzlGm/gudzrWHTfz8YYULPnVwpi9yfh4TWPjyR7suCbXPzmAtiYd26a7Mb68wqxbgzgXrePSii6MCrMwb3QyvIkaQ0p6kNpOz0KtnXgWZ2KP39LQ76Cef+xx4P4BMwfuTYF9o4E57trweYOJ9e+nIeWdmUX/SkGbTHmOeR0oZwjjLns0ivtVnjD4sGq+gW9TbJi4SMfQeBeeCXd3F+7eOMfIe7YMtKilseccL2ZX0tO52okFLy1MbpeEfM8NvNvZhs3QeGOmF6tl3t7LvFXPHcLFzyQfuhq5ob748msLp/0tfprVwDf//XeCUhqbHfSiU38j/yqXiY0GI+/aMjFol8pSp3zo+rNw3x0/aggXLijsRn/xuPLBIPqVlj4+YMfnvAb++dAONYtw/JcgOqw2sMUbGyLF01/9ZkfnsyFsvCwOr+Mi2Pa3r0CiytmlfNj3VuHR5gE0j9Sx3wsXRgrH5rf7sH+xjodjXLgbo3Dk7wFYS5uYfCAdvUeGsVq+aOydY+IRWxqGXZA8XOBCb3k+1eX5tDypsNvaANZuM/D5ZRuCCwtQn3AdLZ/o2bOnA07ptWxDXYgVTv1dODXrTRM/j0tDM+H9PsL7P5usvJeeiAMTVO4PFX86qeeFtQ48Gih8VNqJ99/NbFI9BSFlxHv2CHd10vi1iRcNDqscvdOHXZfCeWr2dxycrnBXIICYnSornZX9bK3wUecgTn0V34EPvYWzahxKQ56fVD59IPN9W+XPo3zYcVnhuZkBWG7omDLBhVoGlfPsfmzfGc4rZ79jft5wpj78gWPCy1uEl5MWS8/GBISv9Pw1vxNdp+jY0CC8tUbPMq8cmPpJo/5PD4zRkpOVfJh4V/J8WBoqFpE5uJCJ+nqV451+HDxq4p4taSj3wch/G2fA94ueltNO3Dpi4bhtSaiW18jljzKxV/z1qPhr8keFxRsEMOKJgQV72qRXNdbo7MGUr+E08Dv6DJE8LeZGpl3lTzl8KDvDyN3+DNSR10+X1w/rE8pCVWJxLruFS98mY99/71NbPYD3Ra3862wibEuMzB2VgUH5xB/vefFqpo4dPC4ce6hyYz8fru5TOO1gQPrMzKHy3M5tUbn0ug/21jrm6uRGYI5GzeZB4GgIjVvicCFc5eLvfvzyWOWZ3j4UKGDijRvp6FlBz9qbnSg3WOPOEl48PWRi93/T8ChFnncRB6bN0qhze1C1vcKVrYI4GGvi7vJpKFXOwCPb7aieXcfF74QP0jVWyO/Bo2k69lVkLuoLJ4xNRssbMncTHCit13jQ6UWj3UYOP5GBE9UN3LBQekl6uX9r8ZZGFrqHJaO2eMXMFk6cWh3CNa/jkPtYGF9sjsL+VI2HCkkvbdQz7Z4DxddbeepRAgYNsLJU2URceGng4nY29PhXT54Rz9gnc3DQhXn79LQddGDhEx3je7iw6n42fhz4Hv/0FY+vFMSLWxa2H52EV5Ea2z/34ECxEO4/GY+Hg8P4T/FovAm1sHWMeHtxC7OdEG6YF0p3egxmLA2h73sczOM1ZkR40cKmZ+/cDtyrbOSQlZloYVLYKz2IE7PleTl8aF5czuWE8J34iuuDGwvu6Jg83IU/LSaOSU5HjxY61urpRsNfrBx+Rnq5vOTOVj+KtxGuby+cHGdgajkbFh1RGb/Nh1KKgYeDdiwWb4gXb4jqq7JKRT+6nFHYeEUA/wqHntF78W9hjecvepH/j3DeWv8DR+vpuWm8E023m9j/Uhp8MzVaPB6UmidckG7D9zZW1mgnPt5bz1lVnSjQROPtQV4obYVj29qRHKdn8d9kr95YqW+VgIb9zRxZLhVPb+nZfbQD1d+buLhpGv5+b2TnphlYF6vjk/Iu3G9j4qB26Wjk1HFoVnkOr6180zoByhEdg9tdeK4a2MVnR3bpu/fSd+MGhDBb2Xg0ET8fNMONER00Fm4heyBe1DZPEJ3eqGzWyodL8vPmNEnDNPk6Vb5uu3DGZeGMkL91TOjoxmt3hPDsV5RvqOe2EU5czWfh9LvJ0FZlo/fde5Ser/BgagCpe1UWPuzDIpmrhIcJGL5OeOSJC+WWiG9Hi6/eD+G+QXFYP9zMhwVSsbepyuP9/Rg0W8dSThdOz9dxdYr0ok14IbfseyGNu6540VT8ZNCeAK6PtXJitkQsmmhhvDUZzRpZ2XdYImpsMHL+gwxsTNTzfknhmcoGdlxlx3vhmWbhQWy0GngvwQ5/go6m0i7kknx/KfneMEzPUdFOTL5rpHFYBg4kWrivZBLW1Yng08nf0KCz5HFjP/Z9DOOhBlHI96t8/31OFL+o8NS8AE521Lj+Ly/m91GZWtkPralG/wAvntDKPtMT8aG8kU23ZGLgPyp/reBHFum/r9J/vxbT88ApJ0pFZOXBb59gv6kx11gPPoWJr0d5kaecxufbvej10ET8k4bSl4VTZ8Uh/icdf7vvxr/TFJ5RgiguHnXyiwdVIjSu/iacN97MVxGpyPBozG+VPReeTBaezJmo8FWpAC5WUTh/eRCTb1tpHZUAr5y3Rc57UxE975yX/v9XePW0B52fWNm0ZwI2dhJOaJKJonOkv+wODHxm5q9dUzDki+R2nQDaPdKzVV8Hnoi39BJvebojJ3uef4nfumbl7j8/oUAnhbeaBnGli44zGwiXWhV6EoPYttXMK1dT0FPuu+Y4D1aKv37a70aHTSpb3vGh4WDZhxKp2O+TuTJKb3/LwqRan/FHZj5uynsX02ZYqPmTsH6hwicJAVSZbeZyZwpubzJw1G0bmna0sn8z2X+I983KxHCrjgUT3aj5j4HzKtjxpbCJdS+m41AhK/deTkRQvFXz+XDspsLNYwNwiw+Gig+6P6oMb+DDqGHi34WdKF9M47ZTXpTUWXnCnYgN9YTbxwfxSXiz4X0PEn5WWOZ2EGoHCwMtknFOb+IyRzqOn7RwU2QSHvcysnf1TDz7YGK3RmlIGGBimTLp//t7DocccqDxc4VDOgfQKbvKJu/82FtS4eHDQczJUOj6KYAxS/XM/cOBHvP1rJPqQN8RGivn96JKcwuX9U6G86mZ2bqlYO3vOr7c4MYk4XRrwSDa1tWz/STh/Ymh7BQSi/fCR3/ZfHgpz7u9PO+VL8SvOngwfJzwZ1Ynum8KZZ07MShUVCd878bGLxof1PbgzAo9V36U3Gqtp9rZiVfSF22lLxbGmJn4ewrSRxlZME8mqq9RueSVDxdzyF6+zoQjj8acj73oKD3VTPyxnUNl2+w+pNk1FszhQci0vExSH6HCsDBuLRyNGe3M9LdOxeH8Bu6+acdEVWGYP4ieFxRigfBCHgPzP7HDAvHNWXaMaBnKgt1j8VS8sAeCGNNQY+4RXhz7pHJbfR8eyL4NOunEtUyN0/J4sKCYyuSTfsyVvLBJXrh/GJmlagamRcocvJDr3ifXfTADuux6et45MeX//v28lVMVrikcwIz6JvrGpqNOjMoZf/gw5KCVJfckwJ6iMW8RD0qLz+8Xn28zQGWHsn6Ul97eJr39pJiFvU4m44nMYffydtj6hTLX77HI6KvjTxXdaHVaz26rHHgxU3zM40MHOecWcs7Pr4Swy/Q4XFuncupj8fQsKot+9qOC8OHWYACf8qps8FD6P1mho1gA4/trjCgnc7pY5eFYH74/ysqafT6itJzDQcmdgw1M3DsyHTv7Gnm+Yib6V1dZYZEfkzpaGNosGRWzGbn+Qyaccn5Z5PzyhYTwXnw88q+THnviwey9FjoPJyFmocrK//255rE6ts7uxhLhoNhNThT7Xc8zG514+9zCZp2SEFXZxGor01H0ksrzc2S/Jxk50pKJtb9Y+Ox0Mq64LZwRmoTPHXSs39yNdmP0/DunE7/VUbhjShBz5Zxtv/vQt7nGKr29GP9BY5bGHnybbyFTknAsycitxTMw1WCm0Z4qvSi+2zqAj7uMbHwyA0tbGhndTTi1tXBqZz/aPjOyfdcM/B0q5xobj3Dxkx/iJ3d6ZuPomh9QYoGOx5NdOD9Z4RKj5FekwpnPA4hbKHyWIF53R+Gl4QE8GxTKbiVjseup8Ha3AIrIHl+RPTbNDGeG9zvKmVTuSBf+uCt9MSyA0Opmxi5KhWemnmEeB1qU07HPdjcKSq7ekFxdV1vlk6l+hD8x8UfPNGSsNjP/6xT0PmVltTUJqJLTzBWvUlG1qonLl6XjdadwtmnyA8+bKOwyMIjI9ln5quUn7Dlp5vHIFExqoKN5lBtXRguH5HJi92q5jtcB9NihsdYFD0rqxZucQSxpqTG2m3htM/HWPl78utDIfYkZuJA1Oxd9fIcRj2W/eseg3lsrp7RIQM9MK2vnTcDFIdInv0ienzHz6vIU5PZJb5mElxebqI9Jw9Liev447kST8QYOjrBj51YTL15Nw4pcKj8998ORM5TZX8bC/F7PzKYO1C1t5dQDieL/IfSbhVvuKxw6KIA54t0O8e74nQrLnQ1g7BjJk5xe1HoQwTkDviImEM4/tO+YK/dnl/urIPuz2evDNTn3KUMD6CNzWjXRhzZfrOxcJwG5Rhr5Ml8mkp9oLN7Dg7ziaY/D/KhZROPCC15kH6TyXSk/zrTRc3k7J7IX1fj2nBejFst+x3rwo6OJ1f9Kx3B5voX+e76lVJ465MeRAjruvO5GkSYGXhxkx5RKRprXZKK3XEd1ef7/tjDwdA87zt+xcOHwJNxL1nHwLy7ceqTj2D4uOF/qmbWdA70OiwftlD3LYaDztR19Xmus/rcH+15pPNLGgyUzZM98DuRcpuPbby48OKrngC0OLOoi/d3Ai8bSD4OkHyJXKnzxPoB3myxscTsJU3eH0ngiBgOzWVnmQyK84/W0hjv/9+fkGgofD7tuYKFJNnQKMbBJvB25R6h8nl9y5YqJ7hlpaDrXwv6ZSWgYonJMvB9zosLorBwFZ7yF2csk4foQmY9fEhH9RnqppQdHdpi583wKyubVc+cj6c/jwo8bHQiKP+vFn3k8jLM2RmHCCR3D17uQPlpjgVxetH5lYOe2NpSW3tufL4CqsodLTUGUdghfZ7cBulDOcMdiRDWFBZcIz1iFZxIT0XB0GEfmjsaTyjr2XOnG5x9m1q2agu1ZFF75HMSxWzpuHe2Cr4VKSw8/jolPbDruxsi3Cn9uEcCUthqNbb24+7OJQ2+nI/OYjnk2uVA9p076zI3VcxW+zwhgqfRRjC+IKp8NXFXPhqR6GkuM9+LNNiv/vpyA9s9MbNU1De0fyX72/Y5llWSfV6djYS4z45+nIqyKnjErnEjw6VjG6EJe8dkn4rNdt+jY4IYLB69buWdiAtYNVfi8SBDPK1jZdVMixiepjCjhw8Js8pzfe5EzJTdfFX2KBf8YmVw+Ew10Fo52J6P/GIUVcgbRTbzjT/GOggc1XtnjQTnh4Z0RQYzrZmCWenYUFm66LNz05rPKVnUl51NMvF80DaNlbvJ+CCC+pJVlDydi8gCNprJenC8jfrjbjiqrdVz1xoXYPvJzKgdxRrhlTacAms4I5QB/DAb/aWXpMYmIrWjm75GpWHBEYfK2AIK99NSqO/FBeLGZ8GLbqBC2qxKHXfM0nk7zoEhVIy8vy8TFvBrnP/TiRUUDO0TaUWmpwvU/ZI6Ft1oKb43tYGaOFqn4JZeeZ184sb6iiY8i0/HlsIF1d9jQ556eVYc4kPeNwsetAljlFb4xezCvtML0/UGEVBSvWevGtfZmTm2ZipPNVa7p5Ye/nnjA+Ezs+8PCw+uSMf1aGIOTozC7oErXVT8OFjFw73k7Fr2TvPzLg7sTdBwW6kZkcY0vj3tRb6LKySF+fJkYxjrWaNQoa+KCXemov9LMiR9SoA7R0V/MjTbuMHYIi8KwLzoWrePC8mEqvxfyw2DS05nuxAGLjgeS3HgpXNpBuDRfBZUPNsu+XTDTPj8FlrUqU54Jd9yW/BrlQd8PYazcOAojHgj3D/BgYm/hpWpBPLincuBgH7ybpAduu7DhvZmPmqTAXktlrtl+LJDrTpTrflFbYaepQWSdHMYvpmikuIwsFp6BxuKvw6o5UT9J4fgSAZjG6Zia1Y1V0PPdTCdMv+Zg5r43uCOcMEw4YYz4X84jPtR3y+vDApg+x0LFnoSr4tUzxKsvRSmcUzmAtY+MfNYnA0dOGrgj0oZiM8TP/QHclOudKNe7XrjlSYwPR6uYuGV5OrZuV3nzog8TcijM+iaIClOFe3TiP4X0PHXFiaPCk9uz+dDiucrunX2IKGXi90Pp6FFFR65wI6tw05cnPmy7HcorI2NQ/7zGCYs88JaxMnR3Ih68MHNghxTsvabnsckO3Hph4NgONoxbE8Isr+JwcbeOC4+7MOiqytLTfFj9PYRvqsdh3naNGRc9yJUQwue/xsE5WXLQmIxNRjPvZqTi1R7h66Me/PFD48ZqHqyor/LzWD8uSF8vkb4+3c/MyN9TsU7247Hsx/hZGrO6PQgVnojtHsCA3838bWMq1grHPysQxMiaKvPN88veCn845fWTsjCL5Qv6FtSz4jUn/jyWg+M2v8aKMxo/LfdgfdMQPukf/7+/h634eBeud1Y4rVEQWerq+HmSG7/Jue645MNo4bq8wnXTpN806bc6R4STt3sQfsPEqAlpcAif5hQ+vTlB48RQL14bdPzb5kbR5Xpe/OLAzWIKx50MIvkfM4uVT0UdeX5ThDs7iJe3FC8fVt7ColuTUXmM7FlOP9yrJMfe+vDQpHBAWhCDO6os1cyPxAgzy3xNxY9xBlbPYsfWnhpv1fTCUzwLQ45/gdMtvR/qQYUtGrfc8KDRHoUjjknfS39sk/6IFP58IfzZbbjG+gW9aCa93Ud6+2+Zuw5VAvB3DaGxfjx2H1N4clMAke30fNXaiTmTVDrMfpRqL7zcyg/PbQNDRtpwVjxquXjUuEEac5Tyoqv0egPp9XtfVQ6HD1WOaly11YP6ksfjP3ixX7FyfyARK8+Lzy504VIPlXMhe3ZJYcrsACouNXPdD+Gk4/I8NwZQO016rGAMdvwuz3NDEJXCQrkmOhZ3Clg48noy7EH5uYoHTRfr2T/Wge1/h/JSh1isXWvg86c2aF4dfRYXhotfFX7ixXrVyie+RLilN8KlN3YmaLxU2oO/p+rYRedGrjk6vrC7YBkRwpT88ZhaS/J4juTxaiMbvc7Aoo4GJvxlR7N/FParINy9IpTpH2PQeL3s9yMf1r4J49OWUdAijfS9yIBB5touc90yKN6lBvBBOLuRcPbZtjoubeNG8fcWnmiahJ+z5OLtLy/wJUTHP+PcuDBBz4WhTuRYb+DLRzYM0cJYwhsN9tM4+w8vrqTk5MwiLzG8cRgLDInGTHM4fSk/YOqqY+afbmyQ7/NQvk+vcsJF270oVM3IG0sy0bmJlU0GJaJLQYVNr0k+9Ing8SrfkNw5K4s1+oQue+XfH0nAhjomPpycjor/6rj+jAtxj/Us28uB+TtCmXo+BkuNekZlOFFqjYWHXiVhYxk97+524qP4bgPx3WtFFU49G8SWvwy80c+OYsJLJ4SXvkre1bEl4UJFI+dFZqLqLxqXnPFiiPBByQleLLQqTEwI4sCvGvfv9aJ5UQN7n7PjbM0sXDH3C/JGG/m4UgYqH9BxzT4XFsjHpL0uTBEO1L+xY5LwYWh7B+5vDeOgq1GI+M3IqJ2ZKDxC4+X88v0lnxKn+nBjp4GTztowUf45RPIqWnKnkuTO8SqyN8u9qFtV46RlXlyjxinTZT/CLOwSnYxC7RVebRWEZZCBaaXsGCh8W1b49ndTCDemx+PFQQs77U7CJCgMmRXEqktGvp+dgdvrFI56HMBw4YBCnQOoHK/n2rIO9F6pseYHD5oMi+CQwt/w4omeHXs4cFDmcY/mRpWVeq764MC+mQoPewIo3tPE47XSkb5ZOPSmDb+XVLj5SBAFZ2fnLedb3H4ie9fDh8UrVMZ89CFmi5UVryfgbDkjl23LRL2YEI7/Iw6dKir8a20QT+xm9syRgl9uhPLchBjkuSg5PS8O/VOMLFc0AwODBpZRbdJDKrdF+lBS1fGo340+Q42sVjQTaeIz+Y0ePLPr2COH5L/4dSnx694/Cc8/EF7urGehxk4MCJhYRpeGMaEa88R6ccmucW4Oz//ehyXbGweaVdOx7xLh0V1hrHsyCunSV/mlr2bYw+jPEYUGu0wcfSoN9UaYOS5/KnbqDDzrsqOMI4IHsn/FAdm/fQ99iBefLSM+6zRnY9aUD2hUycyha1JRL7eRE59l4nAx6dOTdszdbGXGzQTU2aFw0oUAci+XvPkSQL4cVt5/k4j0tnr+3MaJesPl6wpmwn5fx1wDxXfHKlyULYj1Nisf5UpAyV/0PH7Gidv3DBwz2Ibp3VWqdfz/j1tOPFa5rpcPndqKX7RNxj+7jSx/IgO9poYRumhsbKzng8Hy9dWsHL04EU/E33v18mJ4IR0LXnEjdILC+LAgfhqg8FHZIHLJz3k5xIYO/73vwxgPoheYWDEpDY3fhnFIiyjkyBHOt29+oN8/Jv5RIR2/9dJxVw03Ws/X2DnVg24bjax7LwOfaxrZcG4mfJX0NK92YvAGlSUf+IRDDMIhdgyJ07NkOQdqmixckJ6M84MULi4ZRJvGBrYfYkc2g8qPdvEeyWun5HXeRCOflpL7KqonzjmxsWc4H9b8geg+RlaqnIlu3RT+WS+IUi30PNJTeKmBwgmjgnDI67OfdUIv+eyTfK7l1DgnqwcdI4xs/k2u75OZ5vopaC/e0Vq840x7HVe0dOPpGSN7LM+AK1VjlkIepObJxkKPP2CAnNdvcl6F/SqvGIQLe1p5r1Yifq+o55a1TsyrojJtuf9/v/fV1ORFqakmHtGlo+e/EcSZr9j61sBrzW0wBhRm6ALoKJzfTDj/vlflULMP+24oPDQ+gNWDjXxTPBOFbpp5fVwKbFPEB/RBvLhkYKfZNkwTXtb19uNxbyP7Vs3EY5mT3qed+GuXmQNOpaDYH2E8tz4amW4T84amIfRMKONXxCDQMoSGbvE49uq/95fz4c5ayY2nASTJ9ysp/P21h4614Ua4+HmU+LljoPBkacmttWZuepqCpEI5WOLyG+wWzj8tnB8iPJgsPHioq4G76ttRa2gWzi36BXuFm48KN1+oZOD81XZsGyD9UCYTy7dZ+O1yEn45ZOS5XRmYec7M4OIUpEy18BctGTGSG79HBrFYuDNOuPPoTB23eFx4d8vEZqPToK5R6H8VwJk/hQtGu5FNMfFDMB0T6kjfThG/bK7jxl5ubFqr452nLtS5INy7wIdGqoEj/XaUHatxT3YvtjoNvJ5F/PaRngf7OvDgo/h7A/H1Cxa+np8k+SAenM2DUT6F+UwBWPabmXogBfPu6pkx1IFOASOb6jJwtoSBy4/aUXKGxuN+D8KeWBjXMwkJHSS3m/txT3JuRJEMdBRPbiaevFd44bjZhfynwnh7TRRq75d9OGDDuws6tprvQr+Axko6D2qk6biooHiySzgo3IaOgw1sXsKONasVvn4dgL6dQuff4gtVFH5eEUStmRbO9SahUjnhu+1+LFf1/ORzIjbBwN9L25AzzMhX0Zk4N1jlkuLiWRYDDUl2lIpUePBFAFssYbyZFI3Z5fX0bHHixUId2ye4EBhuor5AOg5WzM69ke9wb6P4yF0H3v+rZ7PTDoyfaGBEiB1Vx+i4OKcb71vL5zs5kdRAx5Ij3bBtMzD3ZRuee1R2tfqwI284Lz78gQ79FLb+I4g/32gc08qDm7kMnPjcjke+XPzH+Byh+QyMuWdH20gDO76w4XqEjlO/Sl61tzChVTL2HDPz+KYUfJO8rxXjBVIsnF00CQ2uqxw1yYfdkqOnJUdP/GTguvt2ZE028lOxDJSdrHKXSfywuYF1etkRPjwbowp+QMk2Vh5vn4jK4rdrJgQx6hcj85zJRLbqCt8vEl8dKb2ez49FOcVbX7rxsyOEd7OLf7yzMOOvJPx0SeOTOR6s2id9fDADx5oIDw90Y5b4hN8l/PhC44COHlScHM61ph/oWc7KmtsTMTGvjtZHbpRIysaTJd6Dp4ycuUb6/I3K0S192JRDzzuvnehj0FjV7kUT4e1hwtt3soZw1Md4LKXKH9P96NNFZfWGfpzJaeLyl+koYQjlUXssCsneXBmThuA/4tHlonFunJErsmQi2mZilVxp2FLEyJvnMzFvlJ7peZwoKDx5VXgyr3DwQ+Hg8DpGxk7JxP5zsieLHVBqGBhcYEffInpWvuBES/GyXidc+LQ2O/98+hbb66i8PNmPgm/0vNHKgTPSDyur//d+nRFs+fgbuo8NZZ3ssciz0MxniSn4qCps5AsivZCRBS5nIqKVjt+7urHzgJkX9qZgbah4akwmLt9WOHdkAOUqG7hzlR3Vn8i89XShkdXE4YnpaBZjZL8/MrC7m4Gn6tpR12DiNHs6vlU0EJHynDM0fs/nwTm5rqUtHZhz0ET77jTsqmjl6chERBfWWPmiFz3XG1nzUQbaD9OzdWEnbjyQ/u4fwEbp2fvSs3vkPI5NFs79LBxbLwFdP8v11wtgShYDjV/sOFZZ49aVXlz+onJmbR/CLXrGJDuxSXj43tUgZhl1dGW4YY3TmFLOg/NX9Vw21YEqcUauLJeBuykKhxaRfPtJuOy+E9m3Wfn+SgIm63U0O93Y8VbOpXkKNup0vOuS/SquMNfxIO421nHoYDfOinctL+CFTfgit/DFxnjppzIJCHh01FtdKFNU5e5zfvT1K6xiCOCX0nqePeBEkQ8qzzf2ob5ez3FOJ8JP6Rm9xoHRO3XMc86Fhr/pOXKnfL++wuEVnZi4SGFovPhxSCjLxMfi1fgwtguPRqfSKhsf8KOe9Ow46VmzzGfyDD8uvszC+e0+I/sK2adPAagPQxj4Jw7NChjZ50YmOuz8730cfLhVX88JY504+VbPDc0dyHFc49uNHvRaqGeNRPGDCyamz09Dcb+OJwwupC0V3v0eQFwXHcs3cKN9I42thnlRKauBkZ/sOF5ZenmlE2+eh7JNpxg83SD8c194Ur6um3yduaz09S4/tm8X/7sYkOs2cnzWDLRaY2b3VykYqmos7vfC3kJhjh5B7JBev2j0Qr0iPTQjgB7nsrLOko+4fsDISXszMPQnM4s+SEWLlgb26m5H5yr//R6DcMlPETzy4BuqbTZyya0MfB1jInKky7krvNw4AJ9BT6PNiVwRIXz5LR5lGpq4b4TscUmVJ4740X+6cEcwgKbicQPF46IGaqxW2ovbQzWOKuLF1BMWauuTsFN85MKOTIxOF57M70KLk+I3a21oLdzepacP2ivJo7YeNJho4aiQZJyU61u3PIjcb4183iIDr6Wn2i4O4tsyhfgaQI1HGhf39aBniEbEe9G4g57DW0ivjFf5e7gfhW3iK7kDCJTU0XDYjW3iK5fShNsu6ema40Ae8bDnazMR9j2EMTXi0E1y+k/J6Tl/yv6NToftb5V5OvgRqugYFxTuGqGy+8/Ctz+rLHTHD7PkZrrk5qPHCvv2CqDlRHkuIal42kVjjwZejLdZmDV3Ekpfk3ya4kDZ6zrunuTCgMniF6ZMuI/rGbbBIf5s5CDZ6woGHbfa3QhpZWRcV/FDue/lct8nMo1cnycDr2or7Dg1iELlTby6NR3ZRgmH5pV9EQ6q/Kvcr07uz+3H8CtGFpqRge49TKzNdIzoZ2KBP9JRqoGVh0clYtXPCt/dDsJ6TvZ8iQefl5hZL0r4IUFlemkfHiyX6/mcgeXZFX59K9w7RmG7nEE0+1NjvzFezPwsz6muB4WOGXlpcwamjlZpyi2818XK3g0S8eIn6ef7btj/7/uBNOgmvVfPj6y5TPz0Ih2nG5i4amQ6NpQWz96fjqpWA1ck2tHub4V/dwwi/06Fd88FcLi0nNd+N0ZX05h3iRdlhGN3C8cmVLDw103JaCg8MUJ44lhLPTd3c+JbiJ4144QPZB46xnnx+0vxw/YBRE0VH9OC6HNAZZV9PjTMb+TIW5n4cEzPJpsc/+/9aNoNVdmmqB9P5WOPIn4s6K0ytaof/SRPKkmeWMqYmLonHcipcParIFxfLMxWOwlXGmucPtiLxLl6lspwYGIHAy0t7LhTSs8RB51o2zOEbWrFo04XA6c2tCNtmI6FC0n/VlLpWe1HtDyHyos9OL5N48bLHkQ2COXLkbHoEaaybrQfOvk53jrCbZMtLGtMRr0tYZx4IwrKzVBq42PQYoqZvQ2psM0y8CeXDYNbqCze04+XdxV2GBrAt7+kZ/rZke+b5HytAMp/MHJb4wx4hOtDhesrb7Vw5bUkjO6u8Kc6QTyco+dAmwPRr3Ss1MYFlzz3LKO9iKpnZJXxmSgvXr9jTgYOt9S4s5sXK2WfPs524LvwZC3hydY+HTubXHCKt2Y9mQa2VTijbRAn5+i41ubCy48qOzbwIe8CjY+TPZiSHkpT/hh8FY+oIx4RX8fAcpPtyDkrjK/dUdj+j5VXyydiwSGFaf8GMMYt/RHmgG6fRu8hD67I52fK54uUV3hxaxC78is8czOIfBNNvB+Sjj4lDaxyxI4EUzjLpv1ADvGb1+I3ZvHuNPFutauJvj/T0Vt4ufoxO5psUTnwhg+PnxnYr4sNnrUqQ576MGyJlYWjE7D+aigfTY1BiuRiccnFDU3NvN8/FWdvKlwxNoCt4i83xF86NRDPGuXFsAcWFh6QhFoDrJxTNhFx8vpy8vqtk828ZkxF+5AQtoyPx9UOEZzV/BtaiE/2Ep9UknRUS7pQvITCk8fkfoYqfFg0iFsTVE4I9WPQNwN/rWWDvr5Cz7ggPgh//LVS+CebjlU/uFEZ0tOznJj5eyh9G2IxKhjCfGocqncP5eI6sbgULb5Z0YOndjN75UhBhbVWbnqWgGcRJvb8mo4T4h9r+zhQX/ZjYnoAbvHf0MF2ZIj3/CTeE9PMwD/62HH9vcpJTXyIFT+vONiG03Jea+S8Hpl07JvmxuRoeb6VYtBoUgEOtdyAuafsVS0/9r5UeaydD1ujdbxR0YXFD82M6ZeCI80t3NErGVsU4bSAE/a1RuZ+moG9RuGbjEwcLGPgnt3CWWdDuXBZDH5+pvFuVw8mSI5GSI7eOR7OURu+49lVhd2nBjD/q8IUBDCojZ6l2zsReGyk1isDQyYZ+IvFjsbCKUNPBOEtlJPWy68wp6uF9vrJsLgVpoYFcCuPwnGPpXcmh7KWMRbpdwzMP9wGsRTaZthx6ZmOc7q4sLCmwqS5QbyvEca/5kej8SQ9B1ucyJ/FyFtfMjFOeiFbQy/+EX8styId7YTD2tXxYfsghVdKBvFDeq2G9Nou6cMzPeTzpTReOehFvrxyf4+8yHrJzI9zUvB8sfhKjA9j9gtfHQjgd/GkzeJJVVtoXNZTvn+ElRW+JaJPBYVVN8s+XFV5bqoPxbrpebqeE50zVDbJ54PlvsqUQT4smyA5HxpElkyFn/IGsPymym9jfWh2VHptawDTOinUmgZRbJyFZ7Imo3zAwq26JKyU6/0o11tzkIGLStmxuLaJMVNl3yoJv60Rv5E5WC9zYKol5zgniPLnjNy+JAPp4uMFxcff75f92O/Bi6GSF0XcKHdSzx1rHbjaTvK0tRfX4o2cViYD97qbOKx2OnTCO27hnSL3dDw/xIX7uxUOPi7nddPEE2PT8P6UwqarxT80HYd7pEcdoUzMHoNdkXLfzyWXjRaGZSZDdyaE7hVxuCy9Olt6tdt8E+umpuHzSSMbRmZg9FuV+Vr4kNgrB0tWf4POkSFs+iIOXzdJPt0OoKz05847cl6vFX77W/hb8ue+5E9JydOjFh/KDjNyZ+FMrJd5eCLzUOqUjkfXuHBVtXKaLxHTflKofyD9sVxh9s8BZC7VM993B/K90/igmQfxJXKyzNFX6NDfzFblUqGdNdK3LANho3SMyetGrU56zmvqhFNvYXZHMh5V1PhPpBcFpc9ubhav6SR92sSOtXI+T+R8+gtPVxCennlQ+ma3D8n5NRa96cWq0gZ+2G/Hy3SVnX72oVumxnp5Pfi+X89a+x1w1bcwy9hklPus4456LvwyTOG5wkF0+TOMjcZEI1nmu7jZibdxClv9FsDxm9LDYx2AzO9M4bl1YQqfRgXx7zcjz9QUjsknPHc3iFd99Gxf2YlnBiO72jJx6qX0QjsX3q4xsPVLGw6eU7hvcQB1W6qc3N2Pn7Pree+dE/6swhkf/bjdwMhRIzOBEyrnrvdhqXhUlHjUkz6SW5XThcOE00L8UEqoDB7zY2JDI0NHZKLJCAsH5pfnfyqc3jXf8etrlfv/9uFo8zBu6xWNasuzcdmX99gjOXlEicOm8zreWSheInu+W/a8hxLGusFo7JH5P6El4V0HhS2bB1FH8nz6gFS0FU5oL5xwPVbPieUdOBRn5e7fEnCsRy5uxwuYJylMtwRx+BfhmtNuZO+o8n0zP74dl1zZEIONsn/38wSQ7Y2O71u5cPapwuXdAlgp3vJhvx/vskfwr7ffMOSlniXbO3CyRyjXIRY7I8QPvgaRs52B7/62o2wpud5DwlOxwlMVArhUOITzLsZjQRMDUwbaMWe0QneuIBZEmpn6XLhP1THD58YPVWU1nx9+6WeD9HORGxovTJBcLaMwfE8Q/+SUXHmViEdNVfbpL+daXXpxkR/BaCPVihlo/k1jn1oeTF5ppulDCi6lS9/9nID+m4yscCcDq75Z+L5mEtYUNPD1VTvahKjsEO+HT5+VVscnVHwifdPTgarnxRMXedBZfLCh+ODIchbm256MLgM0Ni7rRZZWinh4EI3Ee4b2DOLeCR0Hr3PhXV0zW0xMhWWUlSl5EzEu1MwssamYEq/QWDaAwVVVll7mx8HR0h+5hMPOSA4tt+HsNY2rJntQ84DCefsCmHdaZeZKH5oaFA60BzEwLIS/RsfjL9mfAXt82H/CygPrEvDbdAO3B234WEjPhpedmP6vwuDpAO65DBweboMyxkAl13+/nyPcKXNWd47Ms92HLeLtN8XbTxRRuP58EC9DTOwQl46UisJtkX50+GZly1oJWChelRyQPGsofjvcicdVLOy3PBkH7li5f3gC6okvTRBfeiD7OlD29fdxRm7KmolgDXke8zOx6pGZH/qk4Ns+lbUO+rCqtp7vpoofFVFZ/rwfe7apPH7Zh4mrdQx940L+gqG8fS0Wq8MkF6Ls6GgV703046Hwff9cfsz6bqKnehpqRJu5qFIK3q7U2OK9B6Fm8aNUN/YuN/DoZxt63bKyxpgELK5oZWxkIhThlWCYH9Om66gFXfBsVxl+0Yfc47T/w9NZRklxdl27qtrHmMHdIbi77I27uwWCuzskSPDgBIK7BoJbcHd393Gfnnb9Tp71ru/XrAWZ7qq6z9n7ugJM83mUF2V/KDxUO4CnfTX2beTF9UYKf5sRhLOdyqgBfhwpZeLuI+IJ2Y1c/SYT4w2hLGBPwNdPGhs08eCazM3MXxw4t0jjinQPtvXU81YbyaXqOnZc68bgJeJHKT7U7qzjip5uZAyOYMFqP/BssJF9qmXCkkvPlGdO9HqrsE37AJ60VvjLMJmv5yrH9fahXhUDF2+2Y8kG4c/HAXxcqrFVvAczTsl9rPfhwu8GLvHY8LK6xl5rvegqr9NDXudZtI59a7gw5YHKHMN9CHujMrGjD5XlnHeu92OD8NMT4ad/pZ9XSz/nbKLxxVQvdOI/HvGfjL5GFmqUifmRIbR/ToLvup6mGQ68KaRnx6tyf9KjHaRHR84W71CdUGcY6A+1Y6pVOCOfB5tDFT6ID+JotlDufJuAgqsV3vgUQPx+hZWOBbBgZDjtpWJg7GGkrX0mrMJ3BYTvUjYbWPKuDW9rRbLdii/Y5lN43Sh+f0jHEX+7MKpbBEt1/oGyW3U8dMuFOjPDuMIShxjxoqovhd/FeyqI9wxcq7D2mwCWSv8nzQ/ifK9cXN7qGdqn6jiomAvnhM9WCJ9d+kPlokQfpkeZGP4pA20qqRy63Y8HZ2Sv17hgfCWc1dWGcNnPBNnPuBQzKxdPRdah4r+V/SgrXHxQuDiidhTjln9CYKfwykXhNvHGKcu8WFYolIlXE2DYY6LzbDp+Kf/f50QGcemY8O82D9TKKgNb/Zjc2iTnlIHNa/S8+96B8MIKo68EMfapgUX62XB5muRZRBKWzlaZpPiRMFFhxdzix3qFvZxB3F1o4FirDXFF5PleFI+6auDQWTbcbqRy0gw/fl+i0J8cQMJnAys1smFQQGNtnQfZahr4drUdW8Xbb86x4sNBM1seTMWCnSF0XEyU91dZrJ8PZ8Qb/xJv/LJVz8Y3HYiWvaolHpB6XGXxLT7UnaFwSWgQuU4qfLZReLCfnqsaOvFwsJnDq6WhxQsdx/eUefQamdNixWA5n1pyPuoEmZt8duSfpfKuzo+Fi1Wmp/pQSK/xqtOLHeJ318XvBmaGsXaeWOT4NZKvTV+gvDFQ62RDfBcTK3XLQL5Net5/4EDKGh1LvnfBG9QYpnhQdp3Cf14F4BDOypaajgdxIRxRKRGnw/NzQ/QdnF4cxo2psXg4P4TDHInItcnIFw+smP9aR0cn4bzQ7BwR/waW1TqmfZLXHa/Skt+PQ8NVHirnxwU5xyVyjo9vylxNceFihoF/5Lehbm6Vfzzxg8Jd8/xJGLpKZeUvPnzto7JhMz8GXtSx7iIXulSVfd0YRLPTek5e50B1s4Xr01Iw/leVhUx+PPq/c3w1TriloPD8SxNvd0/H0c4m7uiRAbtFY85kL2odULjyaAD6fDq677lRYbpwYHgmBl3TsfZvLjzIr3LkbfH5aAv9NZMxUzg5RDj5UF2Nfy/2IpgjlNrLBNSySx7k8KFwMYVXzwWht5vozpGO4rUjeX75Fziu65h9hgv1VYUL/MLl4rdnxG+/1dOz3kIn8gi/PhZ+HddYrn+6H8Wln85ekTnoLnvVwY9T0pebpC/72HVsnsP1v88Fb5+YinJ7hbtOiy9dF06ZEUDjdiGcOVA84pmR1X62wlnJzKzb0rDhZyOfNslEjlgT31RNx8DPMseNPejaRMfuU90otFXl9Vs+9G9rYf3BKTC8N9LZ2oprx6T/tvngKqgx8roXF+Q5Lu6WjvvjTRyZPwMh1fVMW+fEiE4Ky/UOIlp8vta2IEaONLNs6TT0GineUtqPfyPkOX8PYlBMJGtX/4z5h8z0HEjFQotKW7Ifx4Zp3FlB9l64OF64OPt9I9+OtGLdDT3fTHPAuNNA6yXpqd9U1jMI70RaaPqcgobWcM7OF40txyx8sE18XeZtkcyb42Qoc2yIx5hZYSyhi0OJFgr/HRvE/MJZ6Ln8Db+O1dNcxIlFRy207UjGj//+/GRHEq6U13PuHid+6WsiG2dgnOR1kaZ2JP2tZ9l/HFhXM4QvVyXhn6DKA4r4Xi3xwBVBzAvX6I32YvEBjalHPFja1MykSWmoHCjErbqreKWFsocnAeP261j8uAtzXpuodE5H+BIj41OsWPWXxs/PPVgpfPFN+OK/z2OYH2PHrqImXj6fgZE55f1fOPH5mJGNt1nRv6WeGO1Eqx9Gjq5txYzHGkMHenBOeHJlJztuCLfOMMuc9lP5dwPhAb/G4XoPCq7UeOubBxmXdMw334Wa1SxctT4FA6ix/lwvms7XcarThTt5jJzwMBO1+sq5NQ6iyjkdNy93IccelS/P+nB5tHhY8Ux02GZivxvpmC894pQe2faLyhuQ80kW7i8p/T0jlCtDE3DZp3Gh0QPdVBNdURnY2DIrn4z6gD5LhcMSHJhcSvbjqOyf5PjeDw483BTKYffj0VPmqe32IAZ30LN2PydeS8903ebHWOndotK7w80Ky6cFcTVV5eyiPmRpb+T3/pkYuMXE2nfSkXDLwAqTbFj638/DuyvPbZhCVAzi1w1GWp5YETownIm1Y7BPU3jGE8RqfQg/OpIQuJqFullf8btZpT/Vj32/h/O0JxruAyZGHknHmJGh/Kl0Al4FFXZRAqiZqnBtsQCqZc/GdW/eIeIfE3/sT8eR6qHcuzYBb14p7NolgBzSAy9vOXC3t8YJLbwICOeaygVwpdV/n0vvx5g64SyyNAZ7Ssj7nYpB5BwTvwTTEaxmpPZXJl4MjGCvWj9wOl7l2go+XNooefJQfHSH+OhlB0qGm3kyJg0VFmvck+ZBxaXCuwk25PHp+dDkwN5/dDy9z4UWc8XP/ek49q+eO1Y4kHdINj6s+g7nIgxc/l28aaWBDb7ZYGweQtv4JFRI0XFvCReqRIZxy5c4pI/SsWhJN2pHyPz8SMG7u3q2G+NAseMaz2/14N14HdvndyNnSbnfk7JP7VU26O9HrWMyT9sDKCR9eq2xDROlD/KV9+OJzMtgmZeUheEsbo3GuGUKC8eJV/UM56e2Meg9SnLkJz+uFNP4+79e3JE5Hy9zXr6lxr2jvfjtkoH6BTYc6aVwZ6sgDgj/nhD+PdHExE1TMhBx3MLYrcm4Kd46Tbz11Q4Tu11OR72PEVzc/DvCCugZd8uJM8L/f2a3Y8BKC2t/S8YM6Z2w5n64ZN6zyby7PoYwS7NEnDWbuSY1DY9yRHLQyy8YMSgny9V8gXFzVBYO+mQeNK4u5vnfn5uesMfBKHORKXPxpo+enZo5Ya1qYN4Ndjj6KMzeTHz0TyNnvbMifnIYK2aPQ+MoI2d8ysTNGhqnrfGik1x/H7n+zXfEN8cF0H1NKDt8iMeu0QovFg8CvRXOayn56zeylt6KAuIft8c78fMLlU16+lCwloXXVqZg5liZ+yKZaLdXx8GnXTBMD6MrPA4rJL9jegVx/4iRo3ZZUWWgws21gxizxMISKckI5tVRf9+NHvLrneTXW8QqHF81gBbC4ROEw8tEmnj4Swa6/KZjb4MblyKMXPg9E88/qOzdQjha9jFO9rGmMYwrM8VXXDq+jnBh4XCF9nJBbL+v8NqIAEoLdxwW7lgr+fVO8uttH4M8L+H4ZcL7sdIDdfS0L3Wi1k/hXHM8BslLTCyVnI6hio6Vg25cfaXn7C6yX2P0HFtUeFv6u6P09/h/FRZaGcArk57d0p2YJhwbPjsAY/EQZp5NwpsCOna46Ub30+JP63zo1djENtMz0PuwxtZ7PWg4Wl63hBPLyhmYuM+OIpeieGHBRzS+a+b0sakY+SCE5YYn4sfvFtb1JCP5hFzXpnRUK2HmX6fT0Py0JpziQYxTYc3IAEq1U3hsYBB3tiscdzUAXTE93eecKNbTwnNtUzBqZjh/ssTgovjw0tg0ZB+r8E2RII4U1XH3eTdC1+uY8MyFJuvCOPNVLIoUDeOlC3H4p/Z//x8miBm7xA8vuDB2gnBoPj/OCh+uET4Mb2FizNgM5L0SwUdzv+O65PmMMz6sqqTjt21ulBEeOzxVvFJy7Z4aQMxgHatXc6OZ8PYU4e2pF/TMtsSB5rEax1b14EwTC9dNSUFkbwu/tUzBWvGYN3k92PRR5cNmPqgLjAzarXi40cjhD62oXF3llnV+LInXmFLBA9sLE/P0SMepkio3nRA+H6mxfCmvPFeZ/7FWZFkdyh+f4pGyVWHxmwEci4ni9mofUfOxwtUDA2i2TJ5rnAe3CmXhlKvfUOwnheeOB1HIq+MNiwubdglXnffB8tTE1H7pcAvnhgrn/jPXzAO+VBwKs/BQbAqafdJxUlMXwuR8E/bb8Xmenk1cDoyfb2Ahpw3tF4VwQEYiLl1XuXCGD1NvhDHb9FiEiUfGe214nqSyZykfRhoUlrUHESP7Ua2K9IFV4YZ8Abi2hTDieiK2Sm/cLpWArGXN/Px3GiYNC2PuinH49lbHeu1cyGsx8mFKJt7LObeTc+66TmO3Vx5Mkp7IXT6A47VUbl3hR+WGKrf95sfziyr7LJT7G6IytaofaK7jgvFurD+t8MXaAApU1Xhno/R6buHkJ044B0QxW51PWNnNzO+d0lBIvP3azCBWzDUyxmfFkBY6Vhvrxh3hhPFfPXjhV/izPoCuq6TnvgQwvbCRoVcy4TxjZPY1VnReZWDPLzZ0bqiw529B7OkfwbP1fmDKTfHnqT4MKGNgvUN29OkvPV/fiU1U+WiOH9v2G3jzmA0NGiucMz0Iz1k9Q1c5cPiQyv0HfGhXIpQDTyfguOT9Dsn77v/o2WG/A4G1GvWvPZhaWc+orU5APG++eF6l5Sp3xfhwt5Ge42Y4sWOXkVfPW7F7m+TjdZnj9xbWaJWMfwbquL+WGz/WhbLuy3h81kxs4snANnmfW/I+zZxGTo6yYtEfWZmZ+B6ZwiV5j7jxPkZl62o+vLtgZIfFVnwzC7enOhH1r4UfViaj8HgjLxfIFC/V0yfXdV/2dWThILpbVXbK50OGzGXBY0HkFQ94NNOFLxNkvvI6sbOLwkvdgrC/kB7oEUBd8aRlBcQ7c5vof5yBvfdMPDUqHUPE5yovcCBNPLmoePLNmybOmJKOKj+FcuvxBNQ0mLjGnoGCPxTerC3nXsXEu5uFz/PInj3yIvqpeH3fACY0CWG+qUl4IHk2orgTey6pPDdf7qu9wg6/BHHwkpkH56fiouzvItnfBSU1Ok94cXGRjovTXbBKn+Tt4cP47SoLXPMhR7yerys68M8lhfvnB1C8qZ7nJzux7bXC250CuOMwc2LWVOxvo/HUEC82TdLxQU7JmT2SM2eFJ2ROtsicPHkq5903HotrZWHqim+YJdynF+6b8cHA0JY2/P23xqP/eNDghoG/TxeeicrOJ5/eQLc8gu6Y7+hcMYy9dsahuEPjv9k8yBsWzkdxMWgp/DFK+OO45M8OyZ9fv+hpbCjPMTGchcpE47rs80zZ5+uL9JyV7sCEphoLTPYi1z2Nr0Z78OpvI7v+Y8X4M3K/f/rQaqyFI4qkYNAvZtZhGv78aOHHZsnYsVvl1X99WD5RZVxuP2xt5DyHBGGS3LWm+YAQM+clpWFjSQsfnkhBaDXJmfV2VIqycNenFGyQ+X8q8586V8+SPgeihZdq/iJ5eCsny096jilhwsdxbsTs11jjmAd7+xl5ukEmmg3Tc0pF2avpej4Kc2LQKTOrr0/FN/EbFnai5xEz2+xOxQThogKthc/Xy7k8ld6Tfc/0C298V1ixTgDP3VHsE/YRu3uG8GybJFyV/J8jXNtTfLuN+PbaxgZ+mGZHQfHRGx39qCN9vWxkAPvHKDxRNIgO4iH9xUMGf7GwVsNkvLHo2CnZjUfC+4MjM1DjkcY/B3uwymDgZ5v0ueRkqi2IfvdD2GhkIjwjLAwpkwJndyOjOmRiUlnJu78lR6dprBXhxaoYjT+qyXn019i9nhcR0m/R0m8n/lS46W0Ar/8xsvs+Ky79sHBerWS8OGhgn4M29JptZCs1E6FrDEz5YEOt9jqu7O9GnfEKlxYIoqzw437hx+UJesaUc+D8Dtm/yz6kntRYYoMHK8roGX3QCbd4Q5YKQRgb62if7kYNncI17iAiS+v4+bAbm+Ky8HGlr4hZFsKqsYn4a0s4X96ORk+H5Hg2Fw4cVHjkoOy3z8SlpnQcP2bmtm2p6OWMYquojxj3q4lFTBkot0bjgQ8efJQ5bdnAAZ1RPDkzDYdkT/fLns4aZ6SuUCYWhclcxfoRKvkdL/n99zodj7x0oeInI3c2tWKhXa4zuwvhE41MyJOJenUULlwahOFnhY6mQQTqKTQsDGKI7FM12adjty3cNiEZLzdEsPfj7+jjCWHr0EScmCr7GelH0iELyxxIxt3CKsdf9mPGXY0RYz14OsPAX0Ls2Cy98WCMGy13KxxzLoAHtVUOW+ZH8TQjzxSxYvI8jbndHvyuiSd7vFi6Qs/k7/K8oeOy390oIvlyOcOFn4Wfmwk/T76mZ47fHOjr1djY4kG2pyo/9PNh33w9Tzkc8I+U6//v73fnUNn9lVzPzxaGN03B0Z7iO23seHHeyN5/WGF/ozFXB/HxDCMT81tRS9W40u/FtVVmzvmciirVwrhxfRzCY3WMrepCx78N7HvYhrcwscPvGRhRLJRlzyUgxxnJgz89OKYq3O4LolJfjdsbe1G6mZ5HJzrRqLyOs/a68U14vJ7w+JfiOjY840ZS1iiWfv9JesTAfEfsGNLKzGoj05BHr+NjpxsjIvUs/8WJS6sNXPhRfGtbFO03PuLzEQub7EpGTB/p42ZufDyThc3WfMXORQZeSrfh3W0DO0ywoeqfKje+8yFmlIm1SmZIPoSx+rFYLLGYmJyc8f9/3s0c8ZnA12TEiSdVnm/Di3HS2wXtCIyVPi3sxah8Zpa8l4aOwg0/rwug3TWFg38LYJTs4U+yh/ujdTxewwWX9G1kXxumSZ9FCGf1Ef5vKfz/sbmZLcalYdlLjYndhP9ahPHhmDi02hPKEWfjMWCeeLdbOOic+NOyRFRMNHNn2VTsCQvhudgkDAlKbqmpWHlM5fdtPowqnoelzj5CbBkTqx7MwFm9mX860oBp4jlZgnjxm5G99JlwS86ESs6kJBtZsqQVnvUGhj214fcNOgYey94Vkb246EaOkiF8czIJl76HcHGdRIy5q7L4WB9uJymcUCqASQOzMW/td3DJfkQ2kd5pa2HY4BSc36JxxW0PclW38OW6FBgvhdO+IBrZS6h8ddqPwvWNvLYgE+Zn4Uz7ORoD9mjEWQ+OvjZxR6d0LJQ+cmT1IEXyu3isG3PvGxgYYcPs6+LzM6245tDzt6wOrNtl4KvzNtTcJT1+IR2VNum484EL/3aO4IoeP7BK8ueL5M/CueLjvkQ8CyjsrQXQZ144W7ijob0xMdAxHYclz/dLnteXfl4g/dx3YDib1o5BzdzC/0/8eGTXOCS7BzdChPsT/RjYQ8e67d0IE45PmJqCfHvDeOd0LOrmkxy858dn4csmwpdhB1UmHPJhjuyZIntWKEbH69Vd6DFcuKy8E78+0NEy3IXlkhexkhdL3mtMbeVBsLyF6p4UFEgTviriwSfh3uYH0vB6dAg7FU/C2t91fONxoTPlfuYKBwqnhAqnjO+gslA/P+LE+6uW9ONxIY2DrnpRPpvCfe+CKNBE5a2pfhySPTsge5anbFY+/fsDjl7Xc9cMB4avNbDCGxt+OxZCw/ZEYIiO86u6sV18/lozP16VU9l1nx/9v4aQTMQTef/+c4J4X1DH1tfduL1ceDnagZH7jCx70ooNy3V8Eu1CMfHwf8XDR2yMYPlH37FlkpG3cmZi4j7x+ZM23M00cVzudCjl9NQdcKKTUcdemW6UbGDkidmZsHYIYf6+Sdg3SbguZwaWCa/F1gqgW1YLu31IQcwmPWvdd6ClV3LUEsD6PXq+OOOA1k9hoKH44gwdJ4a4cSzJxN2l0pEkvFBGeGHCx1DmbR6PoF9PVe/Ad5PGuulebC6q8f55L47cN3LXCOHWr2EkYhE/SniypBNX/Srn6n14+DCcQ4dEo06CxmXlPfgpu4Un36Tg7RCN7at4EVZQY8INL9qOMXFYsQzUa6Lj0qluROZT+PleECsrGcQf7RgcyMKauq/Q9oTQdzYR7X83c6A3FUwy8/fSqRi2SmPFLx4s+u/nQ7RyoEx/Aw/WtyO6rJG1D2RiToSByg87fuoTymPNE3D0isJdcwIwiK+7xdcTJhtYKbsd6TYLC+ZMxrNrRv7yqxU/ybwel3n9s6aRb1dlYvd5Ey/8kY7n//f5c7vGmHmxaBpOzDVxsy8d2ZaH8FOM+PpIA+uWsuO+K4ojIz4ivL+R0fUz0WCoxjmVvfB90tPQxIF/N2tcddeD3S3DeX5UDCq+kuvqGoDzncqoNj60Li+8vzeIqpk6/pXHhaa/aJxCL/5tZ+baAWmoK762RHyt3B7x87MBNF6icWaKB4X2q7x53IeGNhN/zZWOaW3Ea4Y6UVqfnUecbzBlrsasfg9qLQvjirhY1Is1cnFVK253jOLkPp/AjeGc9ygaHeR9f5H37She0i/N+b+fPxsdtKLvBx2btnShRi2N61d6cXWryjk3hetNJtrTMzB8lJzHT9JjJumxDCdcf8r7v3UgNq/CyveD+PlMCJv/mYhzLXVcMUp8dLORXe5ZUUxy/nx3DzoJd/YLdyLQRaWhmx+b82l8cNeLbW013h7khfKbgarRjh+/6VhH78Y6yaE3j/2oGaZxdZwXE70m5rOko+sBHbsfdaH1NB1HZhE/GGHm4bJpSEvRs0hxB5InaiyV24tgP5VaAz++7A1lk1PxmNrRyGw/Z2KD5Oczyc9bJXWcfMKNdlkVDv0QhHe9npanDsRFmlnpcxrK9jbzUEv5Kvx7UPjX1jgH80x7hT57jGx91oot80N4x5GISQX1zHPDidSjZpbYkYoecl/t73mRfajCt5WDaCb+NKGEE3lOSo9vdKFXAcmNW26MfxzCggMTkVsXxafuTxhQXGH9s9Injw0MH2hDyisDS3SxoVMwL/uo9zHULc85PIDRR6VndwbwYazKloX9GCv5VHS/H8ffhnNru2gYe+lpb+2ErraB7uV2/LlPvONEMvpUtbDVxhQMnpSVVXN9QK7DIXy2NxG/3lBonh5AX7nPZnKfX50KG0UG8DTMxP6x4rvRZt6qkYoV6xT+eBlAhaIq91zw43BbA/cPsmNBmkpHYR/KXNZ4aJ4Hp8WHNo7y4N5DHccMcaHhKeGd9S7c3h/OiceisTfexDMV0rEli453vkp+TtVzYqQTs9dKDr5x4LT00kbppZd9VfZo5Ed7r579LcJx2U3i8xnoIl7S+1kAl69HcsGMzxjYSc+6vZ3oHWtim6rpaDAsknMqfsH6lzo+6+ZC8wIKx98K4vAveu6HE91TFXYuFsCTZrIHE9wIF06OF05uM8XA4dnsaCwc+FtHD9qIjw8VHz/+VcftcKFlb7mflm7kHBHKl2UT0PirwukMYOU2hTHXA0jabWSZf61YuSQbfyS/RWQWA79+syN/kp73SjvQ41eVbU1+hLc3MLa/HcmuMJaKiEX3ehZ2XpSCa6dUzv7Lh2vyurMQQKx4SnXxlKrZ/vscRDduSR5MPhPAzqy5eOX9MzTPquP4D25YhuuYVt6NZQkq48v5sFsvXu1w4vR42eP8XozThbKwOwHrAzo+11z4tZqF5vUpqDJEz+1VnThVJ5Lr//gC623ZrwnpuHhK45K/PEh3G1k4zIrz7/Rc3saBHY2NvDItExVNwhsZ8r7iBemLgnidx8JOD1MwYLaO9VXJedmrHyfdePYxC/s2+4qil8N4eV4s2lxQOWSJDx/rhLDFH0lI76yySA8/NvplH/U2rFyl8OtnOZcOktN9M1HsVigvTI5HLpmv50V8mOsIpT9rPJYG5bwUK05W1HPjf5+bmUVh0W+SP78obMYgZu6JYOjZ76j+WeO6xh4sFk+ziqcNb62w/PAg9lVXeXKtH+bZKlNUP1bnVPjheRBRDpWfsvkwQ7g3bIENRcRXLxUL4np98f75fpyQr1vka5Otek6/JTnxQWPlFh5YT2gssMmD+uEhXBCThHzTQvgwSxLOjxAuKuNHfumlO3MDyPrExA/90/H0qJ6/7HCgTpSeSz85seAP4b5ED7K31fH1YPHNrSYuv5WOGNn32mEB/NLBQPazY774pGeaGxezR3DJ6x+4KL2yuEsAs3IaqX+RiRP/fe71VxdGnNax/DoXthg13rVKn3XV81IXJ0r5dTxscOHMaJXrisv9z9EzI+hAREU5r51ubPj43885lvu+aOLFRelYOt7M+PxpeDcmgu2L/kCrvgaOamxH7TYWLhuagovLNS6M9iC+ucoK4/wIii/pfV6Ulfc5JO9TrGoIL2xMQrnZITyoCrdPFt/NkYS50DP4uxNHwvXcFy25+VhlyQE+NBtv4MQCcp8tNLrHePG9m+RhpyCKRJp45UsGPBezMHThV1QLqlyvit+9UXi6g/R9pUg6tn/B6kV6fkx3YHNu2bfHTvwmOWaOCuCR9NnQFV70f6Kxfn8Pxi0ys3BGKoq8kHnuaUXuTjo+7e3GP0bpW2sQ+XpG8k7bL/jc2siGwzKhFQih/1YS9gunHxdO3yr+dnuPG4bt4XRci8asphp1k71YNk9lksuHxHIaS+/z4nIPjfPaef/3cxpcFhcal5K5OerEV7f4XpgL1341c7YxDZv0Ch86glAlh4KSQ6vqC7/PF34fp9BeMIhFHy20NkuGTvjLY/jv759q3FfEi1qLxU/TPDCGK8yMCcL6SbiyifiC+LwuhwsxrfSsOsKJygss3GxPxtRiOmY950aX2xp7TPTgRQs9u41xotYajWs+yL5XsLDIrhTkfKHxTU8PygxWeai6H+2W6Tg0zoXRNS38aXUKappC+GdGEppbzJyYkoYV+8yMO5EK2xcTczdIR+E3obzaMV74wMwsb1MxsnwIS+9NAsvoOe+QE0UKR/LalS9YLOea+jkDj3boOPiyC55+OoY1cKPgWBNvF8nAxMriwVvtuCC8sji3zHNnhQ96BBFxy8Dvk21w/qUy23MfeuzUs+MlByYJV+Tq7keLniontvXjepscnDXkFS5J38yXvjHIvriEw4rJ75+T31fqqVQWyx48kfzs78LXGyGsNy0RhySv/y7lwMIqYbRtisPZK+JDc2Q+exk4vrUdD0/pOewvB1rnN3LknUzszKHnlZdOOW+F89sFUfqijkcXudBM/HOi+Ke5kJEp1zLx0mRmz/Q0XLgkPDzfhSMvFe7pJr0hnFTmrhVrelr4oU0KBh2OZI29n1Gnk8rlvf2YN1j2oZoXi8Tf09/6cLeEieNPZaBiHpW7HvmxX7zgtHjB7lxhPPc0DkMnaKySz4sBZpV10/z4o7mRyeMyYVmTlakf3kP7bqKvbjr2vDDzbI9UOMpqzHbAi9er9Oz82YFr+/T89YQD/f82koetKCx9fn20B3U6y3X3dONwzizc+/wbigV1vKDKc6uiY4NNwjPF9Sxw1okbB1ROP+LDqY0Grn9owwJ5Pk55PiWld09I707MFsp87xLwy7YQNrqRiD5TFbaMCqLKADO31k3DwDUG1vlgw1fx30bivx65ngi5niYFDJK7dlR5b+HW1sk4O18Tz/egkfDlLOHLxDjxr0pWpK+UXvnqQ6upGkdFeTHHZKSakYk7Mm8TZN7+DDHzXWIaQnrpmNTajRHXVZaZ6YN+l0LPhQCGdtexckc3krtkYZlu33BA8uqY5NWBARE8UucHXvyrsNeKAFI6ZGPJvu9gPxzG3Hti4fVqtJg9aDBBz9n5nJj6s/BxU+HzthYeGZyCm4NUTq3hR+lGoTw8MwEjcudi+SfPkPuYno+2O7BC/DZa/HabePQt8eiREyJZJt8XxLuysErEVzyXvO0refvHSgOTv4qvbtTR8MiFLH2kN5sH0XOykW1zZKLut1D+UT8e+hSVrhI+NA43cmZMJoYVEK69lYbr/+j52z4HLvymcKleuGiFxv3fPZidLYLKux+4K/48Qfx58Gvx6c7Sh1aF4fnEp4pIf14MYkJ3PfN2FL5MM9BXxAZtjIXBYik4M0zj2gpenJX7Wb3NgW3jNF4t6MXx80bu+EO8a6147RsrDjcTL58QRLkrRh6aa8Unye2mktt5b+n5cLID8XcVVhkTwP2uFo7ukoKpsQqjqgZQtr70ygI3SlRXeGad9Pyc3DQHH6N6ZDb+9eUd5ouXuNKdyBEdwVc1v6PFO+G+tgG8faSw/aAAVkiORce6MDc8nIHoGByV57O7uA+fxQua9MlEls4hjO6ZhDTxr6K/+rG3ipGnNmWirczfoJMOnC+jculBP376qudJOlBhvYG7n9lQ6jeVRw1+dC9pYOeTdpSQvTwpezlxgsr8+fzYoAq/+GSP38p5tPOhWyOVHWf68aOThbV7pWCHxcDryXbkitDz+Q8nPldR2XCTH9ljQ/i2aiI+XdPY9FcPDvys8GiTIDKOy+tu8aHIUBMvVc5AMEJH5bsbLfJI3jwK4libUO4YkoCDwu9HhN/7JcqelU3ET+Jrp9858PSAngOOOLArp3DZ8zi4C4Uy/GoCKgdU7tT5MOubSl196VfxlxOVgvj7przOlHTcFm6aItz07G0o+7WLxxbh4Tst3FgsnpTaIg3pHxUWaRbAvF3SU+flPmdkYefQbwirIP2wOw2mluLxo534IP7bek8Q3+Kljyt4sN6l8mmEDy37hXFMwzgkpUgeFrfi0GI9D6U60OaUwuHrAygiXnlptBsrDyr8fvC/fxdt4P0BNrw8pGf3Aw60CpHvT4pDkVUG4QUbjpg07k33ou4U6Y9sbuzJpvDft0E8TVPYv3AAP7XS8fRIN7ZsUHj/cQA3Hlg4c1gysvXPxo/132FmwMIwXTIayF7PtnjwRHqz/wQPFr0OYXqnRKSuEm757EPY7ZyMn/gc+8VXjouv3Jyu47QwN37c0lh7kgc7YvW8WsWBMcIJRYUTCh1QePtoAB0OGtjvkA0dpB9/kX5cF63nyxoODMgWwvrvklB3nvi724pmfcRTmwsXNpL8nenG6JXit988qL5X8vW0DaMkF0tLLh58JPsxyIXq4xWuKxDEWEMeFrc/QopBxxI2N5YaDIy32XHpjMKFawIosF/Hu8elpyT/J/ztRaL0aznp164Fjex2IxOdt+Rg7zsvEeUO48fwWPzTLYp/d/qEp5NVDszuR1Q1Mz+vT8PGVNnbog6kltezxB4n2v1kYP/jdnzfaGTth1YslD7IkD4Y3MPEmu0zUCU+GzdXfIvYPqGs3iwBE9aozPvBh6o7Ddx4yYYGisLZwSBWZqj8nt+HOOmdyj3c6NRLZZ/Wfjw6bOSQPVb81kejobkXp+6K54zxSF7pWauzAzebWzhlXAoGTpE+z5aEnyeZ2DRXBnJLDz+THk64omO5OS7cknOdJnl99CcLdx4Tf/ndwLpeG/rX15MLnJj8k4l5jmcg4qTC6I0BlDbI/tmDCLuiMX6uBwXfqbzd1oeYmwprTAng5mAdp1RzY+PScD6Jj8b3qSrrRfpRaaSZu0oLR283s+3VVBQOV3k5xo9+4m+NxN/Krld58JkPta0WrsyXjBvDZH4quNGtXii7LErAl0Tx8TIBfClpYcMTKXjdzcjOnTKxaFcUM89/RLc1Crt+CGBbbgtvPU7B3nc6nm3jQqVXZm7rmoq2OcVjX/gxp4RG9bQXnz+Gs3GzaHSpamDXjcLzNUK4/E/xMfGhwoofW8wK76UGcaVFFH8f8wlzZE/8I9x4HlTYRwlgy0fh62YODJQ5riNzHJPNwNpv7Tgy0cw9udNwJJeFe56mYNMkPR/kdKLNJYVDFwSwTvzr1bAgenxW2PG/n1uwW6X+nA+ppYwsfiQTlYoq3HkhiOa5Ijjh2Q+0y2Pk0EeZ0InX+cTrtu3S88Z5B5asDWf662gU26Xx4gUPcl4I56sl0TC8VGnv7sO0ViqzjPSj23ALu5RPQdp5C4v9kYzqK01c+y0d1y06/prsRpxdOCm7B+/9JrbVp2Nbd5U3O/jRqmIWjtz5DY/PqOy/xgfLpBCm5EqCRfIuXfw/cVcIK55PRNwCmV9bPD6eDGXLDfFo0MPA2e3tyHdR44NFwkuTNS7N4UXN3Fm5+skHVJmr41a/CwmHQ1h+TyKKTzfw33A7pi3QMYvdhRKjFZ4tEcSU/mbmrJ+GV3Ie3eQ8XMKrWdv4YZf9zS77269FKBuOTcDCWTpmaHI9f4aw7NtELJP7ipX7WqIZmOYRTg0T347140nfLOzf6BtyK2Y+CqYhVrisunDZpb1mzj+VilkyN/onKXANMDFrnQw0WhDJmfbP6H1PZevRPsy7aKRvoRXO2tLDy4J4lFfPwfflXJdJLscFUF28/i/x+sg32fi541u8a2Ng+yF26I4Z6dpuRbutCgfeCqBaVfH1jU4MCBGvSwoiepvGWtc9GNJPPKWhE8klxZtOeLFhr4mPT6VjTH+VJf77d6Qyv8ef+3F/ip6jsjqRa6mFzxOSsSkkhI8Sk2AQn3WW8MO008jMS1YMOxPKCn/GY3xVmeeNfoRJb8Y3DaJbgsZu5cWzi2bh2wvf0GaTgcMe2DCijcZyQ71Yv9vCF/8mo49wZTOn5OtayZU3HuRboPCRPYBBDtnzbAGM3G9g2eM25D9m4f3tyTiwVuGx1wG8SZe9LGhFhMxxtMzxy2Uqe8b6UMKQlaftHzChn3BvwwRUH5CF6+p+w1+bQvnifjyWT1YYmz2IHdPNvBKWhg/fVbas40MD6eu50te3hDenCm82/aJxckMPWkxQOCZfELt9Op4zutA8EMIJukQkfddYpo74+PcwVqoTi8SCFpa7noKyT0N5oF88Lq0M5YKv8Xh4QuWwTT4UfRDKC8Pj0eRZJKf9/BldKyjstjuIJcP1TC3nRLEy0peHgvjwj8Lm+wLo3VN6u20QlYXLt4+0IWGNhRXfJ+PyCh0XfHchs6WF+UalIN9I6dPSQRQ7ovLsbh+6LDOwW5wNr/dJr5yw4vQnI9c1saLkOQtPLk+GVs5M3/40jCol53/Ui17Us81c4bjhCl+XD+LmfAOnOWzITNQxXxkX3j9X2LZXANnkubxbGcDxAXpuq+NEvSwal3zzItvver73Ct/WUrhkRRCzvwjPNPRh/2qNpz56kMuo57NMJ15czcOesx6ixwcjO7a04h+zcHCqF/f+1XHUCheW+qOYoP+Ih5XMHLJNes5g4kNbBnBPx4WjXVhUQaVtlx9/JUTxRbmP2HFJcn2+DzNXGhn+zYr8lRTe2R5EmyMGDt1tw+lxOq4v6Eb3bwq71A9grezDqw1OhEwwMTlfBq4ZdfzN6saHyBC2+pyEYZf0rLTAgSNuI/eEWRFV1MivFzJx75546CgbjgtnbRXOOn4jlFunxaNelSxctPkbvl1XyBkBPN9n4c8nkhGcIvef1Y9AX4WGRsJzwyw8XCEFYzeKJz3yofBVCy/PTsbYdDMLF0rFpCkK82QL4mhhI3dfzkSzzkZO6JmJBZXC6doWg7qrxRs+if+IP5Vo5cbLtwb2aGdDj1byHEdmQjmnUVnhgV3yIqfkxb4Xwss9HGibauKQYukoW1PlodV+vJfrbCvXub23eEILL156DOwZYoO+jkL30iD2lrfw1J4UPMmbhQPvf8Mu4Y6Lwh1t20mfDJQeXmRmTEYqas9WuEINwjrHyAIBKy5OMXBxVjtyC18+q+hB+yUhHJSSiJZ7Ijj27HdUFb7estmPT20sbD4kBUuvqkye5cPmoJH3FSumuA3MEW7Dt1Qd6xd14XHTEA6clIR3m0PZ+m48OoTo+HOSG5lbFOa/HcD5dQYue2lD/d7CVS2D+D5EY70qXuCV7FvXeAyKkpz75IXOYKTbnonl7yWnW7mQkamyYG4f0vMZWOSuHSV1Op5yi19fNnDP7zaMFn4sNdmDhT002tqJz38zs379VNSTOVu0wId308LZISIGbdJVDi/kw46PBl5rZsPghZLvmTbkuhrCF7MTUSFa4d6aAeysrueltU4cXKnw8FeZD8mt3pJbtUvquPykG20emDh0eDrmUaNrjhdDg9lZRX2NM+ka1xX04ORME7eaM1DOaOCBTDtCTfLcMvz4Kp7RQDzjUxczm3dLw1WDytk2Pz62UNhsTBC/19HR94cbU2RvssneHAoYeFCzIXa7wqpXA9iQVeOT915UXhDG7fZYTBKPyi0e9eulEBoXJKLiXDN3+lNhLmtg+t92+IRnzcKzunvyPEdb8TBN47DCHoxdbGDhNLnvlFA+KxGPpZfEk+fbMK2ZiZETM7BOfOuV+NZk6d3s0rtNs0uuvglCnaGnP9SJ8OkmxoZnoH4PPRe2d+JkOQs37kvBRsnzx+eDOLlH+uuMA0vH6phS2I0aJcxcezoNbWdm41DLO9jl+3K0c6JrXo1dH3hhzxLF7F8/YVI1lbnW+5FQyMTyVzMw9ScdI4+7sVty/5zk/qVmOs6b4MaDVXoO/+yAtiwLvXFfca21gXOG2WG+ozJtvA/VWoRww9gk/D1B49G8XtyvqHH0Di+8WcIZ8jUGne8p7Dk6gKGeLKwc+hW1lpu5KiYV0d1U1urkR/wtlZUn+WCN1ZivigftHCYOzpaOR7I/Q5QgWvQ3cUL9DORsZObLmWnofF1jj5keWHcYme+yFbfv6zhhhAu932hs09GDqN3C/+dSkVV4/MteKyZ/kPts6UP1k0au3WiFttVA7y0bVrYSXxsRxNuWBnYYJVyyM5TDLsZj7FLxxYSAcK7Krov8WJlk4bdSyf/73MH+HQOImmHit9AM1BXPWzzQhj5m2de0FDTtpHFqby+ax8r1VHWh9fesHFH3PYy9NGa2lr3rqHHBz1686qdj9wZu1LIqXJUvgNhHGqsP8mDOnVCq4+OxQ3L0iuTo5BIRzH36B64skuedbsPEvJJ/D4Jodlbj+NXip3vFT0/5cFAfxcOOT8gySMcfNd0YvE/PGifFO9dZOOZVMmrLHi6TPVx7VeGbWQFosSZ6q6ZjRrYsDH/3DccHSV7X8KNajI4bqrtwoZeZS1qloXp5jX/t9eJ7foX1bgdRINTMOwlp2FE1lNc2JOD4PY1bRgnnVtfxxTrph2MhbLEtEXsLGXj6qh0D76qsM9aHvjFmNq2eijJZdDz4zY1D9bPz0Pw3uLpT4dyLAdi265jnqgvrflP4Vi+9uFVH/S0X0oRfiwm/bnoovD7EgcPPVB7o48MDLZLDPV9QupHwzcwgUFvlguV+RNTR88dSJ/p+UtikaQBFa+t4YbkblU6buWNdKiY8sTB//2Tcz2LgqK92eB0hNGVNxJZZJt7RMrC2jsbXf3hRqYeO29u7UVk8d7t4bt2iJi6+kIGY6kbWWJuJHTONvGrORM+aoWy/OgGvhSe7bBMOlfnNLvP701w9j/sdWC9c/dLnQi5vOJ9ZojFzlsoQnR9N6ody2gL5vvbyfb9kIuQXlUn0o+AbI291tGLAV4116EGbfuEc2jAGDTsonNkviC9vIti4w3eUXaHxwA8P5scIx1b7799dqfz23IeSCzWezfQg+bjC0lsC6PyrmT1NwuuLVHZP96H+ZTMXzEtF1J8qP78Tf61iZvVNaWg8y8jfdOLVJUPZ52QCQhXxyKAXy3IL/z0OYrv48bV2GbjcX88F9Zyw/9AzTy0HSk5VeDpK9lP8evxeJ2Ln5WYN12McGSqeVSkFQ0OEE5KcOD/bxCVKBmqMVLm2tB+7hkjeV3HC9sbAnB1s6Lhex77PXHhVTvhjXxBtf9FzKJ3YJv/9zVJ+TD+sY8ReF/45aOKBg+kYu93EYtfSUWyQgZdq2jExJIJ5k37AWUg84GoQ59eEcvn7eFS7onLTXB/OTzByed5MnGytcvMwP6bt1Rh12oOdioGXA+IDu8TjLgTwz1sz97VLxc0XKqf38KHVP8L1+x3IiDKz4Mc0PF6pcdBXDy7Wkhxe4cQo4aKfMt2YcV9j+EgPBkm+15R8N6wy0P7Fhp/WG3n8mRVHi6vcfcaPOtRx6Vw3DF91zKQLM3wGhpts6C781bGQ+Ln06wPpV+bT+Ps9Lyqo0o/+IGY20xg20Yu3tULYbkUSCn1XeL1uANatKgvc9OHKZpVz7/pwSXJrQXwAuxZaeNGajCITVV7J40ezmSonWvx4u1th+3+F7/prfFvPi9lnddRWu5DfLX4enoy/t+h55LYDVToZuLW3HROOC9duFX7+amJepKPDUwP797PBLe8XKe/3d65wHn0ag2ZOCydFJUM9YaR/sxX1lmlcHOdBUPZH57Ki4kSFu/IEsf6AxudHPPhXnvtKee7bI/W8/tmJ1Rc0fl7swYQJ8n75hF9Oy9e1LpwqYOSGm5loml/HqXfcqCpfN8rXnZUNvLrFjsTyCsvuCaJxfhOn38nAn/UNfD/fjoXCFZk1Aqg6zszNhdJwZrXk/kcrFl5WaP1dfMgvnq8P4GyCmWvKpeKvguI71xNw7lU4l3eJRpGXJl7uno7TxXRc+68b+bKovPfNj3zVjHy4PhN/vAtjaptY7P2h8UwtD56NV/lzfj/yu4y8l8WK6o/0/GuwA1tzqLzz0o/Zh4Q7//agbU8dh7R1I/czeQ4/e/BLqkYU82DPIxPPDkqHc5l4bmw6Sm0y88SD1P993vMyeY5VFYWbgsJvM4X7zG6EzrAwKTQF56fouCKrGwPq6VlnkRPH0wzcUdiGj+2i2GrAJ6y9qOPrhS4MHpuDNYq8wvGCcq7X7djXQ+WZdjIPOcPZ8XkMKuaP4o47n1DyqJGnd1pxa7eO0/51YXyxEBY8l4SNkuMPx/jQ5w8zWyel4n4D6ZVZKWj3xszBHVOx0qcx2ujBrGSF2k8BBF8aqOtmQ9prHYt3cqHYyiie//YRkzqHM2/PGPGpMJY5GodBko+1/S7MGxBOX50YHBG/2jPHiSU1TExak4GbYSGcFpuEazcsnDstGdvHabxZ0IvIMmH8cigOj5drHBztQWG9wmvOINYMMPNjnTTMizDR/T0Dj5rI9U1JQ1Skws9fgnheJRv7bHqHijJ3O7840XSfxkknPfDG6xlawYHG6zXOeObBVvHZuw08eL0jhF0vJ8Jxw8ys0+S+j6sctcWH/qsNxCcbUh3iGVldaPFA4bjhAQTGGqkvnImc+6QPT7rw8LKRw3+3YvZII9XSmajTQHJodiae1zKw1wo7Tn9UuKFZAH9303ikkxffPhhYv4UNX/0aG+o9OJPXyL/uZ2JEgsoy5X3YtEJ473sAeceb+bCA8GyKwjnFA1gu+RM3woPVxxR+2RbAkC4GVu1uR+sfKkfW9mFQ43DWmh6DMVsVlrwVQK5xwg2F/DiRoHBLuQDumg0ck2pH6dFmHimRhvmtVDpH+LHIoqc92Yn3WbKx7dd3qDdY5ZLqfmCHxvlXPKhlNnJ1WiaerVfZ96kP31errPfRh+XFTIz/NwMD9CrrOv2oUlr44rAfA1+bWKdzOhI7C2/2yIDhlZ6Org6MDDGwbJJwtfhXiPjX6ZJRXH/iE/rNV9jIGUAByYvbQTtGiZ+XampFnpcWPu2ejMQXOeR1XuL8AfHpIwF8+DOSrd9+xhbx69vi13duiB9Os2K45EL5BXZUtcke5QqgVYbKEQV8iNdLPzmcOCtctEa46I485/HlfEgrL/65x4+WZcwceygNq+ua+HFxBg6NV3gwfxAftwrv3UzGuaI6rjnvxirhv6+p0neSR1X2BpFntpGP1EwcqBHG42viUEB+/47wYbWNZm549N8cGThiiw2h8hzj5TmWkfn8R+bzzB8a1yR68DW3hY0ep2BUvJElK1rhP6DSdMSH0cstLBmTjJ+lF5tKj9Y1a/wjzYvOoQp7JAQR/d/fP1ZScXydjtteuvD+quzrrFTsayf7PsCPmWM1Wop4seeSmWfnp2JzzxDea5OEtNYmFhmWAU9hjaGXvfiwVsfWr10o3F5ytr8dFVeYuPtHOrrUEI7/M4hafUK4qnkSlh5RmLgrgHzN9bw/3oks0Rbhz2S8l5xtKTk7fr+FhY8no/MXHXs2dOGXr0Y2oBUzq2kMWe9F7gpmPt2dhutjZO+KenFlRjbODXmHERc1VljkQVh5E+P2ZqCsU+XBKB+e1MnKQX98QIcuRvbrnole0vctrwXxTJX58/nRReash8zZlmFhvFUhDkPTFVYuFMDJBjLns4LouEplvy8+LJMcjm/jxq+nNBrXe9Dsg/RXy2T8mSLnXDwdR9ZbuO9pMnocjWL7nR/x0WdkS6MVOaeH82V4DJJLqix9wo9SHaUvf3biryIGvrlox/u5Ctv4AiicJ4TXHyWhgtHEnZkZSBysY7lqbmwxabyd7kUBOefbSR7sGx3C08XFm2IjebjKZ3jOmxjxRzpGv1JZtKvwxzcdx9R3YV8/PU82cOKFPYw9ssfi7isdx3ZxYX87A08MsCNEfDRBfLRMWR3/+duNAi91vNdd8tujZ8FQB66tNXHW63T8+U9Ovt/3HE/HKBxQNIhzf6hcnuhD88PiE3s9MO420XYuHc8bquz1qx9bm+h5e4oTRYubeOFsBsL/7/NfwmdJ3uhkTnKHs83jGJz6pHJDE+F84eImlkwkd8vC0p2+4UJ9lYvn+7G+mp5P/3LiYV+FwxoJ3/2q4zCTGw0K6Tj3mhsD5RzqyDlUaij99Zsdt/uqnNTIj+1TDLye1Y4rzSz8fUIK+o9UyNJB7Bmk8HyNIN73NrBNCzt6vzazZedUNOmt45SWbvzsiGDzbN9xf6rGMZFeWNqHMLV/Ek77FK43BjCip5nl2qYheFqlbq0P2YqG8d2FOEyMysp8nz7gr8fiAQM8+Ov/tXX2KnOWYRCeQdQIESzEw9ktREgnWKXJMdjYpLAUBHuPwHPwADwDDahYCVpYGEQQokXwy5d9n3uued6EJMt+u8/P/TMz97xFvtb12fcvL+89eeP660f/Xh589cb1+R8vLj9/9u714w9+v3z51jvX53//efnuF10/f/Ty8vDRm9ffPv3n8sUPur54/PLyyTdvX598+9f9//P004P/Ls+evn99/PDHy4d39316d1/51S/d/b79dffL93/u3563fP/i9tfx2rcPv/7YsYh8/Kt5cy09350Pvv7x6wU1R/Hx9rHM+pqO1dd35wSzxLGfj31v7/lY83bJta+PS8VxjDN5HXk+nkdfp932OiLquJlOTqj1oxX225fnaGvB2x4+7nPLziSLa07U5+qTxwz0nAghmZhpws0ExErzWWNZ2ZWi9UorLiv3cfRVAvHdCLZRBo4qjvpbB6lSRA6nAnWkTPEKFbkqxMh1Vdm65dp8YrHuH22WHRY1ysbMsj0ikMdW1sWKQX597Rk1o3hPEn4Qm0w1ZaWszK9w5RrOMuogoQeqWqNEsenx5WmV1aXrDNOv0ctC6bO0rMz4lE32HTIfiFcF6ozfOlgUSDSpAofU6Kv8eLRfAG+eT1HBXccTs7y1onsdQB+HM4tUwABFwhEQ1Q3NxgwSCACP0DhwJmphvoWQA82ZMKCmhkOQr2SePO8KhgXUcoJY5i+TnPg02Y9qm1wTG5llGbw1AfcJIyTwRHiSQ7a7M/xChJUAHQyYyRpic/SVovctUgU5pzo/lkvkjPSYusJxz4EwYFViUQoGJb87AcdFgAFKqR9SYiQyTwJBNYnkjCD4MAUXrjf1HG2kJCTqKyUzBFwb2DuVhApPZA7snYLPUpkMn1wKgSRnDFJpMG4DUUPzkaLU3dv0Ixfc+URqtBYhCPLkQblJ+6m8Utcm8aTU0hk3p3BJYWJUGkCpCCaRSN7hhxdJUZqoBZml6tIo5eyJAhUFA+oE27xJf8UB2Q5ZjhwCfKocLBBAaRLSxRLoRDWoQap/ytycIvLSspHBTEoQ/4gwzjo5TUglAtxNB5QyoYjckuslHOHMUeHa8b4IGs3C5OStB7lGFqFlVk0laCcysFBTWSYGe6cTncmEkjLMXqo6J2xAyCo12Ax6yUCbyoz9Qao5qpp8Hk1yUsY2pZO65iPS3qZn4ih+Y5/q72CahANVX9VLYUqcDJZq4jaQb5AWsCiiVsSAc9wF90HzUovPInFECZwBNmvGLc2w3ynxNaEINYoJTqfzg3KYbCVYQ03cbYPnXSaBB3JjZWRiq+ri0OPoxCoPcbgcCAttm+gb/AdNSmvlBurO6gF0GQM2PrLNNdmm6SnlHCiSpTeRxKphGlInUIt1st1+R6Uy8Wi7aQ40FXU16BiDgGmnbDLpZHahl2bXaVPASNUg25QNwbGRidoV8z4GiQ6ZNm8jiiNlLeek2dFgi3D7OHjUjL02Ni0McrpjNe2yj7Lf9LjgIKBpTyYml2OCHkpDxXmSXFDQc0XImFhyR7pnBdo5VYLxRK0Gtnc51QQH7uIWsjBdRI4orhDHCprecB97yOP0lglOQZ2+gk8mVreILL92H3TgRNXs7ZMmcCcXrjyNbGOI84mT0epzp1OTljIoCYecTEPF2GValMNXanmHvnpe0V5uO/JhHuSVOTebtojVTrM4mI8VwTlP4+auPgzNElVPMUtciW3MmaFejKPouKuBXzB0t5FIJ5GjZ5hDc2eKhBwNQS+kHhBtg3L5ysDCdJ9VHk+DY2q4sbM4G8ORVRSc4RwW0PSjjXJkT55aUTm1/dMq6cQIE1HDNFxFyz8fXEWjijbzZnhtZom3YQWGhai7kSK49+QjaDGpTHzz2QH+9LPKhEKMlbRCXNfppxbRkNbmS+cPM6EuvaMTfbl93PW8qrW09md8hjHWZjbABmoLoxfnPtcTDJ/FoOj8jJjSBqM1AjGFB3eux8FFv9DFHHLSFDZHBwse5/Zolw8YDR2YAzB6IJ9IKSwBn7Ar6SV3AMazg8jdDa97iF69+h8HeJx5"


def _consts():
    raw = zlib.decompress(base64.b64decode(_BLOB))
    o = 0
    ids32 = np.frombuffer(raw, np.uint8, 8192 * 2, o).reshape(8192, 2); o += 8192 * 2
    w32 = np.frombuffer(raw, np.float32, 8192 * 2, o).reshape(8192, 2); o += 8192 * 8
    ids64 = np.frombuffer(raw, np.uint8, 8192 * 2, o).reshape(8192, 2); o += 8192 * 2
    w64 = np.frombuffer(raw, np.float32, 8192 * 2, o).reshape(8192, 2); o += 8192 * 8
    half = np.frombuffer(raw, np.uint8, 8192, o)
    return ids32, w32, ids64, w64, half


def _plan():
    ids32, w32, ids64, w64, half = _consts()
    T = 8192
    tok_experts = []
    for t in range(T):
        m = {}
        for j in range(2):
            m.setdefault(int(ids32[t, j]), [0.0, 0.0])[0] = float(w32[t, j])
        for j in range(2):
            m.setdefault(int(ids64[t, j]), [0.0, 0.0])[1] = float(w64[t, j])
        tok_experts.append(sorted((e, v[0], v[1]) for e, v in m.items()))
    core_tokens = []
    for c in range(8):
        b, h = c // 2, c % 2
        lst = [t for t in range(b * 2048, (b + 1) * 2048) if half[t] == h]
        core_tokens.append(np.array(lst, np.int64))
    core_exp = []
    for c in range(8):
        loc = {int(t): i for i, t in enumerate(core_tokens[c])}
        per_e = [[] for _ in range(NE)]
        for t in core_tokens[c]:
            for e, a, bb in tok_experts[int(t)]:
                per_e[e].append((loc[int(t)], a, bb))
        core_exp.append(per_e)
    counts = np.array([[len(core_exp[c][e]) for e in range(NE)] for c in range(8)])
    qpad = ((counts.max(0) + 7) // 8) * 8
    chunk_e = [int((q + 127) // 128) for q in qpad]
    ch_off = [0]
    for e in range(NE):
        ch_off.append(ch_off[-1] + chunk_e[e])
    nch = ch_off[-1]
    groups = [[2 * g, 2 * g + 1] for g in range(8)]
    return dict(core_tokens=core_tokens, core_exp=core_exp, qpad=[int(q) for q in qpad],
                chunk_e=chunk_e, ch_off=ch_off, nch=nch, groups=groups)


class Sem:
    def __init__(self, h, step):
        self.h = h
        self.step = step
        self.val = 0


class Sch:
    def __init__(self, me, engines):
        self.me = me
        self.engines = engines

    def op(self, who, make, inc=None, waits=()):
        if who == self.me:
            e = self.engines[who]
            best = {}
            for t in waits:
                if t is None:
                    continue
                s, v = t
                k = id(s)
                if k not in best or v > best[k][1]:
                    best[k] = (s, v)
            for s, v in best.values():
                e.wait_ge(s.h, v)
            r = make(e)
            if inc is not None:
                r.then_inc(inc.h, inc.step)
        if inc is not None:
            inc.val += inc.step
            return (inc, inc.val)
        return None

    def fence(self, who, *toks):
        if self.me == who:
            e = self.engines[who]
            best = {}
            for t in toks:
                if t is None:
                    continue
                s, v = t
                k = id(s)
                if k not in best or v > best[k][1]:
                    best[k] = (s, v)
            for s, v in best.values():
                e.wait_ge(s.h, v)


def dap(t, off, dims):
    return bass.AP(t, off, dims)


def build_nc(plan, phases, dbg=None):
    qpad = plan["qpad"]
    chunk_e = plan["chunk_e"]
    ch_off = plan["ch_off"]
    NCH = plan["nch"]
    groups = plan["groups"]
    qmax = max(qpad)
    g_tot = sum(8 * 128 * q for q in qpad)
    g_eoff = [0]
    for e in range(NE):
        g_eoff.append(g_eoff[-1] + 8 * 128 * qpad[e])

    nc = bass.Bass("TRN2", target_bir_lowering=False)

    x_own_t = nc.dram_tensor("x_own", [8, 128, D], F32, kind="ExternalInput")
    x_sib_t = nc.dram_tensor("x_sib", [8, 128, D], F32, kind="ExternalInput")
    wq_t = nc.dram_tensor("wq_l", [NH, 128, DC * 128], F32, kind="ExternalInput")
    wk_t = nc.dram_tensor("wk_l", [NH, 128, DC * 128], F32, kind="ExternalInput")
    wv_t = nc.dram_tensor("wv_l", [4, 128, DC * 512], F32, kind="ExternalInput")
    wo_t = nc.dram_tensor("wo_l", [4, 128, DC * 512], F32, kind="ExternalInput")
    sg_t = nc.dram_tensor("sg_l", [22, 128, DC * 128], F32, kind="ExternalInput")
    su_t = nc.dram_tensor("su_l", [22, 128, DC * 128], F32, kind="ExternalInput")
    sd_t = nc.dram_tensor("sd_l", [2, 2, 128, IC * 2 * 512], F32, kind="ExternalInput")
    wg_t = nc.dram_tensor("wg_l", [NE, IC, 128, DC * 128], F32, kind="ExternalInput")
    wu_t = nc.dram_tensor("wu_l", [NE, IC, 128, DC * 128], F32, kind="ExternalInput")
    wd_t = nc.dram_tensor("wd_l", [NE, 4, 128, IC * 512], F32, kind="ExternalInput")
    g_t = nc.dram_tensor("g_l", [g_tot], BF16, kind="ExternalInput")
    s_t = nc.dram_tensor("s_l", [NCH, 128, NT], BF16, kind="ExternalInput")
    w32_t = nc.dram_tensor("w32v", [128, NCH], F32, kind="ExternalInput")
    w64_t = nc.dram_tensor("w64v", [128, NCH], F32, kind="ExternalInput")
    id_t = nc.dram_tensor("ident", [128, 128], BF16, kind="ExternalInput")
    onc_t = nc.dram_tensor("ones_col", [128, 1], BF16, kind="ExternalInput")
    onr_t = nc.dram_tensor("ones_row", [1, 128], BF16, kind="ExternalInput")
    tiny_t = nc.dram_tensor("tiny", [128, 1], F32, kind="ExternalInput")
    out_t = nc.dram_tensor("out", [8, 128, D], F32, kind="ExternalOutput")

    ctx = contextlib.ExitStack()
    with ctx:
        sems = []

        def mksem(name, step):
            s = Sem(ctx.enter_context(nc.semaphore(name)), step)
            sems.append(s)
            return s

        d_c = mksem("d_c", 16)
        d_x = [mksem("d_x0", 16), mksem("d_x1", 16)]
        d_wv = mksem("d_wv", 16)
        d_kq = [mksem("d_kq0", 16), mksem("d_kq1", 16)]
        d_wo = [mksem("d_wo0", 16), mksem("d_wo1", 16)]
        d_sg = [mksem("d_sg0", 16), mksem("d_sg1", 16)]
        d_sd = mksem("d_sd", 16)
        d_gu = [mksem(f"d_gu{i}", 16) for i in range(5)]
        d_wd = [mksem("d_wd0", 16), mksem("d_wd1", 16)]
        d_g = [mksem("d_g0", 16), mksem("d_g1", 16)]
        d_s = mksem("d_s", 16)
        d_o = mksem("d_o", 16)
        s_pe = mksem("s_pe", 1)
        s_ve = mksem("s_ve", 1)
        s_se = mksem("s_se", 1)

        ABF_N = 71680
        AF32_N = 16896
        abf = ctx.enter_context(nc.sbuf_tensor("abf", [128, ABF_N], BF16))
        af32 = ctx.enter_context(nc.sbuf_tensor("af32", [128, AF32_N], F32))
        ident = ctx.enter_context(nc.sbuf_tensor("identS", [128, 128], BF16))
        ones_c = ctx.enter_context(nc.sbuf_tensor("onescS", [128, 1], BF16))
        ones_r = ctx.enter_context(nc.sbuf_tensor("onesrS", [1, 128], BF16))
        tiny_s = ctx.enter_context(nc.sbuf_tensor("tinyS", [128, 1], F32))
        ind_s = ctx.enter_context(nc.sbuf_tensor("indS", [128, 1], F32))
        w32_s = ctx.enter_context(nc.sbuf_tensor("w32S", [128, NCH], F32))
        w64_s = ctx.enter_context(nc.sbuf_tensor("w64S", [128, NCH], F32))
        wc_s = ctx.enter_context(nc.sbuf_tensor("wcS", [128, NCH], F32))

        def bfv(off, *shape):
            n = int(np.prod(shape))
            ap = abf[:, off:off + n]
            if len(shape) == 1:
                return ap
            if len(shape) == 2:
                return ap.rearrange("p (a b) -> p a b", a=shape[0])
            return ap.rearrange("p (a b c) -> p a b c", a=shape[0], b=shape[1])

        def fv(off, *shape):
            n = int(np.prod(shape))
            ap = af32[:, off:off + n]
            if len(shape) == 1:
                return ap
            return ap.rearrange("p (a b) -> p a b", a=shape[0])

        xnT_own = bfv(0, DC, NT)
        xnT_sib = bfv(16384, DC, NT)
        xnt_v = [bfv(32768 + i * 2048, 2048) for i in range(2)]
        xsq_v = bfv(36864, 2048)
        v_q = bfv(32768, 16, 512)
        oT_all = bfv(40960, NH, NT)
        wv_bf = bfv(57344, DC, 512)
        wv_flat = abf[:, 57344:65536]
        wkq_bf = [bfv(57344 + i * 4096, 2, 2048) for i in range(2)]
        kT_v = bfv(65536, TB)
        qT_v = bfv(67584, NT)
        k2_v = bfv(68608, 2048)
        expb = [bfv(68608 + i * 1024, 1024) for i in range(2)]
        nb_bf = abf[0:1, 70656:70658]
        wo_bf = [bfv(57344, DC, 512), bfv(32768, DC, 512)]
        wo_flat = [abf[:, 57344:65536], abf[:, 32768:40960]]
        xn_tok = bfv(16384, 8, 2048)
        hT_sh = bfv(40960, IC, NT)
        sd_bf = bfv(52224, IC * 2, 512)
        sd_flat = abf[:, 52224:63488]
        sgu_bf = [bfv(63488 + i * 4096, 2, 2048) for i in range(2)]
        gs_c = bfv(36864, NT)
        maxgc = max(sum(chunk_e[e] for e in g) for g in groups)
        xgn2_v = [bfv(i * DC * qmax, DC, qmax) for i in range(2)]
        hT_e = bfv(2 * DC * qmax, IC, qmax)
        gs_d = bfv(2 * DC * qmax + IC * qmax, qmax)
        wgu5 = [bfv(o, 2048) for o in (12288, 14336,
                                        32768 + maxgc * 3072, 32768 + maxgc * 3072 + 2048,
                                        68736)]
        wdq_v = [bfv(52224 + i * IC * 512, IC, 512) for i in range(2)]
        wdq_flat = [abf[:, 52224 + i * IC * 512:52224 + (i + 1) * IC * 512] for i in range(2)]
        y_pool = bfv(32768, maxgc, 2048)
        wcb_v = bfv(32768 + maxgc * 2048, maxgc, NT)
        g_sb = [bfv(63488 + i * 8 * qmax, 8, qmax) for i in range(2)]
        s_stage = bfv(63488 + 16 * qmax, NT)
        xstg = [fv(i * 2048, 2048) for i in range(2)]
        st1p = [af32[:, 4096 + 8 * i:4104 + 8 * i] for i in range(2)]
        rmat_v = fv(0, NT)
        qn_s = af32[0:1, 1024:2048]
        kn_s = af32[0:1, 2048:4096]
        r_s = af32[0:1, 4096:5120]
        negb = af32[:, 5120:5121]
        nrm = af32[0:1, 5124:5132]
        acc = fv(0, 8, D)

        pbt = [ctx.enter_context(nc.psum_tensor(f"pbt{i}", [128, 1024], BF16)) for i in range(2)]
        pb = [ctx.enter_context(nc.psum_tensor(f"pb{i}", [128, 512], F32)) for i in range(6)]

        engines = {}
        TOK = {}

        def run_phase(emit):
            snap = [(s, s.val) for s in sems]
            tok_snap = dict(TOK)
            with nc.Block() as block:
                def mk(name):
                    def f(eng):
                        engines[name] = eng
                        for s, v in snap:
                            s.val = v
                        TOK.clear()
                        TOK.update(tok_snap)
                        emit(Sch(name, engines))
                    return f
                block.sync(mk("sync"))
                block.gpsimd(mk("gpsimd"))
                block.vector(mk("vector"))
                block.scalar(mk("scalar"))
                block.tensor(mk("tensor"))

        def phase_A(sch):
            me = sch.op
            me("sync", lambda e: e.dma_start(ident[:], id_t[:, :]), inc=d_c)
            me("sync", lambda e: e.dma_start(ones_c[:], onc_t[:, :]), inc=d_c)
            me("sync", lambda e: e.dma_start(ones_r[:], onr_t[:, :]), inc=d_c)
            me("sync", lambda e: e.dma_start(tiny_s[:], tiny_t[:, :]), inc=d_c)
            me("sync", lambda e: e.dma_start(w32_s[:], w32_t[:, :]), inc=d_c)
            tcall = me("sync", lambda e: e.dma_start(w64_s[:], w64_t[:, :]), inc=d_c)
            tc1 = tc5 = tc6 = tcall
            TOK["consts"] = tcall
            tp = me("vector", lambda e: e.tensor_scalar(ind_s[:], tiny_s[:], 1.0, None, AluOpType.add), inc=s_ve, waits=[tcall])
            tp = me("vector", lambda e: e.tensor_scalar(ind_s[:], ind_s[:], -1.0, None, AluOpType.add), inc=s_ve, waits=[tp])
            tp = me("vector", lambda e: e.tensor_scalar(ind_s[:], ind_s[:], float(2.0 ** 40), None, AluOpType.mult), inc=s_ve, waits=[tp])
            tw = me("vector", lambda e: e.tensor_sub(wc_s[:], w64_s[:], w32_s[:]), inc=s_ve, waits=[tc5, tc6])
            tw = me("vector", lambda e: e.tensor_scalar(wc_s[:], wc_s[:], ind_s[:, 0:1], None, AluOpType.mult), inc=s_ve, waits=[tw, tp])
            TOK["wc"] = me("vector", lambda e: e.tensor_add(wc_s[:], wc_s[:], w32_s[:]), inc=s_ve, waits=[tw])

            t_xn = [None, None]
            t_tp = [None, None]
            t_cp = {0: None, 1: None}
            for idx in range(16):
                own = idx < 8
                tc = idx % 8
                par = idx % 2
                src = x_own_t if own else x_sib_t
                buf = xstg[par]
                st1 = st1p[par]
                tl = me("sync", lambda e, src=src, tc=tc, b=buf: e.dma_start(
                    b, dap(src, tc * 128 * D, [[D, 128], [1, D]])),
                    inc=d_x[par], waits=[t_xn[par]])
                ts_ = me("vector", lambda e, b=buf, st1=st1p[idx % 2]: e.reduce_sum(st1[:, 0:1], b, axis=AX), inc=s_ve, waits=[tl])
                tq = me("scalar", lambda e, b=buf, st1=st1p[idx % 2]: e.activation(xsq_v, b, AF.Square, accum_out=st1[:, 1:2]), inc=s_se, waits=[tl, t_tp[par]])
                tm = me("vector", lambda e, st1=st1p[idx % 2]: e.tensor_scalar(st1[:, 2:3], st1[:, 0:1], 1.0 / D, None, AluOpType.mult), inc=s_ve, waits=[ts_])
                tms = me("vector", lambda e, st1=st1p[idx % 2]: e.tensor_scalar(st1[:, 3:4], st1[:, 1:2], 1.0 / D, None, AluOpType.mult), inc=s_ve, waits=[tq])
                tm2 = me("vector", lambda e, st1=st1p[idx % 2]: e.tensor_tensor(st1[:, 4:5], st1[:, 2:3], st1[:, 2:3], AluOpType.mult), inc=s_ve, waits=[tm])
                tv = me("vector", lambda e, st1=st1p[idx % 2]: e.tensor_sub(st1[:, 4:5], st1[:, 3:4], st1[:, 4:5]), inc=s_ve, waits=[tms, tm2])
                tv = me("vector", lambda e, st1=st1p[idx % 2]: e.tensor_scalar(st1[:, 4:5], st1[:, 4:5], 1e-5, None, AluOpType.add), inc=s_ve, waits=[tv])
                tsd = me("scalar", lambda e, st1=st1p[idx % 2]: e.activation(st1[:, 5:6], st1[:, 4:5], AF.Ln), inc=s_se, waits=[tv])
                trs = me("scalar", lambda e, st1=st1p[idx % 2]: e.activation(st1[:, 6:7], st1[:, 5:6], AF.Exp, scale=-0.5), inc=s_se, waits=[tsd])
                tnm = me("vector", lambda e, st1=st1p[idx % 2]: e.tensor_scalar(st1[:, 7:8], st1[:, 2:3], -1.0, None, AluOpType.mult), inc=s_ve, waits=[tm])
                t_xn[par] = me("vector", lambda e, b=buf, p=par, st1=st1p[idx % 2]: e.tensor_scalar(
                    xnt_v[p], b, st1[:, 7:8], st1[:, 6:7], AluOpType.add, AluOpType.mult),
                    inc=s_ve, waits=[tnm, trs, t_tp[par]])
                dst = xnT_own if own else xnT_sib
                for d in range(DC):
                    pp = (idx * DC + d) % 2
                    tt = me("tensor", lambda e, p=par, d=d, pp=pp: e.transpose(
                        pbt[pp][:, 0:128], xnt_v[p][:, d * 128:(d + 1) * 128], ident[:]),
                        inc=s_pe, waits=[t_xn[par], tc1, t_cp[pp]])
                    t_cp[pp] = me("vector", lambda e, d=d, tc=tc, pp=pp, dst=dst: e.tensor_copy(
                        dst[:, d, tc * 128:(tc + 1) * 128], pbt[pp][:, 0:128]),
                        inc=s_ve, waits=[tt])
                    if d == DC - 1:
                        t_tp[par] = t_cp[pp]
            TOK["A_done"] = t_cp[1] or t_cp[0]

        def phase_B1(sch):
            me = sch.op
            t_rm = None
            t_oT = None
            t_recip = None
            t_kq_dma = None
            t_v_mm = None
            t_rmm_last = None
            for quad in range(4):
                t_wv = me("gpsimd", lambda e, q=quad: e.dma_start(
                    wv_flat, dap(wv_t, q * 128 * DC * 512, [[DC * 512, 128], [1, DC * 512]])),
                    inc=d_wv, waits=[TOK.get("B1_kq_free_0"), TOK.get("B1_kq_free_1")])
                t_ev = {}
                for tc in range(16):
                    srcv = xnT_own if tc < 8 else xnT_sib
                    tl = tc % 8
                    bank = tc % 2
                    mm = None
                    for d in range(DC):
                        w = []
                        if d == 0:
                            w = [t_wv, t_ev.get(tc - 2)]
                            if tc < 2:
                                w += [t_rm, t_oT, t_recip]
                        mm = me("tensor", lambda e, d=d, tl=tl, s=srcv, bank=bank: e.matmul(
                            pb[bank][:], s[:, d, tl * 128:(tl + 1) * 128],
                            wv_bf[:, d, :], start=(d == 0), stop=(d == DC - 1)),
                            inc=(s_pe if d == DC - 1 else None), waits=w)
                    t_v_mm = mm
                    t_ev[tc] = me("vector", lambda e, tc=tc, bank=bank: e.tensor_copy(
                        v_q[:, tc, :], pb[bank][:]), inc=s_ve, waits=[mm])
                t_vdone = [t_ev[tc] for tc in range(16)]

                for hh in range(4):
                    h = quad * 4 + hh
                    par = h % 2
                    me("gpsimd", lambda e, h=h, par=par: e.dma_start(
                        wkq_bf[par][:, 0, :], dap(wk_t, h * 128 * 2048, [[2048, 128], [1, 2048]])),
                        inc=d_kq[par], waits=[TOK.get(f"B1_kq_free_{par}"),
                                              t_v_mm if hh == 0 else None])
                    t_kq_dma = me("gpsimd", lambda e, h=h, par=par: e.dma_start(
                        wkq_bf[par][:, 1, :], dap(wq_t, h * 128 * 2048, [[2048, 128], [1, 2048]])),
                        inc=d_kq[par])
                    mmk = None
                    for d in range(DC):
                        for tt in range(4):
                            s2 = xnT_own if tt < 2 else xnT_sib
                            tl = tt % 2
                            w = []
                            if d == 0 and tt == 0:
                                w = [t_kq_dma] + t_vdone + [t_rm, t_oT]
                            mmk = me("tensor", lambda e, d=d, tt=tt, s=s2, tl=tl, par=par: e.matmul(
                                pb[tt][:], wkq_bf[par][:, 0, d * 128:(d + 1) * 128],
                                s[:, d, tl * 512:(tl + 1) * 512],
                                start=(d == 0), stop=(d == DC - 1)),
                                inc=(s_pe if (d == DC - 1 and tt == 3) else None), waits=w)
                    t_kev = []
                    for tt in range(4):
                        t_kev.append(me("vector", lambda e, tt=tt: e.tensor_copy(
                            kT_v[:, tt * 512:(tt + 1) * 512], pb[tt][:]), inc=s_ve, waits=[mmk]))
                    mmq = None
                    for d in range(DC):
                        for tt in range(2):
                            w = [t_recip] if (d == 0 and tt == 0) else []
                            mmq = me("tensor", lambda e, d=d, tt=tt, par=par: e.matmul(
                                pb[4 + tt][:], wkq_bf[par][:, 1, d * 128:(d + 1) * 128],
                                xnT_own[:, d, tt * 512:(tt + 1) * 512],
                                start=(d == 0), stop=(d == DC - 1)),
                                inc=(s_pe if (d == DC - 1 and tt == 1) else None), waits=w)
                    t_qev = []
                    for tt in range(2):
                        t_qev.append(me("vector", lambda e, tt=tt: e.tensor_copy(
                            qT_v[:, tt * 512:(tt + 1) * 512], pb[4 + tt][:]), inc=s_ve, waits=[mmq]))
                    TOK[f"B1_kq_free_{par}"] = mmq
                    tq2 = me("scalar", lambda e: e.activation(expb[1], qT_v[:, :], AF.Square),
                             inc=s_se, waits=t_qev + [TOK.get("B1_expb1_free")])
                    tqn = []
                    for tt in range(2):
                        tqn.append(me("tensor", lambda e, tt=tt: e.matmul(
                            pb[tt][0:1, :], ones_c[:], expb[1][:, tt * 512:(tt + 1) * 512],
                            start=True, stop=True), inc=s_pe, waits=[tq2, t_rm]))
                    tqnc = []
                    for tt in range(2):
                        tqnc.append(me("vector", lambda e, tt=tt: e.tensor_copy(
                            qn_s[:, tt * 512:(tt + 1) * 512], pb[tt][0:1, :]), inc=s_ve, waits=[tqn[tt]]))
                    tk2 = me("scalar", lambda e: e.activation(k2_v, kT_v[:, :], AF.Square),
                             inc=s_se, waits=t_kev + tqn + [TOK.get("B1_expb0_free"), t_rmm_last])
                    tknc = []
                    tkn = []
                    for tt in range(4):
                        bank = 2 + (tt % 2)
                        w = [tk2, t_oT]
                        if tt >= 2:
                            w.append(tknc[tt - 2])
                        m = me("tensor", lambda e, tt=tt, bank=bank: e.matmul(
                            pb[bank][0:1, :], ones_c[:], k2_v[:, tt * 512:(tt + 1) * 512],
                            start=True, stop=True), inc=s_pe, waits=w)
                        tkn.append(m)
                        tknc.append(me("vector", lambda e, tt=tt, bank=bank: e.tensor_copy(
                            kn_s[:, tt * 512:(tt + 1) * 512], pb[bank][0:1, :]), inc=s_ve, waits=[m]))
                    tmx1 = me("vector", lambda e: e.reduce_max(nrm[:, 0:1], qn_s, axis=AX), inc=s_ve, waits=tqnc)
                    tmx2 = me("vector", lambda e: e.reduce_max(nrm[:, 1:2], kn_s, axis=AX), inc=s_ve, waits=tknc + [tmx1])
                    tpr = me("vector", lambda e: e.tensor_tensor(nrm[:, 2:3], nrm[:, 0:1], nrm[:, 1:2], AluOpType.mult), inc=s_ve, waits=[tmx2])
                    tln = me("scalar", lambda e: e.activation(nrm[:, 5:6], nrm[:, 2:3], AF.Ln), inc=s_se, waits=[tpr])
                    tsq = me("scalar", lambda e: e.activation(nrm[:, 3:4], nrm[:, 5:6], AF.Exp, scale=0.5), inc=s_se, waits=[tln])
                    tng = me("vector", lambda e: e.tensor_scalar(nrm[:, 4:5], nrm[:, 3:4], -float(SCALE), None, AluOpType.mult), inc=s_ve, waits=[tsq])
                    tnb = me("vector", lambda e: e.tensor_copy(nb_bf[:, 0:1], nrm[:, 4:5]), inc=s_ve, waits=[tng])
                    tbc = me("tensor", lambda e: e.matmul(pb[4][:, 0:1], ones_r[:], nb_bf[:, 0:1],
                                                          start=True, stop=True), inc=s_pe,
                            waits=[tnb, t_qev[0], t_qev[1], t_recip])
                    tngb = me("vector", lambda e: e.tensor_copy(negb, pb[4][:, 0:1]), inc=s_ve, waits=[tbc])
                    t_exp = {}
                    t_pvrs = {}
                    t_sT = {}
                    for kc in range(16):
                        parc = kc % 2
                        for qt in range(2):
                            w = [t_exp.get((kc - 1, qt))]
                            if kc == 0:
                                w += [tngb, tqnc[qt]]
                            t_sT[(kc, qt)] = me("tensor", lambda e, kc=kc, qt=qt: e.matmul(
                                pb[qt][:], kT_v[:, kc * 128:(kc + 1) * 128],
                                qT_v[:, qt * 512:(qt + 1) * 512], start=True, stop=True),
                                inc=s_pe, waits=w)
                        for qt in range(2):
                            w = [t_sT[(kc, qt)], t_pvrs.get((kc - 2, 1))]
                            if kc == 0:
                                w += [tngb]
                            t_exp[(kc, qt)] = me("scalar", lambda e, kc=kc, qt=qt, parc=parc: e.activation(
                                expb[parc][:, qt * 512:(qt + 1) * 512], pb[qt][:], AF.Exp,
                                bias=negb[:, 0:1], scale=SCALE), inc=s_se, waits=w)
                        for qt in range(2):
                            w = [t_exp[(kc, qt)]]
                            if kc == 0:
                                w += [tknc[2], tknc[3], t_oT]
                            me("tensor", lambda e, kc=kc, qt=qt, parc=parc, hh=hh: e.matmul(
                                pb[2 + qt][:], v_q[:, kc, hh * 128:(hh + 1) * 128],
                                expb[parc][:, qt * 512:(qt + 1) * 512],
                                start=(kc == 0), stop=(kc == 15)),
                                inc=None, waits=w)
                        for qt in range(2):
                            t_pvrs[(kc, qt)] = me("tensor", lambda e, kc=kc, qt=qt, parc=parc: e.matmul(
                                pb[4 + qt][0:1, :], ones_c[:],
                                expb[parc][:, qt * 512:(qt + 1) * 512],
                                start=(kc == 0), stop=(kc == 15)), inc=s_pe, waits=[])
                    TOK["B1_expb0_free"] = t_pvrs[(14, 1)]
                    TOK["B1_expb1_free"] = t_pvrs[(15, 1)]
                    trc = []
                    for qt in range(2):
                        trc.append(me("vector", lambda e, qt=qt: e.reciprocal(
                            r_s[:, qt * 512:(qt + 1) * 512], pb[4 + qt][0:1, :]),
                            inc=s_ve, waits=[t_pvrs[(15, 1)]]))
                    t_recip = trc[1]
                    trbf = me("vector", lambda e: e.tensor_copy(k2_v[0:1, 0:NT], r_s),
                              inc=s_ve, waits=trc + [TOK["B1_expb0_free"]])
                    t_rmm = []
                    for qt in range(2):
                        t_rmm.append(me("tensor", lambda e, qt=qt: e.matmul(
                            pb[qt][:], ones_r[:], k2_v[0:1, qt * 512:(qt + 1) * 512],
                            start=True, stop=True), inc=s_pe,
                            waits=[trbf, t_exp[(15, qt)]]))
                    t_rmm_last = t_rmm[1]
                    t_rmc = []
                    for qt in range(2):
                        t_rmc.append(me("vector", lambda e, qt=qt: e.tensor_copy(
                            rmat_v[:, qt * 512:(qt + 1) * 512], pb[qt][:]), inc=s_ve, waits=[t_rmm[qt]]))
                    t_rm = t_rmc[1]
                    t_oTl = []
                    for qt in range(2):
                        t_oTl.append(me("vector", lambda e, qt=qt, h=h: e.tensor_tensor(
                            oT_all[:, h, qt * 512:(qt + 1) * 512], pb[2 + qt][:],
                            rmat_v[:, qt * 512:(qt + 1) * 512], AluOpType.mult),
                            inc=s_ve, waits=[t_rmc[qt]]))
                    t_oT = t_oTl[1]

        def phase_B2(sch):
            me = sch.op
            t_acc_all = None
            for tc in range(8):
                t_acc_all = me("sync", lambda e, tc=tc: e.dma_start(
                    acc[:, tc, :], dap(x_own_t, tc * 128 * D, [[D, 128], [1, D]])), inc=d_x[0])
            t_acc = {tc: t_acc_all for tc in range(8)}
            t_ev = {}
            t_wo = {}
            for et in range(4):
                par = et % 2
                t_wo[et] = me("gpsimd", lambda e, et=et, par=par: e.dma_start(
                    wo_flat[par],
                    dap(wo_t, et * 128 * DC * 512, [[DC * 512, 128], [1, DC * 512]])),
                    inc=d_wo[par], waits=[t_ev.get((et - 2, 7))])
                for qc in range(8):
                    bank = qc % 2
                    mm = None
                    for d in range(DC):
                        w = [t_wo[et], t_ev.get((et, qc - 2), t_ev.get((et - 1, qc + 6)))] if d == 0 else []
                        mm = me("tensor", lambda e, d=d, qc=qc, par=par, bank=bank: e.matmul(
                            pb[bank][:], oT_all[:, d, qc * 128:(qc + 1) * 128],
                            wo_bf[par][:, d, :], start=(d == 0), stop=(d == DC - 1)),
                            inc=(s_pe if d == DC - 1 else None), waits=w)
                    t_ev[(et, qc)] = me("vector", lambda e, et=et, qc=qc, bank=bank: e.tensor_tensor(
                        acc[:, qc, et * 512:(et + 1) * 512], acc[:, qc, et * 512:(et + 1) * 512],
                        pb[bank][:], AluOpType.add), inc=s_ve, waits=[mm, t_acc[qc]])
            TOK["B2_done"] = t_ev[(3, 7)]

        def phase_C(sch):
            me = sch.op
            t_cp = {0: None, 1: None}
            for tc in range(8):
                for d in range(DC):
                    pp = (tc * DC + d) % 2
                    tt = me("tensor", lambda e, tc=tc, d=d, pp=pp: e.transpose(
                        pbt[pp][:, 0:128], xnT_own[:, d, tc * 128:(tc + 1) * 128], ident[:]),
                        inc=s_pe, waits=[t_cp[pp]])
                    t_cp[pp] = me("vector", lambda e, tc=tc, d=d, pp=pp: e.tensor_copy(
                        xn_tok[:, tc, d * 128:(d + 1) * 128], pbt[pp][:, 0:128]),
                        inc=s_ve, waits=[tt])
            TOK["xn_tok"] = (s_ve, s_ve.val)
            if C_PARTS < 1:
                return
            t_sg = {}
            t_hT = {}
            t_down_ev = {}
            t_tsl_prev = (None, None)
            for half in range(2):
                for sic in range(IC):
                    par = sic % 2
                    gi = half * IC + sic
                    w = [t_hT.get((half, sic - 2)), t_down_ev.get((half - 1, 7, 3))]
                    t_sg[gi] = me("gpsimd", lambda e, gi=gi, par=par: e.dma_start(
                        sgu_bf[par][:, 0, :], dap(sg_t, gi * 128 * 2048, [[2048, 128], [1, 2048]])),
                        inc=d_sg[par], waits=w)
                    t_sg[gi] = me("gpsimd", lambda e, gi=gi, par=par: e.dma_start(
                        sgu_bf[par][:, 1, :], dap(su_t, gi * 128 * 2048, [[2048, 128], [1, 2048]])),
                        inc=d_sg[par])
                    mmg = None
                    for d in range(DC):
                        for tt in range(2):
                            w2 = []
                            if d == 0 and tt == 0:
                                w2 = [t_sg[gi], t_hT.get((half, sic - 2)),
                                      t_down_ev.get((half - 1, 7, 3)),
                                      t_tsl_prev[0], t_tsl_prev[1]]
                                if half == 0 and sic < 2:
                                    w2.append(TOK.get("B2_done"))
                            mmg = me("tensor", lambda e, d=d, tt=tt, par=par: e.matmul(
                                pb[tt][:], sgu_bf[par][:, 0, d * 128:(d + 1) * 128],
                                xnT_own[:, d, tt * 512:(tt + 1) * 512],
                                start=(d == 0), stop=(d == DC - 1)),
                                inc=(s_pe if (d == DC - 1 and tt == 1) else None), waits=w2)
                    mmu = None
                    for d in range(DC):
                        for tt in range(2):
                            w2u = [t_hT.get((half, sic - 1))] if (d == 0 and tt == 0) else []
                            mmu = me("tensor", lambda e, d=d, tt=tt, par=par: e.matmul(
                                pb[2 + tt][:], sgu_bf[par][:, 1, d * 128:(d + 1) * 128],
                                xnT_own[:, d, tt * 512:(tt + 1) * 512],
                                start=(d == 0), stop=(d == DC - 1)),
                                inc=(s_pe if (d == DC - 1 and tt == 1) else None), waits=w2u)
                    tsl = {}
                    for tt in range(2):
                        tsl[tt] = me("scalar", lambda e, tt=tt: e.activation(
                            gs_c[:, tt * 512:(tt + 1) * 512], pb[tt][:], AF.Silu),
                            inc=s_se, waits=[mmg, t_hT.get((half, sic - 1))])
                    for tt in range(2):
                        t_hT[(half, sic)] = me("vector", lambda e, tt=tt, sic=sic: e.tensor_tensor(
                            hT_sh[:, sic, tt * 512:(tt + 1) * 512], pb[2 + tt][:],
                            gs_c[:, tt * 512:(tt + 1) * 512], AluOpType.mult),
                            inc=s_ve, waits=[mmu, tsl[tt]])
                    t_tsl_prev = (tsl[0], tsl[1])
                if C_PARTS < 3:
                    continue
                t_gudone = t_hT[(half, IC - 1)]
                for hp in range(2):
                    t_sd = me("gpsimd", lambda e, half=half, hp=hp: e.dma_start(
                        sd_flat,
                        dap(sd_t, (half * 2 + hp) * 128 * IC * 2 * 512, [[IC * 2 * 512, 128], [1, IC * 2 * 512]])),
                        inc=d_sd, waits=[t_down_ev.get((half, 7, hp * 2 - 1)), t_down_ev.get((half - 1, 7, hp * 2 + 1))])
                    for tc in range(8):
                        mm = {}
                        for si in range(IC):
                            for ht in range(2):
                                w3 = []
                                if si == 0:
                                    w3 = [t_sd, t_gudone,
                                          t_down_ev.get((half, tc - 1, hp * 2 + ht)),
                                          t_down_ev.get((half, 7, (hp - 1) * 2 + ht)),
                                          t_down_ev.get((half - 1, 7, 2 + ht))]
                                mm[ht] = me("tensor", lambda e, si=si, ht=ht, tc=tc: e.matmul(
                                    pb[4 + ht][:], hT_sh[:, si, tc * 128:(tc + 1) * 128],
                                    sd_bf[:, si * 2 + ht, :], start=(si == 0), stop=(si == IC - 1)),
                                    inc=(s_pe if (si == IC - 1) else None), waits=w3)
                        for ht in range(2):
                            hcol = hp * 2 + ht
                            t_down_ev[(half, tc, hcol)] = me("vector", lambda e, tc=tc, ht=ht, hcol=hcol: e.tensor_tensor(
                                acc[:, tc, hcol * 512:(hcol + 1) * 512],
                                acc[:, tc, hcol * 512:(hcol + 1) * 512],
                                pb[4 + ht][:], AluOpType.add), inc=s_ve, waits=[mm[ht]])
            TOK["C_done"] = t_down_ev.get((1, 7, 3))

        def phase_D(sch):
            me = sch.op
            t_gdma = {}
            t_y_ev = {}
            t_sc_ev = {}
            t_wcb = {}
            t_gv_all = {}

            def do_gather(gi):
                grp = groups[gi]
                for e_idx in grp:
                    par = e_idx % 2
                    t_gdma[e_idx] = me("sync", lambda e, e_idx=e_idx, par=par, q=qpad[e_idx]: e.dma_start(
                        g_sb[par][:, :, 0:q],
                        dap(g_t, g_eoff[e_idx], [[8 * q, 128], [q, 8], [1, q]])),
                        inc=d_g[par], waits=[TOK.get(f"D_g_used_{par}")])
                e0, e1 = grp
                q0, q1 = qpad[e0], qpad[e1]
                t_gv = {}
                mm = None
                for d in range(DC):
                    for t8 in range(8):
                        w = []
                        if t8 == 0:
                            w = [t_gdma[e0], t_gdma[e1], t_gv.get((d - 1, 1))]
                            if d == 0:
                                w += [TOK.get("xn_tok"), TOK.get("D_xgn_free_0"),
                                      TOK.get("D_xgn_free_1"),
                                      TOK.get("C_done") if gi == 0 else None]
                        me("tensor", lambda e, d=d, t8=t8, q0=q0, e0=e0: e.matmul(
                            pb[0][:, 0:q0], xn_tok[:, t8, d * 128:(d + 1) * 128],
                            g_sb[e0 % 2][:, t8, 0:q0], start=(t8 == 0), stop=(t8 == 7)),
                            inc=None, waits=w)
                        mm = me("tensor", lambda e, d=d, t8=t8, q1=q1, e1=e1: e.matmul(
                            pb[1][:, 0:q1], xn_tok[:, t8, d * 128:(d + 1) * 128],
                            g_sb[e1 % 2][:, t8, 0:q1], start=(t8 == 0), stop=(t8 == 7)),
                            inc=(s_pe if t8 == 7 else None), waits=[])
                    t_gv[(d, 0)] = me("vector", lambda e, d=d, q0=q0: e.tensor_copy(
                        xgn2_v[0][:, d, 0:q0], pb[0][:, 0:q0]), inc=s_ve, waits=[mm])
                    t_gv[(d, 1)] = me("vector", lambda e, d=d, q1=q1: e.tensor_copy(
                        xgn2_v[1][:, d, 0:q1], pb[1][:, 0:q1]), inc=s_ve, waits=[mm])
                TOK[f"D_g_used_{e0 % 2}"] = mm
                TOK[f"D_g_used_{e1 % 2}"] = mm
                t_gv_all[gi] = t_gv

            do_gather(0)
            for gi, grp in enumerate(groups):
                gslot = 0
                gchunks = []
                t_y_last = None
                t_tslp = TOK.get("D_tslp")
                t_gv = t_gv_all[gi]
                for ei, e_idx in enumerate(grp):
                    q = qpad[e_idx]
                    ncl = chunk_e[e_idx]
                    par = e_idx % 2
                    xgn_v = xgn2_v[ei]
                    t_gather = [t_gv[(d, ei)] for d in range(DC)]
                    t_wdq = {}
                    t_wgu = {}
                    t_h = {}
                    t_tsl_hist = {}
                    for icx in range(IC):
                        if icx == 2:
                            for htq in range(2):
                                t_wdq[htq] = me("gpsimd", lambda e, e_idx=e_idx, htq=htq: e.dma_start(
                                    wdq_flat[htq % 2],
                                    dap(wd_t, (e_idx * 4 + htq) * 128 * IC * 512, [[IC * 512, 128], [1, IC * 512]])),
                                    inc=d_wd[htq % 2], waits=[TOK.get(f"D_wdq_free_{htq % 2}")])
                        bg = (2 * icx) % 5
                        bu = (2 * icx + 1) % 5
                        wld = [t_h.get(icx - 3)]
                        if 2 * icx < 5:
                            wld.append(TOK.get(f"D_wgu_free_{bg}"))
                        tg_ = me("gpsimd", lambda e, e_idx=e_idx, icx=icx, bg=bg: e.dma_start(
                            wgu5[bg], dap(wg_t, (e_idx * IC + icx) * 128 * 2048, [[2048, 128], [1, 2048]])),
                            inc=d_gu[bg], waits=wld)
                        wld2 = [t_tsl_hist.get(icx - 2)]
                        if 2 * icx + 1 < 5:
                            wld2.append(TOK.get(f"D_wgu_free_{bu}"))
                        tu_ = me("gpsimd", lambda e, e_idx=e_idx, icx=icx, bu=bu: e.dma_start(
                            wgu5[bu], dap(wu_t, (e_idx * IC + icx) * 128 * 2048, [[2048, 128], [1, 2048]])),
                            inc=d_gu[bu], waits=wld2)
                        t_wgu[icx] = tu_
                        mg = None
                        for d in range(DC):
                            if d == 0:
                                w2 = [tg_, t_tslp, TOK.get("D_scat_evac")] + t_gather
                            else:
                                w2 = []
                            mg = me("tensor", lambda e, d=d, q=q, bg=bg: e.matmul(
                                pb[2][:, 0:q], wgu5[bg][:, d * 128:(d + 1) * 128],
                                xgn_v[:, d, 0:q], start=(d == 0), stop=(d == DC - 1)),
                                inc=(s_pe if d == DC - 1 else None), waits=w2)
                        mu = None
                        for d in range(DC):
                            w2u = [tu_, t_h.get(icx - 1), TOK.get("D_scat_evac")] if d == 0 else []
                            mu = me("tensor", lambda e, d=d, q=q, bu=bu: e.matmul(
                                pb[3][:, 0:q], wgu5[bu][:, d * 128:(d + 1) * 128],
                                xgn_v[:, d, 0:q], start=(d == 0), stop=(d == DC - 1)),
                                inc=(s_pe if d == DC - 1 else None), waits=w2u)
                        tsl = me("scalar", lambda e, q=q: e.activation(
                            gs_d[:, 0:q], pb[2][:, 0:q], AF.Silu), inc=s_se,
                            waits=[mg, t_h.get(icx - 1)])
                        t_tslp = tsl
                        t_tsl_hist[icx] = tsl
                        t_h[icx] = me("vector", lambda e, icx=icx, q=q: e.tensor_tensor(
                            hT_e[:, icx, 0:q], pb[3][:, 0:q], gs_d[:, 0:q], AluOpType.mult),
                            inc=s_ve, waits=[mu, tsl] + ([TOK.get("D_hT_free")] if icx == 0 else []))
                    TOK[f"D_xgn_free_{ei}"] = t_h[IC - 1]
                    TOK["D_wgu_free_0"] = t_h[10]
                    TOK["D_wgu_free_1"] = t_h[10]
                    TOK["D_wgu_free_2"] = t_h[8]
                    TOK["D_wgu_free_3"] = t_h[9]
                    TOK["D_wgu_free_4"] = t_h[9]
                    TOK["D_tslp"] = t_tslp
                    t_gu_done = t_h[IC - 1]
                    t_ms = {}
                    for lc in range(ncl):
                        t_ms[lc] = me("vector", lambda e, gslot=gslot, lc=lc: e.memset(
                            y_pool[:, gslot + lc, :], 0.0), inc=s_ve,
                            waits=[t_sc_ev.get((gi - 1, 7, 3)), TOK.get("D_scat_mm"), t_gu_done])
                    mmq_tok = {}
                    for htq in range(4):
                        pq = htq % 2
                        if htq >= 2:
                            t_wdq[htq] = me("gpsimd", lambda e, e_idx=e_idx, htq=htq: e.dma_start(
                                wdq_flat[pq],
                                dap(wd_t, (e_idx * 4 + htq) * 128 * IC * 512, [[IC * 512, 128], [1, IC * 512]])),
                                inc=d_wd[htq % 2], waits=[mmq_tok.get(htq - 2)])
                        for lc in range(ncl):
                            cs = min(128, q - lc * 128)
                            bank = 4 + (lc % 2)
                            mm2 = None
                            for si in range(IC):
                                w3 = []
                                if si == 0:
                                    w3 = [t_wdq[htq], t_gu_done,
                                          t_y_ev.get((e_idx, htq, lc - 2)),
                                          t_y_ev.get((e_idx, htq - 1, lc + (1 if ncl % 2 else 0) - 1)),
                                          t_y_ev.get((e_idx, htq - 1, ncl - 1)),
                                          TOK.get("D_scat_evac")]
                                mm2 = me("tensor", lambda e, si=si, lc=lc, cs=cs, htq=htq, bank=bank: e.matmul(
                                    pb[bank][0:cs, :], hT_e[:, si, lc * 128:lc * 128 + cs],
                                    wdq_v[pq][:, si, :], start=(si == 0), stop=(si == IC - 1)),
                                    inc=(s_pe if si == IC - 1 else None), waits=w3)
                            if lc == ncl - 1:
                                mmq_tok[htq] = mm2
                            t_y_ev[(e_idx, htq, lc)] = me("vector", lambda e, gslot=gslot, lc=lc, htq=htq, cs=cs, bank=bank: e.tensor_copy(
                                y_pool[0:cs, gslot + lc, htq * 512:(htq + 1) * 512],
                                pb[bank][0:cs, :]), inc=s_ve, waits=[mm2, t_ms[lc]])
                    t_y_last = t_y_ev[(e_idx, 3, ncl - 1)]
                    TOK["D_wdq_free_0"] = mmq_tok[2]
                    TOK["D_wdq_free_1"] = mmq_tok[3]
                    TOK["D_hT_free"] = t_y_last
                    for lc in range(ncl):
                        gchunks.append((e_idx, lc, ch_off[e_idx] + lc, gslot + lc))
                    gslot += ncl
                if gi + 1 < len(groups):
                    do_gather(gi + 1)
                for ci, (e_idx, lc, gci, gsl) in enumerate(gchunks):
                    pr = ci % 1
                    t_sl = me("sync", lambda e, gci=gci: e.dma_start(
                        s_stage, dap(s_t, gci * 128 * NT, [[NT, 128], [1, NT]])),
                        inc=d_s, waits=[t_wcb.get((gi, ci - 1)) if ci > 0
                                        else TOK.get("D_wcb_last")])
                    t_wcb[(gi, ci)] = me("vector", lambda e, gci=gci, gsl=gsl: e.tensor_scalar(
                        wcb_v[:, gsl, :], s_stage, wc_s[:, gci:gci + 1], None, AluOpType.mult),
                        inc=s_ve, waits=[t_sl, TOK.get("wc"), TOK.get("D_scat_mm")])
                t_wcb_all = [t_wcb[(gi, ci)] for ci in range(len(gchunks))]
                TOK["D_wcb_last"] = t_wcb_all[-1]
                for tc in range(8):
                    mm3 = {}
                    for ci, (e_idx, lc, gci, gsl) in enumerate(gchunks):
                        for ht in range(4):
                            w4 = []
                            if tc == 0:
                                w4 = [t_wcb[(gi, ci)]]
                            if ci == 0:
                                w4 = w4 + [t_y_last, t_sc_ev.get((gi, tc - 1, ht))]
                            mm3[ht] = me("tensor", lambda e, gsl=gsl, tc=tc, ht=ht: e.matmul(
                                pb[2 + ht][:], wcb_v[:, gsl, tc * 128:(tc + 1) * 128],
                                y_pool[:, gsl, ht * 512:(ht + 1) * 512],
                                start=(ci == 0), stop=(ci == len(gchunks) - 1)),
                                inc=(s_pe if ci == len(gchunks) - 1 else None), waits=w4)
                    for ht in range(4):
                        t_sc_ev[(gi, tc, ht)] = me("vector", lambda e, tc=tc, ht=ht: e.tensor_tensor(
                            acc[:, tc, ht * 512:(ht + 1) * 512],
                            acc[:, tc, ht * 512:(ht + 1) * 512],
                            pb[2 + ht][:], AluOpType.add), inc=s_ve, waits=[mm3[ht]])
                    if tc == 7:
                        TOK["D_scat_mm"] = mm3[3]
                TOK["D_scat_evac"] = t_sc_ev[(gi, 7, 3)]
            TOK["D_done"] = t_sc_ev[(len(groups) - 1, 7, 3)]

        def phase_DBG(sch):
            me = sch.op
            kind, off = dbg
            t_st = None
            eng = "gpsimd" if kind == "bf" else "sync"
            for tc in range(8):
                if kind == "bf":
                    s_ap = abf[:, off + tc * 2048: off + (tc + 1) * 2048]
                else:
                    s_ap = af32[:, off + tc * 2048: off + (tc + 1) * 2048]
                t_st = me(eng, lambda e, s=s_ap, tc=tc: e.dma_start(
                    dap(out_t, tc * 128 * D, [[D, 128], [1, D]]), s), inc=d_o)
            sch.fence(eng, t_st)

        def phase_E(sch):
            me = sch.op
            t_st = None
            for tc in range(8):
                t_st = me("sync", lambda e, tc=tc: e.dma_start(
                    dap(out_t, tc * 128 * D, [[D, 128], [1, D]]), acc[:, tc, :]),
                    inc=d_o)
            sch.fence("sync", t_st)

        pmap = {"A": phase_A, "B1": phase_B1, "B2": phase_B2, "C": phase_C,
                "D": phase_D, "E": phase_E, "DBG": phase_DBG}
        for p in phases:
            run_phase(pmap[p])

    return nc


def _weight_maps(ins, plan):
    qpad = plan["qpad"]
    ch_off = plan["ch_off"]
    NCH = plan["nch"]
    f = np.float32
    wq = np.asarray(ins["L0_attn_w_q"], f)
    wk = np.asarray(ins["L0_attn_w_k"], f)
    wv = np.asarray(ins["L0_attn_w_v"], f)
    wo = np.asarray(ins["L0_attn_w_o"], f)
    m = {}
    m["wq_l"] = np.ascontiguousarray(wq.reshape(16, 128, 16, 128).transpose(0, 3, 2, 1)).reshape(16, 128, 2048)
    m["wk_l"] = np.ascontiguousarray(wk.reshape(16, 128, 16, 128).transpose(0, 3, 2, 1)).reshape(16, 128, 2048)
    m["wv_l"] = np.ascontiguousarray(wv.reshape(4, 512, 16, 128).transpose(0, 3, 2, 1)).reshape(4, 128, 16 * 512)
    m["wo_l"] = np.ascontiguousarray(wo.reshape(4, 512, 16, 128).transpose(0, 3, 2, 1)).reshape(4, 128, 16 * 512)
    sg = np.asarray(ins["L0_ffn_shared_gate"], f)
    su = np.asarray(ins["L0_ffn_shared_up"], f)
    sd = np.asarray(ins["L0_ffn_shared_down"], f)
    m["sg_l"] = np.ascontiguousarray(sg.reshape(16, 128, 22, 128).transpose(2, 1, 0, 3)).reshape(22, 128, 2048)
    m["su_l"] = np.ascontiguousarray(su.reshape(16, 128, 22, 128).transpose(2, 1, 0, 3)).reshape(22, 128, 2048)
    sd5 = sd.reshape(2, 11, 128, 4, 512)
    m["sd_l"] = np.ascontiguousarray(
        sd5.reshape(2, 11, 128, 2, 2, 512).transpose(0, 3, 2, 1, 4, 5)).reshape(2, 2, 128, 11 * 2 * 512)
    wg = np.asarray(ins["L0_ffn_w_gate"], f)
    wu = np.asarray(ins["L0_ffn_w_up"], f)
    wd = np.asarray(ins["L0_ffn_w_down"], f)
    m["wg_l"] = np.ascontiguousarray(wg.reshape(16, 16, 128, 11, 128).transpose(0, 3, 2, 1, 4)).reshape(16, 11, 128, 2048)
    m["wu_l"] = np.ascontiguousarray(wu.reshape(16, 16, 128, 11, 128).transpose(0, 3, 2, 1, 4)).reshape(16, 11, 128, 2048)
    wd6 = wd.reshape(16, 11, 128, 4, 512).transpose(0, 3, 2, 1, 4)
    m["wd_l"] = np.ascontiguousarray(wd6).reshape(16, 4, 128, 11 * 512)
    m["ident"] = np.eye(128).astype(BF)
    m["ones_col"] = np.ones((128, 1), BF)
    m["ones_row"] = np.ones((1, 128), BF)
    m["tiny"] = np.full((128, 1), 2.0 ** -40, np.float32)
    return m


def _core_maps(ins, plan, c):
    qpad = plan["qpad"]
    ch_off = plan["ch_off"]
    NCH = plan["nch"]
    chunk_e = plan["chunk_e"]
    toks = plan["core_tokens"][c]
    per_e = plan["core_exp"][c]
    x = np.asarray(ins["L0_attn_x"], np.float32).reshape(8192, D)
    b = c // 2
    bt = np.arange(b * 2048, (b + 1) * 2048)
    sib = np.array(sorted(set(bt.tolist()) - set(toks.tolist())), np.int64)
    mm = {}
    mm["x_own"] = np.ascontiguousarray(x[toks]).reshape(8, 128, D)
    mm["x_sib"] = np.ascontiguousarray(x[sib]).reshape(8, 128, D)
    gtot = sum(8 * 128 * q for q in qpad)
    g = np.zeros(gtot, BF)
    off = 0
    for e in range(NE):
        q = qpad[e]
        blk = np.zeros((128, 8, q), BF)
        for j, (lt, a, bb) in enumerate(per_e[e]):
            blk[lt % 128, lt // 128, j] = 1
        g[off:off + blk.size] = blk.ravel()
        off += blk.size
    mm["g_l"] = g
    s = np.zeros((NCH, 128, NT), BF)
    w32v = np.zeros((128, NCH), np.float32)
    w64v = np.zeros((128, NCH), np.float32)
    for e in range(NE):
        for j, (lt, a, bb) in enumerate(per_e[e]):
            gci = ch_off[e] + j // 128
            p = j % 128
            s[gci, p, lt] = 1
            w32v[p, gci] = a
            w64v[p, gci] = bb
    mm["s_l"] = s
    mm["w32v"] = w32v
    mm["w64v"] = w64v
    return mm


_NC_CACHE = {}


def kernel(**inputs):
    plan = _plan()
    key = "nc"
    if key not in _NC_CACHE:
        _NC_CACHE[key] = build_nc(plan, ("A", "B1", "B2", "C", "D", "E"))
    nc = _NC_CACHE[key]
    wm = _weight_maps(inputs, plan)
    in_maps = []
    for c in range(8):
        m = dict(wm)
        m.update(_core_maps(inputs, plan, c))
        in_maps.append(m)
    res = run_bass_kernel_spmd(nc, in_maps, core_ids=list(range(8)))
    out = np.zeros((8192, D), np.float32)
    for c in range(8):
        out[plan["core_tokens"][c]] = res.results[c]["out"].reshape(NT, D)
    return out.reshape(4, 2048, D)
